# version 04
# baseline (speedup 1.0000x reference)
import numpy as np
import ml_dtypes
from contextlib import ExitStack
import concourse.bass as bass
import concourse.bacc as bacc
import concourse.mybir as mybir
from concourse.bass_utils import run_bass_kernel_spmd
from concourse.library_config import mlp as mlp_lib

F32 = mybir.dt.float32
BF16 = mybir.dt.bfloat16
I16 = mybir.dt.int16
U32 = mybir.dt.uint32
AF = mybir.ActivationFunctionType
ALU = mybir.AluOpType
AX = mybir.AxisListType
f32 = np.float32
BF = ml_dtypes.bfloat16

NCORES = 8
D, NH, HD = 1024, 16, 64
B, T = 2, 8192
NTOK = B * T
TPC = NTOK // NCORES
ZM = 256
HID = 2048
EPS_RMS = 1.1920929e-07
EPS_LN = 1e-5
WM, TOPK, IDXD = 512, 64, 64
AL = T - WM
NCH = D // 128

ENGS = ("sync", "gpsimd", "tensor", "vector", "scalar")


class Builder:

    def __init__(self, nc):
        self.nc = nc
        self.phases = [[]]

    def op(self, eng, fn, dma=False):
        self.phases[-1].append((eng, fn, dma))

    def bar(self):
        if self.phases[-1]:
            self.phases.append([])

    def emit(self, block, sems):
        nph = len(self.phases)
        totals = {e: [0] * (nph + 1) for e in ENGS}
        for p, ph in enumerate(self.phases):
            add = {e: 0 for e in ENGS}
            for eng, fn, dma in ph:
                if dma:
                    add[eng] += 16
            has_compute = {e: any((eng == e and not dma) for eng, fn, dma in ph) for e in ENGS}
            for e in ENGS:
                totals[e][p + 1] = totals[e][p] + add[e] + (1 if has_compute[e] else 0)

        def mk(eng_name):
            def emitter(eng):
                waited = {e: 0 for e in ENGS}
                for p, ph in enumerate(self.phases):
                    mine = [(fn, dma) for e, fn, dma in ph if e == eng_name]
                    rest_later = any(
                        e == eng_name for pp in self.phases[p:] for e, _, _ in pp
                    )
                    if not mine and not rest_later:
                        break
                    if p > 0 and mine:
                        for e2 in ENGS:
                            tv = totals[e2][p]
                            if tv > waited[e2]:
                                eng.wait_ge(sems[e2], tv)
                                waited[e2] = tv
                    last_compute = None
                    for i, (fn, dma) in enumerate(mine):
                        if not dma:
                            last_compute = i
                    for i, (fn, dma) in enumerate(mine):
                        ins = fn(eng)
                        if dma:
                            ins.then_inc(sems[eng_name], 16)
                        elif i == last_compute:
                            if ins is None:
                                eng.nop().then_inc(sems[eng_name], 1)
                            else:
                                ins.then_inc(sems[eng_name], 1)
                for e2 in ENGS:
                    tv = totals[e2][nph]
                    if tv > waited[e2]:
                        eng.wait_ge(sems[e2], tv)
            return emitter

        block.sync(mk("sync"))
        block.gpsimd(mk("gpsimd"))
        block.tensor(mk("tensor"))
        block.vector(mk("vector"))
        block.scalar(mk("scalar"))


def host_prep(inputs):
    g = {}
    for name in ["wq", "wk", "wv", "wo", "gate_w", "e0_up", "e0_down", "e1_w1",
                 "e1_w2", "e1_down", "e2_l1", "e2_l2", "e2_l3", "e2_l4",
                 "e3_down", "e3_up", "e3_out"]:
        g["f_" + name] = np.ascontiguousarray(inputs["L0_ffn_" + name].T)
    for name in ["n1_w", "n2_w", "qn_w", "kn_w"]:
        g["f_" + name] = np.ascontiguousarray(inputs["L0_ffn_" + name].reshape(-1, 1))
    for name in ["wq_w", "wk_w", "wv_w", "wo_w", "wqi_w", "wki_w"]:
        g["a_" + name] = np.ascontiguousarray(inputs["L0_attn_" + name])
    for name in ["wq_b", "wk_b", "wv_b", "wo_b", "wqi_b", "wki_b"]:
        g["a_" + name] = np.ascontiguousarray(inputs["L0_attn_" + name].reshape(-1, 1))
    g["identity"] = np.eye(128, dtype=f32)
    return g



def emit_L1(nc, b, st, t, pfx=""):
    x_fm = t["x_fm"]
    wqT = t["wqT"]
    wkT = t["wkT"]
    wvT = t["wvT"]
    n1_w = t["n1_w"]
    qn_w = t["qn_w"]
    kn_w = t["kn_w"]
    o_lnx = t["o_lnx"]
    o_qT = t["o_qT"]
    o_kT = t["o_kT"]
    o_v = t["o_v"]

    NB = TPC // 512

    xs = st.enter_context(nc.sbuf_tensor(pfx + "xs", [128, NCH, 512], F32))
    xcast = st.enter_context(nc.sbuf_tensor(pfx + "xcast", [128, NCH, 512], BF16))
    hb = st.enter_context(nc.sbuf_tensor(pfx + "hb", [128, NCH, 512], BF16))
    lnxt = st.enter_context(nc.sbuf_tensor(pfx + "lnxt", [128, NCH, 512], F32))
    tmpc = st.enter_context(nc.sbuf_tensor(pfx + "tmpc", [128, NCH, 512], F32))
    wq_s = st.enter_context(nc.sbuf_tensor(pfx + "wq_s", [128, NCH, D], BF16))
    wk_s = st.enter_context(nc.sbuf_tensor(pfx + "wk_s", [128, NCH, D], BF16))
    wv_s = st.enter_context(nc.sbuf_tensor(pfx + "wv_s", [128, NCH, D], BF16))
    onesb = st.enter_context(nc.sbuf_tensor(pfx + "onesb", [128, 1], BF16))
    onesf = st.enter_context(nc.sbuf_tensor(pfx + "onesf", [128, 1], F32))
    onesr = st.enter_context(nc.sbuf_tensor(pfx + "onesr", [1, 128], F32))
    n1b = st.enter_context(nc.sbuf_tensor(pfx + "n1b", [128, NCH], F32))
    qn2 = st.enter_context(nc.sbuf_tensor(pfx + "qn2", [128, 1], F32))
    kn2 = st.enter_context(nc.sbuf_tensor(pfx + "kn2", [128, 1], F32))
    rows = st.enter_context(nc.sbuf_tensor(pfx + "rows", [1, 16 * 512], F32))
    rowsb = st.enter_context(nc.sbuf_tensor(pfx + "rowsb", [1, 4 * 512], F32))
    bcs = st.enter_context(nc.sbuf_tensor(pfx + "bcs", [128, 3, 512], F32))
    sq = st.enter_context(nc.sbuf_tensor(pfx + "sq", [128, 512], BF16))
    qtmp = st.enter_context(nc.sbuf_tensor(pfx + "qtmp", [128, 512], F32))
    bchh = st.enter_context(nc.sbuf_tensor(pfx + "bchh", [128, 512], F32))
    stage = st.enter_context(nc.sbuf_tensor(pfx + "stage", [128, 2, 512], BF16))
    qko = st.enter_context(nc.sbuf_tensor(pfx + "qko", [128, 512], BF16))
    ps_mm = st.enter_context(nc.psum_tensor(pfx + "ps_mm", [128, 512], F32))
    ps_mm2 = st.enter_context(nc.psum_tensor(pfx + "ps_mm2", [128, 512], F32))
    ps_r0 = st.enter_context(nc.psum_tensor(pfx + "ps_r0", [1, 512], F32))
    ps_r1 = st.enter_context(nc.psum_tensor(pfx + "ps_r1", [1, 512], F32))
    ps_r2 = st.enter_context(nc.psum_tensor(pfx + "ps_r2", [1, 512], F32))
    ps_r3 = st.enter_context(nc.psum_tensor(pfx + "ps_r3", [1, 512], F32))
    ps_bc = st.enter_context(nc.psum_tensor(pfx + "ps_bc", [128, 512], F32))
    ps_bch = st.enter_context(nc.psum_tensor(pfx + "ps_bch", [128, 512], F32))

    b.op("gpsimd", lambda g: g.dma_start(out=wq_s[:], in_=wqT.rearrange("(c p) o -> p c o", p=128)), dma=True)
    b.op("gpsimd", lambda g: g.dma_start(out=wk_s[:], in_=wkT.rearrange("(c p) o -> p c o", p=128)), dma=True)
    b.op("gpsimd", lambda g: g.dma_start(out=wv_s[:], in_=wvT.rearrange("(c p) o -> p c o", p=128)), dma=True)
    for c in range(NCH):
        b.op("sync", lambda s, c=c: s.dma_start(out=n1b[:, c:c + 1], in_=n1_w[c * 128:(c + 1) * 128, :]), dma=True)
    b.op("sync", lambda s: s.dma_start(out=qn2[0:64, :], in_=qn_w[:, :]), dma=True)
    b.op("sync", lambda s: s.dma_start(out=qn2[64:128, :], in_=qn_w[:, :]), dma=True)
    b.op("sync", lambda s: s.dma_start(out=kn2[0:64, :], in_=kn_w[:, :]), dma=True)
    b.op("sync", lambda s: s.dma_start(out=kn2[64:128, :], in_=kn_w[:, :]), dma=True)
    b.op("vector", lambda v: v.memset(onesb[:], 1.0))
    b.op("vector", lambda v: v.memset(onesf[:], 1.0))
    b.op("vector", lambda v: v.memset(onesr[:], 1.0))
    b.bar()

    for blk in range(NB):
        t0 = blk * 512
        b.op("sync", lambda s, t0=t0: s.dma_start(
            out=xs[:], in_=x_fm.rearrange("(c p) t -> p c t", p=128)[:, :, t0:t0 + 512]), dma=True)
        b.op("gpsimd", lambda g, t0=t0: g.dma_start(
            out=xcast[:], in_=x_fm.rearrange("(c p) t -> p c t", p=128)[:, :, t0:t0 + 512]), dma=True)
        b.bar()
        for c in range(NCH):
            b.op("scalar", lambda s, c=c: s.activation(out=hb[:, c, :], in_=xs[:, c, :], func=AF.Square))
        b.bar()
        for c in range(NCH):
            b.op("tensor", lambda t, c=c: t.matmul(ps_r0[0:1, :], onesb[:, 0:1], xcast[:, c, :],
                                                   start=(c == 0), stop=(c == NCH - 1)))
        for c in range(NCH):
            b.op("tensor", lambda t, c=c: t.matmul(ps_r1[0:1, :], onesb[:, 0:1], hb[:, c, :],
                                                   start=(c == 0), stop=(c == NCH - 1)))
        b.bar()
        b.op("scalar", lambda s: s.activation(out=rows[:, 0:512], in_=ps_r0[0:1, :],
                                              func=AF.Copy, scale=f32(1.0 / D)))
        b.op("scalar", lambda s: s.activation(out=rows[:, 512:1024], in_=ps_r1[0:1, :],
                                              func=AF.Copy, scale=f32(1.0 / D)))
        b.bar()
        b.op("vector", lambda v: v.tensor_tensor(out=rows[:, 1024:1536], in0=rows[:, 0:512],
                                                 in1=rows[:, 0:512], op=ALU.mult))
        b.bar()
        b.op("vector", lambda v: v.tensor_tensor(out=rows[:, 512:1024], in0=rows[:, 512:1024],
                                                 in1=rows[:, 1024:1536], op=ALU.subtract))
        b.bar()
        b.op("scalar", lambda s: s.activation(out=rows[:, 1024:1536], in_=rows[:, 512:1024],
                                              func=AF.Sqrt, bias=f32(EPS_LN)))
        b.bar()
        b.op("vector", lambda v: v.reciprocal(out=rows[:, 1024:1536], in_=rows[:, 1024:1536]))
        b.bar()
        b.op("vector", lambda v: v.tensor_tensor(out=rows[:, 1536:2048], in0=rows[:, 512:1024],
                                                 in1=rows[:, 1024:1536], op=ALU.mult))
        b.bar()
        b.op("vector", lambda v: v.tensor_tensor(out=rows[:, 1536:2048], in0=rows[:, 1536:2048],
                                                 in1=rows[:, 1024:1536], op=ALU.mult))
        b.bar()
        b.op("scalar", lambda s: s.activation(out=rowsb[:, 0:512], in_=rows[:, 1536:2048],
                                              func=AF.Sqrt, bias=f32(EPS_RMS)))
        b.bar()
        b.op("vector", lambda v: v.reciprocal(out=rowsb[:, 0:512], in_=rowsb[:, 0:512]))
        b.bar()
        b.op("vector", lambda v: v.tensor_tensor(out=rows[:, 1536:2048], in0=rowsb[:, 0:512],
                                                 in1=rows[:, 1024:1536], op=ALU.mult))
        b.bar()
        for j, off in enumerate([0, 1024, 1536]):
            b.op("tensor", lambda t, off=off: t.matmul(ps_bc[:, :], onesr[0:1, :], rows[0:1, off:off + 512],
                                                       start=True, stop=True))
            b.bar()
            b.op("vector", lambda v, j=j: v.tensor_copy(out=bcs[:, j, :], in_=ps_bc[:, :]))
            b.bar()
        for c in range(NCH):
            b.op("vector", lambda v, c=c: v.tensor_tensor(out=tmpc[:, c, :], in0=xs[:, c, :],
                                                          in1=bcs[:, 0, :], op=ALU.subtract))
        b.bar()
        for c in range(NCH):
            b.op("vector", lambda v, c=c: v.tensor_tensor(out=lnxt[:, c, :], in0=tmpc[:, c, :],
                                                          in1=bcs[:, 1, :], op=ALU.mult))
        for c in range(NCH):
            b.op("vector", lambda v, c=c: v.tensor_tensor(out=tmpc[:, c, :], in0=tmpc[:, c, :],
                                                          in1=bcs[:, 2, :], op=ALU.mult))
        b.bar()
        for c in range(NCH):
            b.op("sync", lambda s, c=c, t0=t0: s.dma_start(
                out=o_lnx[c * 128:(c + 1) * 128, t0:t0 + 512], in_=lnxt[:, c, :]), dma=True)
            b.op("vector", lambda v, c=c: v.tensor_scalar(
                out=hb[:, c, :], in0=tmpc[:, c, :], scalar1=n1b[:, c:c + 1], scalar2=None, op0=ALU.mult))
        b.bar()
        for which, w_s, nrm, outT, extra in (("q", wq_s, qn2, o_qT, 0.125),
                                            ("k", wk_s, kn2, o_kT, None)):
            for oc in range(NCH):
                b.op("tensor", lambda t, w_s=w_s, oc=oc: [
                    t.matmul(ps_mm[:, :], w_s[:, ic, oc * 128:(oc + 1) * 128], hb[:, ic, :],
                             start=(ic == 0), stop=(ic == NCH - 1))
                    for ic in range(NCH)][-1])
                b.bar()
                b.op("scalar", lambda s: s.activation(out=sq[:], in_=ps_mm[:, :], func=AF.Square))
                b.bar()
                b.op("tensor", lambda t: t.matmul(ps_r2[0:1, :], onesb[0:64, 0:1], sq[0:64, :],
                                                  start=True, stop=True))
                b.op("tensor", lambda t: t.matmul(ps_r3[0:1, :], onesb[64:128, 0:1], sq[64:128, :],
                                                  start=True, stop=True))
                b.bar()
                b.op("scalar", lambda s: s.activation(out=rowsb[:, 512:1024], in_=ps_r2[0:1, :],
                                                      func=AF.Sqrt, scale=f32(1.0 / HD), bias=f32(EPS_RMS)))
                b.op("scalar", lambda s: s.activation(out=rowsb[:, 1024:1536], in_=ps_r3[0:1, :],
                                                      func=AF.Sqrt, scale=f32(1.0 / HD), bias=f32(EPS_RMS)))
                b.bar()
                b.op("vector", lambda v: v.reciprocal(out=rowsb[:, 512:1536], in_=rowsb[:, 512:1536]))
                b.bar()
                b.op("tensor", lambda t: t.matmul(ps_bch[0:64, :], onesr[0:1, 0:64], rowsb[0:1, 512:1024],
                                                  start=True, stop=True))
                b.op("tensor", lambda t: t.matmul(ps_bch[64:128, :], onesr[0:1, 0:64], rowsb[0:1, 1024:1536],
                                                  start=True, stop=True))
                b.bar()
                b.op("vector", lambda v: v.tensor_copy(out=bchh[:, :], in_=ps_bch[:, :]))
                b.bar()
                b.op("vector", lambda v: v.tensor_tensor(out=qtmp[:, :], in0=ps_mm[:, :],
                                                         in1=bchh[:, :], op=ALU.mult))
                b.bar()
                if extra is not None:
                    b.op("vector", lambda v, nrm=nrm, extra=extra: v.tensor_scalar(
                        out=qko[:, :], in0=qtmp[:, :], scalar1=nrm[:, 0:1], scalar2=extra,
                        op0=ALU.mult, op1=ALU.mult))
                else:
                    b.op("vector", lambda v, nrm=nrm: v.tensor_scalar(
                        out=qko[:, :], in0=qtmp[:, :], scalar1=nrm[:, 0:1], scalar2=None, op0=ALU.mult))
                b.bar()
                b.op("sync", lambda s, outT=outT, oc=oc, t0=t0: s.dma_start(
                    out=outT[oc * 128:(oc + 1) * 128, t0:t0 + 512], in_=qko[:, :]), dma=True)
                b.bar()
        for tt in range(4):
            for ob in range(2):
                b.op("tensor", lambda t, tt=tt, ob=ob: [
                    t.matmul(ps_mm2[:, :], hb[:, ic, tt * 128:(tt + 1) * 128],
                             wv_s[:, ic, ob * 512:(ob + 1) * 512],
                             start=(ic == 0), stop=(ic == NCH - 1))
                    for ic in range(NCH)][-1])
                b.bar()
                b.op("scalar", lambda s, ob=ob: s.activation(out=stage[:, ob, :], in_=ps_mm2[:, :], func=AF.Copy))
                b.bar()
            b.op("sync", lambda s, tt=tt, t0=t0: s.dma_start(
                out=o_v[t0 + tt * 128: t0 + (tt + 1) * 128, :], in_=stage.ap().rearrange("p a t -> p (a t)")), dma=True)
            b.bar()







def build_L2():
    nc = bacc.Bacc("TRN2", target_bir_lowering=False)
    qT_p = nc.dram_tensor("qT_p", [4, HD, T], BF16, kind="ExternalInput").ap()
    kT_p = nc.dram_tensor("kT_p", [4, HD, T], BF16, kind="ExternalInput").ap()
    v_p = nc.dram_tensor("v_p", [4, T, HD], BF16, kind="ExternalInput").ap()
    cmask = nc.dram_tensor("cmask", [128, 4 * 512], BF16, kind="ExternalInput").ap()
    o_at = nc.dram_tensor("o_at", [4, T, HD], BF16, kind="ExternalOutput").ap()

    NQB = T // 512

    with ExitStack() as st:
        qTs = st.enter_context(nc.sbuf_tensor("qTs", [64, 4, T], BF16))
        kTs = st.enter_context(nc.sbuf_tensor("kTs", [64, 4, T], BF16))
        vaug = st.enter_context(nc.sbuf_tensor("vaug", [128, 4, T // 128, 65], BF16))
        cm = st.enter_context(nc.sbuf_tensor("cm", [128, 4, 512], BF16))
        onesb = st.enter_context(nc.sbuf_tensor("onesb", [128, 1], BF16))
        onesr = st.enter_context(nc.sbuf_tensor("onesr", [1, 128], F32))
        ksq = st.enter_context(nc.sbuf_tensor("ksq", [64, 512], BF16))
        rows = st.enter_context(nc.sbuf_tensor("rows", [1, T], F32))
        rows2 = st.enter_context(nc.sbuf_tensor("rows2", [1, 64], F32))
        mk1 = st.enter_context(nc.sbuf_tensor("mk1", [1, 1], F32))
        biasq = st.enter_context(nc.sbuf_tensor("biasq", [128, NQB], F32))
        pt = st.enter_context(nc.sbuf_tensor("pt", [128, 2, 512], BF16))
        rz = st.enter_context(nc.sbuf_tensor("rz", [128, 1], F32))
        ot = st.enter_context(nc.sbuf_tensor("ot", [128, 64], BF16))
        ps_s0 = st.enter_context(nc.psum_tensor("ps_s0", [128, 512], F32))
        ps_s1 = st.enter_context(nc.psum_tensor("ps_s1", [128, 512], F32))
        ps_o = [st.enter_context(nc.psum_tensor(f"ps_o{i}", [128, 65], F32)) for i in range(4)]
        ps_r = st.enter_context(nc.psum_tensor("ps_r", [1, 512], F32))
        ps_b = st.enter_context(nc.psum_tensor("ps_b", [128, NQB], F32))
        se_s = st.enter_context(nc.semaphore("se_s"))
        se_e = st.enter_context(nc.semaphore("se_e"))
        se_m = st.enter_context(nc.semaphore("se_m"))
        s_sync = st.enter_context(nc.semaphore("s_sync"))
        s_gpsimd = st.enter_context(nc.semaphore("s_gpsimd"))
        s_tensor = st.enter_context(nc.semaphore("s_tensor"))
        s_vector = st.enter_context(nc.semaphore("s_vector"))
        s_scalar = st.enter_context(nc.semaphore("s_scalar"))
        block = st.enter_context(nc.Block())
        sems = {"sync": s_sync, "gpsimd": s_gpsimd, "tensor": s_tensor,
                "vector": s_vector, "scalar": s_scalar}
        b = Builder(nc)
        ps_s = [ps_s0, ps_s1]

        b.op("sync", lambda s: s.dma_start(out=qTs[:], in_=qT_p.rearrange("a d t -> d a t")), dma=True)
        b.op("sync", lambda s: s.dma_start(out=kTs[:], in_=kT_p.rearrange("a d t -> d a t")), dma=True)
        b.op("sync", lambda s: s.dma_start(
            out=vaug[:, :, :, 0:64], in_=v_p.rearrange("a (n p) d -> p a n d", p=128)), dma=True)
        b.op("sync", lambda s: s.dma_start(out=cm[:], in_=cmask.rearrange("p (r q) -> p r q", r=4)), dma=True)
        b.op("vector", lambda v: v.memset(vaug[:, :, :, 64:65], 1.0))
        b.op("vector", lambda v: v.memset(onesb[:], 1.0))
        b.op("vector", lambda v: v.memset(onesr[:], 1.0))
        b.bar()

        cnt = {"s": 0, "e": 0, "m": 0}

        for pair in range(4):
            for c in range(16):
                b.op("scalar", lambda s, pair=pair, c=c: s.activation(
                    out=ksq[:, :], in_=kTs[:, pair, c * 512:(c + 1) * 512], func=AF.Square))
                b.bar()
                b.op("tensor", lambda t, c=c: t.matmul(ps_r[0:1, :], onesb[0:64, 0:1], ksq[:, :],
                                                       start=True, stop=True))
                b.bar()
                b.op("vector", lambda v, c=c: v.tensor_copy(out=rows[:, c * 512:(c + 1) * 512], in_=ps_r[0:1, :]))
                b.bar()
            b.op("vector", lambda v: v.tensor_reduce(out=mk1[:, :], in_=rows[:, :], axis=AX.X, op=ALU.max))
            b.bar()
            b.op("scalar", lambda s: s.activation(out=mk1[:, :], in_=mk1[:, :], func=AF.Sqrt))
            b.bar()
            for c in range(16):
                b.op("scalar", lambda s, pair=pair, c=c: s.activation(
                    out=ksq[:, :], in_=qTs[:, pair, c * 512:(c + 1) * 512], func=AF.Square))
                b.bar()
                b.op("tensor", lambda t: t.matmul(ps_r[0:1, :], onesb[0:64, 0:1], ksq[:, :],
                                                  start=True, stop=True))
                b.bar()
                b.op("vector", lambda v, c=c: v.tensor_copy(out=rows[:, c * 512:(c + 1) * 512], in_=ps_r[0:1, :]))
                b.bar()
            for qb in range(NQB):
                b.op("vector", lambda v, qb=qb: v.tensor_reduce(
                    out=rows2[:, qb:qb + 1], in_=rows[:, qb * 512:(qb + 1) * 512], axis=AX.X, op=ALU.max))
            b.bar()
            b.op("scalar", lambda s: s.activation(out=rows2[:, 0:NQB], in_=rows2[:, 0:NQB], func=AF.Sqrt))
            b.bar()
            b.op("vector", lambda v: v.tensor_scalar(
                out=rows2[:, 0:NQB], in0=rows2[:, 0:NQB], scalar1=mk1[0:1, 0:1], scalar2=-1.0,
                op0=ALU.mult, op1=ALU.mult))
            b.bar()
            b.op("tensor", lambda t: t.matmul(ps_b[:, :], onesr[0:1, :], rows2[0:1, 0:NQB],
                                              start=True, stop=True))
            b.bar()
            b.op("vector", lambda v: v.tensor_copy(out=biasq[:, :], in_=ps_b[:, :]))
            b.bar()

            for qb in range(NQB):
                nkt = 4 * qb + 4
                e_base = cnt["e"]

                def emit_qblock(pair=pair, qb=qb, nkt=nkt, e_base=e_base, cnt=cnt):
                    pass

                s_waits = []
                for kt in range(nkt):
                    need_e = e_base + kt - 1
                    wait_e = need_e if kt >= 2 else None
                    diag = kt >= 4 * qb
                    r = kt - 4 * qb
                    def s_fn(t, pair=pair, qb=qb, kt=kt, wait_e=wait_e):
                        if wait_e is not None:
                            t.wait_ge(se_e, wait_e)
                        return t.matmul(ps_s[kt % 2][:, :], kTs[:, pair, kt * 128:(kt + 1) * 128],
                                        qTs[:, pair, qb * 512:(qb + 1) * 512], start=True, stop=True)
                    b.op("tensor", lambda t, f=s_fn: (f(t).then_inc(se_s, 1), None)[1])
                    cnt["s"] += 1
                    def e_fn(s, pair=pair, qb=qb, kt=kt, sv=cnt["s"]):
                        s.wait_ge(se_s, sv)
                        return s.activation(out=pt[:, kt % 2, :], in_=ps_s[kt % 2][:, :], func=AF.Exp,
                                            bias=biasq[:, qb:qb + 1])
                    b.op("scalar", lambda s, f=e_fn: (f(s).then_inc(se_e, 1), None)[1])
                    cnt["e"] += 1
                    if diag:
                        def m_fn(v, kt=kt, r=r, ev=cnt["e"]):
                            v.wait_ge(se_e, ev)
                            return v.tensor_tensor(out=pt[:, kt % 2, :], in0=pt[:, kt % 2, :],
                                                   in1=cm[:, r, :], op=ALU.mult)
                        b.op("vector", lambda v, f=m_fn: (f(v).then_inc(se_m, 1), None)[1])
                        cnt["m"] += 1
                    wv = ("m", cnt["m"]) if diag else ("e", cnt["e"])
                    for qs in range(4):
                        def pv_fn(t, pair=pair, kt=kt, qs=qs, nkt=nkt, wv=wv):
                            if qs == 0:
                                t.wait_ge(se_m if wv[0] == "m" else se_e, wv[1])
                            return t.matmul(ps_o[qs][:, :], pt[:, kt % 2, qs * 128:(qs + 1) * 128],
                                            vaug[:, pair, kt, :], start=(kt == 0), stop=(kt == nkt - 1))
                        b.op("tensor", lambda t, f=pv_fn: f(t))
                b.bar()
                for qs in range(4):
                    b.op("scalar", lambda s, qs=qs: s.activation(out=rz[:, :], in_=ps_o[qs][:, 64:65],
                                                                 func=AF.Copy, bias=1e-30))
                    b.bar()
                    b.op("vector", lambda v: v.reciprocal(out=rz[:, :], in_=rz[:, :]))
                    b.bar()
                    b.op("vector", lambda v, qs=qs: v.tensor_scalar(
                        out=ot[:, :], in0=ps_o[qs][:, 0:64], scalar1=rz[:, 0:1], scalar2=None, op0=ALU.mult))
                    b.bar()
                    b.op("sync", lambda s, pair=pair, qb=qb, qs=qs: s.dma_start(
                        out=o_at[pair, qb * 512 + qs * 128: qb * 512 + (qs + 1) * 128, :], in_=ot[:, :]), dma=True)
                    b.bar()

        b.emit(block, sems)
    nc.finalize()
    return nc



def emit_L3(nc, b, st, t, pfx=""):
    at_fm = t["at_fm"]
    lnx_fm = t["lnx_fm"]
    x0_fm = t["x0_fm"]
    n2_w = t["n2_w"]
    esel = t["esel"]
    ident = t["ident"]
    o_x1 = t["o_x1"]
    wdr = {}
    for nm, shp in [("woT", [D, D]), ("gateT", [D, 4]),
                    ("e0_upT", [D, HID]), ("e0_downT", [HID, D]),
                    ("e1_w1T", [D, HID]), ("e1_w2T", [D, HID]), ("e1_downT", [HID, D]),
                    ("e2_l1T", [D, HID]), ("e2_l2T", [HID, HID]), ("e2_l3T", [HID, HID]),
                    ("e2_l4T", [HID, D]),
                    ("e3_downT", [D, D]), ("e3_upT", [D, HID]), ("e3_outT", [HID, D])]:
        wdr[nm] = t[nm]

    SB = 512
    NSB = TPC // SB
    NTT = SB // 128
    NNB = SB // 512

    xp = st.enter_context(nc.sbuf_tensor(pfx + "xp", [128, NCH, SB], F32))
    lnxs = st.enter_context(nc.sbuf_tensor(pfx + "lnxs", [128, NCH, SB], F32))
    ats = st.enter_context(nc.sbuf_tensor(pfx + "ats", [128, NCH, SB], BF16))
    xf = st.enter_context(nc.sbuf_tensor(pfx + "xf", [128, NCH, SB], BF16))
    ga = st.enter_context(nc.sbuf_tensor(pfx + "ga", [128, 16, SB], BF16))
    gb = st.enter_context(nc.sbuf_tensor(pfx + "gb", [128, 16, SB], BF16))
    wbig = st.enter_context(nc.sbuf_tensor(pfx + "wbig", [128, 16, 1024], BF16))
    wgate = st.enter_context(nc.sbuf_tensor(pfx + "wgate", [128, NCH, 4], BF16))
    scr = st.enter_context(nc.sbuf_tensor(pfx + "scr", [128, 512], F32))
    webc = st.enter_context(nc.sbuf_tensor(pfx + "webc", [128, 4, SB], F32))
    wrow = st.enter_context(nc.sbuf_tensor(pfx + "wrow", [4, SB], F32))
    esl = st.enter_context(nc.sbuf_tensor(pfx + "esl", [4, 512], F32))
    idn = st.enter_context(nc.sbuf_tensor(pfx + "idn", [128, 128], F32))
    n2b = st.enter_context(nc.sbuf_tensor(pfx + "n2b", [128, NCH], F32))
    onesb = st.enter_context(nc.sbuf_tensor(pfx + "onesb", [128, 1], BF16))
    onesr = st.enter_context(nc.sbuf_tensor(pfx + "onesr", [1, 128], F32))
    rows = st.enter_context(nc.sbuf_tensor(pfx + "rows", [1, SB], F32))
    bcn = st.enter_context(nc.sbuf_tensor(pfx + "bcn", [128, SB], F32))
    lg = st.enter_context(nc.sbuf_tensor(pfx + "lg", [128, NTT, 4], F32))
    gs = st.enter_context(nc.sbuf_tensor(pfx + "gs", [128, 8], F32))
    ps_a = st.enter_context(nc.psum_tensor(pfx + "ps_a", [128, 512], F32))
    ps_b = st.enter_context(nc.psum_tensor(pfx + "ps_b", [128, 512], F32))
    ps_r = st.enter_context(nc.psum_tensor(pfx + "ps_r", [1, 512], F32))
    ps_bc = st.enter_context(nc.psum_tensor(pfx + "ps_bc", [128, 512], F32))
    ps_g = st.enter_context(nc.psum_tensor(pfx + "ps_g", [128, 4], F32))
    ps_t = st.enter_context(nc.psum_tensor(pfx + "ps_t", [4, 128], F32))
    se_s = st.enter_context(nc.semaphore(pfx + "se_s"))
    se_e = st.enter_context(nc.semaphore(pfx + "se_e"))
    cnt = {"s": 0, "e": 0}
    psl = [ps_a, ps_b]

    b.op("sync", lambda s: s.dma_start(out=esl[:], in_=esel[:, :]), dma=True)
    b.op("sync", lambda s: s.dma_start(out=idn[:], in_=ident[:, :]), dma=True)
    b.op("gpsimd", lambda g: g.dma_start(out=wgate[:], in_=wdr["gateT"].rearrange("(c p) o -> p c o", p=128)), dma=True)
    for c in range(NCH):
        b.op("sync", lambda s, c=c: s.dma_start(out=n2b[:, c:c + 1], in_=n2_w[c * 128:(c + 1) * 128, :]), dma=True)
    b.op("vector", lambda v: v.memset(onesb[:], 1.0))
    b.op("vector", lambda v: v.memset(onesr[:], 1.0))
    b.bar()

    def load_w(name, icr, ocr):
        nic = icr[1] - icr[0]
        oc0, oc1 = ocr
        b.op("gpsimd", lambda g: g.dma_start(
            out=wbig[:, 0:nic, 0:(oc1 - oc0)],
            in_=wdr[name].rearrange("(c p) o -> p c o", p=128)[:, icr[0]:icr[1], oc0:oc1]), dma=True)

    def mm_block(src, oc_list, nic, dst_evac, extra_wait_first=False):
        units = [(oc, nb) for oc in oc_list for nb in range(NNB)]
        e0 = cnt["e"]
        for j, (oc, nb) in enumerate(units):
            ps = psl[j % 2]

            def t_fn(t, oc=oc, nb=nb, ps=ps, wv=(None if j < 2 else e0 + (j - 1))):
                if wv is not None:
                    t.wait_ge(se_e, wv)
                for ic in range(nic):
                    ins = t.matmul(ps[:, :], wbig[:, ic, oc * 128:(oc + 1) * 128],
                                   src[:, ic, nb * 512:(nb + 1) * 512],
                                   start=(ic == 0), stop=(ic == nic - 1))
                return ins
            b.op("tensor", lambda t, f=t_fn: (f(t).then_inc(se_s, 1), None)[1])
            cnt["s"] += 1
            eng, e_fn = dst_evac(j, oc, nb, ps)

            def wrap(e, f=e_fn, sv=cnt["s"]):
                e.wait_ge(se_s, sv)
                return f(e)
            b.op(eng, lambda e, f=wrap: (f(e).then_inc(se_e, 1), None)[1])
            cnt["e"] += 1
        b.bar()

    for sb in range(NSB):
        t0 = sb * SB

        b.op("sync", lambda s, t0=t0: s.dma_start(
            out=ats[:], in_=at_fm.rearrange("(c p) t -> p c t", p=128)[:, :, t0:t0 + SB]), dma=True)
        b.op("sync", lambda s, t0=t0: s.dma_start(
            out=lnxs[:], in_=lnx_fm.rearrange("(c p) t -> p c t", p=128)[:, :, t0:t0 + SB]), dma=True)
        load_w("woT", (0, 8), (0, 1024))
        b.bar()

        for oc in range(2):
            b.op("vector", lambda v, oc=oc: v.tensor_copy(out=xp[:, oc, :], in_=lnxs[:, oc, :]))

        def wo_evac(j, oc, nb, ps):
            def f(v, oc=oc, nb=nb, ps=ps):
                return v.tensor_tensor(out=xp[:, oc, nb * 512:(nb + 1) * 512], in0=ps[:, :],
                                       in1=lnxs[:, oc, nb * 512:(nb + 1) * 512], op=ALU.add)
            return ("vector", f)
        mm_block(ats, list(range(2, NCH)), NCH, wo_evac)

        for c in range(NCH):
            b.op("scalar", lambda s, c=c: s.activation(out=ga[:, c, :], in_=xp[:, c, :], func=AF.Square))
        b.bar()
        for nb in range(NNB):
            for c in range(NCH):
                b.op("tensor", lambda t, c=c, nb=nb: t.matmul(
                    ps_r[0:1, :], onesb[:, 0:1], ga[:, c, nb * 512:(nb + 1) * 512],
                    start=(c == 0), stop=(c == NCH - 1)))
            b.bar()
            b.op("scalar", lambda s, nb=nb: s.activation(out=rows[:, nb * 512:(nb + 1) * 512], in_=ps_r[0:1, :],
                                                         func=AF.Sqrt, scale=f32(1.0 / D), bias=f32(EPS_RMS)))
            b.bar()
        b.op("vector", lambda v: v.reciprocal(out=rows[:, :], in_=rows[:, :]))
        b.bar()
        for nb in range(NNB):
            b.op("tensor", lambda t, nb=nb: t.matmul(ps_bc[:, :], onesr[0:1, :], rows[0:1, nb * 512:(nb + 1) * 512],
                                                     start=True, stop=True))
            b.bar()
            b.op("vector", lambda v, nb=nb: v.tensor_copy(out=bcn[:, nb * 512:(nb + 1) * 512], in_=ps_bc[:, :]))
            b.bar()
        for c in range(NCH):
            for nb in range(NNB):
                b.op("vector", lambda v, c=c, nb=nb: v.tensor_tensor(
                    out=scr[:, :], in0=xp[:, c, nb * 512:(nb + 1) * 512],
                    in1=bcn[:, nb * 512:(nb + 1) * 512], op=ALU.mult))
                b.bar()
                b.op("vector", lambda v, c=c, nb=nb: v.tensor_scalar(
                    out=xf[:, c, nb * 512:(nb + 1) * 512], in0=scr[:, :],
                    scalar1=n2b[:, c:c + 1], scalar2=None, op0=ALU.mult))
                b.bar()

        b.op("sync", lambda s, t0=t0: s.dma_start(
            out=lnxs[:], in_=x0_fm.rearrange("(c p) t -> p c t", p=128)[:, :, t0:t0 + SB]), dma=True)
        b.bar()
        for c in range(NCH):
            b.op("vector", lambda v, c=c: v.scalar_tensor_tensor(
                out=xp[:, c, :], in0=xp[:, c, :], scalar=0.5, in1=lnxs[:, c, :],
                op0=ALU.mult, op1=ALU.add))
        b.bar()

        for tt in range(NTT):
            b.op("tensor", lambda t, tt=tt: [
                t.matmul(ps_g[:, :], xf[:, ic, tt * 128:(tt + 1) * 128], wgate[:, ic, :],
                         start=(ic == 0), stop=(ic == NCH - 1)) for ic in range(NCH)][-1])
            b.bar()
            b.op("vector", lambda v, tt=tt: v.tensor_copy(out=lg[:, tt, :], in_=ps_g[:, :]))
            b.bar()
        for tt in range(NTT):
            b.op("vector", lambda v, tt=tt: v.tensor_reduce(out=gs[:, 0:1], in_=lg[:, tt, :], axis=AX.X, op=ALU.max))
            b.bar()
            b.op("vector", lambda v: v.tensor_scalar(out=gs[:, 1:2], in0=gs[:, 0:1], scalar1=-1.0, scalar2=None, op0=ALU.mult))
            b.bar()
            b.op("scalar", lambda s, tt=tt: s.activation(out=lg[:, tt, :], in_=lg[:, tt, :], func=AF.Exp, bias=gs[:, 1:2]))
            b.bar()
            b.op("vector", lambda v, tt=tt: v.tensor_reduce(out=gs[:, 2:3], in_=lg[:, tt, :], axis=AX.X, op=ALU.add))
            b.bar()
            b.op("vector", lambda v: v.reciprocal(out=gs[:, 3:4], in_=gs[:, 2:3]))
            b.bar()
            b.op("vector", lambda v, tt=tt: v.tensor_scalar(out=lg[:, tt, :], in0=lg[:, tt, :], scalar1=gs[:, 3:4], scalar2=None, op0=ALU.mult))
            b.bar()
            b.op("vector", lambda v, tt=tt: v.tensor_reduce(out=gs[:, 4:5], in_=lg[:, tt, :], axis=AX.X, op=ALU.max))
            b.bar()
            b.op("vector", lambda v, tt=tt: v.tensor_scalar(out=scr[:, 0:4], in0=lg[:, tt, :], scalar1=gs[:, 4:5], scalar2=None, op0=ALU.is_equal))
            b.bar()
            b.op("vector", lambda v: v.tensor_scalar(out=scr[:, 4:8], in0=scr[:, 0:4], scalar1=-1.0, scalar2=1.0, op0=ALU.mult, op1=ALU.add))
            b.bar()
            b.op("vector", lambda v, tt=tt: v.tensor_tensor(out=scr[:, 8:12], in0=lg[:, tt, :], in1=scr[:, 4:8], op=ALU.mult))
            b.bar()
            b.op("vector", lambda v: v.tensor_reduce(out=gs[:, 5:6], in_=scr[:, 8:12], axis=AX.X, op=ALU.max))
            b.bar()
            b.op("vector", lambda v, tt=tt: v.tensor_scalar(out=scr[:, 12:16], in0=lg[:, tt, :], scalar1=gs[:, 5:6], scalar2=None, op0=ALU.is_ge))
            b.bar()
            b.op("vector", lambda v: v.tensor_tensor(out=gs[:, 6:7], in0=gs[:, 4:5], in1=gs[:, 5:6], op=ALU.add))
            b.bar()
            b.op("scalar", lambda s: s.activation(out=gs[:, 6:7], in_=gs[:, 6:7], func=AF.Copy, bias=1e-08))
            b.bar()
            b.op("vector", lambda v: v.reciprocal(out=gs[:, 7:8], in_=gs[:, 6:7]))
            b.bar()
            b.op("vector", lambda v, tt=tt: v.tensor_tensor(out=lg[:, tt, :], in0=lg[:, tt, :], in1=scr[:, 12:16], op=ALU.mult))
            b.bar()
            b.op("vector", lambda v, tt=tt: v.tensor_scalar(out=lg[:, tt, :], in0=lg[:, tt, :], scalar1=gs[:, 7:8], scalar2=None, op0=ALU.mult))
            b.bar()
        for tt in range(NTT):
            b.op("tensor", lambda t, tt=tt: t.transpose(ps_t[:, :], lg[:, tt, :], idn[:, :]))
            b.bar()
            b.op("vector", lambda v, tt=tt: v.tensor_copy(out=wrow[:, tt * 128:(tt + 1) * 128], in_=ps_t[:, :]))
            b.bar()
        for e in range(4):
            for nb in range(NNB):
                b.op("tensor", lambda t, e=e, nb=nb: t.matmul(
                    ps_bc[:, :], esl[:, e * 128:(e + 1) * 128], wrow[0:4, nb * 512:(nb + 1) * 512],
                    start=True, stop=True))
                b.bar()
                b.op("vector", lambda v, e=e, nb=nb: v.tensor_copy(
                    out=webc[:, e, nb * 512:(nb + 1) * 512], in_=ps_bc[:, :]))
                b.bar()

        def act_evac(dst, func):
            def mk(j, oc, nb, ps):
                def f(s, dst=dst, oc=oc, nb=nb, ps=ps, func=func):
                    return s.activation(out=dst[:, oc, nb * 512:(nb + 1) * 512], in_=ps[:, :], func=func)
                return ("scalar", f)
            return mk

        def macc_evac(e, first):
            def mk(j, oc, nb, ps):
                def f(v, e=e, oc=oc, nb=nb, ps=ps):
                    ins = v.tensor_tensor(out=scr[:, :], in0=ps[:, :],
                                          in1=webc[:, e, nb * 512:(nb + 1) * 512], op=ALU.mult)
                    v.scalar_tensor_tensor(out=xp[:, oc, nb * 512:(nb + 1) * 512], in0=scr[:, :],
                                           scalar=0.5, in1=xp[:, oc, nb * 512:(nb + 1) * 512],
                                           op0=ALU.mult, op1=ALU.add)
                    return ins
                return ("vector", f)
            return mk

        load_w("e0_upT", (0, 8), (0, 1024)); b.bar()
        mm_block(xf, list(range(8)), NCH, act_evac(ga, AF.Gelu))
        load_w("e0_upT", (0, 8), (1024, 2048)); b.bar()
        mm_block(xf, list(range(8)), NCH, lambda j, oc, nb, ps: act_evac(ga, AF.Gelu)(j, oc + 8, nb, ps))
        load_w("e0_downT", (0, 16), (0, 1024)); b.bar()
        mm_block(ga, list(range(2, NCH)), 16, macc_evac(0, True))
        load_w("e1_w1T", (0, 8), (0, 1024)); b.bar()
        mm_block(xf, list(range(8)), NCH, act_evac(ga, AF.Silu))
        load_w("e1_w1T", (0, 8), (1024, 2048)); b.bar()
        mm_block(xf, list(range(8)), NCH, lambda j, oc, nb, ps: act_evac(ga, AF.Silu)(j, oc + 8, nb, ps))
        load_w("e1_w2T", (0, 8), (0, 1024)); b.bar()
        mm_block(xf, list(range(8)), NCH, act_evac(gb, AF.Copy))
        load_w("e1_w2T", (0, 8), (1024, 2048)); b.bar()
        mm_block(xf, list(range(8)), NCH, lambda j, oc, nb, ps: act_evac(gb, AF.Copy)(j, oc + 8, nb, ps))
        for hc in range(16):
            b.op("vector", lambda v, hc=hc: v.tensor_tensor(out=ga[:, hc, :], in0=ga[:, hc, :], in1=gb[:, hc, :], op=ALU.mult))
        b.bar()
        load_w("e1_downT", (0, 16), (0, 1024)); b.bar()
        mm_block(ga, list(range(2, NCH)), 16, macc_evac(1, False))
        load_w("e2_l1T", (0, 8), (0, 1024)); b.bar()
        mm_block(xf, list(range(8)), NCH, act_evac(ga, AF.Gelu))
        load_w("e2_l1T", (0, 8), (1024, 2048)); b.bar()
        mm_block(xf, list(range(8)), NCH, lambda j, oc, nb, ps: act_evac(ga, AF.Gelu)(j, oc + 8, nb, ps))
        load_w("e2_l2T", (0, 16), (0, 1024)); b.bar()
        mm_block(ga, list(range(8)), 16, act_evac(gb, AF.Gelu))
        load_w("e2_l2T", (0, 16), (1024, 2048)); b.bar()
        mm_block(ga, list(range(8)), 16, lambda j, oc, nb, ps: act_evac(gb, AF.Gelu)(j, oc + 8, nb, ps))
        load_w("e2_l3T", (0, 16), (0, 1024)); b.bar()
        mm_block(gb, list(range(8)), 16, act_evac(ga, AF.Gelu))
        load_w("e2_l3T", (0, 16), (1024, 2048)); b.bar()
        mm_block(gb, list(range(8)), 16, lambda j, oc, nb, ps: act_evac(ga, AF.Gelu)(j, oc + 8, nb, ps))
        load_w("e2_l4T", (0, 16), (0, 1024)); b.bar()
        mm_block(ga, list(range(2, NCH)), 16, macc_evac(2, False))
        load_w("e3_downT", (0, 8), (0, 1024)); b.bar()
        mm_block(xf, list(range(8)), NCH, act_evac(ga, AF.Gelu))
        load_w("e3_upT", (0, 8), (0, 1024)); b.bar()
        mm_block(ga, list(range(8)), 8, act_evac(gb, AF.Gelu))
        load_w("e3_upT", (0, 8), (1024, 2048)); b.bar()
        mm_block(ga, list(range(8)), 8, lambda j, oc, nb, ps: act_evac(gb, AF.Gelu)(j, oc + 8, nb, ps))
        load_w("e3_outT", (0, 16), (0, 1024)); b.bar()
        mm_block(gb, list(range(2, NCH)), 16, macc_evac(3, False))

        for c in range(NCH):
            b.op("sync", lambda s, c=c, t0=t0: s.dma_start(
                out=o_x1[c * 128:(c + 1) * 128, t0:t0 + SB], in_=xp[:, c, :]), dma=True)
        b.bar()






def emit_L4a(nc, b, st, t, pfx=""):
    x_fm = t["x_fm"]
    wq_w = t["wq_w"]
    wk_w = t["wk_w"]
    wv_w = t["wv_w"]
    wqi_w = t["wqi_w"]
    wki_w = t["wki_w"]
    wq_b = t["wq_b"]
    wk_b = t["wk_b"]
    wv_b = t["wv_b"]
    wqi_b = t["wqi_b"]
    wki_b = t["wki_b"]
    o_qT = t["o_qT"]
    o_kv = t["o_kv"]
    o_qi = t["o_qi"]
    o_ki = t["o_ki"]

    NB = TPC // 512

    xs = st.enter_context(nc.sbuf_tensor(pfx + "xs", [128, NCH, 512], F32))
    xbb = st.enter_context(nc.sbuf_tensor(pfx + "xbb", [128, NCH, 512], BF16))
    lnf = st.enter_context(nc.sbuf_tensor(pfx + "lnf", [128, NCH, 512], F32))
    lnb = st.enter_context(nc.sbuf_tensor(pfx + "lnb", [128, NCH, 512], BF16))
    wqs = st.enter_context(nc.sbuf_tensor(pfx + "wqs", [128, NCH, D], BF16))
    wks = st.enter_context(nc.sbuf_tensor(pfx + "wks", [128, NCH, D], BF16))
    wvs = st.enter_context(nc.sbuf_tensor(pfx + "wvs", [128, NCH, D], BF16))
    wqis = st.enter_context(nc.sbuf_tensor(pfx + "wqis", [128, NCH, IDXD], F32))
    wkis = st.enter_context(nc.sbuf_tensor(pfx + "wkis", [128, NCH, IDXD], F32))
    qbias = st.enter_context(nc.sbuf_tensor(pfx + "qbias", [128, NCH], F32))
    qibias = st.enter_context(nc.sbuf_tensor(pfx + "qibias", [64, 1], F32))
    kibias = st.enter_context(nc.sbuf_tensor(pfx + "kibias", [64, 1], F32))
    kbrow = st.enter_context(nc.sbuf_tensor(pfx + "kbrow", [1, D], F32))
    vbrow = st.enter_context(nc.sbuf_tensor(pfx + "vbrow", [1, D], F32))
    kvbc = st.enter_context(nc.sbuf_tensor(pfx + "kvbc", [128, 2, D], F32))
    onesb = st.enter_context(nc.sbuf_tensor(pfx + "onesb", [128, 1], BF16))
    onesr = st.enter_context(nc.sbuf_tensor(pfx + "onesr", [1, 128], F32))
    rows = st.enter_context(nc.sbuf_tensor(pfx + "rows", [1, 4 * 512], F32))
    bcs = st.enter_context(nc.sbuf_tensor(pfx + "bcs", [128, 2, 512], F32))
    sqv = st.enter_context(nc.sbuf_tensor(pfx + "sqv", [128, 512], BF16))
    qko = st.enter_context(nc.sbuf_tensor(pfx + "qko", [128, 512], BF16))
    kvstage = st.enter_context(nc.sbuf_tensor(pfx + "kvstage", [128, 4, 512], BF16))
    idxo = st.enter_context(nc.sbuf_tensor(pfx + "idxo", [64, 2, 512], F32))
    ps_a = st.enter_context(nc.psum_tensor(pfx + "ps_a", [128, 512], F32))
    ps_b = st.enter_context(nc.psum_tensor(pfx + "ps_b", [128, 512], F32))
    ps_r0 = st.enter_context(nc.psum_tensor(pfx + "ps_r0", [1, 512], F32))
    ps_r1 = st.enter_context(nc.psum_tensor(pfx + "ps_r1", [1, 512], F32))
    ps_bc = st.enter_context(nc.psum_tensor(pfx + "ps_bc", [128, 512], F32))
    ps_i = st.enter_context(nc.psum_tensor(pfx + "ps_i", [64, 512], F32))

    b.op("gpsimd", lambda g: g.dma_start(out=wqs[:], in_=wq_w.rearrange("(c p) o -> p c o", p=128)), dma=True)
    b.op("gpsimd", lambda g: g.dma_start(out=wks[:], in_=wk_w.rearrange("(c p) o -> p c o", p=128)), dma=True)
    b.op("gpsimd", lambda g: g.dma_start(out=wvs[:], in_=wv_w.rearrange("(c p) o -> p c o", p=128)), dma=True)
    b.op("sync", lambda s: s.dma_start(out=wqis[:], in_=wqi_w.rearrange("(c p) o -> p c o", p=128)), dma=True)
    b.op("sync", lambda s: s.dma_start(out=wkis[:], in_=wki_w.rearrange("(c p) o -> p c o", p=128)), dma=True)
    for c in range(NCH):
        b.op("sync", lambda s, c=c: s.dma_start(out=qbias[:, c:c + 1],
                                                in_=wq_b.rearrange("o d -> d o")[c * 128:(c + 1) * 128, :]), dma=True)
    b.op("sync", lambda s: s.dma_start(out=qibias[:], in_=wqi_b.rearrange("o d -> d o")), dma=True)
    b.op("sync", lambda s: s.dma_start(out=kibias[:], in_=wki_b.rearrange("o d -> d o")), dma=True)
    b.op("sync", lambda s: s.dma_start(out=kbrow[:], in_=wk_b[:, :]), dma=True)
    b.op("sync", lambda s: s.dma_start(out=vbrow[:], in_=wv_b[:, :]), dma=True)
    b.op("vector", lambda v: v.memset(onesb[:], 1.0))
    b.op("vector", lambda v: v.memset(onesr[:], 1.0))
    b.bar()
    for j, br in enumerate([kbrow, vbrow]):
        for nb2 in range(2):
            b.op("tensor", lambda t, br=br, nb2=nb2: t.matmul(
                ps_bc[:, :], onesr[0:1, :], br[0:1, nb2 * 512:(nb2 + 1) * 512], start=True, stop=True))
            b.bar()
            b.op("vector", lambda v, j=j, nb2=nb2: v.tensor_copy(
                out=kvbc[:, j, nb2 * 512:(nb2 + 1) * 512], in_=ps_bc[:, :]))
            b.bar()

    for blk in range(NB):
        t0 = blk * 512
        b.op("sync", lambda s, t0=t0: s.dma_start(
            out=xs[:], in_=x_fm.rearrange("(c p) t -> p c t", p=128)[:, :, t0:t0 + 512]), dma=True)
        b.op("gpsimd", lambda g, t0=t0: g.dma_start(
            out=xbb[:], in_=x_fm.rearrange("(c p) t -> p c t", p=128)[:, :, t0:t0 + 512]), dma=True)
        b.bar()
        for c in range(NCH):
            b.op("scalar", lambda s, c=c: s.activation(out=sqv[:, :] if False else lnb[:, c, :],
                                                       in_=xs[:, c, :], func=AF.Square))
        b.bar()
        for c in range(NCH):
            b.op("tensor", lambda t, c=c: t.matmul(ps_r0[0:1, :], onesb[:, 0:1], xbb[:, c, :],
                                                   start=(c == 0), stop=(c == NCH - 1)))
        for c in range(NCH):
            b.op("tensor", lambda t, c=c: t.matmul(ps_r1[0:1, :], onesb[:, 0:1], lnb[:, c, :],
                                                   start=(c == 0), stop=(c == NCH - 1)))
        b.bar()
        b.op("scalar", lambda s: s.activation(out=rows[:, 0:512], in_=ps_r0[0:1, :], func=AF.Copy, scale=f32(1.0 / D)))
        b.op("scalar", lambda s: s.activation(out=rows[:, 512:1024], in_=ps_r1[0:1, :], func=AF.Copy, scale=f32(1.0 / D)))
        b.bar()
        b.op("vector", lambda v: v.tensor_tensor(out=rows[:, 1024:1536], in0=rows[:, 0:512], in1=rows[:, 0:512], op=ALU.mult))
        b.bar()
        b.op("vector", lambda v: v.tensor_tensor(out=rows[:, 512:1024], in0=rows[:, 512:1024], in1=rows[:, 1024:1536], op=ALU.subtract))
        b.bar()
        b.op("scalar", lambda s: s.activation(out=rows[:, 1024:1536], in_=rows[:, 512:1024], func=AF.Sqrt, bias=f32(EPS_LN)))
        b.bar()
        b.op("vector", lambda v: v.reciprocal(out=rows[:, 1024:1536], in_=rows[:, 1024:1536]))
        b.bar()
        for j, off in enumerate([0, 1024]):
            b.op("tensor", lambda t, off=off: t.matmul(ps_bc[:, :], onesr[0:1, :], rows[0:1, off:off + 512],
                                                       start=True, stop=True))
            b.bar()
            b.op("vector", lambda v, j=j: v.tensor_copy(out=bcs[:, j, :], in_=ps_bc[:, :]))
            b.bar()
        for c in range(NCH):
            b.op("vector", lambda v, c=c: v.tensor_tensor(out=lnf[:, c, :], in0=xs[:, c, :], in1=bcs[:, 0, :], op=ALU.subtract))
        b.bar()
        for c in range(NCH):
            b.op("vector", lambda v, c=c: v.tensor_tensor(out=lnf[:, c, :], in0=lnf[:, c, :], in1=bcs[:, 1, :], op=ALU.mult))
        b.bar()
        for c in range(NCH):
            b.op("vector", lambda v, c=c: v.tensor_copy(out=lnb[:, c, :], in_=lnf[:, c, :]))
        b.bar()
        for oc in range(NCH):
            b.op("tensor", lambda t, oc=oc: [
                t.matmul(ps_a[:, :], wqs[:, ic, oc * 128:(oc + 1) * 128], lnb[:, ic, :],
                         start=(ic == 0), stop=(ic == NCH - 1)) for ic in range(NCH)][-1])
            b.bar()
            b.op("vector", lambda v, oc=oc: v.tensor_scalar(out=qko[:, :], in0=ps_a[:, :],
                                                            scalar1=qbias[:, oc:oc + 1], scalar2=0.125,
                                                            op0=ALU.add, op1=ALU.mult))
            b.bar()
            b.op("sync", lambda s, oc=oc, t0=t0: s.dma_start(
                out=o_qT[oc * 128:(oc + 1) * 128, t0:t0 + 512], in_=qko[:, :]), dma=True)
            b.bar()
        for tt in range(4):
            for j, ws in enumerate([wks, wvs]):
                for ob in range(2):
                    b.op("tensor", lambda t, tt=tt, ws=ws, ob=ob: [
                        t.matmul(ps_b[:, :], lnb[:, ic, tt * 128:(tt + 1) * 128],
                                 ws[:, ic, ob * 512:(ob + 1) * 512],
                                 start=(ic == 0), stop=(ic == NCH - 1)) for ic in range(NCH)][-1])
                    b.bar()
                    b.op("vector", lambda v, j=j, ob=ob: v.tensor_tensor(
                        out=kvstage[:, j * 2 + ob, :], in0=ps_b[:, :],
                        in1=kvbc[:, j, ob * 512:(ob + 1) * 512], op=ALU.add))
                    b.bar()
            b.op("sync", lambda s, tt=tt, t0=t0: s.dma_start(
                out=o_kv[t0 + tt * 128: t0 + (tt + 1) * 128, :],
                in_=kvstage.ap().rearrange("p a t -> p (a t)")), dma=True)
            b.bar()
        for j, (ws, bia, oo) in enumerate([(wqis, qibias, o_qi), (wkis, kibias, o_ki)]):
            b.op("tensor", lambda t, ws=ws: [
                t.matmul(ps_i[:, :], ws[:, ic, :], lnf[:, ic, :],
                         start=(ic == 0), stop=(ic == NCH - 1)) for ic in range(NCH)][-1])
            b.bar()
            b.op("vector", lambda v, bia=bia: v.tensor_scalar(out=idxo[:, 0, :], in0=ps_i[:, :],
                                                              scalar1=bia[:, 0:1], scalar2=None, op0=ALU.add))
            b.bar()
            b.op("sync", lambda s, oo=oo, t0=t0: s.dma_start(
                out=oo[:, t0:t0 + 512], in_=idxo[:, 0, :]), dma=True)
            b.bar()






L1_DRAMS = [("x_fm", [D, TPC], F32, "ExternalInput"),
            ("wqT", [D, D], F32, "ExternalInput"),
            ("wkT", [D, D], F32, "ExternalInput"),
            ("wvT", [D, D], F32, "ExternalInput"),
            ("n1_w", [D, 1], F32, "ExternalInput"),
            ("qn_w", [HD, 1], F32, "ExternalInput"),
            ("kn_w", [HD, 1], F32, "ExternalInput"),
            ("o_lnx", [D, TPC], F32, "ExternalOutput"),
            ("o_qT", [D, TPC], BF16, "ExternalOutput"),
            ("o_kT", [D, TPC], BF16, "ExternalOutput"),
            ("o_v", [TPC, D], BF16, "ExternalOutput")]

L3_W_SHAPES = [("woT", [D, D]), ("gateT", [D, 4]),
               ("e0_upT", [D, HID]), ("e0_downT", [HID, D]),
               ("e1_w1T", [D, HID]), ("e1_w2T", [D, HID]), ("e1_downT", [HID, D]),
               ("e2_l1T", [D, HID]), ("e2_l2T", [HID, HID]), ("e2_l3T", [HID, HID]),
               ("e2_l4T", [HID, D]),
               ("e3_downT", [D, D]), ("e3_upT", [D, HID]), ("e3_outT", [HID, D])]

L3_DRAMS = ([("at_fm", [D, TPC], BF16, "ExternalInput"),
             ("lnx_fm", [D, TPC], F32, "ExternalInput"),
             ("x0_fm", [D, TPC], F32, "ExternalInput"),
             ("n2_w", [D, 1], F32, "ExternalInput"),
             ("esel", [4, 512], F32, "ExternalInput"),
             ("ident", [128, 128], F32, "ExternalInput"),
             ("o_x1", [D, TPC], F32, "ExternalOutput")]
            + [(nm, shp, F32, "ExternalInput") for nm, shp in L3_W_SHAPES])

L4A_DRAMS = [("x_fm", [D, TPC], F32, "ExternalInput"),
             ("wq_w", [D, D], F32, "ExternalInput"),
             ("wk_w", [D, D], F32, "ExternalInput"),
             ("wv_w", [D, D], F32, "ExternalInput"),
             ("wqi_w", [D, IDXD], F32, "ExternalInput"),
             ("wki_w", [D, IDXD], F32, "ExternalInput"),
             ("wq_b", [1, D], F32, "ExternalInput"),
             ("wk_b", [1, D], F32, "ExternalInput"),
             ("wv_b", [1, D], F32, "ExternalInput"),
             ("wqi_b", [1, IDXD], F32, "ExternalInput"),
             ("wki_b", [1, IDXD], F32, "ExternalInput"),
             ("o_qT", [D, TPC], BF16, "ExternalOutput"),
             ("o_kv", [TPC, 2 * D], BF16, "ExternalOutput"),
             ("o_qi", [IDXD, TPC], F32, "ExternalOutput"),
             ("o_ki", [IDXD, TPC], F32, "ExternalOutput")]


def _mk_build(emit_fn, drams, pfx):
    def build():
        nc = bacc.Bacc("TRN2", target_bir_lowering=False)
        t = {nm: nc.dram_tensor(nm, shp, dt, kind=kd).ap() for nm, shp, dt, kd in drams}
        with ExitStack() as st:
            block = st.enter_context(nc.Block())
            sems = {e: st.enter_context(nc.semaphore("s_" + e)) for e in ENGS}
            b = Builder(nc)
            with ExitStack() as sst:
                emit_fn(nc, b, sst, t, pfx)
            b.emit(block, sems)
        nc.finalize()
        return nc
    return build


build_L1 = _mk_build(emit_L1, L1_DRAMS, "l1_")
build_L3 = _mk_build(emit_L3, L3_DRAMS, "l3_")
build_L4a = _mk_build(emit_L4a, L4A_DRAMS, "l4a_")


def emit_L4b(nc, b, st, t, pfx="", nqt=None):
    qT_d = t["qT_d"]
    qiT_d = t["qiT_d"]
    g_ki = t["g_ki"]
    kv_d = t["kv_d"]
    x1_d = t["x1_d"]
    wo_w = t["wo_w"]
    wo_b = t["wo_b"]
    ident = t["ident"]
    identb = t["identb"]
    cbase = t["cbase"]
    o_x2 = t["o_x2"]
    bK = t["bK"]
    bV = t["bV"]
    vaDR = t["vaDR"]

    NQT = TPC // 128
    NAB = AL // 512

    qts = st.enter_context(nc.sbuf_tensor(pfx + "qts", [128, NCH, 128], BF16))
    qis = st.enter_context(nc.sbuf_tensor(pfx + "qis", [64, 128], F32))
    kis = st.enter_context(nc.sbuf_tensor(pfx + "kis", [64, AL], F32))
    sidx = st.enter_context(nc.sbuf_tensor(pfx + "sidx", [128, AL], F32))
    cand = st.enter_context(nc.sbuf_tensor(pfx + "cand", [128, 960], F32))
    candg = st.enter_context(nc.sbuf_tensor(pfx + "candg", [128, 960], F32))
    candi = st.enter_context(nc.sbuf_tensor(pfx + "candi", [128, 960], U32))
    cbs = st.enter_context(nc.sbuf_tensor(pfx + "cbs", [128, 960], F32))
    tv = st.enter_context(nc.sbuf_tensor(pfx + "tv", [128, 64], F32))
    picks = st.enter_context(nc.sbuf_tensor(pfx + "picks", [128, 64], F32))
    xkv = st.enter_context(nc.sbuf_tensor(pfx + "xkv", [128, 2, 64], F32))
    xkvi = st.enter_context(nc.sbuf_tensor(pfx + "xkvi", [128, 2, 64], I16))
    idxk = st.enter_context(nc.sbuf_tensor(pfx + "idxk", [128, 512], I16))
    idxv = st.enter_context(nc.sbuf_tensor(pfx + "idxv", [128, 512], I16))
    gkt = st.enter_context(nc.sbuf_tensor(pfx + "gkt", [128, 8, 512], BF16))
    gv = st.enter_context(nc.sbuf_tensor(pfx + "gv", [128, 8, 1024], BF16))
    kwTs = st.enter_context(nc.sbuf_tensor(pfx + "kwTs", [128, NCH, WM], BF16))
    vws = st.enter_context(nc.sbuf_tensor(pfx + "vws", [128, 4, D], BF16))
    wobr = st.enter_context(nc.sbuf_tensor(pfx + "wobr", [128, NCH], F32))
    pa = st.enter_context(nc.sbuf_tensor(pfx + "pa", [128, 256], BF16))
    pwt = st.enter_context(nc.sbuf_tensor(pfx + "pwt", [128, 8, 4, 128], BF16))
    vas = st.enter_context(nc.sbuf_tensor(pfx + "vas", [64, 2048], F32))
    vtr = st.enter_context(nc.sbuf_tensor(pfx + "vtr", [128, 64], F32))
    vaE = st.enter_context(nc.sbuf_tensor(pfx + "vaE", [64, 1024], F32))
    vaO = st.enter_context(nc.sbuf_tensor(pfx + "vaO", [64, 1024], F32))
    atw = st.enter_context(nc.sbuf_tensor(pfx + "atw", [64, 2, 1024], F32))
    attn_fm = st.enter_context(nc.sbuf_tensor(pfx + "attn_fm", [128, NCH, 128], BF16))
    zrow = st.enter_context(nc.sbuf_tensor(pfx + "zrow", [1, 2048], F32))
    zh = st.enter_context(nc.sbuf_tensor(pfx + "zh", [1, 2048], F32))
    rzp = st.enter_context(nc.sbuf_tensor(pfx + "rzp", [64, 2, 16], F32))
    mkcol = st.enter_context(nc.sbuf_tensor(pfx + "mkcol", [128, 80], F32))
    mks = st.enter_context(nc.sbuf_tensor(pfx + "mks", [1, 8], F32))
    ebias = st.enter_context(nc.sbuf_tensor(pfx + "ebias", [128, 1], F32))
    ksqb = st.enter_context(nc.sbuf_tensor(pfx + "ksqb", [128, 1024], BF16))
    x1s = st.enter_context(nc.sbuf_tensor(pfx + "x1s", [128, NCH, 128], F32))
    x2s = st.enter_context(nc.sbuf_tensor(pfx + "x2s", [128, NCH, 128], F32))
    idn = st.enter_context(nc.sbuf_tensor(pfx + "idn", [128, 128], F32))
    idnb = st.enter_context(nc.sbuf_tensor(pfx + "idnb", [128, 128], BF16))
    onesb = st.enter_context(nc.sbuf_tensor(pfx + "onesb", [128, 1], BF16))
    onesr = st.enter_context(nc.sbuf_tensor(pfx + "onesr", [1, 128], F32))
    ps_a = st.enter_context(nc.psum_tensor(pfx + "ps_a", [128, 512], F32))
    ps_b = st.enter_context(nc.psum_tensor(pfx + "ps_b", [128, 512], F32))
    ps_sc = st.enter_context(nc.psum_tensor(pfx + "ps_sc", [128, 512], F32))
    ps_va = st.enter_context(nc.psum_tensor(pfx + "ps_va", [64, 512], F32))
    ps_owE = st.enter_context(nc.psum_tensor(pfx + "ps_owE", [64, 1024], F32))
    ps_owO = st.enter_context(nc.psum_tensor(pfx + "ps_owO", [64, 1024], F32))
    se_v = st.enter_context(nc.semaphore(pfx + "se_v"))
    cnt = {"v": 0}

    def vop(fn):
        w = cnt["v"]

        def g(v, fn=fn, w=w):
            if w > 0:
                v.wait_ge(se_v, w)
            return fn(v)
        b.op("vector", lambda v, g=g: (g(v).then_inc(se_v, 1), None)[1])
        cnt["v"] += 1

    b.op("gpsimd", lambda g: g.dma_start(out=idnb[:], in_=identb[:, :]), dma=True)
    b.op("sync", lambda s: s.dma_start(out=kis[:, 0:1536], in_=g_ki[0:IDXD, 0:1536]), dma=True)
    for s2 in range(1, 4):
        b.op("sync", lambda s, s2=s2: s.dma_start(
            out=kis[:, 1536 + 2048 * (s2 - 1):1536 + 2048 * s2],
            in_=g_ki[IDXD * s2:IDXD * (s2 + 1), :]), dma=True)
    b.op("sync", lambda s: s.dma_start(
        out=vws[:], in_=kv_d[1536:2048, D:2 * D].rearrange("(n p) d -> p n d", p=128)), dma=True)
    b.op("sync", lambda s: s.dma_start(out=cbs[:], in_=cbase[:, :]), dma=True)
    b.op("sync", lambda s: s.dma_start(out=idn[:], in_=ident[:, :]), dma=True)
    for c in range(NCH):
        b.op("sync", lambda s, c=c: s.dma_start(
            out=wobr[:, c:c + 1], in_=wo_b.rearrange("o d -> d o")[c * 128:(c + 1) * 128, :]), dma=True)
    b.op("vector", lambda v: v.memset(onesb[:], 1.0))
    b.op("vector", lambda v: v.memset(onesr[:], 1.0))
    b.op("vector", lambda v: v.memset(idxk[:], 0))
    b.op("vector", lambda v: v.memset(idxv[:], 0))
    b.bar()

    for tt in range(4):
        b.op("sync", lambda s, tt=tt: s.dma_start(
            out=ksqb[:], in_=kv_d[1536 + 128 * tt:1536 + 128 * (tt + 1), 0:D]), dma=True)
        b.bar()
        b.op("vector", lambda v: v.tensor_copy(out=x1s.ap().rearrange("p c t -> p (c t)"), in_=ksqb[:]))
        b.bar()
        for cc in range(NCH):
            b.op("tensor", lambda t, cc=cc: t.transpose(
                ps_a[:, 0:128], x1s.ap().rearrange("p c t -> p (c t)")[:, cc * 128:(cc + 1) * 128],
                idn[:, :]))
            b.bar()
            b.op("vector", lambda v, cc=cc, tt=tt: v.tensor_copy(
                out=kwTs[:, cc, tt * 128:(tt + 1) * 128], in_=ps_a[:, 0:128]))
            b.bar()

    for kt in range(64):
        b.op("gpsimd", lambda g, kt=kt: g.dma_start(
            out=ksqb[:], in_=kv_d[kt * 128:(kt + 1) * 128, 0:D]), dma=True)
        b.bar()
        b.op("scalar", lambda s: s.activation(out=ksqb[:], in_=ksqb[:], func=AF.Square))
        b.bar()
        b.op("vector", lambda v, kt=kt: v.tensor_reduce(
            out=mkcol[:, 64:80],
            in_=ksqb.ap().rearrange("p (h d) -> p h d", h=16), axis=AX.X, op=ALU.add))
        b.bar()
        b.op("vector", lambda v, kt=kt: v.tensor_reduce(
            out=mkcol[:, kt:kt + 1], in_=mkcol[:, 64:80], axis=AX.X, op=ALU.max))
        b.bar()
    b.op("vector", lambda v: v.tensor_reduce(out=mkcol[:, 64:65], in_=mkcol[:, 0:64], axis=AX.X, op=ALU.max))
    b.bar()
    b.op("tensor", lambda t: t.transpose(ps_a[0:1, 0:128], mkcol[:, 64:65], idn[:, :]))
    b.bar()
    b.op("vector", lambda v: v.tensor_copy(out=zh[:, 0:128], in_=ps_a[0:1, 0:128]))
    b.bar()
    b.op("vector", lambda v: v.tensor_reduce(out=mks[:, 0:1], in_=zh[:, 0:128], axis=AX.X, op=ALU.max))
    b.bar()
    b.op("scalar", lambda s: s.activation(out=mks[:, 0:1], in_=mks[:, 0:1], func=AF.Sqrt))
    b.bar()

    for qt in range(nqt if nqt is not None else NQT):
        q0 = qt * 128
        b.op("sync", lambda s, q0=q0: s.dma_start(
            out=qts[:], in_=qT_d.rearrange("(c p) t -> p c t", p=128)[:, :, q0:q0 + 128]), dma=True)
        b.op("sync", lambda s, q0=q0: s.dma_start(out=qis[:], in_=qiT_d[:, q0:q0 + 128]), dma=True)
        b.op("sync", lambda s, q0=q0: s.dma_start(
            out=x1s[:], in_=x1_d.rearrange("(c p) t -> p c t", p=128)[:, :, q0:q0 + 128]), dma=True)
        b.bar()
        for c in range(NCH):
            b.op("scalar", lambda s, c=c: s.activation(out=ksqb[:, c * 128:(c + 1) * 128],
                                                       in_=qts[:, c, :], func=AF.Square))
        b.bar()
        for h in range(16):
            ps = [ps_a, ps_b][h % 2]
            b.op("tensor", lambda t, h=h, ps=ps: t.matmul(
                ps[0:1, 0:128], onesb[64 * (h % 2):64 * (h % 2) + 64, 0:1],
                ksqb[64 * (h % 2):64 * (h % 2) + 64, (h // 2) * 128:(h // 2 + 1) * 128],
                start=True, stop=True))
            b.bar()
            b.op("vector", lambda v, h=h, ps=ps: v.tensor_copy(out=zh[:, h * 128:(h + 1) * 128], in_=ps[0:1, 0:128]))
            b.bar()
        b.op("vector", lambda v: v.tensor_reduce(out=mks[:, 1:2], in_=zh[:, 0:2048], axis=AX.X, op=ALU.max))
        b.bar()
        b.op("scalar", lambda s: s.activation(out=mks[:, 1:2], in_=mks[:, 1:2], func=AF.Sqrt))
        b.bar()
        b.op("vector", lambda v: v.tensor_tensor(out=mks[:, 2:3], in0=mks[:, 1:2], in1=mks[:, 0:1], op=ALU.mult))
        b.bar()
        b.op("vector", lambda v: v.tensor_scalar(out=mks[:, 2:3], in0=mks[:, 2:3], scalar1=-1.0, scalar2=None, op0=ALU.mult))
        b.bar()
        b.op("tensor", lambda t: t.matmul(ps_a[:, 0:1], onesr[0:1, :], mks[0:1, 2:3], start=True, stop=True))
        b.bar()
        b.op("vector", lambda v: v.tensor_copy(out=ebias[:, :], in_=ps_a[:, 0:1]))
        b.bar()

        for ab in range(NAB):
            ps = [ps_a, ps_b][ab % 2]
            b.op("tensor", lambda t, ab=ab, ps=ps: t.matmul(
                ps[:, :], qis[:, :], kis[:, ab * 512:(ab + 1) * 512], start=True, stop=True))
            b.bar()
            b.op("vector", lambda v, ab=ab, ps=ps: v.tensor_copy(
                out=sidx[:, ab * 512:(ab + 1) * 512], in_=ps[:, :]))
            b.bar()

        for c in range(120):
            vop(lambda v, c=c: v.max(out=cand[:, c * 8:(c + 1) * 8], in_=sidx[:, c * 64:(c + 1) * 64]))
            vop(lambda v, c=c: v.max_index(out=candi[:, c * 8:(c + 1) * 8],
                                           in_max=cand[:, c * 8:(c + 1) * 8],
                                           in_=sidx[:, c * 64:(c + 1) * 64]) if False else
                v.max_index(out=candi[:, c * 8:(c + 1) * 8], in_max=cand[:, c * 8:(c + 1) * 8],
                            in_values=sidx[:, c * 64:(c + 1) * 64]))
        vop(lambda v: v.tensor_copy(out=candg[:, :], in_=cand[:, :]))
        for it in range(8):
            vop(lambda v, it=it: v.max(out=tv[:, it * 8:(it + 1) * 8], in_=candg[:, :]))
            vop(lambda v, it=it: v.match_replace(out=candg[:, :], in_to_replace=tv[:, it * 8:(it + 1) * 8],
                                                 in_values=candg[:, :], imm_value=-1e30))
        vop(lambda v: v.tensor_copy(out=candg[:, :], in_=candi[:, :]))
        vop(lambda v: v.tensor_tensor(out=candg[:, :], in0=candg[:, :], in1=cbs[:, :], op=ALU.add))
        vop(lambda v: v.tensor_scalar(out=cand[:, :], in0=cand[:, :], scalar1=tv[:, 63:64], scalar2=None, op0=ALU.is_ge))
        vop(lambda v: v.tensor_tensor(out=candg[:, :], in0=candg[:, :], in1=cand[:, :], op=ALU.mult))
        for it in range(8):
            vop(lambda v, it=it: v.max(out=picks[:, it * 8:(it + 1) * 8], in_=candg[:, :]))
            vop(lambda v, it=it: v.match_replace(out=candg[:, :], in_to_replace=picks[:, it * 8:(it + 1) * 8],
                                                 in_values=candg[:, :], imm_value=0.0))
        vop(lambda v: v.tensor_scalar(out=xkv[:, 0, :], in0=picks[:, :], scalar1=2.0, scalar2=-2.0,
                                      op0=ALU.mult, op1=ALU.add))
        vop(lambda v: v.tensor_scalar(out=xkv[:, 0, :], in0=xkv[:, 0, :], scalar1=0.0, scalar2=None, op0=ALU.max))
        vop(lambda v: v.tensor_scalar(out=xkv[:, 1, :], in0=picks[:, :], scalar1=1536.5, scalar2=1024.0,
                                      op0=ALU.is_ge, op1=ALU.mult))
        vop(lambda v: v.tensor_tensor(out=xkv[:, 0, :], in0=xkv[:, 0, :], in1=xkv[:, 1, :], op=ALU.add))
        vop(lambda v: v.tensor_scalar(out=xkv[:, 1, :], in0=xkv[:, 0, :], scalar1=1.0, scalar2=None, op0=ALU.add))
        vop(lambda v: v.tensor_copy(out=xkvi[:, :, :], in_=xkv[:, :, :]))
        b.bar()
        for col, bT in [(0, bK), (1, bV)]:
            b.op("sync", lambda s, col=col, bT=bT: s.dma_start(
                out=bT.rearrange("jl (q jh) -> q jl jh", q=128),
                in_=xkvi.ap().rearrange("p a (jl jh) -> p a jl jh", jl=16)[:, col, :, :]), dma=True)
        b.bar()
        for tgt, bT in [(idxk, bK), (idxv, bV)]:
            b.op("sync", lambda s, tgt=tgt, bT=bT: s.dma_start(out=tgt[0:16, :], in_=bT[:, :]), dma=True)
            b.op("sync", lambda s, tgt=tgt, bT=bT: s.dma_start(out=tgt[16:32, :], in_=bT[:, :]), dma=True)
        b.bar()

        for call in range(16):
            b.op("gpsimd", lambda g, call=call: g.dma_gather(
                gkt[:, :, 0:512], kv_d.rearrange("t (a c) -> (t a) c", a=2),
                idxk[:, call * 32:(call + 1) * 32], 512, 512, 1024, transpose=True), dma=True)
            b.op("gpsimd", lambda g, call=call: g.dma_gather(
                gv[:, 0:4, :], kv_d.rearrange("t (a c) -> (t a) c", a=2),
                idxv[:, call * 32:(call + 1) * 32], 512, 512, 1024), dma=True)
            b.bar()
            for mloc in range(4):
                m = call * 4 + mloc
                for h in range(16):
                    b.op("tensor", lambda t, mloc=mloc, m=m, h=h: t.matmul(
                        ps_sc[:, mloc * 32 + h * 2: mloc * 32 + h * 2 + 2],
                        gkt[64 * (h % 2):64 * (h % 2) + 64, h // 2, mloc * 128: mloc * 128 + 128],
                        qts[64 * (h % 2):64 * (h % 2) + 64, h // 2, 2 * m:2 * m + 2],
                        start=True, stop=True))
            b.bar()
            b.op("scalar", lambda s: s.activation(out=pa[:, 0:128], in_=ps_sc[:, 0:128], func=AF.Exp, bias=ebias[:, 0:1]))
            b.bar()
            b.op("vector", lambda v: v.memset(pa.ap().rearrange("p (m h a) -> p m h a", m=8, h=16)[64:128, 0:4, :, 0], 0.0))
            b.op("vector", lambda v: v.memset(pa.ap().rearrange("p (m h a) -> p m h a", m=8, h=16)[0:64, 0:4, :, 1], 0.0))
            b.bar()
            b.op("tensor", lambda t: t.matmul(ps_a[0:1, 0:128], onesb[:, 0:1], pa[:, 0:128], start=True, stop=True))
            b.bar()
            b.op("vector", lambda v, call=call: v.tensor_copy(out=zrow[:, call * 128:(call + 1) * 128], in_=ps_a[0:1, 0:128]))
            b.bar()
            for mloc in range(4):
                for h in range(16):
                    b.op("tensor", lambda t, mloc=mloc, h=h: t.matmul(
                        ps_va[:, mloc * 32 + h * 2: mloc * 32 + h * 2 + 2],
                        gv[:, mloc, h * 64:(h + 1) * 64],
                        pa[:, mloc * 32 + h * 2: mloc * 32 + h * 2 + 2],
                        start=True, stop=True))
            b.bar()
            b.op("vector", lambda v, call=call: v.tensor_copy(out=vas[:, call * 128:(call + 1) * 128], in_=ps_va[:, 0:128]))
            b.bar()

        for hg in range(2):
            for hl in range(8):
                h = hg * 8 + hl
                ps = [ps_a, ps_b][hl % 2]
                for wt in range(4):
                    b.op("tensor", lambda t, h=h, wt=wt, ps=ps: t.matmul(
                        ps[:, wt * 128:(wt + 1) * 128],
                        kwTs[64 * (h % 2):64 * (h % 2) + 64, h // 2, wt * 128:(wt + 1) * 128],
                        qts[64 * (h % 2):64 * (h % 2) + 64, h // 2, :],
                        start=True, stop=True))
                b.bar()
                b.op("scalar", lambda s, hl=hl, ps=ps: s.activation(
                    out=pwt.ap().rearrange("p h w q -> p (h w q)")[:, hl * 512:(hl + 1) * 512],
                    in_=ps[:, :], func=AF.Exp, bias=ebias[:, 0:1]))
                b.bar()
            for hl in range(8):
                h = hg * 8 + hl
                b.op("tensor", lambda t, hl=hl: [t.matmul(
                    ps_b[0:1, 0:128], onesb[:, 0:1], pwt[:, hl, wt, :],
                    start=(wt == 0), stop=(wt == 3)) for wt in range(4)][-1])
                b.bar()
                b.op("vector", lambda v, h=h: v.tensor_copy(
                    out=zh.ap().rearrange("o (m h a) -> o h m a", m=64, h=16)[:, h, :, :],
                    in_=ps_b.ap().rearrange("o (q x) -> o q x", x=1)[0:1, 0:128, :]))
                b.bar()
            if hg == 0:
                for par, pso in [(0, ps_owE), (1, ps_owO)]:
                    for hl in range(8):
                        h = hl
                        for wt in range(4):
                            b.op("tensor", lambda t, par=par, pso=pso, h=h, hl=hl, wt=wt: t.matmul(
                                pso[:, h * 64:(h + 1) * 64],
                                pwt.ap().rearrange("p h w (m a) -> p h w m a", a=2)[:, hl, wt, :, par],
                                vws[:, wt, h * 64:(h + 1) * 64],
                                start=(wt == 0), stop=(wt == 3)))
                b.bar()
        b.op("vector", lambda v: v.tensor_tensor(out=zh[:, :], in0=zh[:, :], in1=zrow[:, :], op=ALU.add))
        b.bar()
        b.op("vector", lambda v: v.tensor_scalar(out=zh[:, :], in0=zh[:, :], scalar1=1e-30, scalar2=None, op0=ALU.add))
        b.bar()
        b.op("vector", lambda v: v.reciprocal(out=zh[:, :], in_=zh[:, :]))
        b.bar()
        for par in range(2):
            for h in range(16):
                b.op("tensor", lambda t, par=par, h=h: t.transpose(
                    ps_b[0:64, par * 16 + h: par * 16 + h + 1],
                    zh.ap().rearrange("o (m h a) -> o m h a", m=64, h=16)[:, :, h, par],
                    idn[0:1, 0:1]))
        b.bar()
        b.op("vector", lambda v: v.tensor_copy(out=rzp.ap().rearrange("p a h -> p (a h)"), in_=ps_b[0:64, 0:32]))
        b.bar()

        for par, pso in [(0, ps_owE), (1, ps_owO)]:
            for hl in range(8):
                h = 8 + hl
                for wt in range(4):
                    b.op("tensor", lambda t, par=par, pso=pso, h=h, hl=hl, wt=wt: t.matmul(
                        pso[:, h * 64:(h + 1) * 64],
                        pwt.ap().rearrange("p h w (m a) -> p h w m a", a=2)[:, hl, wt, :, par],
                        vws[:, wt, h * 64:(h + 1) * 64],
                        start=(wt == 0), stop=(wt == 3)))
        b.bar()

        for ch in range(16):
            b.op("tensor", lambda t, ch=ch: t.transpose(ps_a[:, 0:64] if False else ps_sc[:, 0:64],
                                                        vas[:, ch * 128:(ch + 1) * 128], idn[0:64, 0:64]))
            b.bar()
            b.op("vector", lambda v, ch=ch: v.tensor_copy(out=vtr[:, :], in_=ps_sc[:, 0:64]))
            b.bar()
            b.op("sync", lambda s, ch=ch: s.dma_start(out=vaDR[ch * 128:(ch + 1) * 128, :], in_=vtr[:, :]), dma=True)
            b.bar()
        b.op("sync", lambda s: s.dma_start(
            out=vaE.ap().rearrange("m (h d) -> m h d", h=16),
            in_=vaDR.rearrange("(m h a) d -> m h a d", h=16, a=2)[:, :, 0, :]), dma=True)
        b.op("sync", lambda s: s.dma_start(
            out=vaO.ap().rearrange("m (h d) -> m h d", h=16),
            in_=vaDR.rearrange("(m h a) d -> m h a d", h=16, a=2)[:, :, 1, :]), dma=True)
        b.bar()

        for par, vaP, pso in [(0, vaE, ps_owE), (1, vaO, ps_owO)]:
            b.op("vector", lambda v, par=par, vaP=vaP, pso=pso: v.tensor_tensor(
                out=atw[:, par, :], in0=pso[:, :], in1=vaP[:, :], op=ALU.add))
        b.bar()
        for par in range(2):
            for h in range(16):
                b.op("vector", lambda v, par=par, h=h: v.tensor_scalar(
                    out=atw[:, par, h * 64:(h + 1) * 64], in0=atw[:, par, h * 64:(h + 1) * 64],
                    scalar1=rzp[:, par, h:h + 1], scalar2=None, op0=ALU.mult))
        b.bar()
        for c in range(NCH):
            for par in range(2):
                b.op("tensor", lambda t, c=c, par=par: t.transpose(
                    [ps_a, ps_b][par][:, 0:64],
                    atw.ap().rearrange("p a (c e) -> p a c e", c=NCH)[:, par, c, :],
                    idn[0:64, 0:64]))
            b.bar()
            b.op("vector", lambda v, c=c: v.tensor_copy(
                out=attn_fm.ap().rearrange("p c (m a) -> p c m a", a=2)[:, c, :, 0], in_=ps_a[:, 0:64]))
            b.op("vector", lambda v, c=c: v.tensor_copy(
                out=attn_fm.ap().rearrange("p c (m a) -> p c m a", a=2)[:, c, :, 1], in_=ps_b[:, 0:64]))
            b.bar()

        b.op("gpsimd", lambda g: g.dma_start(
            out=gv[:], in_=wo_w.rearrange("(c p) o -> p c o", p=128)), dma=True)
        b.bar()
        for oc in range(NCH):
            b.op("tensor", lambda t, oc=oc: [t.matmul(
                ps_sc[:, 0:128], gv[:, ic, oc * 128:(oc + 1) * 128], attn_fm[:, ic, :],
                start=(ic == 0), stop=(ic == NCH - 1)) for ic in range(NCH)][-1])
            b.bar()
            b.op("vector", lambda v, oc=oc: v.tensor_scalar(
                out=x2s[:, oc, :], in0=ps_sc[:, 0:128], scalar1=wobr[:, oc:oc + 1], scalar2=None, op0=ALU.add))
            b.bar()
            b.op("vector", lambda v, oc=oc: v.tensor_tensor(
                out=x2s[:, oc, :], in0=x2s[:, oc, :], in1=x1s[:, oc, :], op=ALU.add))
            b.bar()
        for c in range(NCH):
            b.op("sync", lambda s, c=c, q0=q0: s.dma_start(
                out=o_x2[c * 128:(c + 1) * 128, q0:q0 + 128], in_=x2s[:, c, :]), dma=True)
        b.bar()






def emit_MKN(nc, b, st, t, pfx=""):
    kT_l = t["kT_l"]
    mk_c = t["mk_c"]
    slab = st.enter_context(nc.sbuf_tensor(pfx + "slab", [128, NCH, 512], BF16))
    mkt = st.enter_context(nc.sbuf_tensor(pfx + "mkt", [1, 64], F32))
    m16 = st.enter_context(nc.sbuf_tensor(pfx + "m16", [1, 128], F32))
    onesb = st.enter_context(nc.sbuf_tensor(pfx + "onesb", [128, 1], BF16))
    ps_r = st.enter_context(nc.psum_tensor(pfx + "ps_r", [1, 512], F32))
    b.op("vector", lambda v: v.memset(onesb[:], 1.0))
    b.op("vector", lambda v: v.memset(m16[:], 0.0))
    b.bar()
    for tt in range(4):
        b.op("sync", lambda s, tt=tt: s.dma_start(
            out=slab[:], in_=kT_l.rearrange("(c p) t -> p c t", p=128)[:, :, tt * 512:(tt + 1) * 512]), dma=True)
        b.bar()
        for c in range(NCH):
            b.op("scalar", lambda s, c=c: s.activation(out=slab[:, c, :], in_=slab[:, c, :], func=AF.Square))
        b.bar()
        for h in range(16):
            b.op("tensor", lambda t2, h=h: t2.matmul(
                ps_r[0:1, :], onesb[64 * (h % 2):64 * (h % 2) + 64, 0:1],
                slab[64 * (h % 2):64 * (h % 2) + 64, h // 2, :], start=True, stop=True))
            b.bar()
            b.op("vector", lambda v, h=h, tt=tt: v.tensor_reduce(
                out=mkt[:, h * 4 + tt:h * 4 + tt + 1], in_=ps_r[0:1, :], axis=AX.X, op=ALU.max))
            b.bar()
    for h in range(16):
        b.op("vector", lambda v, h=h: v.tensor_reduce(
            out=m16[:, h:h + 1], in_=mkt[:, h * 4:(h + 1) * 4], axis=AX.X, op=ALU.max))
    b.bar()
    b.op("sync", lambda s: s.dma_start(out=mk_c[:, :], in_=m16[:, :]), dma=True)
    b.bar()



def emit_CATT(nc, b, st, t, pfx=""):
    qT_l = t["qT_l"]
    g_kT = t["g_kT"]
    g_v = t["g_v"]
    g_mk = t["g_mk"]
    cmask2 = t["cmask2"]
    at_fm = t["at_fm"]

    kslab = st.enter_context(nc.sbuf_tensor(pfx + "kslab", [128, 2, T], BF16))
    vaug = st.enter_context(nc.sbuf_tensor(pfx + "vaug", [128, 64, 4, 65], BF16))
    qs = st.enter_context(nc.sbuf_tensor(pfx + "qs", [128, NCH, 512], BF16))
    qsq = st.enter_context(nc.sbuf_tensor(pfx + "qsq", [128, NCH, 512], BF16))
    cmsk = st.enter_context(nc.sbuf_tensor(pfx + "cmsk", [128, 2, 16, 512], BF16))
    pt = st.enter_context(nc.sbuf_tensor(pfx + "pt", [128, 2, 512], BF16))
    m4 = st.enter_context(nc.sbuf_tensor(pfx + "m4", [1, 512], F32))
    mkh = st.enter_context(nc.sbuf_tensor(pfx + "mkh", [1, 16], F32))
    qn = st.enter_context(nc.sbuf_tensor(pfx + "qn", [1, 16], F32))
    brow = st.enter_context(nc.sbuf_tensor(pfx + "brow", [1, 16], F32))
    biasq = st.enter_context(nc.sbuf_tensor(pfx + "biasq", [128, 16], F32))
    rz = st.enter_context(nc.sbuf_tensor(pfx + "rz", [1, 4, 512], F32))
    rzb = st.enter_context(nc.sbuf_tensor(pfx + "rzb", [64, 4, 512], F32))
    atb = st.enter_context(nc.sbuf_tensor(pfx + "atb", [64, 4, 512], BF16))
    onesb = st.enter_context(nc.sbuf_tensor(pfx + "onesb", [128, 1], BF16))
    onesr = st.enter_context(nc.sbuf_tensor(pfx + "onesr", [1, 128], F32))
    ps_s0 = st.enter_context(nc.psum_tensor(pfx + "ps_s0", [128, 512], F32))
    ps_s1 = st.enter_context(nc.psum_tensor(pfx + "ps_s1", [128, 512], F32))
    ps_pv = [st.enter_context(nc.psum_tensor(pfx + f"ps_pv{i}", [128, 512], F32)) for i in range(4)]
    ps_q = st.enter_context(nc.psum_tensor(pfx + "ps_q", [1, 512], F32))
    ps_bq = st.enter_context(nc.psum_tensor(pfx + "ps_bq", [128, 16], F32))
    se_s = st.enter_context(nc.semaphore(pfx + "se_s"))
    se_e = st.enter_context(nc.semaphore(pfx + "se_e"))
    se_m = st.enter_context(nc.semaphore(pfx + "se_m"))
    psl = [ps_s0, ps_s1]
    cnt = {"s": 0, "e": 0, "m": 0}

    b.op("sync", lambda s: s.dma_start(out=cmsk[:], in_=cmask2.rearrange("p (a r q) -> p a r q", a=2, r=16)), dma=True)
    b.op("sync", lambda s: s.dma_start(out=m4[:], in_=g_mk.rearrange("s w -> (s w)")), dma=True)
    b.op("vector", lambda v: v.memset(onesb[:], 1.0))
    b.op("vector", lambda v: v.memset(onesr[:], 1.0))
    b.op("vector", lambda v: v.memset(vaug.ap()[:, :, :, 64:65], 1.0))
    b.bar()
    for h in range(16):
        b.op("vector", lambda v, h=h: v.tensor_reduce(
            out=mkh[:, h:h + 1], in_=m4.ap().rearrange("o (s w) -> o w s", w=128)[:, h, :],
            axis=AX.X, op=ALU.max))
    b.bar()
    b.op("scalar", lambda s: s.activation(out=mkh[:, :], in_=mkh[:, :], func=AF.Sqrt))
    b.bar()
    b.op("vector", lambda v: v.tensor_scalar(out=mkh[:, :], in0=mkh[:, :], scalar1=-1.0, scalar2=None, op0=ALU.mult))
    b.bar()

    for s4 in range(4):
        nkt = 16 * (s4 + 1)
        b.op("sync", lambda s, s4=s4: s.dma_start(
            out=qs[:], in_=qT_l.rearrange("(c p) t -> p c t", p=128)[:, :, s4 * 512:(s4 + 1) * 512]), dma=True)
        b.bar()
        for c in range(NCH):
            b.op("scalar", lambda s, c=c: s.activation(out=qsq[:, c, :], in_=qs[:, c, :], func=AF.Square))
        b.bar()
        for h in range(16):
            b.op("tensor", lambda t2, h=h: t2.matmul(
                ps_q[0:1, :], onesb[64 * (h % 2):64 * (h % 2) + 64, 0:1],
                qsq[64 * (h % 2):64 * (h % 2) + 64, h // 2, :], start=True, stop=True))
            b.bar()
            b.op("vector", lambda v, h=h: v.tensor_reduce(
                out=qn[:, h:h + 1], in_=ps_q[0:1, :], axis=AX.X, op=ALU.max))
            b.bar()
        b.op("scalar", lambda s: s.activation(out=qn[:, :], in_=qn[:, :], func=AF.Sqrt))
        b.bar()
        b.op("vector", lambda v: v.tensor_tensor(out=brow[:, :], in0=qn[:, :], in1=mkh[:, :], op=ALU.mult))
        b.bar()
        b.op("tensor", lambda t2: t2.matmul(ps_bq[:, :], onesr[0:1, :], brow[0:1, :], start=True, stop=True))
        b.bar()
        b.op("vector", lambda v: v.tensor_copy(out=biasq[:, :], in_=ps_bq[:, :]))
        b.bar()

        for g4 in range(4):
            for j2 in range(4):
                for tt in range(s4 + 1):
                    bglob = 4 * tt + (j2 if tt % 2 == 0 else 3 - j2)
                    b.op("sync", lambda s, j2=j2, tt=tt, bglob=bglob, g4=g4: s.dma_start(
                        out=kslab[:, :, bglob * 512:(bglob + 1) * 512],
                        in_=g_kT.rearrange("(s2 c p) t -> p s2 c t", s2=4, p=128)
                        [:, j2, 2 * g4:2 * g4 + 2, tt * 512:(tt + 1) * 512]), dma=True)
                    for n in range(4):
                        b.op("gpsimd", lambda g, j2=j2, tt=tt, bglob=bglob, g4=g4, n=n: g.dma_start(
                            out=vaug[:, bglob * 4 + n, :, 0:64],
                            in_=g_v.rearrange("(r p) (gg hh d) -> p r gg hh d", p=128, gg=4, hh=4)
                            [:, 16 * j2 + 4 * tt + n, g4, :, :]), dma=True)
            b.bar()
            e_base = cnt["e"]
            for kt in range(nkt):
                for hl in range(4):
                    h = 4 * g4 + hl
                    u = kt * 4 + hl
                    wv = e_base + u - 1 if u >= 2 else None

                    def s_fn(t2, kt=kt, hl=hl, h=h, u=u, wv=wv):
                        if wv is not None:
                            t2.wait_ge(se_e, wv)
                        return t2.matmul(psl[u % 2][:, :],
                                         kslab[64 * (hl % 2):64 * (hl % 2) + 64, hl // 2, kt * 128:(kt + 1) * 128],
                                         qs[64 * (h % 2):64 * (h % 2) + 64, h // 2, :],
                                         start=True, stop=True)
                    b.op("tensor", lambda t2, f=s_fn: (f(t2).then_inc(se_s, 1), None)[1])
                    cnt["s"] += 1

                    def e_fn(s, u=u, h=h, sv=cnt["s"]):
                        s.wait_ge(se_s, sv)
                        return s.activation(out=pt[:, u % 2, :], in_=psl[u % 2][:, :], func=AF.Exp,
                                            bias=biasq[:, h:h + 1])
                    b.op("scalar", lambda s, f=e_fn: (f(s).then_inc(se_e, 1), None)[1])
                    cnt["e"] += 1
                    masked = kt >= 16 * s4
                    if masked:
                        r = kt - 16 * s4

                        def m_fn(v, u=u, r=r, s4=s4, ev=cnt["e"]):
                            v.wait_ge(se_e, ev)
                            return v.tensor_tensor(out=pt[:, u % 2, :], in0=pt[:, u % 2, :],
                                                   in1=cmsk[:, s4 % 2, r, :], op=ALU.mult)
                        b.op("vector", lambda v, f=m_fn: (f(v).then_inc(se_m, 1), None)[1])
                        cnt["m"] += 1
                    wv2 = ("m", cnt["m"]) if masked else ("e", cnt["e"])

                    def pv_fn(t2, kt=kt, hl=hl, u=u, nkt=nkt, wv2=wv2):
                        t2.wait_ge(se_m if wv2[0] == "m" else se_e, wv2[1])
                        return t2.matmul(ps_pv[hl][0:65, :], vaug[:, kt, hl, :], pt[:, u % 2, :],
                                         start=(kt == 0), stop=(kt == nkt - 1))
                    b.op("tensor", lambda t2, f=pv_fn: f(t2))
            b.bar()
            for hl in range(4):
                b.op("scalar", lambda s, hl=hl: s.activation(out=rz[:, hl, :], in_=ps_pv[hl][64:65, :],
                                                             func=AF.Copy, bias=1e-30))
            b.bar()
            b.op("vector", lambda v: v.reciprocal(out=rz.ap().rearrange("o a q -> o (a q)"),
                                                  in_=rz.ap().rearrange("o a q -> o (a q)")))
            b.bar()
            for hl in range(4):
                dst = psl[hl % 2]
                base = 64 * (hl // 2)
                b.op("tensor", lambda t2, hl=hl, dst=dst, base=base: t2.matmul(
                    dst[base:base + 64, :], onesr[0:1, 0:64], rz[0:1, hl, :], start=True, stop=True))
            b.bar()
            for hl in range(4):
                dst = psl[hl % 2]
                base = 64 * (hl // 2)
                b.op("vector", lambda v, hl=hl, dst=dst, base=base: v.tensor_copy(
                    out=rzb[:, hl, :], in_=dst[base:base + 64, :]))
            b.bar()
            for hl in range(4):
                b.op("vector", lambda v, hl=hl: v.tensor_tensor(
                    out=atb[:, hl, :], in0=ps_pv[hl][0:64, :], in1=rzb[:, hl, :], op=ALU.mult))
            b.bar()
            for hl in range(4):
                h = 4 * g4 + hl
                b.op("sync", lambda s, hl=hl, h=h, s4=s4: s.dma_start(
                    out=at_fm[64 * h:64 * (h + 1), 512 * s4:512 * (s4 + 1)], in_=atb[:, hl, :]), dma=True)
            b.bar()


L4B_DRAMS = [("qT_d", [D, TPC], BF16, "ExternalInput"),
             ("qiT_d", [IDXD, TPC], F32, "ExternalInput"),
             ("g_ki", [4 * IDXD, TPC], F32, "ExternalInput"),
             ("kv_d", [T, 2 * D], BF16, "ExternalInput"),
             ("x1_d", [D, TPC], F32, "ExternalInput"),
             ("wo_w", [D, D], F32, "ExternalInput"),
             ("wo_b", [1, D], F32, "ExternalInput"),
             ("ident", [128, 128], F32, "ExternalInput"),
             ("identb", [128, 128], F32, "ExternalInput"),
             ("cbase", [128, 960], F32, "ExternalInput"),
             ("o_x2", [D, TPC], F32, "ExternalOutput"),
             ("bK", [16, 512], I16, "Internal"),
             ("bV", [16, 512], I16, "Internal"),
             ("vaDR", [2048, 64], F32, "Internal")]

build_L4b = _mk_build(emit_L4b, L4B_DRAMS, "l4b_")


RG4 = [[0, 1, 2, 3], [4, 5, 6, 7]]

MEGA_EXT_IN = ([("x_fm", [D, TPC], F32),
                ("wqT", [D, D], F32), ("wkT", [D, D], F32), ("wvT", [D, D], F32),
                ("n1_w", [D, 1], F32), ("qn_w", [HD, 1], F32), ("kn_w", [HD, 1], F32),
                ("n2_w", [D, 1], F32), ("esel", [4, 512], F32), ("ident", [128, 128], F32),
                ("wq_w", [D, D], F32), ("wk_w", [D, D], F32), ("wv_w", [D, D], F32),
                ("wqi_w", [D, IDXD], F32), ("wki_w", [D, IDXD], F32),
                ("wq_b", [1, D], F32), ("wk_b", [1, D], F32), ("wv_b", [1, D], F32),
                ("wqi_b", [1, IDXD], F32), ("wki_b", [1, IDXD], F32),
                ("wo_w", [D, D], F32), ("wo_b", [1, D], F32),
                ("cbase", [128, 960], F32), ("cmask2", [128, 2 * 16 * 512], BF16)]
               + [(nm, shp, F32) for nm, shp in L3_W_SHAPES])

MEGA_INTERNAL = [("i_lnx", [D, TPC], F32), ("i_qT", [D, TPC], BF16),
                 ("i_kT", [D, TPC], BF16), ("i_v", [TPC, D], BF16),
                 ("i_mkc", [1, 128], F32), ("i_at", [D, TPC], BF16),
                 ("i_x1", [D, TPC], F32), ("i_x2", [D, TPC], F32),
                 ("i_sqT", [D, TPC], BF16), ("i_skv", [TPC, 2 * D], BF16),
                 ("i_sqi", [IDXD, TPC], F32), ("i_ski", [IDXD, TPC], F32),
                 ("g_kT", [4 * D, TPC], BF16), ("g_v", [4 * TPC, D], BF16),
                 ("g_mk", [4, 128], F32), ("g_kv", [T, 2 * D], BF16),
                 ("g_ki", [4 * IDXD, TPC], F32),
                 ("bK", [16, 512], I16), ("bV", [16, 512], I16), ("vaDR", [2048, 64], F32)]


def build_MEGA(dbg=False, upto=3):
    nc = bacc.Bacc("TRN2", target_bir_lowering=False)
    t = {nm: nc.dram_tensor(nm, shp, dt, kind="ExternalInput").ap() for nm, shp, dt in MEGA_EXT_IN}
    t["o_out"] = nc.dram_tensor("o_out", [D, TPC], F32, kind="ExternalOutput").ap()
    dbg_names = {"i_at", "i_x1", "i_x2", "i_lnx"} if dbg else set()
    for nm, shp, dt in MEGA_INTERNAL:
        kind = "ExternalOutput" if nm in dbg_names else "Internal"
        t[nm] = nc.dram_tensor(nm, shp, dt, kind=kind).ap()
    t["identb"] = t["ident"]

    with ExitStack() as st:
        block = st.enter_context(nc.Block())
        sems = {e: st.enter_context(nc.semaphore("s_" + e)) for e in ENGS}
        b = Builder(nc)

        def cc(src, dst):
            b.op("gpsimd", lambda g, src=src, dst=dst: g.collective_compute(
                "AllGather", ALU.bypass, replica_groups=RG4, ins=[src.opt()], outs=[dst.opt()]))

        for half in (0, 1):
            if half == 1 and upto < 3:
                break
            xin = t["x_fm"] if half == 0 else t["i_x2"]
            xout = t["i_x1"] if half == 0 else t["o_out"]
            with ExitStack() as s1:
                emit_L1(nc, b, s1, {**t, "x_fm": xin, "o_lnx": t["i_lnx"], "o_qT": t["i_qT"],
                                    "o_kT": t["i_kT"], "o_v": t["i_v"]}, pfx=f"h{half}l1_")
            b.bar()
            with ExitStack() as s2:
                emit_MKN(nc, b, s2, {"kT_l": t["i_kT"], "mk_c": t["i_mkc"]}, pfx=f"h{half}mk_")
            b.bar()
            cc(t["i_kT"], t["g_kT"])
            b.bar()
            cc(t["i_v"], t["g_v"])
            b.bar()
            cc(t["i_mkc"], t["g_mk"])
            b.bar()
            with ExitStack() as s3:
                emit_CATT(nc, b, s3, {"qT_l": t["i_qT"], "g_kT": t["g_kT"], "g_v": t["g_v"],
                                      "g_mk": t["g_mk"], "cmask2": t["cmask2"],
                                      "at_fm": t["i_at"]}, pfx=f"h{half}ca_")
            b.bar()
            with ExitStack() as s4:
                emit_L3(nc, b, s4, {**t, "at_fm": t["i_at"], "lnx_fm": t["i_lnx"],
                                    "x0_fm": xin, "o_x1": xout}, pfx=f"h{half}l3_")
            b.bar()
            if half == 0 and upto >= 2:
                with ExitStack() as s5:
                    emit_L4a(nc, b, s5, {**t, "x_fm": t["i_x1"], "o_qT": t["i_sqT"],
                                         "o_kv": t["i_skv"], "o_qi": t["i_sqi"],
                                         "o_ki": t["i_ski"]}, pfx="sa_")
                b.bar()
                cc(t["i_skv"], t["g_kv"])
                b.bar()
                cc(t["i_ski"], t["g_ki"])
                b.bar()
                with ExitStack() as s6:
                    emit_L4b(nc, b, s6, {**t, "qT_d": t["i_sqT"], "qiT_d": t["i_sqi"],
                                         "kv_d": t["g_kv"], "x1_d": t["i_x1"],
                                         "o_x2": t["i_x2"]}, pfx="sb_")
                b.bar()
        b.emit(block, sems)
    nc.finalize()
    return nc


def _cmask2_host(j):
    m = np.zeros((128, 2, 16, 512), dtype=BF)
    qv = np.arange(512)[None, :]
    kv = np.arange(128)[:, None]
    for par in range(2):
        jj = j if par == 0 else 3 - j
        for r in range(16):
            if r < 4 * jj:
                m[:, par, r, :] = 1.0
            elif r < 4 * jj + 4:
                rr = r - 4 * jj
                m[:, par, r, :] = (qv >= kv + 128 * rr).astype(BF)
    return np.ascontiguousarray(m.reshape(128, 2 * 16 * 512))


def _mega_in_maps(inputs):
    g = host_prep(inputs)
    perm = _stripe_perm()
    esel = _const_esel()
    cb = _const_cbase()
    x0 = inputs["L0_attn_x"].reshape(NTOK, D)
    shared = {"wqT": g["f_wq"], "wkT": g["f_wk"], "wvT": g["f_wv"],
              "n1_w": g["f_n1_w"], "qn_w": g["f_qn_w"], "kn_w": g["f_kn_w"],
              "n2_w": g["f_n2_w"], "esel": esel, "ident": g["identity"],
              "woT": g["f_wo"], "gateT": g["f_gate_w"],
              "wo_w": g["a_wo_w"], "wo_b": g["a_wo_b"].reshape(1, -1),
              "cbase": cb}
    for nm in ["e0_up", "e0_down", "e1_w1", "e1_w2", "e1_down", "e2_l1",
               "e2_l2", "e2_l3", "e2_l4", "e3_down", "e3_up", "e3_out"]:
        shared[nm + "T"] = g["f_" + nm]
    for nm in ["wq_w", "wk_w", "wv_w", "wqi_w", "wki_w"]:
        shared[nm] = g["a_" + nm]
    for nm in ["wq_b", "wk_b", "wv_b", "wqi_b", "wki_b"]:
        shared[nm] = g["a_" + nm].reshape(1, -1)
    maps = []
    for c in range(NCORES):
        bb, j = c // 4, c % 4
        rows = bb * T + perm[j * TPC:(j + 1) * TPC]
        maps.append({**shared,
                     "x_fm": np.ascontiguousarray(x0[rows].T),
                     "cmask2": _cmask2_host(j)})
    return maps, perm


def _const_cmask():
    cmq = np.zeros((128, 4, 512), dtype=BF)
    for r in range(4):
        for kk in range(128):
            cmq[kk, r, :] = (np.arange(512) >= kk + 128 * r).astype(BF)
    return np.ascontiguousarray(cmq.reshape(128, 4 * 512))


def _const_esel():
    esel = np.zeros((4, 512), dtype=f32)
    for e in range(4):
        esel[e, e * 128:(e + 1) * 128] = 1.0
    return esel


def _const_cbase():
    cb = np.zeros((128, 960), dtype=f32)
    for c in range(120):
        cb[:, c * 8:(c + 1) * 8] = c * 64 + 1
    return cb


def _stripe_perm():
    perm = np.empty(T, dtype=np.int64)
    for j2 in range(4):
        for tt in range(4):
            blk = 4 * tt + (j2 if tt % 2 == 0 else 3 - j2)
            r0 = j2 * TPC + tt * 512
            perm[r0:r0 + 512] = np.arange(blk * 512, (blk + 1) * 512)
    return perm


def kernel(**inputs):
    maps, perm = _mega_in_maps(inputs)
    nc = build_MEGA()
    res = run_bass_kernel_spmd(nc, maps, core_ids=list(range(NCORES))).results
    out = np.empty((NTOK, D), np.float32)
    for c in range(NCORES):
        bb, j = c // 4, c % 4
        rows = bb * T + perm[j * TPC:(j + 1) * TPC]
        out[rows] = res[c]["o_out"].T
    return np.ascontiguousarray(out.reshape(B, T, D))


def kernel_multilaunch(**inputs):
    g = host_prep(inputs)
    cmask = _const_cmask()
    esel = _const_esel()
    cb = _const_cbase()
    nc1 = build_L1()
    nc2 = build_L2()
    nc3 = build_L3()
    nc4a = build_L4a()
    nc4b = build_L4b()

    def run(nc, maps):
        return run_bass_kernel_spmd(nc, maps, core_ids=list(range(NCORES))).results

    l1_w = {"wqT": g["f_wq"], "wkT": g["f_wk"], "wvT": g["f_wv"],
            "n1_w": g["f_n1_w"], "qn_w": g["f_qn_w"], "kn_w": g["f_kn_w"]}
    l3_w = {"n2_w": g["f_n2_w"], "esel": esel, "ident": g["identity"],
            "woT": g["f_wo"], "gateT": g["f_gate_w"]}
    for nm in ["e0_up", "e0_down", "e1_w1", "e1_w2", "e1_down", "e2_l1",
               "e2_l2", "e2_l3", "e2_l4", "e3_down", "e3_up", "e3_out"]:
        l3_w[nm + "T"] = g["f_" + nm]
    l4a_w = {}
    for nm in ["wq_w", "wk_w", "wv_w", "wqi_w", "wki_w"]:
        l4a_w[nm] = g["a_" + nm]
    for nm in ["wq_b", "wk_b", "wv_b", "wqi_b", "wki_b"]:
        l4a_w[nm] = g["a_" + nm].reshape(1, -1)

    def ffn_pass(x_sh):
        r1 = run(nc1, [{"x_fm": x_sh[c], **l1_w} for c in range(NCORES)])
        qT_full = [np.concatenate([r1[bb * 4 + j]["o_qT"] for j in range(4)], axis=1) for bb in range(B)]
        kT_full = [np.concatenate([r1[bb * 4 + j]["o_kT"] for j in range(4)], axis=1) for bb in range(B)]
        v_full = [np.concatenate([r1[bb * 4 + j]["o_v"] for j in range(4)], axis=0) for bb in range(B)]
        m2 = []
        for c in range(NCORES):
            bb = c // 4
            hs = [4 * (c % 4) + i for i in range(4)]
            m2.append({
                "qT_p": np.stack([qT_full[bb][h * HD:(h + 1) * HD] for h in hs]),
                "kT_p": np.stack([kT_full[bb][h * HD:(h + 1) * HD] for h in hs]),
                "v_p": np.stack([np.ascontiguousarray(v_full[bb][:, h * HD:(h + 1) * HD]) for h in hs]),
                "cmask": cmask})
        r2 = run(nc2, m2)
        m3 = []
        for c in range(NCORES):
            bb = c // 4
            t0 = (c * TPC) % T
            at = np.empty((D, TPC), dtype=BF)
            for h in range(NH):
                at[h * HD:(h + 1) * HD] = r2[bb * 4 + h // 4]["o_at"][h % 4][t0:t0 + TPC].T
            m3.append({"at_fm": at, "lnx_fm": r1[c]["o_lnx"], "x0_fm": x_sh[c], **l3_w})
        r3 = run(nc3, m3)
        return [r3[c]["o_x1"] for c in range(NCORES)]

    x0 = inputs["L0_attn_x"].reshape(NTOK, D)
    x0_sh = [np.ascontiguousarray(x0[c * TPC:(c + 1) * TPC].T) for c in range(NCORES)]
    x1_sh = ffn_pass(x0_sh)

    r4a = run(nc4a, [{"x_fm": x1_sh[c], **l4a_w} for c in range(NCORES)])
    perm = _stripe_perm()
    kv_full, ki_full = [], []
    for bb in range(B):
        kv = np.concatenate([r4a[bb * 4 + j]["o_kv"] for j in range(4)], axis=0)
        kv_full.append(np.ascontiguousarray(kv[perm]))
        ki = np.concatenate([r4a[bb * 4 + j]["o_ki"] for j in range(4)], axis=1)
        ki_full.append(np.ascontiguousarray(np.concatenate(
            [ki[:, perm[TPC * j2:TPC * (j2 + 1)]] for j2 in range(4)], axis=0)))
    m4b = []
    for c in range(NCORES):
        bb = c // 4
        m4b.append({"qT_d": r4a[c]["o_qT"], "qiT_d": r4a[c]["o_qi"], "g_ki": ki_full[bb],
                    "kv_d": kv_full[bb], "x1_d": x1_sh[c],
                    "wo_w": g["a_wo_w"], "wo_b": g["a_wo_b"].reshape(1, -1),
                    "ident": g["identity"], "identb": g["identity"], "cbase": cb})
    r4b = run(nc4b, m4b)
    x2_sh = [np.ascontiguousarray(r4b[c]["o_x2"]) for c in range(NCORES)]

    x3_sh = ffn_pass(x2_sh)
    out = np.concatenate([s.T for s in x3_sh], axis=0).reshape(B, T, D)
    return np.ascontiguousarray(out.astype(np.float32))



# version 05
# speedup vs baseline: 1.1452x; 1.1452x over previous
import numpy as np
import ml_dtypes
from contextlib import ExitStack
import concourse.bass as bass
import concourse.bacc as bacc
import concourse.mybir as mybir
from concourse.bass_utils import run_bass_kernel_spmd
from concourse.library_config import mlp as mlp_lib

F32 = mybir.dt.float32
BF16 = mybir.dt.bfloat16
I16 = mybir.dt.int16
U32 = mybir.dt.uint32
AF = mybir.ActivationFunctionType
ALU = mybir.AluOpType
AX = mybir.AxisListType
f32 = np.float32
BF = ml_dtypes.bfloat16

NCORES = 8
D, NH, HD = 1024, 16, 64
B, T = 2, 8192
NTOK = B * T
TPC = NTOK // NCORES
ZM = 256
HID = 2048
EPS_RMS = 1.1920929e-07
EPS_LN = 1e-5
WM, TOPK, IDXD = 512, 64, 64
AL = T - WM
NCH = D // 128

ENGS = ("sync", "gpsimd", "tensor", "vector", "scalar")


class Builder:

    def __init__(self, nc):
        self.nc = nc
        self.phases = [[]]

    def op(self, eng, fn, dma=False):
        self.phases[-1].append((eng, fn, dma))

    def bar(self):
        if self.phases[-1]:
            self.phases.append([])

    def emit(self, block, sems):
        nph = len(self.phases)
        totals = {e: [0] * (nph + 1) for e in ENGS}
        for p, ph in enumerate(self.phases):
            add = {e: 0 for e in ENGS}
            for eng, fn, dma in ph:
                if dma:
                    add[eng] += 16
            has_compute = {e: any((eng == e and not dma) for eng, fn, dma in ph) for e in ENGS}
            for e in ENGS:
                totals[e][p + 1] = totals[e][p] + add[e] + (1 if has_compute[e] else 0)

        def mk(eng_name):
            def emitter(eng):
                waited = {e: 0 for e in ENGS}
                for p, ph in enumerate(self.phases):
                    mine = [(fn, dma) for e, fn, dma in ph if e == eng_name]
                    rest_later = any(
                        e == eng_name for pp in self.phases[p:] for e, _, _ in pp
                    )
                    if not mine and not rest_later:
                        break
                    if p > 0 and mine:
                        for e2 in ENGS:
                            tv = totals[e2][p]
                            if tv > waited[e2]:
                                eng.wait_ge(sems[e2], tv)
                                waited[e2] = tv
                    last_compute = None
                    for i, (fn, dma) in enumerate(mine):
                        if not dma:
                            last_compute = i
                    for i, (fn, dma) in enumerate(mine):
                        ins = fn(eng)
                        if dma:
                            ins.then_inc(sems[eng_name], 16)
                        elif i == last_compute:
                            if ins is None:
                                eng.nop().then_inc(sems[eng_name], 1)
                            else:
                                ins.then_inc(sems[eng_name], 1)
                for e2 in ENGS:
                    tv = totals[e2][nph]
                    if tv > waited[e2]:
                        eng.wait_ge(sems[e2], tv)
            return emitter

        block.sync(mk("sync"))
        block.gpsimd(mk("gpsimd"))
        block.tensor(mk("tensor"))
        block.vector(mk("vector"))
        block.scalar(mk("scalar"))


def host_prep(inputs):
    g = {}
    for name in ["wq", "wk", "wv", "wo", "gate_w", "e0_up", "e0_down", "e1_w1",
                 "e1_w2", "e1_down", "e2_l1", "e2_l2", "e2_l3", "e2_l4",
                 "e3_down", "e3_up", "e3_out"]:
        g["f_" + name] = np.ascontiguousarray(inputs["L0_ffn_" + name].T)
    for name in ["n1_w", "n2_w", "qn_w", "kn_w"]:
        g["f_" + name] = np.ascontiguousarray(inputs["L0_ffn_" + name].reshape(-1, 1))
    for name in ["wq_w", "wk_w", "wv_w", "wo_w", "wqi_w", "wki_w"]:
        g["a_" + name] = np.ascontiguousarray(inputs["L0_attn_" + name])
    for name in ["wq_b", "wk_b", "wv_b", "wo_b", "wqi_b", "wki_b"]:
        g["a_" + name] = np.ascontiguousarray(inputs["L0_attn_" + name].reshape(-1, 1))
    g["identity"] = np.eye(128, dtype=f32)
    return g



def emit_L1(nc, b, st, t, pfx=""):
    x_fm = t["x_fm"]
    wqT = t["wqT"]
    wkT = t["wkT"]
    wvT = t["wvT"]
    n1_w = t["n1_w"]
    qn_w = t["qn_w"]
    kn_w = t["kn_w"]
    o_lnx = t["o_lnx"]
    o_qT = t["o_qT"]
    o_kT = t["o_kT"]
    o_v = t["o_v"]

    NB = TPC // 512

    xs = st.enter_context(nc.sbuf_tensor(pfx + "xs", [128, NCH, 512], F32))
    xcast = st.enter_context(nc.sbuf_tensor(pfx + "xcast", [128, NCH, 512], BF16))
    hb = st.enter_context(nc.sbuf_tensor(pfx + "hb", [128, NCH, 512], BF16))
    lnxt = st.enter_context(nc.sbuf_tensor(pfx + "lnxt", [128, NCH, 512], F32))
    tmpc = st.enter_context(nc.sbuf_tensor(pfx + "tmpc", [128, NCH, 512], F32))
    wq_s = st.enter_context(nc.sbuf_tensor(pfx + "wq_s", [128, NCH, D], BF16))
    wk_s = st.enter_context(nc.sbuf_tensor(pfx + "wk_s", [128, NCH, D], BF16))
    wv_s = st.enter_context(nc.sbuf_tensor(pfx + "wv_s", [128, NCH, D], BF16))
    onesb = st.enter_context(nc.sbuf_tensor(pfx + "onesb", [128, 1], BF16))
    onesf = st.enter_context(nc.sbuf_tensor(pfx + "onesf", [128, 1], F32))
    onesr = st.enter_context(nc.sbuf_tensor(pfx + "onesr", [1, 128], F32))
    n1b = st.enter_context(nc.sbuf_tensor(pfx + "n1b", [128, NCH], F32))
    qn2 = st.enter_context(nc.sbuf_tensor(pfx + "qn2", [128, 1], F32))
    kn2 = st.enter_context(nc.sbuf_tensor(pfx + "kn2", [128, 1], F32))
    rows = st.enter_context(nc.sbuf_tensor(pfx + "rows", [1, 16 * 512], F32))
    rowsb = st.enter_context(nc.sbuf_tensor(pfx + "rowsb", [1, 4 * 512], F32))
    bcs = st.enter_context(nc.sbuf_tensor(pfx + "bcs", [128, 3, 512], F32))
    sq = st.enter_context(nc.sbuf_tensor(pfx + "sq", [128, 512], BF16))
    qtmp = st.enter_context(nc.sbuf_tensor(pfx + "qtmp", [128, 512], F32))
    bchh = st.enter_context(nc.sbuf_tensor(pfx + "bchh", [128, 512], F32))
    stage = st.enter_context(nc.sbuf_tensor(pfx + "stage", [128, 2, 512], BF16))
    qko = st.enter_context(nc.sbuf_tensor(pfx + "qko", [128, 512], BF16))
    ps_mm = st.enter_context(nc.psum_tensor(pfx + "ps_mm", [128, 512], F32))
    ps_mm2 = st.enter_context(nc.psum_tensor(pfx + "ps_mm2", [128, 512], F32))
    ps_r0 = st.enter_context(nc.psum_tensor(pfx + "ps_r0", [1, 512], F32))
    ps_r1 = st.enter_context(nc.psum_tensor(pfx + "ps_r1", [1, 512], F32))
    ps_r2 = st.enter_context(nc.psum_tensor(pfx + "ps_r2", [1, 512], F32))
    ps_r3 = st.enter_context(nc.psum_tensor(pfx + "ps_r3", [1, 512], F32))
    ps_bc = st.enter_context(nc.psum_tensor(pfx + "ps_bc", [128, 512], F32))
    ps_bch = st.enter_context(nc.psum_tensor(pfx + "ps_bch", [128, 512], F32))

    b.op("gpsimd", lambda g: g.dma_start(out=wq_s[:], in_=wqT.rearrange("(c p) o -> p c o", p=128)), dma=True)
    b.op("gpsimd", lambda g: g.dma_start(out=wk_s[:], in_=wkT.rearrange("(c p) o -> p c o", p=128)), dma=True)
    b.op("gpsimd", lambda g: g.dma_start(out=wv_s[:], in_=wvT.rearrange("(c p) o -> p c o", p=128)), dma=True)
    for c in range(NCH):
        b.op("sync", lambda s, c=c: s.dma_start(out=n1b[:, c:c + 1], in_=n1_w[c * 128:(c + 1) * 128, :]), dma=True)
    b.op("sync", lambda s: s.dma_start(out=qn2[0:64, :], in_=qn_w[:, :]), dma=True)
    b.op("sync", lambda s: s.dma_start(out=qn2[64:128, :], in_=qn_w[:, :]), dma=True)
    b.op("sync", lambda s: s.dma_start(out=kn2[0:64, :], in_=kn_w[:, :]), dma=True)
    b.op("sync", lambda s: s.dma_start(out=kn2[64:128, :], in_=kn_w[:, :]), dma=True)
    b.op("vector", lambda v: v.memset(onesb[:], 1.0))
    b.op("vector", lambda v: v.memset(onesf[:], 1.0))
    b.op("vector", lambda v: v.memset(onesr[:], 1.0))
    b.bar()

    for blk in range(NB):
        t0 = blk * 512
        b.op("sync", lambda s, t0=t0: s.dma_start(
            out=xs[:], in_=x_fm.rearrange("(c p) t -> p c t", p=128)[:, :, t0:t0 + 512]), dma=True)
        b.op("gpsimd", lambda g, t0=t0: g.dma_start(
            out=xcast[:], in_=x_fm.rearrange("(c p) t -> p c t", p=128)[:, :, t0:t0 + 512]), dma=True)
        b.bar()
        for c in range(NCH):
            b.op("scalar", lambda s, c=c: s.activation(out=hb[:, c, :], in_=xs[:, c, :], func=AF.Square))
        b.bar()
        for c in range(NCH):
            b.op("tensor", lambda t, c=c: t.matmul(ps_r0[0:1, :], onesb[:, 0:1], xcast[:, c, :],
                                                   start=(c == 0), stop=(c == NCH - 1)))
        for c in range(NCH):
            b.op("tensor", lambda t, c=c: t.matmul(ps_r1[0:1, :], onesb[:, 0:1], hb[:, c, :],
                                                   start=(c == 0), stop=(c == NCH - 1)))
        b.bar()
        b.op("scalar", lambda s: s.activation(out=rows[:, 0:512], in_=ps_r0[0:1, :],
                                              func=AF.Copy, scale=f32(1.0 / D)))
        b.op("scalar", lambda s: s.activation(out=rows[:, 512:1024], in_=ps_r1[0:1, :],
                                              func=AF.Copy, scale=f32(1.0 / D)))
        b.bar()
        b.op("vector", lambda v: v.tensor_tensor(out=rows[:, 1024:1536], in0=rows[:, 0:512],
                                                 in1=rows[:, 0:512], op=ALU.mult))
        b.bar()
        b.op("vector", lambda v: v.tensor_tensor(out=rows[:, 512:1024], in0=rows[:, 512:1024],
                                                 in1=rows[:, 1024:1536], op=ALU.subtract))
        b.bar()
        b.op("scalar", lambda s: s.activation(out=rows[:, 1024:1536], in_=rows[:, 512:1024],
                                              func=AF.Sqrt, bias=f32(EPS_LN)))
        b.bar()
        b.op("vector", lambda v: v.reciprocal(out=rows[:, 1024:1536], in_=rows[:, 1024:1536]))
        b.bar()
        b.op("vector", lambda v: v.tensor_tensor(out=rows[:, 1536:2048], in0=rows[:, 512:1024],
                                                 in1=rows[:, 1024:1536], op=ALU.mult))
        b.bar()
        b.op("vector", lambda v: v.tensor_tensor(out=rows[:, 1536:2048], in0=rows[:, 1536:2048],
                                                 in1=rows[:, 1024:1536], op=ALU.mult))
        b.bar()
        b.op("scalar", lambda s: s.activation(out=rowsb[:, 0:512], in_=rows[:, 1536:2048],
                                              func=AF.Sqrt, bias=f32(EPS_RMS)))
        b.bar()
        b.op("vector", lambda v: v.reciprocal(out=rowsb[:, 0:512], in_=rowsb[:, 0:512]))
        b.bar()
        b.op("vector", lambda v: v.tensor_tensor(out=rows[:, 1536:2048], in0=rowsb[:, 0:512],
                                                 in1=rows[:, 1024:1536], op=ALU.mult))
        b.bar()
        for j, off in enumerate([0, 1024, 1536]):
            b.op("tensor", lambda t, off=off: t.matmul(ps_bc[:, :], onesr[0:1, :], rows[0:1, off:off + 512],
                                                       start=True, stop=True))
            b.bar()
            b.op("vector", lambda v, j=j: v.tensor_copy(out=bcs[:, j, :], in_=ps_bc[:, :]))
            b.bar()
        for c in range(NCH):
            b.op("vector", lambda v, c=c: v.tensor_tensor(out=tmpc[:, c, :], in0=xs[:, c, :],
                                                          in1=bcs[:, 0, :], op=ALU.subtract))
        b.bar()
        for c in range(NCH):
            b.op("vector", lambda v, c=c: v.tensor_tensor(out=lnxt[:, c, :], in0=tmpc[:, c, :],
                                                          in1=bcs[:, 1, :], op=ALU.mult))
        for c in range(NCH):
            b.op("vector", lambda v, c=c: v.tensor_tensor(out=tmpc[:, c, :], in0=tmpc[:, c, :],
                                                          in1=bcs[:, 2, :], op=ALU.mult))
        b.bar()
        for c in range(NCH):
            b.op("sync", lambda s, c=c, t0=t0: s.dma_start(
                out=o_lnx[c * 128:(c + 1) * 128, t0:t0 + 512], in_=lnxt[:, c, :]), dma=True)
            b.op("vector", lambda v, c=c: v.tensor_scalar(
                out=hb[:, c, :], in0=tmpc[:, c, :], scalar1=n1b[:, c:c + 1], scalar2=None, op0=ALU.mult))
        b.bar()
        for which, w_s, nrm, outT, extra in (("q", wq_s, qn2, o_qT, 0.125),
                                            ("k", wk_s, kn2, o_kT, None)):
            for oc in range(NCH):
                b.op("tensor", lambda t, w_s=w_s, oc=oc: [
                    t.matmul(ps_mm[:, :], w_s[:, ic, oc * 128:(oc + 1) * 128], hb[:, ic, :],
                             start=(ic == 0), stop=(ic == NCH - 1))
                    for ic in range(NCH)][-1])
                b.bar()
                b.op("scalar", lambda s: s.activation(out=sq[:], in_=ps_mm[:, :], func=AF.Square))
                b.bar()
                b.op("tensor", lambda t: t.matmul(ps_r2[0:1, :], onesb[0:64, 0:1], sq[0:64, :],
                                                  start=True, stop=True))
                b.op("tensor", lambda t: t.matmul(ps_r3[0:1, :], onesb[64:128, 0:1], sq[64:128, :],
                                                  start=True, stop=True))
                b.bar()
                b.op("scalar", lambda s: s.activation(out=rowsb[:, 512:1024], in_=ps_r2[0:1, :],
                                                      func=AF.Sqrt, scale=f32(1.0 / HD), bias=f32(EPS_RMS)))
                b.op("scalar", lambda s: s.activation(out=rowsb[:, 1024:1536], in_=ps_r3[0:1, :],
                                                      func=AF.Sqrt, scale=f32(1.0 / HD), bias=f32(EPS_RMS)))
                b.bar()
                b.op("vector", lambda v: v.reciprocal(out=rowsb[:, 512:1536], in_=rowsb[:, 512:1536]))
                b.bar()
                b.op("tensor", lambda t: t.matmul(ps_bch[0:64, :], onesr[0:1, 0:64], rowsb[0:1, 512:1024],
                                                  start=True, stop=True))
                b.op("tensor", lambda t: t.matmul(ps_bch[64:128, :], onesr[0:1, 0:64], rowsb[0:1, 1024:1536],
                                                  start=True, stop=True))
                b.bar()
                b.op("vector", lambda v: v.tensor_copy(out=bchh[:, :], in_=ps_bch[:, :]))
                b.bar()
                b.op("vector", lambda v: v.tensor_tensor(out=qtmp[:, :], in0=ps_mm[:, :],
                                                         in1=bchh[:, :], op=ALU.mult))
                b.bar()
                if extra is not None:
                    b.op("vector", lambda v, nrm=nrm, extra=extra: v.tensor_scalar(
                        out=qko[:, :], in0=qtmp[:, :], scalar1=nrm[:, 0:1], scalar2=extra,
                        op0=ALU.mult, op1=ALU.mult))
                else:
                    b.op("vector", lambda v, nrm=nrm: v.tensor_scalar(
                        out=qko[:, :], in0=qtmp[:, :], scalar1=nrm[:, 0:1], scalar2=None, op0=ALU.mult))
                b.bar()
                b.op("sync", lambda s, outT=outT, oc=oc, t0=t0: s.dma_start(
                    out=outT[oc * 128:(oc + 1) * 128, t0:t0 + 512], in_=qko[:, :]), dma=True)
                b.bar()
        for tt in range(4):
            for ob in range(2):
                b.op("tensor", lambda t, tt=tt, ob=ob: [
                    t.matmul(ps_mm2[:, :], hb[:, ic, tt * 128:(tt + 1) * 128],
                             wv_s[:, ic, ob * 512:(ob + 1) * 512],
                             start=(ic == 0), stop=(ic == NCH - 1))
                    for ic in range(NCH)][-1])
                b.bar()
                b.op("scalar", lambda s, ob=ob: s.activation(out=stage[:, ob, :], in_=ps_mm2[:, :], func=AF.Copy))
                b.bar()
            b.op("sync", lambda s, tt=tt, t0=t0: s.dma_start(
                out=o_v[t0 + tt * 128: t0 + (tt + 1) * 128, :], in_=stage.ap().rearrange("p a t -> p (a t)")), dma=True)
            b.bar()







def build_L2():
    nc = bacc.Bacc("TRN2", target_bir_lowering=False)
    qT_p = nc.dram_tensor("qT_p", [4, HD, T], BF16, kind="ExternalInput").ap()
    kT_p = nc.dram_tensor("kT_p", [4, HD, T], BF16, kind="ExternalInput").ap()
    v_p = nc.dram_tensor("v_p", [4, T, HD], BF16, kind="ExternalInput").ap()
    cmask = nc.dram_tensor("cmask", [128, 4 * 512], BF16, kind="ExternalInput").ap()
    o_at = nc.dram_tensor("o_at", [4, T, HD], BF16, kind="ExternalOutput").ap()

    NQB = T // 512

    with ExitStack() as st:
        qTs = st.enter_context(nc.sbuf_tensor("qTs", [64, 4, T], BF16))
        kTs = st.enter_context(nc.sbuf_tensor("kTs", [64, 4, T], BF16))
        vaug = st.enter_context(nc.sbuf_tensor("vaug", [128, 4, T // 128, 65], BF16))
        cm = st.enter_context(nc.sbuf_tensor("cm", [128, 4, 512], BF16))
        onesb = st.enter_context(nc.sbuf_tensor("onesb", [128, 1], BF16))
        onesr = st.enter_context(nc.sbuf_tensor("onesr", [1, 128], F32))
        ksq = st.enter_context(nc.sbuf_tensor("ksq", [64, 512], BF16))
        rows = st.enter_context(nc.sbuf_tensor("rows", [1, T], F32))
        rows2 = st.enter_context(nc.sbuf_tensor("rows2", [1, 64], F32))
        mk1 = st.enter_context(nc.sbuf_tensor("mk1", [1, 1], F32))
        biasq = st.enter_context(nc.sbuf_tensor("biasq", [128, NQB], F32))
        pt = st.enter_context(nc.sbuf_tensor("pt", [128, 2, 512], BF16))
        rz = st.enter_context(nc.sbuf_tensor("rz", [128, 1], F32))
        ot = st.enter_context(nc.sbuf_tensor("ot", [128, 64], BF16))
        ps_s0 = st.enter_context(nc.psum_tensor("ps_s0", [128, 512], F32))
        ps_s1 = st.enter_context(nc.psum_tensor("ps_s1", [128, 512], F32))
        ps_o = [st.enter_context(nc.psum_tensor(f"ps_o{i}", [128, 65], F32)) for i in range(4)]
        ps_r = st.enter_context(nc.psum_tensor("ps_r", [1, 512], F32))
        ps_b = st.enter_context(nc.psum_tensor("ps_b", [128, NQB], F32))
        se_s = st.enter_context(nc.semaphore("se_s"))
        se_e = st.enter_context(nc.semaphore("se_e"))
        se_m = st.enter_context(nc.semaphore("se_m"))
        s_sync = st.enter_context(nc.semaphore("s_sync"))
        s_gpsimd = st.enter_context(nc.semaphore("s_gpsimd"))
        s_tensor = st.enter_context(nc.semaphore("s_tensor"))
        s_vector = st.enter_context(nc.semaphore("s_vector"))
        s_scalar = st.enter_context(nc.semaphore("s_scalar"))
        block = st.enter_context(nc.Block())
        sems = {"sync": s_sync, "gpsimd": s_gpsimd, "tensor": s_tensor,
                "vector": s_vector, "scalar": s_scalar}
        b = Builder(nc)
        ps_s = [ps_s0, ps_s1]

        b.op("sync", lambda s: s.dma_start(out=qTs[:], in_=qT_p.rearrange("a d t -> d a t")), dma=True)
        b.op("sync", lambda s: s.dma_start(out=kTs[:], in_=kT_p.rearrange("a d t -> d a t")), dma=True)
        b.op("sync", lambda s: s.dma_start(
            out=vaug[:, :, :, 0:64], in_=v_p.rearrange("a (n p) d -> p a n d", p=128)), dma=True)
        b.op("sync", lambda s: s.dma_start(out=cm[:], in_=cmask.rearrange("p (r q) -> p r q", r=4)), dma=True)
        b.op("vector", lambda v: v.memset(vaug[:, :, :, 64:65], 1.0))
        b.op("vector", lambda v: v.memset(onesb[:], 1.0))
        b.op("vector", lambda v: v.memset(onesr[:], 1.0))
        b.bar()

        cnt = {"s": 0, "e": 0, "m": 0}

        for pair in range(4):
            for c in range(16):
                b.op("scalar", lambda s, pair=pair, c=c: s.activation(
                    out=ksq[:, :], in_=kTs[:, pair, c * 512:(c + 1) * 512], func=AF.Square))
                b.bar()
                b.op("tensor", lambda t, c=c: t.matmul(ps_r[0:1, :], onesb[0:64, 0:1], ksq[:, :],
                                                       start=True, stop=True))
                b.bar()
                b.op("vector", lambda v, c=c: v.tensor_copy(out=rows[:, c * 512:(c + 1) * 512], in_=ps_r[0:1, :]))
                b.bar()
            b.op("vector", lambda v: v.tensor_reduce(out=mk1[:, :], in_=rows[:, :], axis=AX.X, op=ALU.max))
            b.bar()
            b.op("scalar", lambda s: s.activation(out=mk1[:, :], in_=mk1[:, :], func=AF.Sqrt))
            b.bar()
            for c in range(16):
                b.op("scalar", lambda s, pair=pair, c=c: s.activation(
                    out=ksq[:, :], in_=qTs[:, pair, c * 512:(c + 1) * 512], func=AF.Square))
                b.bar()
                b.op("tensor", lambda t: t.matmul(ps_r[0:1, :], onesb[0:64, 0:1], ksq[:, :],
                                                  start=True, stop=True))
                b.bar()
                b.op("vector", lambda v, c=c: v.tensor_copy(out=rows[:, c * 512:(c + 1) * 512], in_=ps_r[0:1, :]))
                b.bar()
            for qb in range(NQB):
                b.op("vector", lambda v, qb=qb: v.tensor_reduce(
                    out=rows2[:, qb:qb + 1], in_=rows[:, qb * 512:(qb + 1) * 512], axis=AX.X, op=ALU.max))
            b.bar()
            b.op("scalar", lambda s: s.activation(out=rows2[:, 0:NQB], in_=rows2[:, 0:NQB], func=AF.Sqrt))
            b.bar()
            b.op("vector", lambda v: v.tensor_scalar(
                out=rows2[:, 0:NQB], in0=rows2[:, 0:NQB], scalar1=mk1[0:1, 0:1], scalar2=-1.0,
                op0=ALU.mult, op1=ALU.mult))
            b.bar()
            b.op("tensor", lambda t: t.matmul(ps_b[:, :], onesr[0:1, :], rows2[0:1, 0:NQB],
                                              start=True, stop=True))
            b.bar()
            b.op("vector", lambda v: v.tensor_copy(out=biasq[:, :], in_=ps_b[:, :]))
            b.bar()

            for qb in range(NQB):
                nkt = 4 * qb + 4
                e_base = cnt["e"]

                def emit_qblock(pair=pair, qb=qb, nkt=nkt, e_base=e_base, cnt=cnt):
                    pass

                s_waits = []
                for kt in range(nkt):
                    need_e = e_base + kt - 1
                    wait_e = need_e if kt >= 2 else None
                    diag = kt >= 4 * qb
                    r = kt - 4 * qb
                    def s_fn(t, pair=pair, qb=qb, kt=kt, wait_e=wait_e):
                        if wait_e is not None:
                            t.wait_ge(se_e, wait_e)
                        return t.matmul(ps_s[kt % 2][:, :], kTs[:, pair, kt * 128:(kt + 1) * 128],
                                        qTs[:, pair, qb * 512:(qb + 1) * 512], start=True, stop=True)
                    b.op("tensor", lambda t, f=s_fn: (f(t).then_inc(se_s, 1), None)[1])
                    cnt["s"] += 1
                    def e_fn(s, pair=pair, qb=qb, kt=kt, sv=cnt["s"]):
                        s.wait_ge(se_s, sv)
                        return s.activation(out=pt[:, kt % 2, :], in_=ps_s[kt % 2][:, :], func=AF.Exp,
                                            bias=biasq[:, qb:qb + 1])
                    b.op("scalar", lambda s, f=e_fn: (f(s).then_inc(se_e, 1), None)[1])
                    cnt["e"] += 1
                    if diag:
                        def m_fn(v, kt=kt, r=r, ev=cnt["e"]):
                            v.wait_ge(se_e, ev)
                            return v.tensor_tensor(out=pt[:, kt % 2, :], in0=pt[:, kt % 2, :],
                                                   in1=cm[:, r, :], op=ALU.mult)
                        b.op("vector", lambda v, f=m_fn: (f(v).then_inc(se_m, 1), None)[1])
                        cnt["m"] += 1
                    wv = ("m", cnt["m"]) if diag else ("e", cnt["e"])
                    for qs in range(4):
                        def pv_fn(t, pair=pair, kt=kt, qs=qs, nkt=nkt, wv=wv):
                            if qs == 0:
                                t.wait_ge(se_m if wv[0] == "m" else se_e, wv[1])
                            return t.matmul(ps_o[qs][:, :], pt[:, kt % 2, qs * 128:(qs + 1) * 128],
                                            vaug[:, pair, kt, :], start=(kt == 0), stop=(kt == nkt - 1))
                        b.op("tensor", lambda t, f=pv_fn: f(t))
                b.bar()
                for qs in range(4):
                    b.op("scalar", lambda s, qs=qs: s.activation(out=rz[:, :], in_=ps_o[qs][:, 64:65],
                                                                 func=AF.Copy, bias=1e-30))
                    b.bar()
                    b.op("vector", lambda v: v.reciprocal(out=rz[:, :], in_=rz[:, :]))
                    b.bar()
                    b.op("vector", lambda v, qs=qs: v.tensor_scalar(
                        out=ot[:, :], in0=ps_o[qs][:, 0:64], scalar1=rz[:, 0:1], scalar2=None, op0=ALU.mult))
                    b.bar()
                    b.op("sync", lambda s, pair=pair, qb=qb, qs=qs: s.dma_start(
                        out=o_at[pair, qb * 512 + qs * 128: qb * 512 + (qs + 1) * 128, :], in_=ot[:, :]), dma=True)
                    b.bar()

        b.emit(block, sems)
    nc.finalize()
    return nc



def emit_L3(nc, b, st, t, pfx=""):
    at_fm = t["at_fm"]
    lnx_fm = t["lnx_fm"]
    x0_fm = t["x0_fm"]
    n2_w = t["n2_w"]
    esel = t["esel"]
    ident = t["ident"]
    o_x1 = t["o_x1"]
    wdr = {}
    for nm, shp in [("woT", [D, D]), ("gateT", [D, 4]),
                    ("e0_upT", [D, HID]), ("e0_downT", [HID, D]),
                    ("e1_w1T", [D, HID]), ("e1_w2T", [D, HID]), ("e1_downT", [HID, D]),
                    ("e2_l1T", [D, HID]), ("e2_l2T", [HID, HID]), ("e2_l3T", [HID, HID]),
                    ("e2_l4T", [HID, D]),
                    ("e3_downT", [D, D]), ("e3_upT", [D, HID]), ("e3_outT", [HID, D])]:
        wdr[nm] = t[nm]

    SB = 512
    NSB = TPC // SB
    NTT = SB // 128
    NNB = SB // 512

    xp = st.enter_context(nc.sbuf_tensor(pfx + "xp", [128, NCH, SB], F32))
    lnxs = st.enter_context(nc.sbuf_tensor(pfx + "lnxs", [128, NCH, SB], F32))
    ats = st.enter_context(nc.sbuf_tensor(pfx + "ats", [128, NCH, SB], BF16))
    xf = st.enter_context(nc.sbuf_tensor(pfx + "xf", [128, NCH, SB], BF16))
    ga = st.enter_context(nc.sbuf_tensor(pfx + "ga", [128, 16, SB], BF16))
    gb = st.enter_context(nc.sbuf_tensor(pfx + "gb", [128, 16, SB], BF16))
    wbig = st.enter_context(nc.sbuf_tensor(pfx + "wbig", [128, 16, 1024], BF16))
    wgate = st.enter_context(nc.sbuf_tensor(pfx + "wgate", [128, NCH, 4], BF16))
    scr = st.enter_context(nc.sbuf_tensor(pfx + "scr", [128, 512], F32))
    webc = st.enter_context(nc.sbuf_tensor(pfx + "webc", [128, 4, SB], F32))
    wrow = st.enter_context(nc.sbuf_tensor(pfx + "wrow", [4, SB], F32))
    esl = st.enter_context(nc.sbuf_tensor(pfx + "esl", [4, 512], F32))
    idn = st.enter_context(nc.sbuf_tensor(pfx + "idn", [128, 128], F32))
    n2b = st.enter_context(nc.sbuf_tensor(pfx + "n2b", [128, NCH], F32))
    onesb = st.enter_context(nc.sbuf_tensor(pfx + "onesb", [128, 1], BF16))
    onesr = st.enter_context(nc.sbuf_tensor(pfx + "onesr", [1, 128], F32))
    rows = st.enter_context(nc.sbuf_tensor(pfx + "rows", [1, SB], F32))
    bcn = st.enter_context(nc.sbuf_tensor(pfx + "bcn", [128, SB], F32))
    lg = st.enter_context(nc.sbuf_tensor(pfx + "lg", [128, NTT, 4], F32))
    gs = st.enter_context(nc.sbuf_tensor(pfx + "gs", [128, 8], F32))
    ps_a = st.enter_context(nc.psum_tensor(pfx + "ps_a", [128, 512], F32))
    ps_b = st.enter_context(nc.psum_tensor(pfx + "ps_b", [128, 512], F32))
    ps_r = st.enter_context(nc.psum_tensor(pfx + "ps_r", [1, 512], F32))
    ps_bc = st.enter_context(nc.psum_tensor(pfx + "ps_bc", [128, 512], F32))
    ps_g = st.enter_context(nc.psum_tensor(pfx + "ps_g", [128, 4], F32))
    ps_t = st.enter_context(nc.psum_tensor(pfx + "ps_t", [4, 128], F32))
    se_s = st.enter_context(nc.semaphore(pfx + "se_s"))
    se_e = st.enter_context(nc.semaphore(pfx + "se_e"))
    cnt = {"s": 0, "e": 0}
    psl = [ps_a, ps_b]

    b.op("sync", lambda s: s.dma_start(out=esl[:], in_=esel[:, :]), dma=True)
    b.op("sync", lambda s: s.dma_start(out=idn[:], in_=ident[:, :]), dma=True)
    b.op("gpsimd", lambda g: g.dma_start(out=wgate[:], in_=wdr["gateT"].rearrange("(c p) o -> p c o", p=128)), dma=True)
    for c in range(NCH):
        b.op("sync", lambda s, c=c: s.dma_start(out=n2b[:, c:c + 1], in_=n2_w[c * 128:(c + 1) * 128, :]), dma=True)
    b.op("vector", lambda v: v.memset(onesb[:], 1.0))
    b.op("vector", lambda v: v.memset(onesr[:], 1.0))
    b.bar()

    def load_w(name, icr, ocr):
        nic = icr[1] - icr[0]
        oc0, oc1 = ocr
        b.op("gpsimd", lambda g: g.dma_start(
            out=wbig[:, 0:nic, 0:(oc1 - oc0)],
            in_=wdr[name].rearrange("(c p) o -> p c o", p=128)[:, icr[0]:icr[1], oc0:oc1]), dma=True)

    def mm_block(src, oc_list, nic, dst_evac, extra_wait_first=False):
        units = [(oc, nb) for oc in oc_list for nb in range(NNB)]
        e0 = cnt["e"]
        for j, (oc, nb) in enumerate(units):
            ps = psl[j % 2]

            def t_fn(t, oc=oc, nb=nb, ps=ps, wv=(None if j < 2 else e0 + (j - 1))):
                if wv is not None:
                    t.wait_ge(se_e, wv)
                for ic in range(nic):
                    ins = t.matmul(ps[:, :], wbig[:, ic, oc * 128:(oc + 1) * 128],
                                   src[:, ic, nb * 512:(nb + 1) * 512],
                                   start=(ic == 0), stop=(ic == nic - 1))
                return ins
            b.op("tensor", lambda t, f=t_fn: (f(t).then_inc(se_s, 1), None)[1])
            cnt["s"] += 1
            eng, e_fn = dst_evac(j, oc, nb, ps)

            def wrap(e, f=e_fn, sv=cnt["s"]):
                e.wait_ge(se_s, sv)
                return f(e)
            b.op(eng, lambda e, f=wrap: (f(e).then_inc(se_e, 1), None)[1])
            cnt["e"] += 1
        b.bar()

    for sb in range(NSB):
        t0 = sb * SB

        b.op("sync", lambda s, t0=t0: s.dma_start(
            out=ats[:], in_=at_fm.rearrange("(c p) t -> p c t", p=128)[:, :, t0:t0 + SB]), dma=True)
        b.op("sync", lambda s, t0=t0: s.dma_start(
            out=lnxs[:], in_=lnx_fm.rearrange("(c p) t -> p c t", p=128)[:, :, t0:t0 + SB]), dma=True)
        load_w("woT", (0, 8), (0, 1024))
        b.bar()

        for oc in range(2):
            b.op("vector", lambda v, oc=oc: v.tensor_copy(out=xp[:, oc, :], in_=lnxs[:, oc, :]))

        def wo_evac(j, oc, nb, ps):
            def f(v, oc=oc, nb=nb, ps=ps):
                return v.tensor_tensor(out=xp[:, oc, nb * 512:(nb + 1) * 512], in0=ps[:, :],
                                       in1=lnxs[:, oc, nb * 512:(nb + 1) * 512], op=ALU.add)
            return ("vector", f)
        mm_block(ats, list(range(2, NCH)), NCH, wo_evac)

        for c in range(NCH):
            b.op("scalar", lambda s, c=c: s.activation(out=ga[:, c, :], in_=xp[:, c, :], func=AF.Square))
        b.bar()
        for nb in range(NNB):
            for c in range(NCH):
                b.op("tensor", lambda t, c=c, nb=nb: t.matmul(
                    ps_r[0:1, :], onesb[:, 0:1], ga[:, c, nb * 512:(nb + 1) * 512],
                    start=(c == 0), stop=(c == NCH - 1)))
            b.bar()
            b.op("scalar", lambda s, nb=nb: s.activation(out=rows[:, nb * 512:(nb + 1) * 512], in_=ps_r[0:1, :],
                                                         func=AF.Sqrt, scale=f32(1.0 / D), bias=f32(EPS_RMS)))
            b.bar()
        b.op("vector", lambda v: v.reciprocal(out=rows[:, :], in_=rows[:, :]))
        b.bar()
        for nb in range(NNB):
            b.op("tensor", lambda t, nb=nb: t.matmul(ps_bc[:, :], onesr[0:1, :], rows[0:1, nb * 512:(nb + 1) * 512],
                                                     start=True, stop=True))
            b.bar()
            b.op("vector", lambda v, nb=nb: v.tensor_copy(out=bcn[:, nb * 512:(nb + 1) * 512], in_=ps_bc[:, :]))
            b.bar()
        for c in range(NCH):
            for nb in range(NNB):
                b.op("vector", lambda v, c=c, nb=nb: v.tensor_tensor(
                    out=scr[:, :], in0=xp[:, c, nb * 512:(nb + 1) * 512],
                    in1=bcn[:, nb * 512:(nb + 1) * 512], op=ALU.mult))
                b.bar()
                b.op("vector", lambda v, c=c, nb=nb: v.tensor_scalar(
                    out=xf[:, c, nb * 512:(nb + 1) * 512], in0=scr[:, :],
                    scalar1=n2b[:, c:c + 1], scalar2=None, op0=ALU.mult))
                b.bar()

        b.op("sync", lambda s, t0=t0: s.dma_start(
            out=lnxs[:], in_=x0_fm.rearrange("(c p) t -> p c t", p=128)[:, :, t0:t0 + SB]), dma=True)
        b.bar()
        for c in range(NCH):
            b.op("vector", lambda v, c=c: v.scalar_tensor_tensor(
                out=xp[:, c, :], in0=xp[:, c, :], scalar=0.5, in1=lnxs[:, c, :],
                op0=ALU.mult, op1=ALU.add))
        b.bar()

        for tt in range(NTT):
            b.op("tensor", lambda t, tt=tt: [
                t.matmul(ps_g[:, :], xf[:, ic, tt * 128:(tt + 1) * 128], wgate[:, ic, :],
                         start=(ic == 0), stop=(ic == NCH - 1)) for ic in range(NCH)][-1])
            b.bar()
            b.op("vector", lambda v, tt=tt: v.tensor_copy(out=lg[:, tt, :], in_=ps_g[:, :]))
            b.bar()
        for tt in range(NTT):
            b.op("vector", lambda v, tt=tt: v.tensor_reduce(out=gs[:, 0:1], in_=lg[:, tt, :], axis=AX.X, op=ALU.max))
            b.bar()
            b.op("vector", lambda v: v.tensor_scalar(out=gs[:, 1:2], in0=gs[:, 0:1], scalar1=-1.0, scalar2=None, op0=ALU.mult))
            b.bar()
            b.op("scalar", lambda s, tt=tt: s.activation(out=lg[:, tt, :], in_=lg[:, tt, :], func=AF.Exp, bias=gs[:, 1:2]))
            b.bar()
            b.op("vector", lambda v, tt=tt: v.tensor_reduce(out=gs[:, 2:3], in_=lg[:, tt, :], axis=AX.X, op=ALU.add))
            b.bar()
            b.op("vector", lambda v: v.reciprocal(out=gs[:, 3:4], in_=gs[:, 2:3]))
            b.bar()
            b.op("vector", lambda v, tt=tt: v.tensor_scalar(out=lg[:, tt, :], in0=lg[:, tt, :], scalar1=gs[:, 3:4], scalar2=None, op0=ALU.mult))
            b.bar()
            b.op("vector", lambda v, tt=tt: v.tensor_reduce(out=gs[:, 4:5], in_=lg[:, tt, :], axis=AX.X, op=ALU.max))
            b.bar()
            b.op("vector", lambda v, tt=tt: v.tensor_scalar(out=scr[:, 0:4], in0=lg[:, tt, :], scalar1=gs[:, 4:5], scalar2=None, op0=ALU.is_equal))
            b.bar()
            b.op("vector", lambda v: v.tensor_scalar(out=scr[:, 4:8], in0=scr[:, 0:4], scalar1=-1.0, scalar2=1.0, op0=ALU.mult, op1=ALU.add))
            b.bar()
            b.op("vector", lambda v, tt=tt: v.tensor_tensor(out=scr[:, 8:12], in0=lg[:, tt, :], in1=scr[:, 4:8], op=ALU.mult))
            b.bar()
            b.op("vector", lambda v: v.tensor_reduce(out=gs[:, 5:6], in_=scr[:, 8:12], axis=AX.X, op=ALU.max))
            b.bar()
            b.op("vector", lambda v, tt=tt: v.tensor_scalar(out=scr[:, 12:16], in0=lg[:, tt, :], scalar1=gs[:, 5:6], scalar2=None, op0=ALU.is_ge))
            b.bar()
            b.op("vector", lambda v: v.tensor_tensor(out=gs[:, 6:7], in0=gs[:, 4:5], in1=gs[:, 5:6], op=ALU.add))
            b.bar()
            b.op("scalar", lambda s: s.activation(out=gs[:, 6:7], in_=gs[:, 6:7], func=AF.Copy, bias=1e-08))
            b.bar()
            b.op("vector", lambda v: v.reciprocal(out=gs[:, 7:8], in_=gs[:, 6:7]))
            b.bar()
            b.op("vector", lambda v, tt=tt: v.tensor_tensor(out=lg[:, tt, :], in0=lg[:, tt, :], in1=scr[:, 12:16], op=ALU.mult))
            b.bar()
            b.op("vector", lambda v, tt=tt: v.tensor_scalar(out=lg[:, tt, :], in0=lg[:, tt, :], scalar1=gs[:, 7:8], scalar2=None, op0=ALU.mult))
            b.bar()
        for tt in range(NTT):
            b.op("tensor", lambda t, tt=tt: t.transpose(ps_t[:, :], lg[:, tt, :], idn[:, :]))
            b.bar()
            b.op("vector", lambda v, tt=tt: v.tensor_copy(out=wrow[:, tt * 128:(tt + 1) * 128], in_=ps_t[:, :]))
            b.bar()
        for e in range(4):
            for nb in range(NNB):
                b.op("tensor", lambda t, e=e, nb=nb: t.matmul(
                    ps_bc[:, :], esl[:, e * 128:(e + 1) * 128], wrow[0:4, nb * 512:(nb + 1) * 512],
                    start=True, stop=True))
                b.bar()
                b.op("vector", lambda v, e=e, nb=nb: v.tensor_copy(
                    out=webc[:, e, nb * 512:(nb + 1) * 512], in_=ps_bc[:, :]))
                b.bar()

        def act_evac(dst, func):
            def mk(j, oc, nb, ps):
                def f(s, dst=dst, oc=oc, nb=nb, ps=ps, func=func):
                    return s.activation(out=dst[:, oc, nb * 512:(nb + 1) * 512], in_=ps[:, :], func=func)
                return ("scalar", f)
            return mk

        def macc_evac(e, first):
            def mk(j, oc, nb, ps):
                def f(v, e=e, oc=oc, nb=nb, ps=ps):
                    ins = v.tensor_tensor(out=scr[:, :], in0=ps[:, :],
                                          in1=webc[:, e, nb * 512:(nb + 1) * 512], op=ALU.mult)
                    v.scalar_tensor_tensor(out=xp[:, oc, nb * 512:(nb + 1) * 512], in0=scr[:, :],
                                           scalar=0.5, in1=xp[:, oc, nb * 512:(nb + 1) * 512],
                                           op0=ALU.mult, op1=ALU.add)
                    return ins
                return ("vector", f)
            return mk

        load_w("e0_upT", (0, 8), (0, 1024)); b.bar()
        mm_block(xf, list(range(8)), NCH, act_evac(ga, AF.Gelu))
        load_w("e0_upT", (0, 8), (1024, 2048)); b.bar()
        mm_block(xf, list(range(8)), NCH, lambda j, oc, nb, ps: act_evac(ga, AF.Gelu)(j, oc + 8, nb, ps))
        load_w("e0_downT", (0, 16), (0, 1024)); b.bar()
        mm_block(ga, list(range(2, NCH)), 16, macc_evac(0, True))
        load_w("e1_w1T", (0, 8), (0, 1024)); b.bar()
        mm_block(xf, list(range(8)), NCH, act_evac(ga, AF.Silu))
        load_w("e1_w1T", (0, 8), (1024, 2048)); b.bar()
        mm_block(xf, list(range(8)), NCH, lambda j, oc, nb, ps: act_evac(ga, AF.Silu)(j, oc + 8, nb, ps))
        load_w("e1_w2T", (0, 8), (0, 1024)); b.bar()
        mm_block(xf, list(range(8)), NCH, act_evac(gb, AF.Copy))
        load_w("e1_w2T", (0, 8), (1024, 2048)); b.bar()
        mm_block(xf, list(range(8)), NCH, lambda j, oc, nb, ps: act_evac(gb, AF.Copy)(j, oc + 8, nb, ps))
        for hc in range(16):
            b.op("vector", lambda v, hc=hc: v.tensor_tensor(out=ga[:, hc, :], in0=ga[:, hc, :], in1=gb[:, hc, :], op=ALU.mult))
        b.bar()
        load_w("e1_downT", (0, 16), (0, 1024)); b.bar()
        mm_block(ga, list(range(2, NCH)), 16, macc_evac(1, False))
        load_w("e2_l1T", (0, 8), (0, 1024)); b.bar()
        mm_block(xf, list(range(8)), NCH, act_evac(ga, AF.Gelu))
        load_w("e2_l1T", (0, 8), (1024, 2048)); b.bar()
        mm_block(xf, list(range(8)), NCH, lambda j, oc, nb, ps: act_evac(ga, AF.Gelu)(j, oc + 8, nb, ps))
        load_w("e2_l2T", (0, 16), (0, 1024)); b.bar()
        mm_block(ga, list(range(8)), 16, act_evac(gb, AF.Gelu))
        load_w("e2_l2T", (0, 16), (1024, 2048)); b.bar()
        mm_block(ga, list(range(8)), 16, lambda j, oc, nb, ps: act_evac(gb, AF.Gelu)(j, oc + 8, nb, ps))
        load_w("e2_l3T", (0, 16), (0, 1024)); b.bar()
        mm_block(gb, list(range(8)), 16, act_evac(ga, AF.Gelu))
        load_w("e2_l3T", (0, 16), (1024, 2048)); b.bar()
        mm_block(gb, list(range(8)), 16, lambda j, oc, nb, ps: act_evac(ga, AF.Gelu)(j, oc + 8, nb, ps))
        load_w("e2_l4T", (0, 16), (0, 1024)); b.bar()
        mm_block(ga, list(range(2, NCH)), 16, macc_evac(2, False))
        load_w("e3_downT", (0, 8), (0, 1024)); b.bar()
        mm_block(xf, list(range(8)), NCH, act_evac(ga, AF.Gelu))
        load_w("e3_upT", (0, 8), (0, 1024)); b.bar()
        mm_block(ga, list(range(8)), 8, act_evac(gb, AF.Gelu))
        load_w("e3_upT", (0, 8), (1024, 2048)); b.bar()
        mm_block(ga, list(range(8)), 8, lambda j, oc, nb, ps: act_evac(gb, AF.Gelu)(j, oc + 8, nb, ps))
        load_w("e3_outT", (0, 16), (0, 1024)); b.bar()
        mm_block(gb, list(range(2, NCH)), 16, macc_evac(3, False))

        for c in range(NCH):
            b.op("sync", lambda s, c=c, t0=t0: s.dma_start(
                out=o_x1[c * 128:(c + 1) * 128, t0:t0 + SB], in_=xp[:, c, :]), dma=True)
        b.bar()






def emit_L4a(nc, b, st, t, pfx=""):
    x_fm = t["x_fm"]
    wq_w = t["wq_w"]
    wk_w = t["wk_w"]
    wv_w = t["wv_w"]
    wqi_w = t["wqi_w"]
    wki_w = t["wki_w"]
    wq_b = t["wq_b"]
    wk_b = t["wk_b"]
    wv_b = t["wv_b"]
    wqi_b = t["wqi_b"]
    wki_b = t["wki_b"]
    o_qT = t["o_qT"]
    o_kv = t["o_kv"]
    o_qi = t["o_qi"]
    o_ki = t["o_ki"]

    NB = TPC // 512

    xs = st.enter_context(nc.sbuf_tensor(pfx + "xs", [128, NCH, 512], F32))
    xbb = st.enter_context(nc.sbuf_tensor(pfx + "xbb", [128, NCH, 512], BF16))
    lnf = st.enter_context(nc.sbuf_tensor(pfx + "lnf", [128, NCH, 512], F32))
    lnb = st.enter_context(nc.sbuf_tensor(pfx + "lnb", [128, NCH, 512], BF16))
    wqs = st.enter_context(nc.sbuf_tensor(pfx + "wqs", [128, NCH, D], BF16))
    wks = st.enter_context(nc.sbuf_tensor(pfx + "wks", [128, NCH, D], BF16))
    wvs = st.enter_context(nc.sbuf_tensor(pfx + "wvs", [128, NCH, D], BF16))
    wqis = st.enter_context(nc.sbuf_tensor(pfx + "wqis", [128, NCH, IDXD], F32))
    wkis = st.enter_context(nc.sbuf_tensor(pfx + "wkis", [128, NCH, IDXD], F32))
    qbias = st.enter_context(nc.sbuf_tensor(pfx + "qbias", [128, NCH], F32))
    qibias = st.enter_context(nc.sbuf_tensor(pfx + "qibias", [64, 1], F32))
    kibias = st.enter_context(nc.sbuf_tensor(pfx + "kibias", [64, 1], F32))
    kbrow = st.enter_context(nc.sbuf_tensor(pfx + "kbrow", [1, D], F32))
    vbrow = st.enter_context(nc.sbuf_tensor(pfx + "vbrow", [1, D], F32))
    kvbc = st.enter_context(nc.sbuf_tensor(pfx + "kvbc", [128, 2, D], F32))
    onesb = st.enter_context(nc.sbuf_tensor(pfx + "onesb", [128, 1], BF16))
    onesr = st.enter_context(nc.sbuf_tensor(pfx + "onesr", [1, 128], F32))
    rows = st.enter_context(nc.sbuf_tensor(pfx + "rows", [1, 4 * 512], F32))
    bcs = st.enter_context(nc.sbuf_tensor(pfx + "bcs", [128, 2, 512], F32))
    sqv = st.enter_context(nc.sbuf_tensor(pfx + "sqv", [128, 512], BF16))
    qko = st.enter_context(nc.sbuf_tensor(pfx + "qko", [128, 512], BF16))
    kvstage = st.enter_context(nc.sbuf_tensor(pfx + "kvstage", [128, 4, 512], BF16))
    idxo = st.enter_context(nc.sbuf_tensor(pfx + "idxo", [64, 2, 512], F32))
    ps_a = st.enter_context(nc.psum_tensor(pfx + "ps_a", [128, 512], F32))
    ps_b = st.enter_context(nc.psum_tensor(pfx + "ps_b", [128, 512], F32))
    ps_r0 = st.enter_context(nc.psum_tensor(pfx + "ps_r0", [1, 512], F32))
    ps_r1 = st.enter_context(nc.psum_tensor(pfx + "ps_r1", [1, 512], F32))
    ps_bc = st.enter_context(nc.psum_tensor(pfx + "ps_bc", [128, 512], F32))
    ps_i = st.enter_context(nc.psum_tensor(pfx + "ps_i", [64, 512], F32))

    b.op("gpsimd", lambda g: g.dma_start(out=wqs[:], in_=wq_w.rearrange("(c p) o -> p c o", p=128)), dma=True)
    b.op("gpsimd", lambda g: g.dma_start(out=wks[:], in_=wk_w.rearrange("(c p) o -> p c o", p=128)), dma=True)
    b.op("gpsimd", lambda g: g.dma_start(out=wvs[:], in_=wv_w.rearrange("(c p) o -> p c o", p=128)), dma=True)
    b.op("sync", lambda s: s.dma_start(out=wqis[:], in_=wqi_w.rearrange("(c p) o -> p c o", p=128)), dma=True)
    b.op("sync", lambda s: s.dma_start(out=wkis[:], in_=wki_w.rearrange("(c p) o -> p c o", p=128)), dma=True)
    for c in range(NCH):
        b.op("sync", lambda s, c=c: s.dma_start(out=qbias[:, c:c + 1],
                                                in_=wq_b.rearrange("o d -> d o")[c * 128:(c + 1) * 128, :]), dma=True)
    b.op("sync", lambda s: s.dma_start(out=qibias[:], in_=wqi_b.rearrange("o d -> d o")), dma=True)
    b.op("sync", lambda s: s.dma_start(out=kibias[:], in_=wki_b.rearrange("o d -> d o")), dma=True)
    b.op("sync", lambda s: s.dma_start(out=kbrow[:], in_=wk_b[:, :]), dma=True)
    b.op("sync", lambda s: s.dma_start(out=vbrow[:], in_=wv_b[:, :]), dma=True)
    b.op("vector", lambda v: v.memset(onesb[:], 1.0))
    b.op("vector", lambda v: v.memset(onesr[:], 1.0))
    b.bar()
    for j, br in enumerate([kbrow, vbrow]):
        for nb2 in range(2):
            b.op("tensor", lambda t, br=br, nb2=nb2: t.matmul(
                ps_bc[:, :], onesr[0:1, :], br[0:1, nb2 * 512:(nb2 + 1) * 512], start=True, stop=True))
            b.bar()
            b.op("vector", lambda v, j=j, nb2=nb2: v.tensor_copy(
                out=kvbc[:, j, nb2 * 512:(nb2 + 1) * 512], in_=ps_bc[:, :]))
            b.bar()

    for blk in range(NB):
        t0 = blk * 512
        b.op("sync", lambda s, t0=t0: s.dma_start(
            out=xs[:], in_=x_fm.rearrange("(c p) t -> p c t", p=128)[:, :, t0:t0 + 512]), dma=True)
        b.op("gpsimd", lambda g, t0=t0: g.dma_start(
            out=xbb[:], in_=x_fm.rearrange("(c p) t -> p c t", p=128)[:, :, t0:t0 + 512]), dma=True)
        b.bar()
        for c in range(NCH):
            b.op("scalar", lambda s, c=c: s.activation(out=sqv[:, :] if False else lnb[:, c, :],
                                                       in_=xs[:, c, :], func=AF.Square))
        b.bar()
        for c in range(NCH):
            b.op("tensor", lambda t, c=c: t.matmul(ps_r0[0:1, :], onesb[:, 0:1], xbb[:, c, :],
                                                   start=(c == 0), stop=(c == NCH - 1)))
        for c in range(NCH):
            b.op("tensor", lambda t, c=c: t.matmul(ps_r1[0:1, :], onesb[:, 0:1], lnb[:, c, :],
                                                   start=(c == 0), stop=(c == NCH - 1)))
        b.bar()
        b.op("scalar", lambda s: s.activation(out=rows[:, 0:512], in_=ps_r0[0:1, :], func=AF.Copy, scale=f32(1.0 / D)))
        b.op("scalar", lambda s: s.activation(out=rows[:, 512:1024], in_=ps_r1[0:1, :], func=AF.Copy, scale=f32(1.0 / D)))
        b.bar()
        b.op("vector", lambda v: v.tensor_tensor(out=rows[:, 1024:1536], in0=rows[:, 0:512], in1=rows[:, 0:512], op=ALU.mult))
        b.bar()
        b.op("vector", lambda v: v.tensor_tensor(out=rows[:, 512:1024], in0=rows[:, 512:1024], in1=rows[:, 1024:1536], op=ALU.subtract))
        b.bar()
        b.op("scalar", lambda s: s.activation(out=rows[:, 1024:1536], in_=rows[:, 512:1024], func=AF.Sqrt, bias=f32(EPS_LN)))
        b.bar()
        b.op("vector", lambda v: v.reciprocal(out=rows[:, 1024:1536], in_=rows[:, 1024:1536]))
        b.bar()
        for j, off in enumerate([0, 1024]):
            b.op("tensor", lambda t, off=off: t.matmul(ps_bc[:, :], onesr[0:1, :], rows[0:1, off:off + 512],
                                                       start=True, stop=True))
            b.bar()
            b.op("vector", lambda v, j=j: v.tensor_copy(out=bcs[:, j, :], in_=ps_bc[:, :]))
            b.bar()
        for c in range(NCH):
            b.op("vector", lambda v, c=c: v.tensor_tensor(out=lnf[:, c, :], in0=xs[:, c, :], in1=bcs[:, 0, :], op=ALU.subtract))
        b.bar()
        for c in range(NCH):
            b.op("vector", lambda v, c=c: v.tensor_tensor(out=lnf[:, c, :], in0=lnf[:, c, :], in1=bcs[:, 1, :], op=ALU.mult))
        b.bar()
        for c in range(NCH):
            b.op("vector", lambda v, c=c: v.tensor_copy(out=lnb[:, c, :], in_=lnf[:, c, :]))
        b.bar()
        for oc in range(NCH):
            b.op("tensor", lambda t, oc=oc: [
                t.matmul(ps_a[:, :], wqs[:, ic, oc * 128:(oc + 1) * 128], lnb[:, ic, :],
                         start=(ic == 0), stop=(ic == NCH - 1)) for ic in range(NCH)][-1])
            b.bar()
            b.op("vector", lambda v, oc=oc: v.tensor_scalar(out=qko[:, :], in0=ps_a[:, :],
                                                            scalar1=qbias[:, oc:oc + 1], scalar2=0.125,
                                                            op0=ALU.add, op1=ALU.mult))
            b.bar()
            b.op("sync", lambda s, oc=oc, t0=t0: s.dma_start(
                out=o_qT[oc * 128:(oc + 1) * 128, t0:t0 + 512], in_=qko[:, :]), dma=True)
            b.bar()
        for tt in range(4):
            for j, ws in enumerate([wks, wvs]):
                for ob in range(2):
                    b.op("tensor", lambda t, tt=tt, ws=ws, ob=ob: [
                        t.matmul(ps_b[:, :], lnb[:, ic, tt * 128:(tt + 1) * 128],
                                 ws[:, ic, ob * 512:(ob + 1) * 512],
                                 start=(ic == 0), stop=(ic == NCH - 1)) for ic in range(NCH)][-1])
                    b.bar()
                    b.op("vector", lambda v, j=j, ob=ob: v.tensor_tensor(
                        out=kvstage[:, j * 2 + ob, :], in0=ps_b[:, :],
                        in1=kvbc[:, j, ob * 512:(ob + 1) * 512], op=ALU.add))
                    b.bar()
            b.op("sync", lambda s, tt=tt, t0=t0: s.dma_start(
                out=o_kv[t0 + tt * 128: t0 + (tt + 1) * 128, :],
                in_=kvstage.ap().rearrange("p a t -> p (a t)")), dma=True)
            b.bar()
        for j, (ws, bia, oo) in enumerate([(wqis, qibias, o_qi), (wkis, kibias, o_ki)]):
            b.op("tensor", lambda t, ws=ws: [
                t.matmul(ps_i[:, :], ws[:, ic, :], lnf[:, ic, :],
                         start=(ic == 0), stop=(ic == NCH - 1)) for ic in range(NCH)][-1])
            b.bar()
            b.op("vector", lambda v, bia=bia: v.tensor_scalar(out=idxo[:, 0, :], in0=ps_i[:, :],
                                                              scalar1=bia[:, 0:1], scalar2=None, op0=ALU.add))
            b.bar()
            b.op("sync", lambda s, oo=oo, t0=t0: s.dma_start(
                out=oo[:, t0:t0 + 512], in_=idxo[:, 0, :]), dma=True)
            b.bar()






L1_DRAMS = [("x_fm", [D, TPC], F32, "ExternalInput"),
            ("wqT", [D, D], F32, "ExternalInput"),
            ("wkT", [D, D], F32, "ExternalInput"),
            ("wvT", [D, D], F32, "ExternalInput"),
            ("n1_w", [D, 1], F32, "ExternalInput"),
            ("qn_w", [HD, 1], F32, "ExternalInput"),
            ("kn_w", [HD, 1], F32, "ExternalInput"),
            ("o_lnx", [D, TPC], F32, "ExternalOutput"),
            ("o_qT", [D, TPC], BF16, "ExternalOutput"),
            ("o_kT", [D, TPC], BF16, "ExternalOutput"),
            ("o_v", [TPC, D], BF16, "ExternalOutput")]

L3_W_SHAPES = [("woT", [D, D]), ("gateT", [D, 4]),
               ("e0_upT", [D, HID]), ("e0_downT", [HID, D]),
               ("e1_w1T", [D, HID]), ("e1_w2T", [D, HID]), ("e1_downT", [HID, D]),
               ("e2_l1T", [D, HID]), ("e2_l2T", [HID, HID]), ("e2_l3T", [HID, HID]),
               ("e2_l4T", [HID, D]),
               ("e3_downT", [D, D]), ("e3_upT", [D, HID]), ("e3_outT", [HID, D])]

L3_DRAMS = ([("at_fm", [D, TPC], BF16, "ExternalInput"),
             ("lnx_fm", [D, TPC], F32, "ExternalInput"),
             ("x0_fm", [D, TPC], F32, "ExternalInput"),
             ("n2_w", [D, 1], F32, "ExternalInput"),
             ("esel", [4, 512], F32, "ExternalInput"),
             ("ident", [128, 128], F32, "ExternalInput"),
             ("o_x1", [D, TPC], F32, "ExternalOutput")]
            + [(nm, shp, F32, "ExternalInput") for nm, shp in L3_W_SHAPES])

L4A_DRAMS = [("x_fm", [D, TPC], F32, "ExternalInput"),
             ("wq_w", [D, D], F32, "ExternalInput"),
             ("wk_w", [D, D], F32, "ExternalInput"),
             ("wv_w", [D, D], F32, "ExternalInput"),
             ("wqi_w", [D, IDXD], F32, "ExternalInput"),
             ("wki_w", [D, IDXD], F32, "ExternalInput"),
             ("wq_b", [1, D], F32, "ExternalInput"),
             ("wk_b", [1, D], F32, "ExternalInput"),
             ("wv_b", [1, D], F32, "ExternalInput"),
             ("wqi_b", [1, IDXD], F32, "ExternalInput"),
             ("wki_b", [1, IDXD], F32, "ExternalInput"),
             ("o_qT", [D, TPC], BF16, "ExternalOutput"),
             ("o_kv", [TPC, 2 * D], BF16, "ExternalOutput"),
             ("o_qi", [IDXD, TPC], F32, "ExternalOutput"),
             ("o_ki", [IDXD, TPC], F32, "ExternalOutput")]


def _mk_build(emit_fn, drams, pfx):
    def build():
        nc = bacc.Bacc("TRN2", target_bir_lowering=False)
        t = {nm: nc.dram_tensor(nm, shp, dt, kind=kd).ap() for nm, shp, dt, kd in drams}
        with ExitStack() as st:
            block = st.enter_context(nc.Block())
            sems = {e: st.enter_context(nc.semaphore("s_" + e)) for e in ENGS}
            b = Builder(nc)
            with ExitStack() as sst:
                emit_fn(nc, b, sst, t, pfx)
            b.emit(block, sems)
        nc.finalize()
        return nc
    return build


build_L1 = _mk_build(emit_L1, L1_DRAMS, "l1_")
build_L3 = _mk_build(emit_L3, L3_DRAMS, "l3_")
build_L4a = _mk_build(emit_L4a, L4A_DRAMS, "l4a_")


def emit_L4b(nc, b, st, t, pfx="", nqt=None):
    qT_d = t["qT_d"]
    qiT_d = t["qiT_d"]
    g_ki = t["g_ki"]
    kv_d = t["kv_d"]
    x1_d = t["x1_d"]
    wo_w = t["wo_w"]
    wo_b = t["wo_b"]
    ident = t["ident"]
    identb = t["identb"]
    cbase = t["cbase"]
    o_x2 = t["o_x2"]
    bK = t["bK"]
    bV = t["bV"]
    vaDR = t["vaDR"]

    NQT = TPC // 128
    NAB = AL // 512

    qts = st.enter_context(nc.sbuf_tensor(pfx + "qts", [128, NCH, 128], BF16))
    qis = st.enter_context(nc.sbuf_tensor(pfx + "qis", [64, 128], F32))
    kis = st.enter_context(nc.sbuf_tensor(pfx + "kis", [64, AL], F32))
    sidx = st.enter_context(nc.sbuf_tensor(pfx + "sidx", [128, AL], F32))
    cand = st.enter_context(nc.sbuf_tensor(pfx + "cand", [128, 960], F32))
    candg = st.enter_context(nc.sbuf_tensor(pfx + "candg", [128, 960], F32))
    candi = st.enter_context(nc.sbuf_tensor(pfx + "candi", [128, 960], U32))
    cbs = st.enter_context(nc.sbuf_tensor(pfx + "cbs", [128, 960], F32))
    tv = st.enter_context(nc.sbuf_tensor(pfx + "tv", [128, 64], F32))
    picks = st.enter_context(nc.sbuf_tensor(pfx + "picks", [128, 64], F32))
    xkv = st.enter_context(nc.sbuf_tensor(pfx + "xkv", [128, 2, 64], F32))
    xkvi = st.enter_context(nc.sbuf_tensor(pfx + "xkvi", [128, 2, 64], I16))
    idxk = st.enter_context(nc.sbuf_tensor(pfx + "idxk", [128, 512], I16))
    idxv = st.enter_context(nc.sbuf_tensor(pfx + "idxv", [128, 512], I16))
    gkt = st.enter_context(nc.sbuf_tensor(pfx + "gkt", [128, 8, 512], BF16))
    gv = st.enter_context(nc.sbuf_tensor(pfx + "gv", [128, 8, 1024], BF16))
    kwTs = st.enter_context(nc.sbuf_tensor(pfx + "kwTs", [128, NCH, WM], BF16))
    vws = st.enter_context(nc.sbuf_tensor(pfx + "vws", [128, 4, D], BF16))
    wobr = st.enter_context(nc.sbuf_tensor(pfx + "wobr", [128, NCH], F32))
    pa = st.enter_context(nc.sbuf_tensor(pfx + "pa", [128, 256], BF16))
    pwt = st.enter_context(nc.sbuf_tensor(pfx + "pwt", [128, 8, 4, 128], BF16))
    vas = st.enter_context(nc.sbuf_tensor(pfx + "vas", [64, 2048], F32))
    vtr = st.enter_context(nc.sbuf_tensor(pfx + "vtr", [128, 64], F32))
    vaE = st.enter_context(nc.sbuf_tensor(pfx + "vaE", [64, 1024], F32))
    vaO = st.enter_context(nc.sbuf_tensor(pfx + "vaO", [64, 1024], F32))
    atw = st.enter_context(nc.sbuf_tensor(pfx + "atw", [64, 2, 1024], F32))
    attn_fm = st.enter_context(nc.sbuf_tensor(pfx + "attn_fm", [128, NCH, 128], BF16))
    zrow = st.enter_context(nc.sbuf_tensor(pfx + "zrow", [1, 2048], F32))
    zh = st.enter_context(nc.sbuf_tensor(pfx + "zh", [1, 2048], F32))
    rzp = st.enter_context(nc.sbuf_tensor(pfx + "rzp", [64, 2, 16], F32))
    mkcol = st.enter_context(nc.sbuf_tensor(pfx + "mkcol", [128, 80], F32))
    mks = st.enter_context(nc.sbuf_tensor(pfx + "mks", [1, 8], F32))
    ebias = st.enter_context(nc.sbuf_tensor(pfx + "ebias", [128, 1], F32))
    ksqb = st.enter_context(nc.sbuf_tensor(pfx + "ksqb", [128, 1024], BF16))
    x1s = st.enter_context(nc.sbuf_tensor(pfx + "x1s", [128, NCH, 128], F32))
    x2s = st.enter_context(nc.sbuf_tensor(pfx + "x2s", [128, NCH, 128], F32))
    idn = st.enter_context(nc.sbuf_tensor(pfx + "idn", [128, 128], F32))
    idnb = st.enter_context(nc.sbuf_tensor(pfx + "idnb", [128, 128], BF16))
    onesb = st.enter_context(nc.sbuf_tensor(pfx + "onesb", [128, 1], BF16))
    onesr = st.enter_context(nc.sbuf_tensor(pfx + "onesr", [1, 128], F32))
    ps_a = st.enter_context(nc.psum_tensor(pfx + "ps_a", [128, 512], F32))
    ps_b = st.enter_context(nc.psum_tensor(pfx + "ps_b", [128, 512], F32))
    ps_sc = st.enter_context(nc.psum_tensor(pfx + "ps_sc", [128, 512], F32))
    ps_va = st.enter_context(nc.psum_tensor(pfx + "ps_va", [64, 512], F32))
    ps_owE = st.enter_context(nc.psum_tensor(pfx + "ps_owE", [64, 1024], F32))
    ps_owO = st.enter_context(nc.psum_tensor(pfx + "ps_owO", [64, 1024], F32))
    se_v = st.enter_context(nc.semaphore(pfx + "se_v"))
    cnt = {"v": 0}

    def vop(fn):
        w = cnt["v"]

        def g(v, fn=fn, w=w):
            if w > 0:
                v.wait_ge(se_v, w)
            return fn(v)
        b.op("vector", lambda v, g=g: (g(v).then_inc(se_v, 1), None)[1])
        cnt["v"] += 1

    b.op("gpsimd", lambda g: g.dma_start(out=idnb[:], in_=identb[:, :]), dma=True)
    b.op("sync", lambda s: s.dma_start(out=kis[:, 0:1536], in_=g_ki[0:IDXD, 0:1536]), dma=True)
    for s2 in range(1, 4):
        b.op("sync", lambda s, s2=s2: s.dma_start(
            out=kis[:, 1536 + 2048 * (s2 - 1):1536 + 2048 * s2],
            in_=g_ki[IDXD * s2:IDXD * (s2 + 1), :]), dma=True)
    b.op("sync", lambda s: s.dma_start(
        out=vws[:], in_=kv_d[1536:2048, D:2 * D].rearrange("(n p) d -> p n d", p=128)), dma=True)
    b.op("sync", lambda s: s.dma_start(out=cbs[:], in_=cbase[:, :]), dma=True)
    b.op("sync", lambda s: s.dma_start(out=idn[:], in_=ident[:, :]), dma=True)
    for c in range(NCH):
        b.op("sync", lambda s, c=c: s.dma_start(
            out=wobr[:, c:c + 1], in_=wo_b.rearrange("o d -> d o")[c * 128:(c + 1) * 128, :]), dma=True)
    b.op("vector", lambda v: v.memset(onesb[:], 1.0))
    b.op("vector", lambda v: v.memset(onesr[:], 1.0))
    b.op("vector", lambda v: v.memset(idxk[:], 0))
    b.op("vector", lambda v: v.memset(idxv[:], 0))
    b.bar()

    for tt in range(4):
        b.op("sync", lambda s, tt=tt: s.dma_start(
            out=ksqb[:], in_=kv_d[1536 + 128 * tt:1536 + 128 * (tt + 1), 0:D]), dma=True)
        b.bar()
        b.op("vector", lambda v: v.tensor_copy(out=x1s.ap().rearrange("p c t -> p (c t)"), in_=ksqb[:]))
        b.bar()
        for cc in range(NCH):
            b.op("tensor", lambda t, cc=cc: t.transpose(
                ps_a[:, 0:128], x1s.ap().rearrange("p c t -> p (c t)")[:, cc * 128:(cc + 1) * 128],
                idn[:, :]))
            b.bar()
            b.op("vector", lambda v, cc=cc, tt=tt: v.tensor_copy(
                out=kwTs[:, cc, tt * 128:(tt + 1) * 128], in_=ps_a[:, 0:128]))
            b.bar()

    for kt in range(64):
        b.op("gpsimd", lambda g, kt=kt: g.dma_start(
            out=ksqb[:], in_=kv_d[kt * 128:(kt + 1) * 128, 0:D]), dma=True)
        b.bar()
        b.op("scalar", lambda s: s.activation(out=ksqb[:], in_=ksqb[:], func=AF.Square))
        b.bar()
        b.op("vector", lambda v, kt=kt: v.tensor_reduce(
            out=mkcol[:, 64:80],
            in_=ksqb.ap().rearrange("p (h d) -> p h d", h=16), axis=AX.X, op=ALU.add))
        b.bar()
        b.op("vector", lambda v, kt=kt: v.tensor_reduce(
            out=mkcol[:, kt:kt + 1], in_=mkcol[:, 64:80], axis=AX.X, op=ALU.max))
        b.bar()
    b.op("vector", lambda v: v.tensor_reduce(out=mkcol[:, 64:65], in_=mkcol[:, 0:64], axis=AX.X, op=ALU.max))
    b.bar()
    b.op("tensor", lambda t: t.transpose(ps_a[0:1, 0:128], mkcol[:, 64:65], idn[:, :]))
    b.bar()
    b.op("vector", lambda v: v.tensor_copy(out=zh[:, 0:128], in_=ps_a[0:1, 0:128]))
    b.bar()
    b.op("vector", lambda v: v.tensor_reduce(out=mks[:, 0:1], in_=zh[:, 0:128], axis=AX.X, op=ALU.max))
    b.bar()
    b.op("scalar", lambda s: s.activation(out=mks[:, 0:1], in_=mks[:, 0:1], func=AF.Sqrt))
    b.bar()

    for qt in range(nqt if nqt is not None else NQT):
        q0 = qt * 128
        b.op("sync", lambda s, q0=q0: s.dma_start(
            out=qts[:], in_=qT_d.rearrange("(c p) t -> p c t", p=128)[:, :, q0:q0 + 128]), dma=True)
        b.op("sync", lambda s, q0=q0: s.dma_start(out=qis[:], in_=qiT_d[:, q0:q0 + 128]), dma=True)
        b.op("sync", lambda s, q0=q0: s.dma_start(
            out=x1s[:], in_=x1_d.rearrange("(c p) t -> p c t", p=128)[:, :, q0:q0 + 128]), dma=True)
        b.bar()
        for c in range(NCH):
            b.op("scalar", lambda s, c=c: s.activation(out=ksqb[:, c * 128:(c + 1) * 128],
                                                       in_=qts[:, c, :], func=AF.Square))
        b.bar()
        for h in range(16):
            ps = [ps_a, ps_b][h % 2]
            b.op("tensor", lambda t, h=h, ps=ps: t.matmul(
                ps[0:1, 0:128], onesb[64 * (h % 2):64 * (h % 2) + 64, 0:1],
                ksqb[64 * (h % 2):64 * (h % 2) + 64, (h // 2) * 128:(h // 2 + 1) * 128],
                start=True, stop=True))
            b.bar()
            b.op("vector", lambda v, h=h, ps=ps: v.tensor_copy(out=zh[:, h * 128:(h + 1) * 128], in_=ps[0:1, 0:128]))
            b.bar()
        b.op("vector", lambda v: v.tensor_reduce(out=mks[:, 1:2], in_=zh[:, 0:2048], axis=AX.X, op=ALU.max))
        b.bar()
        b.op("scalar", lambda s: s.activation(out=mks[:, 1:2], in_=mks[:, 1:2], func=AF.Sqrt))
        b.bar()
        b.op("vector", lambda v: v.tensor_tensor(out=mks[:, 2:3], in0=mks[:, 1:2], in1=mks[:, 0:1], op=ALU.mult))
        b.bar()
        b.op("vector", lambda v: v.tensor_scalar(out=mks[:, 2:3], in0=mks[:, 2:3], scalar1=-1.0, scalar2=None, op0=ALU.mult))
        b.bar()
        b.op("tensor", lambda t: t.matmul(ps_a[:, 0:1], onesr[0:1, :], mks[0:1, 2:3], start=True, stop=True))
        b.bar()
        b.op("vector", lambda v: v.tensor_copy(out=ebias[:, :], in_=ps_a[:, 0:1]))
        b.bar()

        for ab in range(NAB):
            ps = [ps_a, ps_b][ab % 2]
            b.op("tensor", lambda t, ab=ab, ps=ps: t.matmul(
                ps[:, :], qis[:, :], kis[:, ab * 512:(ab + 1) * 512], start=True, stop=True))
            b.bar()
            b.op("vector", lambda v, ab=ab, ps=ps: v.tensor_copy(
                out=sidx[:, ab * 512:(ab + 1) * 512], in_=ps[:, :]))
            b.bar()

        for c in range(120):
            vop(lambda v, c=c: v.max(out=cand[:, c * 8:(c + 1) * 8], in_=sidx[:, c * 64:(c + 1) * 64]))
            vop(lambda v, c=c: v.max_index(out=candi[:, c * 8:(c + 1) * 8],
                                           in_max=cand[:, c * 8:(c + 1) * 8],
                                           in_=sidx[:, c * 64:(c + 1) * 64]) if False else
                v.max_index(out=candi[:, c * 8:(c + 1) * 8], in_max=cand[:, c * 8:(c + 1) * 8],
                            in_values=sidx[:, c * 64:(c + 1) * 64]))
        vop(lambda v: v.tensor_copy(out=candg[:, :], in_=cand[:, :]))
        for it in range(8):
            vop(lambda v, it=it: v.max(out=tv[:, it * 8:(it + 1) * 8], in_=candg[:, :]))
            vop(lambda v, it=it: v.match_replace(out=candg[:, :], in_to_replace=tv[:, it * 8:(it + 1) * 8],
                                                 in_values=candg[:, :], imm_value=-1e30))
        vop(lambda v: v.tensor_copy(out=candg[:, :], in_=candi[:, :]))
        vop(lambda v: v.tensor_tensor(out=candg[:, :], in0=candg[:, :], in1=cbs[:, :], op=ALU.add))
        vop(lambda v: v.tensor_scalar(out=cand[:, :], in0=cand[:, :], scalar1=tv[:, 63:64], scalar2=None, op0=ALU.is_ge))
        vop(lambda v: v.tensor_tensor(out=candg[:, :], in0=candg[:, :], in1=cand[:, :], op=ALU.mult))
        for it in range(8):
            vop(lambda v, it=it: v.max(out=picks[:, it * 8:(it + 1) * 8], in_=candg[:, :]))
            vop(lambda v, it=it: v.match_replace(out=candg[:, :], in_to_replace=picks[:, it * 8:(it + 1) * 8],
                                                 in_values=candg[:, :], imm_value=0.0))
        vop(lambda v: v.tensor_scalar(out=xkv[:, 0, :], in0=picks[:, :], scalar1=2.0, scalar2=-2.0,
                                      op0=ALU.mult, op1=ALU.add))
        vop(lambda v: v.tensor_scalar(out=xkv[:, 0, :], in0=xkv[:, 0, :], scalar1=0.0, scalar2=None, op0=ALU.max))
        vop(lambda v: v.tensor_scalar(out=xkv[:, 1, :], in0=picks[:, :], scalar1=1536.5, scalar2=1024.0,
                                      op0=ALU.is_ge, op1=ALU.mult))
        vop(lambda v: v.tensor_tensor(out=xkv[:, 0, :], in0=xkv[:, 0, :], in1=xkv[:, 1, :], op=ALU.add))
        vop(lambda v: v.tensor_scalar(out=xkv[:, 1, :], in0=xkv[:, 0, :], scalar1=1.0, scalar2=None, op0=ALU.add))
        vop(lambda v: v.tensor_copy(out=xkvi[:, :, :], in_=xkv[:, :, :]))
        b.bar()
        for col, bT in [(0, bK), (1, bV)]:
            b.op("sync", lambda s, col=col, bT=bT: s.dma_start(
                out=bT.rearrange("jl (q jh) -> q jl jh", q=128),
                in_=xkvi.ap().rearrange("p a (jl jh) -> p a jl jh", jl=16)[:, col, :, :]), dma=True)
        b.bar()
        for tgt, bT in [(idxk, bK), (idxv, bV)]:
            b.op("sync", lambda s, tgt=tgt, bT=bT: s.dma_start(out=tgt[0:16, :], in_=bT[:, :]), dma=True)
            b.op("sync", lambda s, tgt=tgt, bT=bT: s.dma_start(out=tgt[16:32, :], in_=bT[:, :]), dma=True)
        b.bar()

        for call in range(16):
            b.op("gpsimd", lambda g, call=call: g.dma_gather(
                gkt[:, :, 0:512], kv_d.rearrange("t (a c) -> (t a) c", a=2),
                idxk[:, call * 32:(call + 1) * 32], 512, 512, 1024, transpose=True), dma=True)
            b.op("gpsimd", lambda g, call=call: g.dma_gather(
                gv[:, 0:4, :], kv_d.rearrange("t (a c) -> (t a) c", a=2),
                idxv[:, call * 32:(call + 1) * 32], 512, 512, 1024), dma=True)
            b.bar()
            for mloc in range(4):
                m = call * 4 + mloc
                for h in range(16):
                    b.op("tensor", lambda t, mloc=mloc, m=m, h=h: t.matmul(
                        ps_sc[:, mloc * 32 + h * 2: mloc * 32 + h * 2 + 2],
                        gkt[64 * (h % 2):64 * (h % 2) + 64, h // 2, mloc * 128: mloc * 128 + 128],
                        qts[64 * (h % 2):64 * (h % 2) + 64, h // 2, 2 * m:2 * m + 2],
                        start=True, stop=True))
            b.bar()
            b.op("scalar", lambda s: s.activation(out=pa[:, 0:128], in_=ps_sc[:, 0:128], func=AF.Exp, bias=ebias[:, 0:1]))
            b.bar()
            b.op("vector", lambda v: v.memset(pa.ap().rearrange("p (m h a) -> p m h a", m=8, h=16)[64:128, 0:4, :, 0], 0.0))
            b.op("vector", lambda v: v.memset(pa.ap().rearrange("p (m h a) -> p m h a", m=8, h=16)[0:64, 0:4, :, 1], 0.0))
            b.bar()
            b.op("tensor", lambda t: t.matmul(ps_a[0:1, 0:128], onesb[:, 0:1], pa[:, 0:128], start=True, stop=True))
            b.bar()
            b.op("vector", lambda v, call=call: v.tensor_copy(out=zrow[:, call * 128:(call + 1) * 128], in_=ps_a[0:1, 0:128]))
            b.bar()
            for mloc in range(4):
                for h in range(16):
                    b.op("tensor", lambda t, mloc=mloc, h=h: t.matmul(
                        ps_va[:, mloc * 32 + h * 2: mloc * 32 + h * 2 + 2],
                        gv[:, mloc, h * 64:(h + 1) * 64],
                        pa[:, mloc * 32 + h * 2: mloc * 32 + h * 2 + 2],
                        start=True, stop=True))
            b.bar()
            b.op("vector", lambda v, call=call: v.tensor_copy(out=vas[:, call * 128:(call + 1) * 128], in_=ps_va[:, 0:128]))
            b.bar()

        for hg in range(2):
            for hl in range(8):
                h = hg * 8 + hl
                ps = [ps_a, ps_b][hl % 2]
                for wt in range(4):
                    b.op("tensor", lambda t, h=h, wt=wt, ps=ps: t.matmul(
                        ps[:, wt * 128:(wt + 1) * 128],
                        kwTs[64 * (h % 2):64 * (h % 2) + 64, h // 2, wt * 128:(wt + 1) * 128],
                        qts[64 * (h % 2):64 * (h % 2) + 64, h // 2, :],
                        start=True, stop=True))
                b.bar()
                b.op("scalar", lambda s, hl=hl, ps=ps: s.activation(
                    out=pwt.ap().rearrange("p h w q -> p (h w q)")[:, hl * 512:(hl + 1) * 512],
                    in_=ps[:, :], func=AF.Exp, bias=ebias[:, 0:1]))
                b.bar()
            for hl in range(8):
                h = hg * 8 + hl
                b.op("tensor", lambda t, hl=hl: [t.matmul(
                    ps_b[0:1, 0:128], onesb[:, 0:1], pwt[:, hl, wt, :],
                    start=(wt == 0), stop=(wt == 3)) for wt in range(4)][-1])
                b.bar()
                b.op("vector", lambda v, h=h: v.tensor_copy(
                    out=zh.ap().rearrange("o (m h a) -> o h m a", m=64, h=16)[:, h, :, :],
                    in_=ps_b.ap().rearrange("o (q x) -> o q x", x=1)[0:1, 0:128, :]))
                b.bar()
            if hg == 0:
                for par, pso in [(0, ps_owE), (1, ps_owO)]:
                    for hl in range(8):
                        h = hl
                        for wt in range(4):
                            b.op("tensor", lambda t, par=par, pso=pso, h=h, hl=hl, wt=wt: t.matmul(
                                pso[:, h * 64:(h + 1) * 64],
                                pwt.ap().rearrange("p h w (m a) -> p h w m a", a=2)[:, hl, wt, :, par],
                                vws[:, wt, h * 64:(h + 1) * 64],
                                start=(wt == 0), stop=(wt == 3)))
                b.bar()
        b.op("vector", lambda v: v.tensor_tensor(out=zh[:, :], in0=zh[:, :], in1=zrow[:, :], op=ALU.add))
        b.bar()
        b.op("vector", lambda v: v.tensor_scalar(out=zh[:, :], in0=zh[:, :], scalar1=1e-30, scalar2=None, op0=ALU.add))
        b.bar()
        b.op("vector", lambda v: v.reciprocal(out=zh[:, :], in_=zh[:, :]))
        b.bar()
        for par in range(2):
            for h in range(16):
                b.op("tensor", lambda t, par=par, h=h: t.transpose(
                    ps_b[0:64, par * 16 + h: par * 16 + h + 1],
                    zh.ap().rearrange("o (m h a) -> o m h a", m=64, h=16)[:, :, h, par],
                    idn[0:1, 0:1]))
        b.bar()
        b.op("vector", lambda v: v.tensor_copy(out=rzp.ap().rearrange("p a h -> p (a h)"), in_=ps_b[0:64, 0:32]))
        b.bar()

        for par, pso in [(0, ps_owE), (1, ps_owO)]:
            for hl in range(8):
                h = 8 + hl
                for wt in range(4):
                    b.op("tensor", lambda t, par=par, pso=pso, h=h, hl=hl, wt=wt: t.matmul(
                        pso[:, h * 64:(h + 1) * 64],
                        pwt.ap().rearrange("p h w (m a) -> p h w m a", a=2)[:, hl, wt, :, par],
                        vws[:, wt, h * 64:(h + 1) * 64],
                        start=(wt == 0), stop=(wt == 3)))
        b.bar()

        for ch in range(16):
            b.op("tensor", lambda t, ch=ch: t.transpose(ps_a[:, 0:64] if False else ps_sc[:, 0:64],
                                                        vas[:, ch * 128:(ch + 1) * 128], idn[0:64, 0:64]))
            b.bar()
            b.op("vector", lambda v, ch=ch: v.tensor_copy(out=vtr[:, :], in_=ps_sc[:, 0:64]))
            b.bar()
            b.op("sync", lambda s, ch=ch: s.dma_start(out=vaDR[ch * 128:(ch + 1) * 128, :], in_=vtr[:, :]), dma=True)
            b.bar()
        b.op("sync", lambda s: s.dma_start(
            out=vaE.ap().rearrange("m (h d) -> m h d", h=16),
            in_=vaDR.rearrange("(m h a) d -> m h a d", h=16, a=2)[:, :, 0, :]), dma=True)
        b.op("sync", lambda s: s.dma_start(
            out=vaO.ap().rearrange("m (h d) -> m h d", h=16),
            in_=vaDR.rearrange("(m h a) d -> m h a d", h=16, a=2)[:, :, 1, :]), dma=True)
        b.bar()

        for par, vaP, pso in [(0, vaE, ps_owE), (1, vaO, ps_owO)]:
            b.op("vector", lambda v, par=par, vaP=vaP, pso=pso: v.tensor_tensor(
                out=atw[:, par, :], in0=pso[:, :], in1=vaP[:, :], op=ALU.add))
        b.bar()
        for par in range(2):
            for h in range(16):
                b.op("vector", lambda v, par=par, h=h: v.tensor_scalar(
                    out=atw[:, par, h * 64:(h + 1) * 64], in0=atw[:, par, h * 64:(h + 1) * 64],
                    scalar1=rzp[:, par, h:h + 1], scalar2=None, op0=ALU.mult))
        b.bar()
        for c in range(NCH):
            for par in range(2):
                b.op("tensor", lambda t, c=c, par=par: t.transpose(
                    [ps_a, ps_b][par][:, 0:64],
                    atw.ap().rearrange("p a (c e) -> p a c e", c=NCH)[:, par, c, :],
                    idn[0:64, 0:64]))
            b.bar()
            b.op("vector", lambda v, c=c: v.tensor_copy(
                out=attn_fm.ap().rearrange("p c (m a) -> p c m a", a=2)[:, c, :, 0], in_=ps_a[:, 0:64]))
            b.op("vector", lambda v, c=c: v.tensor_copy(
                out=attn_fm.ap().rearrange("p c (m a) -> p c m a", a=2)[:, c, :, 1], in_=ps_b[:, 0:64]))
            b.bar()

        b.op("gpsimd", lambda g: g.dma_start(
            out=gv[:], in_=wo_w.rearrange("(c p) o -> p c o", p=128)), dma=True)
        b.bar()
        for oc in range(NCH):
            b.op("tensor", lambda t, oc=oc: [t.matmul(
                ps_sc[:, 0:128], gv[:, ic, oc * 128:(oc + 1) * 128], attn_fm[:, ic, :],
                start=(ic == 0), stop=(ic == NCH - 1)) for ic in range(NCH)][-1])
            b.bar()
            b.op("vector", lambda v, oc=oc: v.tensor_scalar(
                out=x2s[:, oc, :], in0=ps_sc[:, 0:128], scalar1=wobr[:, oc:oc + 1], scalar2=None, op0=ALU.add))
            b.bar()
            b.op("vector", lambda v, oc=oc: v.tensor_tensor(
                out=x2s[:, oc, :], in0=x2s[:, oc, :], in1=x1s[:, oc, :], op=ALU.add))
            b.bar()
        for c in range(NCH):
            b.op("sync", lambda s, c=c, q0=q0: s.dma_start(
                out=o_x2[c * 128:(c + 1) * 128, q0:q0 + 128], in_=x2s[:, c, :]), dma=True)
        b.bar()






def emit_MKN(nc, b, st, t, pfx=""):
    kT_l = t["kT_l"]
    mk_c = t["mk_c"]
    slab = st.enter_context(nc.sbuf_tensor(pfx + "slab", [128, NCH, 512], BF16))
    mkt = st.enter_context(nc.sbuf_tensor(pfx + "mkt", [1, 64], F32))
    m16 = st.enter_context(nc.sbuf_tensor(pfx + "m16", [1, 128], F32))
    onesb = st.enter_context(nc.sbuf_tensor(pfx + "onesb", [128, 1], BF16))
    ps_r = st.enter_context(nc.psum_tensor(pfx + "ps_r", [1, 512], F32))
    b.op("vector", lambda v: v.memset(onesb[:], 1.0))
    b.op("vector", lambda v: v.memset(m16[:], 0.0))
    b.bar()
    for tt in range(4):
        b.op("sync", lambda s, tt=tt: s.dma_start(
            out=slab[:], in_=kT_l.rearrange("(c p) t -> p c t", p=128)[:, :, tt * 512:(tt + 1) * 512]), dma=True)
        b.bar()
        for c in range(NCH):
            b.op("scalar", lambda s, c=c: s.activation(out=slab[:, c, :], in_=slab[:, c, :], func=AF.Square))
        b.bar()
        for h in range(16):
            b.op("tensor", lambda t2, h=h: t2.matmul(
                ps_r[0:1, :], onesb[64 * (h % 2):64 * (h % 2) + 64, 0:1],
                slab[64 * (h % 2):64 * (h % 2) + 64, h // 2, :], start=True, stop=True))
            b.bar()
            b.op("vector", lambda v, h=h, tt=tt: v.tensor_reduce(
                out=mkt[:, h * 4 + tt:h * 4 + tt + 1], in_=ps_r[0:1, :], axis=AX.X, op=ALU.max))
            b.bar()
    for h in range(16):
        b.op("vector", lambda v, h=h: v.tensor_reduce(
            out=m16[:, h:h + 1], in_=mkt[:, h * 4:(h + 1) * 4], axis=AX.X, op=ALU.max))
    b.bar()
    b.op("sync", lambda s: s.dma_start(out=mk_c[:, :], in_=m16[:, :]), dma=True)
    b.bar()



def emit_CATT(nc, b, st, t, pfx=""):
    qT_l = t["qT_l"]
    g_kT = t["g_kT"]
    g_v = t["g_v"]
    g_mk = t["g_mk"]
    cmask2 = t["cmask2"]
    at_fm = t["at_fm"]

    kslab = st.enter_context(nc.sbuf_tensor(pfx + "kslab", [128, 2, T], BF16))
    vaug = st.enter_context(nc.sbuf_tensor(pfx + "vaug", [128, 64, 4, 65], BF16))
    qs = st.enter_context(nc.sbuf_tensor(pfx + "qs", [128, NCH, 512], BF16))
    qsq = st.enter_context(nc.sbuf_tensor(pfx + "qsq", [128, NCH, 512], BF16))
    cmsk = st.enter_context(nc.sbuf_tensor(pfx + "cmsk", [128, 2, 16, 512], BF16))
    pt = st.enter_context(nc.sbuf_tensor(pfx + "pt", [128, 2, 512], BF16))
    m4 = st.enter_context(nc.sbuf_tensor(pfx + "m4", [1, 512], F32))
    mkh = st.enter_context(nc.sbuf_tensor(pfx + "mkh", [1, 16], F32))
    qn = st.enter_context(nc.sbuf_tensor(pfx + "qn", [1, 16], F32))
    brow = st.enter_context(nc.sbuf_tensor(pfx + "brow", [1, 16], F32))
    biasq = st.enter_context(nc.sbuf_tensor(pfx + "biasq", [128, 16], F32))
    rz = st.enter_context(nc.sbuf_tensor(pfx + "rz", [1, 4, 512], F32))
    rzb = st.enter_context(nc.sbuf_tensor(pfx + "rzb", [64, 4, 512], F32))
    atb = st.enter_context(nc.sbuf_tensor(pfx + "atb", [64, 4, 512], BF16))
    onesb = st.enter_context(nc.sbuf_tensor(pfx + "onesb", [128, 1], BF16))
    onesr = st.enter_context(nc.sbuf_tensor(pfx + "onesr", [1, 128], F32))
    ps_s0 = st.enter_context(nc.psum_tensor(pfx + "ps_s0", [128, 512], F32))
    ps_s1 = st.enter_context(nc.psum_tensor(pfx + "ps_s1", [128, 512], F32))
    ps_pv = [st.enter_context(nc.psum_tensor(pfx + f"ps_pv{i}", [128, 512], F32)) for i in range(4)]
    ps_q = st.enter_context(nc.psum_tensor(pfx + "ps_q", [1, 512], F32))
    ps_bq = st.enter_context(nc.psum_tensor(pfx + "ps_bq", [128, 16], F32))
    se_s = st.enter_context(nc.semaphore(pfx + "se_s"))
    se_e = st.enter_context(nc.semaphore(pfx + "se_e"))
    se_m = st.enter_context(nc.semaphore(pfx + "se_m"))
    psl = [ps_s0, ps_s1]
    cnt = {"s": 0, "e": 0, "m": 0}

    b.op("sync", lambda s: s.dma_start(out=cmsk[:], in_=cmask2.rearrange("p (a r q) -> p a r q", a=2, r=16)), dma=True)
    b.op("sync", lambda s: s.dma_start(out=m4[:], in_=g_mk.rearrange("s w -> (s w)")), dma=True)
    b.op("vector", lambda v: v.memset(onesb[:], 1.0))
    b.op("vector", lambda v: v.memset(onesr[:], 1.0))
    b.op("vector", lambda v: v.memset(vaug.ap()[:, :, :, 64:65], 1.0))
    b.bar()
    for h in range(16):
        b.op("vector", lambda v, h=h: v.tensor_reduce(
            out=mkh[:, h:h + 1], in_=m4.ap().rearrange("o (s w) -> o w s", w=128)[:, h, :],
            axis=AX.X, op=ALU.max))
    b.bar()
    b.op("scalar", lambda s: s.activation(out=mkh[:, :], in_=mkh[:, :], func=AF.Sqrt))
    b.bar()
    b.op("vector", lambda v: v.tensor_scalar(out=mkh[:, :], in0=mkh[:, :], scalar1=-1.0, scalar2=None, op0=ALU.mult))
    b.bar()

    for s4 in range(4):
        nkt = 16 * (s4 + 1)
        b.op("sync", lambda s, s4=s4: s.dma_start(
            out=qs[:], in_=qT_l.rearrange("(c p) t -> p c t", p=128)[:, :, s4 * 512:(s4 + 1) * 512]), dma=True)
        b.bar()
        for c in range(NCH):
            b.op("scalar", lambda s, c=c: s.activation(out=qsq[:, c, :], in_=qs[:, c, :], func=AF.Square))
        b.bar()
        for h in range(16):
            b.op("tensor", lambda t2, h=h: t2.matmul(
                ps_q[0:1, :], onesb[64 * (h % 2):64 * (h % 2) + 64, 0:1],
                qsq[64 * (h % 2):64 * (h % 2) + 64, h // 2, :], start=True, stop=True))
            b.bar()
            b.op("vector", lambda v, h=h: v.tensor_reduce(
                out=qn[:, h:h + 1], in_=ps_q[0:1, :], axis=AX.X, op=ALU.max))
            b.bar()
        b.op("scalar", lambda s: s.activation(out=qn[:, :], in_=qn[:, :], func=AF.Sqrt))
        b.bar()
        b.op("vector", lambda v: v.tensor_tensor(out=brow[:, :], in0=qn[:, :], in1=mkh[:, :], op=ALU.mult))
        b.bar()
        b.op("tensor", lambda t2: t2.matmul(ps_bq[:, :], onesr[0:1, :], brow[0:1, :], start=True, stop=True))
        b.bar()
        b.op("vector", lambda v: v.tensor_copy(out=biasq[:, :], in_=ps_bq[:, :]))
        b.bar()

        for g4 in range(4):
            for j2 in range(4):
                for tt in range(s4 + 1):
                    bglob = 4 * tt + (j2 if tt % 2 == 0 else 3 - j2)
                    b.op("sync", lambda s, j2=j2, tt=tt, bglob=bglob, g4=g4: s.dma_start(
                        out=kslab[:, :, bglob * 512:(bglob + 1) * 512],
                        in_=g_kT.rearrange("(s2 c p) t -> p s2 c t", s2=4, p=128)
                        [:, j2, 2 * g4:2 * g4 + 2, tt * 512:(tt + 1) * 512]), dma=True)
                    for n in range(4):
                        b.op("gpsimd", lambda g, j2=j2, tt=tt, bglob=bglob, g4=g4, n=n: g.dma_start(
                            out=vaug[:, bglob * 4 + n, :, 0:64],
                            in_=g_v.rearrange("(r p) (gg hh d) -> p r gg hh d", p=128, gg=4, hh=4)
                            [:, 16 * j2 + 4 * tt + n, g4, :, :]), dma=True)
            b.bar()
            e_base = cnt["e"]
            pv_ops = []
            for kt in range(nkt):
                for hl in range(4):
                    h = 4 * g4 + hl
                    u = kt * 4 + hl
                    wv = e_base + u - 1 if u >= 2 else None

                    def s_fn(t2, kt=kt, hl=hl, h=h, u=u, wv=wv):
                        if wv is not None:
                            t2.wait_ge(se_e, wv)
                        return t2.matmul(psl[u % 2][:, :],
                                         kslab[64 * (hl % 2):64 * (hl % 2) + 64, hl // 2, kt * 128:(kt + 1) * 128],
                                         qs[64 * (h % 2):64 * (h % 2) + 64, h // 2, :],
                                         start=True, stop=True)
                    b.op("tensor", lambda t2, f=s_fn: (f(t2).then_inc(se_s, 1), None)[1])
                    cnt["s"] += 1

                    def e_fn(s, u=u, h=h, sv=cnt["s"]):
                        s.wait_ge(se_s, sv)
                        return s.activation(out=pt[:, u % 2, :], in_=psl[u % 2][:, :], func=AF.Exp,
                                            bias=biasq[:, h:h + 1])
                    b.op("scalar", lambda s, f=e_fn: (f(s).then_inc(se_e, 1), None)[1])
                    cnt["e"] += 1
                    masked = kt >= 16 * s4
                    if masked:
                        r = kt - 16 * s4

                        def m_fn(v, u=u, r=r, s4=s4, ev=cnt["e"]):
                            v.wait_ge(se_e, ev)
                            return v.tensor_tensor(out=pt[:, u % 2, :], in0=pt[:, u % 2, :],
                                                   in1=cmsk[:, s4 % 2, r, :], op=ALU.mult)
                        b.op("vector", lambda v, f=m_fn: (f(v).then_inc(se_m, 1), None)[1])
                        cnt["m"] += 1
                    wv2 = ("m", cnt["m"]) if masked else ("e", cnt["e"])

                    def pv_fn(t2, kt=kt, hl=hl, u=u, nkt=nkt, wv2=wv2):
                        t2.wait_ge(se_m if wv2[0] == "m" else se_e, wv2[1])
                        return t2.matmul(ps_pv[hl][0:65, :], vaug[:, kt, hl, :], pt[:, u % 2, :],
                                         start=(kt == 0), stop=(kt == nkt - 1))
                    pv_ops.append(pv_fn)
                    if len(pv_ops) >= 2:
                        b.op("tensor", lambda t2, f=pv_ops[-2]: f(t2))
            b.op("tensor", lambda t2, f=pv_ops[-1]: f(t2))
            b.bar()
            for hl in range(4):
                b.op("scalar", lambda s, hl=hl: s.activation(out=rz[:, hl, :], in_=ps_pv[hl][64:65, :],
                                                             func=AF.Copy, bias=1e-30))
            b.bar()
            b.op("vector", lambda v: v.reciprocal(out=rz.ap().rearrange("o a q -> o (a q)"),
                                                  in_=rz.ap().rearrange("o a q -> o (a q)")))
            b.bar()
            for hl in range(4):
                dst = psl[hl % 2]
                base = 64 * (hl // 2)
                b.op("tensor", lambda t2, hl=hl, dst=dst, base=base: t2.matmul(
                    dst[base:base + 64, :], onesr[0:1, 0:64], rz[0:1, hl, :], start=True, stop=True))
            b.bar()
            for hl in range(4):
                dst = psl[hl % 2]
                base = 64 * (hl // 2)
                b.op("vector", lambda v, hl=hl, dst=dst, base=base: v.tensor_copy(
                    out=rzb[:, hl, :], in_=dst[base:base + 64, :]))
            b.bar()
            for hl in range(4):
                b.op("vector", lambda v, hl=hl: v.tensor_tensor(
                    out=atb[:, hl, :], in0=ps_pv[hl][0:64, :], in1=rzb[:, hl, :], op=ALU.mult))
            b.bar()
            for hl in range(4):
                h = 4 * g4 + hl
                b.op("sync", lambda s, hl=hl, h=h, s4=s4: s.dma_start(
                    out=at_fm[64 * h:64 * (h + 1), 512 * s4:512 * (s4 + 1)], in_=atb[:, hl, :]), dma=True)
            b.bar()


L4B_DRAMS = [("qT_d", [D, TPC], BF16, "ExternalInput"),
             ("qiT_d", [IDXD, TPC], F32, "ExternalInput"),
             ("g_ki", [4 * IDXD, TPC], F32, "ExternalInput"),
             ("kv_d", [T, 2 * D], BF16, "ExternalInput"),
             ("x1_d", [D, TPC], F32, "ExternalInput"),
             ("wo_w", [D, D], F32, "ExternalInput"),
             ("wo_b", [1, D], F32, "ExternalInput"),
             ("ident", [128, 128], F32, "ExternalInput"),
             ("identb", [128, 128], F32, "ExternalInput"),
             ("cbase", [128, 960], F32, "ExternalInput"),
             ("o_x2", [D, TPC], F32, "ExternalOutput"),
             ("bK", [16, 512], I16, "Internal"),
             ("bV", [16, 512], I16, "Internal"),
             ("vaDR", [2048, 64], F32, "Internal")]

build_L4b = _mk_build(emit_L4b, L4B_DRAMS, "l4b_")


RG4 = [[0, 1, 2, 3], [4, 5, 6, 7]]

MEGA_EXT_IN = ([("x_fm", [D, TPC], F32),
                ("wqT", [D, D], F32), ("wkT", [D, D], F32), ("wvT", [D, D], F32),
                ("n1_w", [D, 1], F32), ("qn_w", [HD, 1], F32), ("kn_w", [HD, 1], F32),
                ("n2_w", [D, 1], F32), ("esel", [4, 512], F32), ("ident", [128, 128], F32),
                ("wq_w", [D, D], F32), ("wk_w", [D, D], F32), ("wv_w", [D, D], F32),
                ("wqi_w", [D, IDXD], F32), ("wki_w", [D, IDXD], F32),
                ("wq_b", [1, D], F32), ("wk_b", [1, D], F32), ("wv_b", [1, D], F32),
                ("wqi_b", [1, IDXD], F32), ("wki_b", [1, IDXD], F32),
                ("wo_w", [D, D], F32), ("wo_b", [1, D], F32),
                ("cbase", [128, 960], F32), ("cmask2", [128, 2 * 16 * 512], BF16)]
               + [(nm, shp, F32) for nm, shp in L3_W_SHAPES])

MEGA_INTERNAL = [("i_lnx", [D, TPC], F32), ("i_qT", [D, TPC], BF16),
                 ("i_kT", [D, TPC], BF16), ("i_v", [TPC, D], BF16),
                 ("i_mkc", [1, 128], F32), ("i_at", [D, TPC], BF16),
                 ("i_x1", [D, TPC], F32), ("i_x2", [D, TPC], F32),
                 ("i_sqT", [D, TPC], BF16), ("i_skv", [TPC, 2 * D], BF16),
                 ("i_sqi", [IDXD, TPC], F32), ("i_ski", [IDXD, TPC], F32),
                 ("g_kT", [4 * D, TPC], BF16), ("g_v", [4 * TPC, D], BF16),
                 ("g_mk", [4, 128], F32), ("g_kv", [T, 2 * D], BF16),
                 ("g_ki", [4 * IDXD, TPC], F32),
                 ("bK", [16, 512], I16), ("bV", [16, 512], I16), ("vaDR", [2048, 64], F32)]


def build_MEGA(dbg=False, upto=3):
    nc = bacc.Bacc("TRN2", target_bir_lowering=False)
    t = {nm: nc.dram_tensor(nm, shp, dt, kind="ExternalInput").ap() for nm, shp, dt in MEGA_EXT_IN}
    t["o_out"] = nc.dram_tensor("o_out", [D, TPC], F32, kind="ExternalOutput").ap()
    dbg_names = {"i_at", "i_x1", "i_x2", "i_lnx"} if dbg else set()
    for nm, shp, dt in MEGA_INTERNAL:
        kind = "ExternalOutput" if nm in dbg_names else "Internal"
        t[nm] = nc.dram_tensor(nm, shp, dt, kind=kind).ap()
    t["identb"] = t["ident"]

    with ExitStack() as st:
        block = st.enter_context(nc.Block())
        sems = {e: st.enter_context(nc.semaphore("s_" + e)) for e in ENGS}
        b = Builder(nc)

        def cc(src, dst):
            b.op("gpsimd", lambda g, src=src, dst=dst: g.collective_compute(
                "AllGather", ALU.bypass, replica_groups=RG4, ins=[src.opt()], outs=[dst.opt()]))

        for half in (0, 1):
            if half == 1 and upto < 3:
                break
            xin = t["x_fm"] if half == 0 else t["i_x2"]
            xout = t["i_x1"] if half == 0 else t["o_out"]
            with ExitStack() as s1:
                emit_L1(nc, b, s1, {**t, "x_fm": xin, "o_lnx": t["i_lnx"], "o_qT": t["i_qT"],
                                    "o_kT": t["i_kT"], "o_v": t["i_v"]}, pfx=f"h{half}l1_")
            b.bar()
            with ExitStack() as s2:
                emit_MKN(nc, b, s2, {"kT_l": t["i_kT"], "mk_c": t["i_mkc"]}, pfx=f"h{half}mk_")
            b.bar()
            cc(t["i_kT"], t["g_kT"])
            b.bar()
            cc(t["i_v"], t["g_v"])
            b.bar()
            cc(t["i_mkc"], t["g_mk"])
            b.bar()
            with ExitStack() as s3:
                emit_CATT(nc, b, s3, {"qT_l": t["i_qT"], "g_kT": t["g_kT"], "g_v": t["g_v"],
                                      "g_mk": t["g_mk"], "cmask2": t["cmask2"],
                                      "at_fm": t["i_at"]}, pfx=f"h{half}ca_")
            b.bar()
            with ExitStack() as s4:
                emit_L3(nc, b, s4, {**t, "at_fm": t["i_at"], "lnx_fm": t["i_lnx"],
                                    "x0_fm": xin, "o_x1": xout}, pfx=f"h{half}l3_")
            b.bar()
            if half == 0 and upto >= 2:
                with ExitStack() as s5:
                    emit_L4a(nc, b, s5, {**t, "x_fm": t["i_x1"], "o_qT": t["i_sqT"],
                                         "o_kv": t["i_skv"], "o_qi": t["i_sqi"],
                                         "o_ki": t["i_ski"]}, pfx="sa_")
                b.bar()
                cc(t["i_skv"], t["g_kv"])
                b.bar()
                cc(t["i_ski"], t["g_ki"])
                b.bar()
                with ExitStack() as s6:
                    emit_L4b(nc, b, s6, {**t, "qT_d": t["i_sqT"], "qiT_d": t["i_sqi"],
                                         "kv_d": t["g_kv"], "x1_d": t["i_x1"],
                                         "o_x2": t["i_x2"]}, pfx="sb_")
                b.bar()
        b.emit(block, sems)
    nc.finalize()
    return nc


def _cmask2_host(j):
    m = np.zeros((128, 2, 16, 512), dtype=BF)
    qv = np.arange(512)[None, :]
    kv = np.arange(128)[:, None]
    for par in range(2):
        jj = j if par == 0 else 3 - j
        for r in range(16):
            if r < 4 * jj:
                m[:, par, r, :] = 1.0
            elif r < 4 * jj + 4:
                rr = r - 4 * jj
                m[:, par, r, :] = (qv >= kv + 128 * rr).astype(BF)
    return np.ascontiguousarray(m.reshape(128, 2 * 16 * 512))


def _mega_in_maps(inputs):
    g = host_prep(inputs)
    perm = _stripe_perm()
    esel = _const_esel()
    cb = _const_cbase()
    x0 = inputs["L0_attn_x"].reshape(NTOK, D)
    shared = {"wqT": g["f_wq"], "wkT": g["f_wk"], "wvT": g["f_wv"],
              "n1_w": g["f_n1_w"], "qn_w": g["f_qn_w"], "kn_w": g["f_kn_w"],
              "n2_w": g["f_n2_w"], "esel": esel, "ident": g["identity"],
              "woT": g["f_wo"], "gateT": g["f_gate_w"],
              "wo_w": g["a_wo_w"], "wo_b": g["a_wo_b"].reshape(1, -1),
              "cbase": cb}
    for nm in ["e0_up", "e0_down", "e1_w1", "e1_w2", "e1_down", "e2_l1",
               "e2_l2", "e2_l3", "e2_l4", "e3_down", "e3_up", "e3_out"]:
        shared[nm + "T"] = g["f_" + nm]
    for nm in ["wq_w", "wk_w", "wv_w", "wqi_w", "wki_w"]:
        shared[nm] = g["a_" + nm]
    for nm in ["wq_b", "wk_b", "wv_b", "wqi_b", "wki_b"]:
        shared[nm] = g["a_" + nm].reshape(1, -1)
    maps = []
    for c in range(NCORES):
        bb, j = c // 4, c % 4
        rows = bb * T + perm[j * TPC:(j + 1) * TPC]
        maps.append({**shared,
                     "x_fm": np.ascontiguousarray(x0[rows].T),
                     "cmask2": _cmask2_host(j)})
    return maps, perm


def _const_cmask():
    cmq = np.zeros((128, 4, 512), dtype=BF)
    for r in range(4):
        for kk in range(128):
            cmq[kk, r, :] = (np.arange(512) >= kk + 128 * r).astype(BF)
    return np.ascontiguousarray(cmq.reshape(128, 4 * 512))


def _const_esel():
    esel = np.zeros((4, 512), dtype=f32)
    for e in range(4):
        esel[e, e * 128:(e + 1) * 128] = 1.0
    return esel


def _const_cbase():
    cb = np.zeros((128, 960), dtype=f32)
    for c in range(120):
        cb[:, c * 8:(c + 1) * 8] = c * 64 + 1
    return cb


def _stripe_perm():
    perm = np.empty(T, dtype=np.int64)
    for j2 in range(4):
        for tt in range(4):
            blk = 4 * tt + (j2 if tt % 2 == 0 else 3 - j2)
            r0 = j2 * TPC + tt * 512
            perm[r0:r0 + 512] = np.arange(blk * 512, (blk + 1) * 512)
    return perm


def kernel(**inputs):
    maps, perm = _mega_in_maps(inputs)
    nc = build_MEGA()
    res = run_bass_kernel_spmd(nc, maps, core_ids=list(range(NCORES))).results
    out = np.empty((NTOK, D), np.float32)
    for c in range(NCORES):
        bb, j = c // 4, c % 4
        rows = bb * T + perm[j * TPC:(j + 1) * TPC]
        out[rows] = res[c]["o_out"].T
    return np.ascontiguousarray(out.reshape(B, T, D))


def kernel_multilaunch(**inputs):
    g = host_prep(inputs)
    cmask = _const_cmask()
    esel = _const_esel()
    cb = _const_cbase()
    nc1 = build_L1()
    nc2 = build_L2()
    nc3 = build_L3()
    nc4a = build_L4a()
    nc4b = build_L4b()

    def run(nc, maps):
        return run_bass_kernel_spmd(nc, maps, core_ids=list(range(NCORES))).results

    l1_w = {"wqT": g["f_wq"], "wkT": g["f_wk"], "wvT": g["f_wv"],
            "n1_w": g["f_n1_w"], "qn_w": g["f_qn_w"], "kn_w": g["f_kn_w"]}
    l3_w = {"n2_w": g["f_n2_w"], "esel": esel, "ident": g["identity"],
            "woT": g["f_wo"], "gateT": g["f_gate_w"]}
    for nm in ["e0_up", "e0_down", "e1_w1", "e1_w2", "e1_down", "e2_l1",
               "e2_l2", "e2_l3", "e2_l4", "e3_down", "e3_up", "e3_out"]:
        l3_w[nm + "T"] = g["f_" + nm]
    l4a_w = {}
    for nm in ["wq_w", "wk_w", "wv_w", "wqi_w", "wki_w"]:
        l4a_w[nm] = g["a_" + nm]
    for nm in ["wq_b", "wk_b", "wv_b", "wqi_b", "wki_b"]:
        l4a_w[nm] = g["a_" + nm].reshape(1, -1)

    def ffn_pass(x_sh):
        r1 = run(nc1, [{"x_fm": x_sh[c], **l1_w} for c in range(NCORES)])
        qT_full = [np.concatenate([r1[bb * 4 + j]["o_qT"] for j in range(4)], axis=1) for bb in range(B)]
        kT_full = [np.concatenate([r1[bb * 4 + j]["o_kT"] for j in range(4)], axis=1) for bb in range(B)]
        v_full = [np.concatenate([r1[bb * 4 + j]["o_v"] for j in range(4)], axis=0) for bb in range(B)]
        m2 = []
        for c in range(NCORES):
            bb = c // 4
            hs = [4 * (c % 4) + i for i in range(4)]
            m2.append({
                "qT_p": np.stack([qT_full[bb][h * HD:(h + 1) * HD] for h in hs]),
                "kT_p": np.stack([kT_full[bb][h * HD:(h + 1) * HD] for h in hs]),
                "v_p": np.stack([np.ascontiguousarray(v_full[bb][:, h * HD:(h + 1) * HD]) for h in hs]),
                "cmask": cmask})
        r2 = run(nc2, m2)
        m3 = []
        for c in range(NCORES):
            bb = c // 4
            t0 = (c * TPC) % T
            at = np.empty((D, TPC), dtype=BF)
            for h in range(NH):
                at[h * HD:(h + 1) * HD] = r2[bb * 4 + h // 4]["o_at"][h % 4][t0:t0 + TPC].T
            m3.append({"at_fm": at, "lnx_fm": r1[c]["o_lnx"], "x0_fm": x_sh[c], **l3_w})
        r3 = run(nc3, m3)
        return [r3[c]["o_x1"] for c in range(NCORES)]

    x0 = inputs["L0_attn_x"].reshape(NTOK, D)
    x0_sh = [np.ascontiguousarray(x0[c * TPC:(c + 1) * TPC].T) for c in range(NCORES)]
    x1_sh = ffn_pass(x0_sh)

    r4a = run(nc4a, [{"x_fm": x1_sh[c], **l4a_w} for c in range(NCORES)])
    perm = _stripe_perm()
    kv_full, ki_full = [], []
    for bb in range(B):
        kv = np.concatenate([r4a[bb * 4 + j]["o_kv"] for j in range(4)], axis=0)
        kv_full.append(np.ascontiguousarray(kv[perm]))
        ki = np.concatenate([r4a[bb * 4 + j]["o_ki"] for j in range(4)], axis=1)
        ki_full.append(np.ascontiguousarray(np.concatenate(
            [ki[:, perm[TPC * j2:TPC * (j2 + 1)]] for j2 in range(4)], axis=0)))
    m4b = []
    for c in range(NCORES):
        bb = c // 4
        m4b.append({"qT_d": r4a[c]["o_qT"], "qiT_d": r4a[c]["o_qi"], "g_ki": ki_full[bb],
                    "kv_d": kv_full[bb], "x1_d": x1_sh[c],
                    "wo_w": g["a_wo_w"], "wo_b": g["a_wo_b"].reshape(1, -1),
                    "ident": g["identity"], "identb": g["identity"], "cbase": cb})
    r4b = run(nc4b, m4b)
    x2_sh = [np.ascontiguousarray(r4b[c]["o_x2"]) for c in range(NCORES)]

    x3_sh = ffn_pass(x2_sh)
    out = np.concatenate([s.T for s in x3_sh], axis=0).reshape(B, T, D)
    return np.ascontiguousarray(out.astype(np.float32))



# version 07
# speedup vs baseline: 1.1741x; 1.0252x over previous
import numpy as np
import ml_dtypes
from contextlib import ExitStack
import concourse.bass as bass
import concourse.bacc as bacc
import concourse.mybir as mybir
from concourse.bass_utils import run_bass_kernel_spmd
from concourse.library_config import mlp as mlp_lib

F32 = mybir.dt.float32
BF16 = mybir.dt.bfloat16
I16 = mybir.dt.int16
U32 = mybir.dt.uint32
AF = mybir.ActivationFunctionType
ALU = mybir.AluOpType
AX = mybir.AxisListType
f32 = np.float32
BF = ml_dtypes.bfloat16

NCORES = 8
D, NH, HD = 1024, 16, 64
B, T = 2, 8192
NTOK = B * T
TPC = NTOK // NCORES
ZM = 256
HID = 2048
EPS_RMS = 1.1920929e-07
EPS_LN = 1e-5
WM, TOPK, IDXD = 512, 64, 64
AL = T - WM
NCH = D // 128

ENGS = ("sync", "gpsimd", "tensor", "vector", "scalar")


class Builder:

    def __init__(self, nc):
        self.nc = nc
        self.phases = [[]]

    def op(self, eng, fn, dma=False):
        self.phases[-1].append((eng, fn, dma))

    def bar(self):
        if self.phases[-1]:
            self.phases.append([])

    def emit(self, block, sems):
        nph = len(self.phases)
        totals = {e: [0] * (nph + 1) for e in ENGS}
        for p, ph in enumerate(self.phases):
            add = {e: 0 for e in ENGS}
            for eng, fn, dma in ph:
                if dma:
                    add[eng] += 16
            has_compute = {e: any((eng == e and not dma) for eng, fn, dma in ph) for e in ENGS}
            for e in ENGS:
                totals[e][p + 1] = totals[e][p] + add[e] + (1 if has_compute[e] else 0)

        def mk(eng_name):
            def emitter(eng):
                waited = {e: 0 for e in ENGS}
                for p, ph in enumerate(self.phases):
                    mine = [(fn, dma) for e, fn, dma in ph if e == eng_name]
                    rest_later = any(
                        e == eng_name for pp in self.phases[p:] for e, _, _ in pp
                    )
                    if not mine and not rest_later:
                        break
                    if p > 0 and mine:
                        for e2 in ENGS:
                            tv = totals[e2][p]
                            if tv > waited[e2]:
                                eng.wait_ge(sems[e2], tv)
                                waited[e2] = tv
                    last_compute = None
                    for i, (fn, dma) in enumerate(mine):
                        if not dma:
                            last_compute = i
                    for i, (fn, dma) in enumerate(mine):
                        ins = fn(eng)
                        if dma:
                            ins.then_inc(sems[eng_name], 16)
                        elif i == last_compute:
                            if ins is None:
                                eng.nop().then_inc(sems[eng_name], 1)
                            else:
                                ins.then_inc(sems[eng_name], 1)
                for e2 in ENGS:
                    tv = totals[e2][nph]
                    if tv > waited[e2]:
                        eng.wait_ge(sems[e2], tv)
            return emitter

        block.sync(mk("sync"))
        block.gpsimd(mk("gpsimd"))
        block.tensor(mk("tensor"))
        block.vector(mk("vector"))
        block.scalar(mk("scalar"))


def host_prep(inputs):
    g = {}
    for name in ["wq", "wk", "wv", "wo", "gate_w", "e0_up", "e0_down", "e1_w1",
                 "e1_w2", "e1_down", "e2_l1", "e2_l2", "e2_l3", "e2_l4",
                 "e3_down", "e3_up", "e3_out"]:
        g["f_" + name] = np.ascontiguousarray(inputs["L0_ffn_" + name].T)
    for name in ["n1_w", "n2_w", "qn_w", "kn_w"]:
        g["f_" + name] = np.ascontiguousarray(inputs["L0_ffn_" + name].reshape(-1, 1))
    for name in ["wq_w", "wk_w", "wv_w", "wo_w", "wqi_w", "wki_w"]:
        g["a_" + name] = np.ascontiguousarray(inputs["L0_attn_" + name])
    for name in ["wq_b", "wk_b", "wv_b", "wo_b", "wqi_b", "wki_b"]:
        g["a_" + name] = np.ascontiguousarray(inputs["L0_attn_" + name].reshape(-1, 1))
    g["identity"] = np.eye(128, dtype=f32)
    return g



def emit_L1(nc, b, st, t, pfx=""):
    x_fm = t["x_fm"]
    wqT = t["wqT"]
    wkT = t["wkT"]
    wvT = t["wvT"]
    n1_w = t["n1_w"]
    qn_w = t["qn_w"]
    kn_w = t["kn_w"]
    o_lnx = t["o_lnx"]
    o_qT = t["o_qT"]
    o_kT = t["o_kT"]
    o_v = t["o_v"]

    NB = TPC // 512

    xs = st.enter_context(nc.sbuf_tensor(pfx + "xs", [128, NCH, 512], F32))
    xcast = st.enter_context(nc.sbuf_tensor(pfx + "xcast", [128, NCH, 512], BF16))
    hb = st.enter_context(nc.sbuf_tensor(pfx + "hb", [128, NCH, 512], BF16))
    lnxt = st.enter_context(nc.sbuf_tensor(pfx + "lnxt", [128, NCH, 512], F32))
    tmpc = st.enter_context(nc.sbuf_tensor(pfx + "tmpc", [128, NCH, 512], F32))
    wq_s = st.enter_context(nc.sbuf_tensor(pfx + "wq_s", [128, NCH, D], BF16))
    wk_s = st.enter_context(nc.sbuf_tensor(pfx + "wk_s", [128, NCH, D], BF16))
    wv_s = st.enter_context(nc.sbuf_tensor(pfx + "wv_s", [128, NCH, D], BF16))
    onesb = st.enter_context(nc.sbuf_tensor(pfx + "onesb", [128, 1], BF16))
    onesf = st.enter_context(nc.sbuf_tensor(pfx + "onesf", [128, 1], F32))
    onesr = st.enter_context(nc.sbuf_tensor(pfx + "onesr", [1, 128], F32))
    n1b = st.enter_context(nc.sbuf_tensor(pfx + "n1b", [128, NCH], F32))
    qn2 = st.enter_context(nc.sbuf_tensor(pfx + "qn2", [128, 1], F32))
    kn2 = st.enter_context(nc.sbuf_tensor(pfx + "kn2", [128, 1], F32))
    rows = st.enter_context(nc.sbuf_tensor(pfx + "rows", [1, 16 * 512], F32))
    rowsb = st.enter_context(nc.sbuf_tensor(pfx + "rowsb", [1, 4 * 512], F32))
    bcs = st.enter_context(nc.sbuf_tensor(pfx + "bcs", [128, 3, 512], F32))
    sq = st.enter_context(nc.sbuf_tensor(pfx + "sq", [128, 512], BF16))
    qtmp = st.enter_context(nc.sbuf_tensor(pfx + "qtmp", [128, 512], F32))
    bchh = st.enter_context(nc.sbuf_tensor(pfx + "bchh", [128, 512], F32))
    stage = st.enter_context(nc.sbuf_tensor(pfx + "stage", [128, 2, 512], BF16))
    qko = st.enter_context(nc.sbuf_tensor(pfx + "qko", [128, 512], BF16))
    ps_mm = st.enter_context(nc.psum_tensor(pfx + "ps_mm", [128, 512], F32))
    ps_mm2 = st.enter_context(nc.psum_tensor(pfx + "ps_mm2", [128, 512], F32))
    ps_r0 = st.enter_context(nc.psum_tensor(pfx + "ps_r0", [1, 512], F32))
    ps_r1 = st.enter_context(nc.psum_tensor(pfx + "ps_r1", [1, 512], F32))
    ps_r2 = st.enter_context(nc.psum_tensor(pfx + "ps_r2", [1, 512], F32))
    ps_r3 = st.enter_context(nc.psum_tensor(pfx + "ps_r3", [1, 512], F32))
    ps_bc = st.enter_context(nc.psum_tensor(pfx + "ps_bc", [128, 512], F32))
    ps_bch = st.enter_context(nc.psum_tensor(pfx + "ps_bch", [128, 512], F32))

    b.op("gpsimd", lambda g: g.dma_start(out=wq_s[:], in_=wqT.rearrange("(c p) o -> p c o", p=128)), dma=True)
    b.op("gpsimd", lambda g: g.dma_start(out=wk_s[:], in_=wkT.rearrange("(c p) o -> p c o", p=128)), dma=True)
    b.op("gpsimd", lambda g: g.dma_start(out=wv_s[:], in_=wvT.rearrange("(c p) o -> p c o", p=128)), dma=True)
    for c in range(NCH):
        b.op("sync", lambda s, c=c: s.dma_start(out=n1b[:, c:c + 1], in_=n1_w[c * 128:(c + 1) * 128, :]), dma=True)
    b.op("sync", lambda s: s.dma_start(out=qn2[0:64, :], in_=qn_w[:, :]), dma=True)
    b.op("sync", lambda s: s.dma_start(out=qn2[64:128, :], in_=qn_w[:, :]), dma=True)
    b.op("sync", lambda s: s.dma_start(out=kn2[0:64, :], in_=kn_w[:, :]), dma=True)
    b.op("sync", lambda s: s.dma_start(out=kn2[64:128, :], in_=kn_w[:, :]), dma=True)
    b.op("vector", lambda v: v.memset(onesb[:], 1.0))
    b.op("vector", lambda v: v.memset(onesf[:], 1.0))
    b.op("vector", lambda v: v.memset(onesr[:], 1.0))
    b.bar()

    for blk in range(NB):
        t0 = blk * 512
        b.op("sync", lambda s, t0=t0: s.dma_start(
            out=xs[:], in_=x_fm.rearrange("(c p) t -> p c t", p=128)[:, :, t0:t0 + 512]), dma=True)
        b.op("gpsimd", lambda g, t0=t0: g.dma_start(
            out=xcast[:], in_=x_fm.rearrange("(c p) t -> p c t", p=128)[:, :, t0:t0 + 512]), dma=True)
        b.bar()
        for c in range(NCH):
            b.op("scalar", lambda s, c=c: s.activation(out=hb[:, c, :], in_=xs[:, c, :], func=AF.Square))
        b.bar()
        for c in range(NCH):
            b.op("tensor", lambda t, c=c: t.matmul(ps_r0[0:1, :], onesb[:, 0:1], xcast[:, c, :],
                                                   start=(c == 0), stop=(c == NCH - 1)))
        for c in range(NCH):
            b.op("tensor", lambda t, c=c: t.matmul(ps_r1[0:1, :], onesb[:, 0:1], hb[:, c, :],
                                                   start=(c == 0), stop=(c == NCH - 1)))
        b.bar()
        b.op("scalar", lambda s: s.activation(out=rows[:, 0:512], in_=ps_r0[0:1, :],
                                              func=AF.Copy, scale=f32(1.0 / D)))
        b.op("scalar", lambda s: s.activation(out=rows[:, 512:1024], in_=ps_r1[0:1, :],
                                              func=AF.Copy, scale=f32(1.0 / D)))
        b.bar()
        b.op("vector", lambda v: v.tensor_tensor(out=rows[:, 1024:1536], in0=rows[:, 0:512],
                                                 in1=rows[:, 0:512], op=ALU.mult))
        b.bar()
        b.op("vector", lambda v: v.tensor_tensor(out=rows[:, 512:1024], in0=rows[:, 512:1024],
                                                 in1=rows[:, 1024:1536], op=ALU.subtract))
        b.bar()
        b.op("scalar", lambda s: s.activation(out=rows[:, 1024:1536], in_=rows[:, 512:1024],
                                              func=AF.Sqrt, bias=f32(EPS_LN)))
        b.bar()
        b.op("vector", lambda v: v.reciprocal(out=rows[:, 1024:1536], in_=rows[:, 1024:1536]))
        b.bar()
        b.op("vector", lambda v: v.tensor_tensor(out=rows[:, 1536:2048], in0=rows[:, 512:1024],
                                                 in1=rows[:, 1024:1536], op=ALU.mult))
        b.bar()
        b.op("vector", lambda v: v.tensor_tensor(out=rows[:, 1536:2048], in0=rows[:, 1536:2048],
                                                 in1=rows[:, 1024:1536], op=ALU.mult))
        b.bar()
        b.op("scalar", lambda s: s.activation(out=rowsb[:, 0:512], in_=rows[:, 1536:2048],
                                              func=AF.Sqrt, bias=f32(EPS_RMS)))
        b.bar()
        b.op("vector", lambda v: v.reciprocal(out=rowsb[:, 0:512], in_=rowsb[:, 0:512]))
        b.bar()
        b.op("vector", lambda v: v.tensor_tensor(out=rows[:, 1536:2048], in0=rowsb[:, 0:512],
                                                 in1=rows[:, 1024:1536], op=ALU.mult))
        b.bar()
        for j, off in enumerate([0, 1024, 1536]):
            b.op("tensor", lambda t, off=off: t.matmul(ps_bc[:, :], onesr[0:1, :], rows[0:1, off:off + 512],
                                                       start=True, stop=True))
            b.bar()
            b.op("vector", lambda v, j=j: v.tensor_copy(out=bcs[:, j, :], in_=ps_bc[:, :]))
            b.bar()
        for c in range(NCH):
            b.op("vector", lambda v, c=c: v.tensor_tensor(out=tmpc[:, c, :], in0=xs[:, c, :],
                                                          in1=bcs[:, 0, :], op=ALU.subtract))
        b.bar()
        for c in range(NCH):
            b.op("vector", lambda v, c=c: v.tensor_tensor(out=lnxt[:, c, :], in0=tmpc[:, c, :],
                                                          in1=bcs[:, 1, :], op=ALU.mult))
        for c in range(NCH):
            b.op("vector", lambda v, c=c: v.tensor_tensor(out=tmpc[:, c, :], in0=tmpc[:, c, :],
                                                          in1=bcs[:, 2, :], op=ALU.mult))
        b.bar()
        for c in range(NCH):
            b.op("sync", lambda s, c=c, t0=t0: s.dma_start(
                out=o_lnx[c * 128:(c + 1) * 128, t0:t0 + 512], in_=lnxt[:, c, :]), dma=True)
            b.op("vector", lambda v, c=c: v.tensor_scalar(
                out=hb[:, c, :], in0=tmpc[:, c, :], scalar1=n1b[:, c:c + 1], scalar2=None, op0=ALU.mult))
        b.bar()
        for which, w_s, nrm, outT, extra in (("q", wq_s, qn2, o_qT, 0.125),
                                            ("k", wk_s, kn2, o_kT, None)):
            for oc in range(NCH):
                b.op("tensor", lambda t, w_s=w_s, oc=oc: [
                    t.matmul(ps_mm[:, :], w_s[:, ic, oc * 128:(oc + 1) * 128], hb[:, ic, :],
                             start=(ic == 0), stop=(ic == NCH - 1))
                    for ic in range(NCH)][-1])
                b.bar()
                b.op("scalar", lambda s: s.activation(out=sq[:], in_=ps_mm[:, :], func=AF.Square))
                b.bar()
                b.op("tensor", lambda t: t.matmul(ps_r2[0:1, :], onesb[0:64, 0:1], sq[0:64, :],
                                                  start=True, stop=True))
                b.op("tensor", lambda t: t.matmul(ps_r3[0:1, :], onesb[64:128, 0:1], sq[64:128, :],
                                                  start=True, stop=True))
                b.bar()
                b.op("scalar", lambda s: s.activation(out=rowsb[:, 512:1024], in_=ps_r2[0:1, :],
                                                      func=AF.Sqrt, scale=f32(1.0 / HD), bias=f32(EPS_RMS)))
                b.op("scalar", lambda s: s.activation(out=rowsb[:, 1024:1536], in_=ps_r3[0:1, :],
                                                      func=AF.Sqrt, scale=f32(1.0 / HD), bias=f32(EPS_RMS)))
                b.bar()
                b.op("vector", lambda v: v.reciprocal(out=rowsb[:, 512:1536], in_=rowsb[:, 512:1536]))
                b.bar()
                b.op("tensor", lambda t: t.matmul(ps_bch[0:64, :], onesr[0:1, 0:64], rowsb[0:1, 512:1024],
                                                  start=True, stop=True))
                b.op("tensor", lambda t: t.matmul(ps_bch[64:128, :], onesr[0:1, 0:64], rowsb[0:1, 1024:1536],
                                                  start=True, stop=True))
                b.bar()
                b.op("vector", lambda v: v.tensor_copy(out=bchh[:, :], in_=ps_bch[:, :]))
                b.bar()
                b.op("vector", lambda v: v.tensor_tensor(out=qtmp[:, :], in0=ps_mm[:, :],
                                                         in1=bchh[:, :], op=ALU.mult))
                b.bar()
                if extra is not None:
                    b.op("vector", lambda v, nrm=nrm, extra=extra: v.tensor_scalar(
                        out=qko[:, :], in0=qtmp[:, :], scalar1=nrm[:, 0:1], scalar2=extra,
                        op0=ALU.mult, op1=ALU.mult))
                else:
                    b.op("vector", lambda v, nrm=nrm: v.tensor_scalar(
                        out=qko[:, :], in0=qtmp[:, :], scalar1=nrm[:, 0:1], scalar2=None, op0=ALU.mult))
                b.bar()
                b.op("sync", lambda s, outT=outT, oc=oc, t0=t0: s.dma_start(
                    out=outT[oc * 128:(oc + 1) * 128, t0:t0 + 512], in_=qko[:, :]), dma=True)
                b.bar()
        for tt in range(4):
            for ob in range(2):
                b.op("tensor", lambda t, tt=tt, ob=ob: [
                    t.matmul(ps_mm2[:, :], hb[:, ic, tt * 128:(tt + 1) * 128],
                             wv_s[:, ic, ob * 512:(ob + 1) * 512],
                             start=(ic == 0), stop=(ic == NCH - 1))
                    for ic in range(NCH)][-1])
                b.bar()
                b.op("scalar", lambda s, ob=ob: s.activation(out=stage[:, ob, :], in_=ps_mm2[:, :], func=AF.Copy))
                b.bar()
            b.op("sync", lambda s, tt=tt, t0=t0: s.dma_start(
                out=o_v[t0 + tt * 128: t0 + (tt + 1) * 128, :], in_=stage.ap().rearrange("p a t -> p (a t)")), dma=True)
            b.bar()







def build_L2():
    nc = bacc.Bacc("TRN2", target_bir_lowering=False)
    qT_p = nc.dram_tensor("qT_p", [4, HD, T], BF16, kind="ExternalInput").ap()
    kT_p = nc.dram_tensor("kT_p", [4, HD, T], BF16, kind="ExternalInput").ap()
    v_p = nc.dram_tensor("v_p", [4, T, HD], BF16, kind="ExternalInput").ap()
    cmask = nc.dram_tensor("cmask", [128, 4 * 512], BF16, kind="ExternalInput").ap()
    o_at = nc.dram_tensor("o_at", [4, T, HD], BF16, kind="ExternalOutput").ap()

    NQB = T // 512

    with ExitStack() as st:
        qTs = st.enter_context(nc.sbuf_tensor("qTs", [64, 4, T], BF16))
        kTs = st.enter_context(nc.sbuf_tensor("kTs", [64, 4, T], BF16))
        vaug = st.enter_context(nc.sbuf_tensor("vaug", [128, 4, T // 128, 65], BF16))
        cm = st.enter_context(nc.sbuf_tensor("cm", [128, 4, 512], BF16))
        onesb = st.enter_context(nc.sbuf_tensor("onesb", [128, 1], BF16))
        onesr = st.enter_context(nc.sbuf_tensor("onesr", [1, 128], F32))
        ksq = st.enter_context(nc.sbuf_tensor("ksq", [64, 512], BF16))
        rows = st.enter_context(nc.sbuf_tensor("rows", [1, T], F32))
        rows2 = st.enter_context(nc.sbuf_tensor("rows2", [1, 64], F32))
        mk1 = st.enter_context(nc.sbuf_tensor("mk1", [1, 1], F32))
        biasq = st.enter_context(nc.sbuf_tensor("biasq", [128, NQB], F32))
        pt = st.enter_context(nc.sbuf_tensor("pt", [128, 2, 512], BF16))
        rz = st.enter_context(nc.sbuf_tensor("rz", [128, 1], F32))
        ot = st.enter_context(nc.sbuf_tensor("ot", [128, 64], BF16))
        ps_s0 = st.enter_context(nc.psum_tensor("ps_s0", [128, 512], F32))
        ps_s1 = st.enter_context(nc.psum_tensor("ps_s1", [128, 512], F32))
        ps_o = [st.enter_context(nc.psum_tensor(f"ps_o{i}", [128, 65], F32)) for i in range(4)]
        ps_r = st.enter_context(nc.psum_tensor("ps_r", [1, 512], F32))
        ps_b = st.enter_context(nc.psum_tensor("ps_b", [128, NQB], F32))
        se_s = st.enter_context(nc.semaphore("se_s"))
        se_e = st.enter_context(nc.semaphore("se_e"))
        se_m = st.enter_context(nc.semaphore("se_m"))
        s_sync = st.enter_context(nc.semaphore("s_sync"))
        s_gpsimd = st.enter_context(nc.semaphore("s_gpsimd"))
        s_tensor = st.enter_context(nc.semaphore("s_tensor"))
        s_vector = st.enter_context(nc.semaphore("s_vector"))
        s_scalar = st.enter_context(nc.semaphore("s_scalar"))
        block = st.enter_context(nc.Block())
        sems = {"sync": s_sync, "gpsimd": s_gpsimd, "tensor": s_tensor,
                "vector": s_vector, "scalar": s_scalar}
        b = Builder(nc)
        ps_s = [ps_s0, ps_s1]

        b.op("sync", lambda s: s.dma_start(out=qTs[:], in_=qT_p.rearrange("a d t -> d a t")), dma=True)
        b.op("sync", lambda s: s.dma_start(out=kTs[:], in_=kT_p.rearrange("a d t -> d a t")), dma=True)
        b.op("sync", lambda s: s.dma_start(
            out=vaug[:, :, :, 0:64], in_=v_p.rearrange("a (n p) d -> p a n d", p=128)), dma=True)
        b.op("sync", lambda s: s.dma_start(out=cm[:], in_=cmask.rearrange("p (r q) -> p r q", r=4)), dma=True)
        b.op("vector", lambda v: v.memset(vaug[:, :, :, 64:65], 1.0))
        b.op("vector", lambda v: v.memset(onesb[:], 1.0))
        b.op("vector", lambda v: v.memset(onesr[:], 1.0))
        b.bar()

        cnt = {"s": 0, "e": 0, "m": 0}

        for pair in range(4):
            for c in range(16):
                b.op("scalar", lambda s, pair=pair, c=c: s.activation(
                    out=ksq[:, :], in_=kTs[:, pair, c * 512:(c + 1) * 512], func=AF.Square))
                b.bar()
                b.op("tensor", lambda t, c=c: t.matmul(ps_r[0:1, :], onesb[0:64, 0:1], ksq[:, :],
                                                       start=True, stop=True))
                b.bar()
                b.op("vector", lambda v, c=c: v.tensor_copy(out=rows[:, c * 512:(c + 1) * 512], in_=ps_r[0:1, :]))
                b.bar()
            b.op("vector", lambda v: v.tensor_reduce(out=mk1[:, :], in_=rows[:, :], axis=AX.X, op=ALU.max))
            b.bar()
            b.op("scalar", lambda s: s.activation(out=mk1[:, :], in_=mk1[:, :], func=AF.Sqrt))
            b.bar()
            for c in range(16):
                b.op("scalar", lambda s, pair=pair, c=c: s.activation(
                    out=ksq[:, :], in_=qTs[:, pair, c * 512:(c + 1) * 512], func=AF.Square))
                b.bar()
                b.op("tensor", lambda t: t.matmul(ps_r[0:1, :], onesb[0:64, 0:1], ksq[:, :],
                                                  start=True, stop=True))
                b.bar()
                b.op("vector", lambda v, c=c: v.tensor_copy(out=rows[:, c * 512:(c + 1) * 512], in_=ps_r[0:1, :]))
                b.bar()
            for qb in range(NQB):
                b.op("vector", lambda v, qb=qb: v.tensor_reduce(
                    out=rows2[:, qb:qb + 1], in_=rows[:, qb * 512:(qb + 1) * 512], axis=AX.X, op=ALU.max))
            b.bar()
            b.op("scalar", lambda s: s.activation(out=rows2[:, 0:NQB], in_=rows2[:, 0:NQB], func=AF.Sqrt))
            b.bar()
            b.op("vector", lambda v: v.tensor_scalar(
                out=rows2[:, 0:NQB], in0=rows2[:, 0:NQB], scalar1=mk1[0:1, 0:1], scalar2=-1.0,
                op0=ALU.mult, op1=ALU.mult))
            b.bar()
            b.op("tensor", lambda t: t.matmul(ps_b[:, :], onesr[0:1, :], rows2[0:1, 0:NQB],
                                              start=True, stop=True))
            b.bar()
            b.op("vector", lambda v: v.tensor_copy(out=biasq[:, :], in_=ps_b[:, :]))
            b.bar()

            for qb in range(NQB):
                nkt = 4 * qb + 4
                e_base = cnt["e"]

                def emit_qblock(pair=pair, qb=qb, nkt=nkt, e_base=e_base, cnt=cnt):
                    pass

                s_waits = []
                for kt in range(nkt):
                    need_e = e_base + kt - 1
                    wait_e = need_e if kt >= 2 else None
                    diag = kt >= 4 * qb
                    r = kt - 4 * qb
                    def s_fn(t, pair=pair, qb=qb, kt=kt, wait_e=wait_e):
                        if wait_e is not None:
                            t.wait_ge(se_e, wait_e)
                        return t.matmul(ps_s[kt % 2][:, :], kTs[:, pair, kt * 128:(kt + 1) * 128],
                                        qTs[:, pair, qb * 512:(qb + 1) * 512], start=True, stop=True)
                    b.op("tensor", lambda t, f=s_fn: (f(t).then_inc(se_s, 1), None)[1])
                    cnt["s"] += 1
                    def e_fn(s, pair=pair, qb=qb, kt=kt, sv=cnt["s"]):
                        s.wait_ge(se_s, sv)
                        return s.activation(out=pt[:, kt % 2, :], in_=ps_s[kt % 2][:, :], func=AF.Exp,
                                            bias=biasq[:, qb:qb + 1])
                    b.op("scalar", lambda s, f=e_fn: (f(s).then_inc(se_e, 1), None)[1])
                    cnt["e"] += 1
                    if diag:
                        def m_fn(v, kt=kt, r=r, ev=cnt["e"]):
                            v.wait_ge(se_e, ev)
                            return v.tensor_tensor(out=pt[:, kt % 2, :], in0=pt[:, kt % 2, :],
                                                   in1=cm[:, r, :], op=ALU.mult)
                        b.op("vector", lambda v, f=m_fn: (f(v).then_inc(se_m, 1), None)[1])
                        cnt["m"] += 1
                    wv = ("m", cnt["m"]) if diag else ("e", cnt["e"])
                    for qs in range(4):
                        def pv_fn(t, pair=pair, kt=kt, qs=qs, nkt=nkt, wv=wv):
                            if qs == 0:
                                t.wait_ge(se_m if wv[0] == "m" else se_e, wv[1])
                            return t.matmul(ps_o[qs][:, :], pt[:, kt % 2, qs * 128:(qs + 1) * 128],
                                            vaug[:, pair, kt, :], start=(kt == 0), stop=(kt == nkt - 1))
                        b.op("tensor", lambda t, f=pv_fn: f(t))
                b.bar()
                for qs in range(4):
                    b.op("scalar", lambda s, qs=qs: s.activation(out=rz[:, :], in_=ps_o[qs][:, 64:65],
                                                                 func=AF.Copy, bias=1e-30))
                    b.bar()
                    b.op("vector", lambda v: v.reciprocal(out=rz[:, :], in_=rz[:, :]))
                    b.bar()
                    b.op("vector", lambda v, qs=qs: v.tensor_scalar(
                        out=ot[:, :], in0=ps_o[qs][:, 0:64], scalar1=rz[:, 0:1], scalar2=None, op0=ALU.mult))
                    b.bar()
                    b.op("sync", lambda s, pair=pair, qb=qb, qs=qs: s.dma_start(
                        out=o_at[pair, qb * 512 + qs * 128: qb * 512 + (qs + 1) * 128, :], in_=ot[:, :]), dma=True)
                    b.bar()

        b.emit(block, sems)
    nc.finalize()
    return nc



def emit_L3(nc, b, st, t, pfx=""):
    at_fm = t["at_fm"]
    lnx_fm = t["lnx_fm"]
    x0_fm = t["x0_fm"]
    n2_w = t["n2_w"]
    esel = t["esel"]
    ident = t["ident"]
    o_x1 = t["o_x1"]
    wdr = {}
    for nm, shp in [("woT", [D, D]), ("gateT", [D, 4]),
                    ("e0_upT", [D, HID]), ("e0_downT", [HID, D]),
                    ("e1_w1T", [D, HID]), ("e1_w2T", [D, HID]), ("e1_downT", [HID, D]),
                    ("e2_l1T", [D, HID]), ("e2_l2T", [HID, HID]), ("e2_l3T", [HID, HID]),
                    ("e2_l4T", [HID, D]),
                    ("e3_downT", [D, D]), ("e3_upT", [D, HID]), ("e3_outT", [HID, D])]:
        wdr[nm] = t[nm]

    SB = 512
    NSB = TPC // SB
    NTT = SB // 128
    NNB = SB // 512

    xp = st.enter_context(nc.sbuf_tensor(pfx + "xp", [128, NCH, SB], F32))
    lnxs = st.enter_context(nc.sbuf_tensor(pfx + "lnxs", [128, NCH, SB], F32))
    ats = st.enter_context(nc.sbuf_tensor(pfx + "ats", [128, NCH, SB], BF16))
    xf = st.enter_context(nc.sbuf_tensor(pfx + "xf", [128, NCH, SB], BF16))
    ga = st.enter_context(nc.sbuf_tensor(pfx + "ga", [128, 16, SB], BF16))
    gb = st.enter_context(nc.sbuf_tensor(pfx + "gb", [128, 16, SB], BF16))
    wbig = st.enter_context(nc.sbuf_tensor(pfx + "wbig", [128, 16, 1024], BF16))
    wgate = st.enter_context(nc.sbuf_tensor(pfx + "wgate", [128, NCH, 4], BF16))
    scr = st.enter_context(nc.sbuf_tensor(pfx + "scr", [128, 512], F32))
    webc = st.enter_context(nc.sbuf_tensor(pfx + "webc", [128, 4, SB], F32))
    wrow = st.enter_context(nc.sbuf_tensor(pfx + "wrow", [4, SB], F32))
    esl = st.enter_context(nc.sbuf_tensor(pfx + "esl", [4, 512], F32))
    idn = st.enter_context(nc.sbuf_tensor(pfx + "idn", [128, 128], F32))
    n2b = st.enter_context(nc.sbuf_tensor(pfx + "n2b", [128, NCH], F32))
    onesb = st.enter_context(nc.sbuf_tensor(pfx + "onesb", [128, 1], BF16))
    onesr = st.enter_context(nc.sbuf_tensor(pfx + "onesr", [1, 128], F32))
    rows = st.enter_context(nc.sbuf_tensor(pfx + "rows", [1, SB], F32))
    bcn = st.enter_context(nc.sbuf_tensor(pfx + "bcn", [128, SB], F32))
    lg = st.enter_context(nc.sbuf_tensor(pfx + "lg", [128, NTT, 4], F32))
    gs = st.enter_context(nc.sbuf_tensor(pfx + "gs", [128, 8], F32))
    ps_a = st.enter_context(nc.psum_tensor(pfx + "ps_a", [128, 512], F32))
    ps_b = st.enter_context(nc.psum_tensor(pfx + "ps_b", [128, 512], F32))
    ps_r = st.enter_context(nc.psum_tensor(pfx + "ps_r", [1, 512], F32))
    ps_bc = st.enter_context(nc.psum_tensor(pfx + "ps_bc", [128, 512], F32))
    ps_g = st.enter_context(nc.psum_tensor(pfx + "ps_g", [128, 4], F32))
    ps_t = st.enter_context(nc.psum_tensor(pfx + "ps_t", [4, 128], F32))
    se_s = st.enter_context(nc.semaphore(pfx + "se_s"))
    se_e = st.enter_context(nc.semaphore(pfx + "se_e"))
    cnt = {"s": 0, "e": 0}
    psl = [ps_a, ps_b]

    b.op("sync", lambda s: s.dma_start(out=esl[:], in_=esel[:, :]), dma=True)
    b.op("sync", lambda s: s.dma_start(out=idn[:], in_=ident[:, :]), dma=True)
    b.op("gpsimd", lambda g: g.dma_start(out=wgate[:], in_=wdr["gateT"].rearrange("(c p) o -> p c o", p=128)), dma=True)
    for c in range(NCH):
        b.op("sync", lambda s, c=c: s.dma_start(out=n2b[:, c:c + 1], in_=n2_w[c * 128:(c + 1) * 128, :]), dma=True)
    b.op("vector", lambda v: v.memset(onesb[:], 1.0))
    b.op("vector", lambda v: v.memset(onesr[:], 1.0))
    b.bar()

    def load_w(name, icr, ocr):
        nic = icr[1] - icr[0]
        oc0, oc1 = ocr
        b.op("gpsimd", lambda g: g.dma_start(
            out=wbig[:, 0:nic, 0:(oc1 - oc0)],
            in_=wdr[name].rearrange("(c p) o -> p c o", p=128)[:, icr[0]:icr[1], oc0:oc1]), dma=True)

    def mm_block(src, oc_list, nic, dst_evac, extra_wait_first=False):
        units = [(oc, nb) for oc in oc_list for nb in range(NNB)]
        e0 = cnt["e"]
        for j, (oc, nb) in enumerate(units):
            ps = psl[j % 2]

            def t_fn(t, oc=oc, nb=nb, ps=ps, wv=(None if j < 2 else e0 + (j - 1))):
                if wv is not None:
                    t.wait_ge(se_e, wv)
                for ic in range(nic):
                    ins = t.matmul(ps[:, :], wbig[:, ic, oc * 128:(oc + 1) * 128],
                                   src[:, ic, nb * 512:(nb + 1) * 512],
                                   start=(ic == 0), stop=(ic == nic - 1))
                return ins
            b.op("tensor", lambda t, f=t_fn: (f(t).then_inc(se_s, 1), None)[1])
            cnt["s"] += 1
            eng, e_fn = dst_evac(j, oc, nb, ps)

            def wrap(e, f=e_fn, sv=cnt["s"]):
                e.wait_ge(se_s, sv)
                return f(e)
            b.op(eng, lambda e, f=wrap: (f(e).then_inc(se_e, 1), None)[1])
            cnt["e"] += 1
        b.bar()

    for sb in range(NSB):
        t0 = sb * SB

        b.op("sync", lambda s, t0=t0: s.dma_start(
            out=ats[:], in_=at_fm.rearrange("(c p) t -> p c t", p=128)[:, :, t0:t0 + SB]), dma=True)
        b.op("sync", lambda s, t0=t0: s.dma_start(
            out=lnxs[:], in_=lnx_fm.rearrange("(c p) t -> p c t", p=128)[:, :, t0:t0 + SB]), dma=True)
        load_w("woT", (0, 8), (0, 1024))
        b.bar()

        for oc in range(2):
            b.op("vector", lambda v, oc=oc: v.tensor_copy(out=xp[:, oc, :], in_=lnxs[:, oc, :]))

        def wo_evac(j, oc, nb, ps):
            def f(v, oc=oc, nb=nb, ps=ps):
                return v.tensor_tensor(out=xp[:, oc, nb * 512:(nb + 1) * 512], in0=ps[:, :],
                                       in1=lnxs[:, oc, nb * 512:(nb + 1) * 512], op=ALU.add)
            return ("vector", f)
        mm_block(ats, list(range(2, NCH)), NCH, wo_evac)

        for c in range(NCH):
            b.op("scalar", lambda s, c=c: s.activation(out=ga[:, c, :], in_=xp[:, c, :], func=AF.Square))
        b.bar()
        for nb in range(NNB):
            for c in range(NCH):
                b.op("tensor", lambda t, c=c, nb=nb: t.matmul(
                    ps_r[0:1, :], onesb[:, 0:1], ga[:, c, nb * 512:(nb + 1) * 512],
                    start=(c == 0), stop=(c == NCH - 1)))
            b.bar()
            b.op("scalar", lambda s, nb=nb: s.activation(out=rows[:, nb * 512:(nb + 1) * 512], in_=ps_r[0:1, :],
                                                         func=AF.Sqrt, scale=f32(1.0 / D), bias=f32(EPS_RMS)))
            b.bar()
        b.op("vector", lambda v: v.reciprocal(out=rows[:, :], in_=rows[:, :]))
        b.bar()
        for nb in range(NNB):
            b.op("tensor", lambda t, nb=nb: t.matmul(ps_bc[:, :], onesr[0:1, :], rows[0:1, nb * 512:(nb + 1) * 512],
                                                     start=True, stop=True))
            b.bar()
            b.op("vector", lambda v, nb=nb: v.tensor_copy(out=bcn[:, nb * 512:(nb + 1) * 512], in_=ps_bc[:, :]))
            b.bar()
        for c in range(NCH):
            for nb in range(NNB):
                b.op("vector", lambda v, c=c, nb=nb: v.tensor_tensor(
                    out=scr[:, :], in0=xp[:, c, nb * 512:(nb + 1) * 512],
                    in1=bcn[:, nb * 512:(nb + 1) * 512], op=ALU.mult))
                b.bar()
                b.op("vector", lambda v, c=c, nb=nb: v.tensor_scalar(
                    out=xf[:, c, nb * 512:(nb + 1) * 512], in0=scr[:, :],
                    scalar1=n2b[:, c:c + 1], scalar2=None, op0=ALU.mult))
                b.bar()

        b.op("sync", lambda s, t0=t0: s.dma_start(
            out=lnxs[:], in_=x0_fm.rearrange("(c p) t -> p c t", p=128)[:, :, t0:t0 + SB]), dma=True)
        b.bar()
        for c in range(NCH):
            b.op("vector", lambda v, c=c: v.scalar_tensor_tensor(
                out=xp[:, c, :], in0=xp[:, c, :], scalar=0.5, in1=lnxs[:, c, :],
                op0=ALU.mult, op1=ALU.add))
        b.bar()

        for tt in range(NTT):
            b.op("tensor", lambda t, tt=tt: [
                t.matmul(ps_g[:, :], xf[:, ic, tt * 128:(tt + 1) * 128], wgate[:, ic, :],
                         start=(ic == 0), stop=(ic == NCH - 1)) for ic in range(NCH)][-1])
            b.bar()
            b.op("vector", lambda v, tt=tt: v.tensor_copy(out=lg[:, tt, :], in_=ps_g[:, :]))
            b.bar()
        for tt in range(NTT):
            b.op("vector", lambda v, tt=tt: v.tensor_reduce(out=gs[:, 0:1], in_=lg[:, tt, :], axis=AX.X, op=ALU.max))
            b.bar()
            b.op("vector", lambda v: v.tensor_scalar(out=gs[:, 1:2], in0=gs[:, 0:1], scalar1=-1.0, scalar2=None, op0=ALU.mult))
            b.bar()
            b.op("scalar", lambda s, tt=tt: s.activation(out=lg[:, tt, :], in_=lg[:, tt, :], func=AF.Exp, bias=gs[:, 1:2]))
            b.bar()
            b.op("vector", lambda v, tt=tt: v.tensor_reduce(out=gs[:, 2:3], in_=lg[:, tt, :], axis=AX.X, op=ALU.add))
            b.bar()
            b.op("vector", lambda v: v.reciprocal(out=gs[:, 3:4], in_=gs[:, 2:3]))
            b.bar()
            b.op("vector", lambda v, tt=tt: v.tensor_scalar(out=lg[:, tt, :], in0=lg[:, tt, :], scalar1=gs[:, 3:4], scalar2=None, op0=ALU.mult))
            b.bar()
            b.op("vector", lambda v, tt=tt: v.tensor_reduce(out=gs[:, 4:5], in_=lg[:, tt, :], axis=AX.X, op=ALU.max))
            b.bar()
            b.op("vector", lambda v, tt=tt: v.tensor_scalar(out=scr[:, 0:4], in0=lg[:, tt, :], scalar1=gs[:, 4:5], scalar2=None, op0=ALU.is_equal))
            b.bar()
            b.op("vector", lambda v: v.tensor_scalar(out=scr[:, 4:8], in0=scr[:, 0:4], scalar1=-1.0, scalar2=1.0, op0=ALU.mult, op1=ALU.add))
            b.bar()
            b.op("vector", lambda v, tt=tt: v.tensor_tensor(out=scr[:, 8:12], in0=lg[:, tt, :], in1=scr[:, 4:8], op=ALU.mult))
            b.bar()
            b.op("vector", lambda v: v.tensor_reduce(out=gs[:, 5:6], in_=scr[:, 8:12], axis=AX.X, op=ALU.max))
            b.bar()
            b.op("vector", lambda v, tt=tt: v.tensor_scalar(out=scr[:, 12:16], in0=lg[:, tt, :], scalar1=gs[:, 5:6], scalar2=None, op0=ALU.is_ge))
            b.bar()
            b.op("vector", lambda v: v.tensor_tensor(out=gs[:, 6:7], in0=gs[:, 4:5], in1=gs[:, 5:6], op=ALU.add))
            b.bar()
            b.op("scalar", lambda s: s.activation(out=gs[:, 6:7], in_=gs[:, 6:7], func=AF.Copy, bias=1e-08))
            b.bar()
            b.op("vector", lambda v: v.reciprocal(out=gs[:, 7:8], in_=gs[:, 6:7]))
            b.bar()
            b.op("vector", lambda v, tt=tt: v.tensor_tensor(out=lg[:, tt, :], in0=lg[:, tt, :], in1=scr[:, 12:16], op=ALU.mult))
            b.bar()
            b.op("vector", lambda v, tt=tt: v.tensor_scalar(out=lg[:, tt, :], in0=lg[:, tt, :], scalar1=gs[:, 7:8], scalar2=None, op0=ALU.mult))
            b.bar()
        for tt in range(NTT):
            b.op("tensor", lambda t, tt=tt: t.transpose(ps_t[:, :], lg[:, tt, :], idn[:, :]))
            b.bar()
            b.op("vector", lambda v, tt=tt: v.tensor_copy(out=wrow[:, tt * 128:(tt + 1) * 128], in_=ps_t[:, :]))
            b.bar()
        for e in range(4):
            for nb in range(NNB):
                b.op("tensor", lambda t, e=e, nb=nb: t.matmul(
                    ps_bc[:, :], esl[:, e * 128:(e + 1) * 128], wrow[0:4, nb * 512:(nb + 1) * 512],
                    start=True, stop=True))
                b.bar()
                b.op("vector", lambda v, e=e, nb=nb: v.tensor_copy(
                    out=webc[:, e, nb * 512:(nb + 1) * 512], in_=ps_bc[:, :]))
                b.bar()

        def act_evac(dst, func):
            def mk(j, oc, nb, ps):
                def f(s, dst=dst, oc=oc, nb=nb, ps=ps, func=func):
                    return s.activation(out=dst[:, oc, nb * 512:(nb + 1) * 512], in_=ps[:, :], func=func)
                return ("scalar", f)
            return mk

        def macc_evac(e, first):
            def mk(j, oc, nb, ps):
                def f(v, e=e, oc=oc, nb=nb, ps=ps):
                    ins = v.tensor_tensor(out=scr[:, :], in0=ps[:, :],
                                          in1=webc[:, e, nb * 512:(nb + 1) * 512], op=ALU.mult)
                    v.scalar_tensor_tensor(out=xp[:, oc, nb * 512:(nb + 1) * 512], in0=scr[:, :],
                                           scalar=0.5, in1=xp[:, oc, nb * 512:(nb + 1) * 512],
                                           op0=ALU.mult, op1=ALU.add)
                    return ins
                return ("vector", f)
            return mk

        load_w("e0_upT", (0, 8), (0, 1024)); b.bar()
        mm_block(xf, list(range(8)), NCH, act_evac(ga, AF.Gelu))
        load_w("e0_upT", (0, 8), (1024, 2048)); b.bar()
        mm_block(xf, list(range(8)), NCH, lambda j, oc, nb, ps: act_evac(ga, AF.Gelu)(j, oc + 8, nb, ps))
        load_w("e0_downT", (0, 16), (0, 1024)); b.bar()
        mm_block(ga, list(range(2, NCH)), 16, macc_evac(0, True))
        load_w("e1_w1T", (0, 8), (0, 1024)); b.bar()
        mm_block(xf, list(range(8)), NCH, act_evac(ga, AF.Silu))
        load_w("e1_w1T", (0, 8), (1024, 2048)); b.bar()
        mm_block(xf, list(range(8)), NCH, lambda j, oc, nb, ps: act_evac(ga, AF.Silu)(j, oc + 8, nb, ps))
        load_w("e1_w2T", (0, 8), (0, 1024)); b.bar()
        mm_block(xf, list(range(8)), NCH, act_evac(gb, AF.Copy))
        load_w("e1_w2T", (0, 8), (1024, 2048)); b.bar()
        mm_block(xf, list(range(8)), NCH, lambda j, oc, nb, ps: act_evac(gb, AF.Copy)(j, oc + 8, nb, ps))
        for hc in range(16):
            b.op("vector", lambda v, hc=hc: v.tensor_tensor(out=ga[:, hc, :], in0=ga[:, hc, :], in1=gb[:, hc, :], op=ALU.mult))
        b.bar()
        load_w("e1_downT", (0, 16), (0, 1024)); b.bar()
        mm_block(ga, list(range(2, NCH)), 16, macc_evac(1, False))
        load_w("e2_l1T", (0, 8), (0, 1024)); b.bar()
        mm_block(xf, list(range(8)), NCH, act_evac(ga, AF.Gelu))
        load_w("e2_l1T", (0, 8), (1024, 2048)); b.bar()
        mm_block(xf, list(range(8)), NCH, lambda j, oc, nb, ps: act_evac(ga, AF.Gelu)(j, oc + 8, nb, ps))
        load_w("e2_l2T", (0, 16), (0, 1024)); b.bar()
        mm_block(ga, list(range(8)), 16, act_evac(gb, AF.Gelu))
        load_w("e2_l2T", (0, 16), (1024, 2048)); b.bar()
        mm_block(ga, list(range(8)), 16, lambda j, oc, nb, ps: act_evac(gb, AF.Gelu)(j, oc + 8, nb, ps))
        load_w("e2_l3T", (0, 16), (0, 1024)); b.bar()
        mm_block(gb, list(range(8)), 16, act_evac(ga, AF.Gelu))
        load_w("e2_l3T", (0, 16), (1024, 2048)); b.bar()
        mm_block(gb, list(range(8)), 16, lambda j, oc, nb, ps: act_evac(ga, AF.Gelu)(j, oc + 8, nb, ps))
        load_w("e2_l4T", (0, 16), (0, 1024)); b.bar()
        mm_block(ga, list(range(2, NCH)), 16, macc_evac(2, False))
        load_w("e3_downT", (0, 8), (0, 1024)); b.bar()
        mm_block(xf, list(range(8)), NCH, act_evac(ga, AF.Gelu))
        load_w("e3_upT", (0, 8), (0, 1024)); b.bar()
        mm_block(ga, list(range(8)), 8, act_evac(gb, AF.Gelu))
        load_w("e3_upT", (0, 8), (1024, 2048)); b.bar()
        mm_block(ga, list(range(8)), 8, lambda j, oc, nb, ps: act_evac(gb, AF.Gelu)(j, oc + 8, nb, ps))
        load_w("e3_outT", (0, 16), (0, 1024)); b.bar()
        mm_block(gb, list(range(2, NCH)), 16, macc_evac(3, False))

        for c in range(NCH):
            b.op("sync", lambda s, c=c, t0=t0: s.dma_start(
                out=o_x1[c * 128:(c + 1) * 128, t0:t0 + SB], in_=xp[:, c, :]), dma=True)
        b.bar()






def emit_L4a(nc, b, st, t, pfx=""):
    x_fm = t["x_fm"]
    wq_w = t["wq_w"]
    wk_w = t["wk_w"]
    wv_w = t["wv_w"]
    wqi_w = t["wqi_w"]
    wki_w = t["wki_w"]
    wq_b = t["wq_b"]
    wk_b = t["wk_b"]
    wv_b = t["wv_b"]
    wqi_b = t["wqi_b"]
    wki_b = t["wki_b"]
    o_qT = t["o_qT"]
    o_kv = t["o_kv"]
    o_qi = t["o_qi"]
    o_ki = t["o_ki"]

    NB = TPC // 512

    xs = st.enter_context(nc.sbuf_tensor(pfx + "xs", [128, NCH, 512], F32))
    xbb = st.enter_context(nc.sbuf_tensor(pfx + "xbb", [128, NCH, 512], BF16))
    lnf = st.enter_context(nc.sbuf_tensor(pfx + "lnf", [128, NCH, 512], F32))
    lnb = st.enter_context(nc.sbuf_tensor(pfx + "lnb", [128, NCH, 512], BF16))
    wqs = st.enter_context(nc.sbuf_tensor(pfx + "wqs", [128, NCH, D], BF16))
    wks = st.enter_context(nc.sbuf_tensor(pfx + "wks", [128, NCH, D], BF16))
    wvs = st.enter_context(nc.sbuf_tensor(pfx + "wvs", [128, NCH, D], BF16))
    wqis = st.enter_context(nc.sbuf_tensor(pfx + "wqis", [128, NCH, IDXD], F32))
    wkis = st.enter_context(nc.sbuf_tensor(pfx + "wkis", [128, NCH, IDXD], F32))
    qbias = st.enter_context(nc.sbuf_tensor(pfx + "qbias", [128, NCH], F32))
    qibias = st.enter_context(nc.sbuf_tensor(pfx + "qibias", [64, 1], F32))
    kibias = st.enter_context(nc.sbuf_tensor(pfx + "kibias", [64, 1], F32))
    kbrow = st.enter_context(nc.sbuf_tensor(pfx + "kbrow", [1, D], F32))
    vbrow = st.enter_context(nc.sbuf_tensor(pfx + "vbrow", [1, D], F32))
    kvbc = st.enter_context(nc.sbuf_tensor(pfx + "kvbc", [128, 2, D], F32))
    onesb = st.enter_context(nc.sbuf_tensor(pfx + "onesb", [128, 1], BF16))
    onesr = st.enter_context(nc.sbuf_tensor(pfx + "onesr", [1, 128], F32))
    rows = st.enter_context(nc.sbuf_tensor(pfx + "rows", [1, 4 * 512], F32))
    bcs = st.enter_context(nc.sbuf_tensor(pfx + "bcs", [128, 2, 512], F32))
    sqv = st.enter_context(nc.sbuf_tensor(pfx + "sqv", [128, 512], BF16))
    qko = st.enter_context(nc.sbuf_tensor(pfx + "qko", [128, 512], BF16))
    kvstage = st.enter_context(nc.sbuf_tensor(pfx + "kvstage", [128, 4, 512], BF16))
    idxo = st.enter_context(nc.sbuf_tensor(pfx + "idxo", [64, 2, 512], F32))
    ps_a = st.enter_context(nc.psum_tensor(pfx + "ps_a", [128, 512], F32))
    ps_b = st.enter_context(nc.psum_tensor(pfx + "ps_b", [128, 512], F32))
    ps_r0 = st.enter_context(nc.psum_tensor(pfx + "ps_r0", [1, 512], F32))
    ps_r1 = st.enter_context(nc.psum_tensor(pfx + "ps_r1", [1, 512], F32))
    ps_bc = st.enter_context(nc.psum_tensor(pfx + "ps_bc", [128, 512], F32))
    ps_i = st.enter_context(nc.psum_tensor(pfx + "ps_i", [64, 512], F32))

    b.op("gpsimd", lambda g: g.dma_start(out=wqs[:], in_=wq_w.rearrange("(c p) o -> p c o", p=128)), dma=True)
    b.op("gpsimd", lambda g: g.dma_start(out=wks[:], in_=wk_w.rearrange("(c p) o -> p c o", p=128)), dma=True)
    b.op("gpsimd", lambda g: g.dma_start(out=wvs[:], in_=wv_w.rearrange("(c p) o -> p c o", p=128)), dma=True)
    b.op("sync", lambda s: s.dma_start(out=wqis[:], in_=wqi_w.rearrange("(c p) o -> p c o", p=128)), dma=True)
    b.op("sync", lambda s: s.dma_start(out=wkis[:], in_=wki_w.rearrange("(c p) o -> p c o", p=128)), dma=True)
    for c in range(NCH):
        b.op("sync", lambda s, c=c: s.dma_start(out=qbias[:, c:c + 1],
                                                in_=wq_b.rearrange("o d -> d o")[c * 128:(c + 1) * 128, :]), dma=True)
    b.op("sync", lambda s: s.dma_start(out=qibias[:], in_=wqi_b.rearrange("o d -> d o")), dma=True)
    b.op("sync", lambda s: s.dma_start(out=kibias[:], in_=wki_b.rearrange("o d -> d o")), dma=True)
    b.op("sync", lambda s: s.dma_start(out=kbrow[:], in_=wk_b[:, :]), dma=True)
    b.op("sync", lambda s: s.dma_start(out=vbrow[:], in_=wv_b[:, :]), dma=True)
    b.op("vector", lambda v: v.memset(onesb[:], 1.0))
    b.op("vector", lambda v: v.memset(onesr[:], 1.0))
    b.bar()
    for j, br in enumerate([kbrow, vbrow]):
        for nb2 in range(2):
            b.op("tensor", lambda t, br=br, nb2=nb2: t.matmul(
                ps_bc[:, :], onesr[0:1, :], br[0:1, nb2 * 512:(nb2 + 1) * 512], start=True, stop=True))
            b.bar()
            b.op("vector", lambda v, j=j, nb2=nb2: v.tensor_copy(
                out=kvbc[:, j, nb2 * 512:(nb2 + 1) * 512], in_=ps_bc[:, :]))
            b.bar()

    for blk in range(NB):
        t0 = blk * 512
        b.op("sync", lambda s, t0=t0: s.dma_start(
            out=xs[:], in_=x_fm.rearrange("(c p) t -> p c t", p=128)[:, :, t0:t0 + 512]), dma=True)
        b.op("gpsimd", lambda g, t0=t0: g.dma_start(
            out=xbb[:], in_=x_fm.rearrange("(c p) t -> p c t", p=128)[:, :, t0:t0 + 512]), dma=True)
        b.bar()
        for c in range(NCH):
            b.op("scalar", lambda s, c=c: s.activation(out=sqv[:, :] if False else lnb[:, c, :],
                                                       in_=xs[:, c, :], func=AF.Square))
        b.bar()
        for c in range(NCH):
            b.op("tensor", lambda t, c=c: t.matmul(ps_r0[0:1, :], onesb[:, 0:1], xbb[:, c, :],
                                                   start=(c == 0), stop=(c == NCH - 1)))
        for c in range(NCH):
            b.op("tensor", lambda t, c=c: t.matmul(ps_r1[0:1, :], onesb[:, 0:1], lnb[:, c, :],
                                                   start=(c == 0), stop=(c == NCH - 1)))
        b.bar()
        b.op("scalar", lambda s: s.activation(out=rows[:, 0:512], in_=ps_r0[0:1, :], func=AF.Copy, scale=f32(1.0 / D)))
        b.op("scalar", lambda s: s.activation(out=rows[:, 512:1024], in_=ps_r1[0:1, :], func=AF.Copy, scale=f32(1.0 / D)))
        b.bar()
        b.op("vector", lambda v: v.tensor_tensor(out=rows[:, 1024:1536], in0=rows[:, 0:512], in1=rows[:, 0:512], op=ALU.mult))
        b.bar()
        b.op("vector", lambda v: v.tensor_tensor(out=rows[:, 512:1024], in0=rows[:, 512:1024], in1=rows[:, 1024:1536], op=ALU.subtract))
        b.bar()
        b.op("scalar", lambda s: s.activation(out=rows[:, 1024:1536], in_=rows[:, 512:1024], func=AF.Sqrt, bias=f32(EPS_LN)))
        b.bar()
        b.op("vector", lambda v: v.reciprocal(out=rows[:, 1024:1536], in_=rows[:, 1024:1536]))
        b.bar()
        for j, off in enumerate([0, 1024]):
            b.op("tensor", lambda t, off=off: t.matmul(ps_bc[:, :], onesr[0:1, :], rows[0:1, off:off + 512],
                                                       start=True, stop=True))
            b.bar()
            b.op("vector", lambda v, j=j: v.tensor_copy(out=bcs[:, j, :], in_=ps_bc[:, :]))
            b.bar()
        for c in range(NCH):
            b.op("vector", lambda v, c=c: v.tensor_tensor(out=lnf[:, c, :], in0=xs[:, c, :], in1=bcs[:, 0, :], op=ALU.subtract))
        b.bar()
        for c in range(NCH):
            b.op("vector", lambda v, c=c: v.tensor_tensor(out=lnf[:, c, :], in0=lnf[:, c, :], in1=bcs[:, 1, :], op=ALU.mult))
        b.bar()
        for c in range(NCH):
            b.op("vector", lambda v, c=c: v.tensor_copy(out=lnb[:, c, :], in_=lnf[:, c, :]))
        b.bar()
        for oc in range(NCH):
            b.op("tensor", lambda t, oc=oc: [
                t.matmul(ps_a[:, :], wqs[:, ic, oc * 128:(oc + 1) * 128], lnb[:, ic, :],
                         start=(ic == 0), stop=(ic == NCH - 1)) for ic in range(NCH)][-1])
            b.bar()
            b.op("vector", lambda v, oc=oc: v.tensor_scalar(out=qko[:, :], in0=ps_a[:, :],
                                                            scalar1=qbias[:, oc:oc + 1], scalar2=0.125,
                                                            op0=ALU.add, op1=ALU.mult))
            b.bar()
            b.op("sync", lambda s, oc=oc, t0=t0: s.dma_start(
                out=o_qT[oc * 128:(oc + 1) * 128, t0:t0 + 512], in_=qko[:, :]), dma=True)
            b.bar()
        for tt in range(4):
            for j, ws in enumerate([wks, wvs]):
                for ob in range(2):
                    b.op("tensor", lambda t, tt=tt, ws=ws, ob=ob: [
                        t.matmul(ps_b[:, :], lnb[:, ic, tt * 128:(tt + 1) * 128],
                                 ws[:, ic, ob * 512:(ob + 1) * 512],
                                 start=(ic == 0), stop=(ic == NCH - 1)) for ic in range(NCH)][-1])
                    b.bar()
                    b.op("vector", lambda v, j=j, ob=ob: v.tensor_tensor(
                        out=kvstage[:, j * 2 + ob, :], in0=ps_b[:, :],
                        in1=kvbc[:, j, ob * 512:(ob + 1) * 512], op=ALU.add))
                    b.bar()
            b.op("sync", lambda s, tt=tt, t0=t0: s.dma_start(
                out=o_kv[t0 + tt * 128: t0 + (tt + 1) * 128, :],
                in_=kvstage.ap().rearrange("p a t -> p (a t)")), dma=True)
            b.bar()
        for j, (ws, bia, oo) in enumerate([(wqis, qibias, o_qi), (wkis, kibias, o_ki)]):
            b.op("tensor", lambda t, ws=ws: [
                t.matmul(ps_i[:, :], ws[:, ic, :], lnf[:, ic, :],
                         start=(ic == 0), stop=(ic == NCH - 1)) for ic in range(NCH)][-1])
            b.bar()
            b.op("vector", lambda v, bia=bia: v.tensor_scalar(out=idxo[:, 0, :], in0=ps_i[:, :],
                                                              scalar1=bia[:, 0:1], scalar2=None, op0=ALU.add))
            b.bar()
            b.op("sync", lambda s, oo=oo, t0=t0: s.dma_start(
                out=oo[:, t0:t0 + 512], in_=idxo[:, 0, :]), dma=True)
            b.bar()






L1_DRAMS = [("x_fm", [D, TPC], F32, "ExternalInput"),
            ("wqT", [D, D], F32, "ExternalInput"),
            ("wkT", [D, D], F32, "ExternalInput"),
            ("wvT", [D, D], F32, "ExternalInput"),
            ("n1_w", [D, 1], F32, "ExternalInput"),
            ("qn_w", [HD, 1], F32, "ExternalInput"),
            ("kn_w", [HD, 1], F32, "ExternalInput"),
            ("o_lnx", [D, TPC], F32, "ExternalOutput"),
            ("o_qT", [D, TPC], BF16, "ExternalOutput"),
            ("o_kT", [D, TPC], BF16, "ExternalOutput"),
            ("o_v", [TPC, D], BF16, "ExternalOutput")]

L3_W_SHAPES = [("woT", [D, D]), ("gateT", [D, 4]),
               ("e0_upT", [D, HID]), ("e0_downT", [HID, D]),
               ("e1_w1T", [D, HID]), ("e1_w2T", [D, HID]), ("e1_downT", [HID, D]),
               ("e2_l1T", [D, HID]), ("e2_l2T", [HID, HID]), ("e2_l3T", [HID, HID]),
               ("e2_l4T", [HID, D]),
               ("e3_downT", [D, D]), ("e3_upT", [D, HID]), ("e3_outT", [HID, D])]

L3_DRAMS = ([("at_fm", [D, TPC], BF16, "ExternalInput"),
             ("lnx_fm", [D, TPC], F32, "ExternalInput"),
             ("x0_fm", [D, TPC], F32, "ExternalInput"),
             ("n2_w", [D, 1], F32, "ExternalInput"),
             ("esel", [4, 512], F32, "ExternalInput"),
             ("ident", [128, 128], F32, "ExternalInput"),
             ("o_x1", [D, TPC], F32, "ExternalOutput")]
            + [(nm, shp, F32, "ExternalInput") for nm, shp in L3_W_SHAPES])

L4A_DRAMS = [("x_fm", [D, TPC], F32, "ExternalInput"),
             ("wq_w", [D, D], F32, "ExternalInput"),
             ("wk_w", [D, D], F32, "ExternalInput"),
             ("wv_w", [D, D], F32, "ExternalInput"),
             ("wqi_w", [D, IDXD], F32, "ExternalInput"),
             ("wki_w", [D, IDXD], F32, "ExternalInput"),
             ("wq_b", [1, D], F32, "ExternalInput"),
             ("wk_b", [1, D], F32, "ExternalInput"),
             ("wv_b", [1, D], F32, "ExternalInput"),
             ("wqi_b", [1, IDXD], F32, "ExternalInput"),
             ("wki_b", [1, IDXD], F32, "ExternalInput"),
             ("o_qT", [D, TPC], BF16, "ExternalOutput"),
             ("o_kv", [TPC, 2 * D], BF16, "ExternalOutput"),
             ("o_qi", [IDXD, TPC], F32, "ExternalOutput"),
             ("o_ki", [IDXD, TPC], F32, "ExternalOutput")]


def _mk_build(emit_fn, drams, pfx):
    def build():
        nc = bacc.Bacc("TRN2", target_bir_lowering=False)
        t = {nm: nc.dram_tensor(nm, shp, dt, kind=kd).ap() for nm, shp, dt, kd in drams}
        with ExitStack() as st:
            block = st.enter_context(nc.Block())
            sems = {e: st.enter_context(nc.semaphore("s_" + e)) for e in ENGS}
            b = Builder(nc)
            with ExitStack() as sst:
                emit_fn(nc, b, sst, t, pfx)
            b.emit(block, sems)
        nc.finalize()
        return nc
    return build


build_L1 = _mk_build(emit_L1, L1_DRAMS, "l1_")
build_L3 = _mk_build(emit_L3, L3_DRAMS, "l3_")
build_L4a = _mk_build(emit_L4a, L4A_DRAMS, "l4a_")


def emit_L4b(nc, b, st, t, pfx="", nqt=None):
    qT_d = t["qT_d"]
    qiT_d = t["qiT_d"]
    g_ki = t["g_ki"]
    kv_d = t["kv_d"]
    x1_d = t["x1_d"]
    wo_w = t["wo_w"]
    wo_b = t["wo_b"]
    ident = t["ident"]
    identb = t["identb"]
    cbase = t["cbase"]
    o_x2 = t["o_x2"]
    bK = t["bK"]
    bV = t["bV"]
    vaDR = t["vaDR"]

    NQT = TPC // 128
    NAB = AL // 512

    qts = st.enter_context(nc.sbuf_tensor(pfx + "qts", [128, NCH, 128], BF16))
    qis = st.enter_context(nc.sbuf_tensor(pfx + "qis", [64, 128], F32))
    kis = st.enter_context(nc.sbuf_tensor(pfx + "kis", [64, AL], F32))
    sidx = st.enter_context(nc.sbuf_tensor(pfx + "sidx", [128, AL], F32))
    cand = st.enter_context(nc.sbuf_tensor(pfx + "cand", [128, 960], F32))
    candg = st.enter_context(nc.sbuf_tensor(pfx + "candg", [128, 960], F32))
    candi = st.enter_context(nc.sbuf_tensor(pfx + "candi", [128, 960], U32))
    cbs = st.enter_context(nc.sbuf_tensor(pfx + "cbs", [128, 960], F32))
    tv = st.enter_context(nc.sbuf_tensor(pfx + "tv", [128, 64], F32))
    picks = st.enter_context(nc.sbuf_tensor(pfx + "picks", [128, 64], F32))
    xkv = st.enter_context(nc.sbuf_tensor(pfx + "xkv", [128, 2, 64], F32))
    xkvi = st.enter_context(nc.sbuf_tensor(pfx + "xkvi", [128, 2, 64], I16))
    idxk = st.enter_context(nc.sbuf_tensor(pfx + "idxk", [128, 512], I16))
    idxv = st.enter_context(nc.sbuf_tensor(pfx + "idxv", [128, 512], I16))
    gkt = st.enter_context(nc.sbuf_tensor(pfx + "gkt", [128, 8, 512], BF16))
    gv = st.enter_context(nc.sbuf_tensor(pfx + "gv", [128, 8, 1024], BF16))
    kwTs = st.enter_context(nc.sbuf_tensor(pfx + "kwTs", [128, NCH, WM], BF16))
    vws = st.enter_context(nc.sbuf_tensor(pfx + "vws", [128, 4, D], BF16))
    wobr = st.enter_context(nc.sbuf_tensor(pfx + "wobr", [128, NCH], F32))
    pa = st.enter_context(nc.sbuf_tensor(pfx + "pa", [128, 256], BF16))
    pwt = st.enter_context(nc.sbuf_tensor(pfx + "pwt", [128, 8, 4, 128], BF16))
    vas = st.enter_context(nc.sbuf_tensor(pfx + "vas", [64, 2048], F32))
    vtr = st.enter_context(nc.sbuf_tensor(pfx + "vtr", [128, 64], F32))
    vtr8 = st.enter_context(nc.sbuf_tensor(pfx + "vtr8", [128, 8, 64], F32))
    vaE = st.enter_context(nc.sbuf_tensor(pfx + "vaE", [64, 1024], F32))
    vaO = st.enter_context(nc.sbuf_tensor(pfx + "vaO", [64, 1024], F32))
    atw = st.enter_context(nc.sbuf_tensor(pfx + "atw", [64, 2, 1024], F32))
    attn_fm = st.enter_context(nc.sbuf_tensor(pfx + "attn_fm", [128, NCH, 128], BF16))
    zrow = st.enter_context(nc.sbuf_tensor(pfx + "zrow", [1, 2048], F32))
    zh = st.enter_context(nc.sbuf_tensor(pfx + "zh", [1, 2048], F32))
    rzp = st.enter_context(nc.sbuf_tensor(pfx + "rzp", [64, 2, 16], F32))
    mkcol = st.enter_context(nc.sbuf_tensor(pfx + "mkcol", [128, 80], F32))
    mks = st.enter_context(nc.sbuf_tensor(pfx + "mks", [1, 8], F32))
    ebias = st.enter_context(nc.sbuf_tensor(pfx + "ebias", [128, 1], F32))
    ksqb = st.enter_context(nc.sbuf_tensor(pfx + "ksqb", [128, 1024], BF16))
    x1s = st.enter_context(nc.sbuf_tensor(pfx + "x1s", [128, NCH, 128], F32))
    x2s = st.enter_context(nc.sbuf_tensor(pfx + "x2s", [128, NCH, 128], F32))
    idn = st.enter_context(nc.sbuf_tensor(pfx + "idn", [128, 128], F32))
    idnb = st.enter_context(nc.sbuf_tensor(pfx + "idnb", [128, 128], BF16))
    onesb = st.enter_context(nc.sbuf_tensor(pfx + "onesb", [128, 1], BF16))
    onesr = st.enter_context(nc.sbuf_tensor(pfx + "onesr", [1, 128], F32))
    ps_a = st.enter_context(nc.psum_tensor(pfx + "ps_a", [128, 512], F32))
    ps_b = st.enter_context(nc.psum_tensor(pfx + "ps_b", [128, 512], F32))
    ps_sc = st.enter_context(nc.psum_tensor(pfx + "ps_sc", [128, 512], F32))
    ps_va = st.enter_context(nc.psum_tensor(pfx + "ps_va", [64, 512], F32))
    ps_owE = st.enter_context(nc.psum_tensor(pfx + "ps_owE", [64, 1024], F32))
    ps_owO = st.enter_context(nc.psum_tensor(pfx + "ps_owO", [64, 1024], F32))
    se_v = st.enter_context(nc.semaphore(pfx + "se_v"))
    cnt = {"v": 0}

    def vop(fn):
        w = cnt["v"]

        def g(v, fn=fn, w=w):
            if w > 0:
                v.wait_ge(se_v, w)
            return fn(v)
        b.op("vector", lambda v, g=g: (g(v).then_inc(se_v, 1), None)[1])
        cnt["v"] += 1

    b.op("gpsimd", lambda g: g.dma_start(out=idnb[:], in_=identb[:, :]), dma=True)
    b.op("sync", lambda s: s.dma_start(out=kis[:, 0:1536], in_=g_ki[0:IDXD, 0:1536]), dma=True)
    for s2 in range(1, 4):
        b.op("sync", lambda s, s2=s2: s.dma_start(
            out=kis[:, 1536 + 2048 * (s2 - 1):1536 + 2048 * s2],
            in_=g_ki[IDXD * s2:IDXD * (s2 + 1), :]), dma=True)
    b.op("sync", lambda s: s.dma_start(
        out=vws[:], in_=kv_d[1536:2048, D:2 * D].rearrange("(n p) d -> p n d", p=128)), dma=True)
    b.op("sync", lambda s: s.dma_start(out=cbs[:], in_=cbase[:, :]), dma=True)
    b.op("sync", lambda s: s.dma_start(out=idn[:], in_=ident[:, :]), dma=True)
    for c in range(NCH):
        b.op("sync", lambda s, c=c: s.dma_start(
            out=wobr[:, c:c + 1], in_=wo_b.rearrange("o d -> d o")[c * 128:(c + 1) * 128, :]), dma=True)
    b.op("vector", lambda v: v.memset(onesb[:], 1.0))
    b.op("vector", lambda v: v.memset(onesr[:], 1.0))
    b.op("vector", lambda v: v.memset(idxk[:], 0))
    b.op("vector", lambda v: v.memset(idxv[:], 0))
    b.bar()

    for tt in range(4):
        b.op("sync", lambda s, tt=tt: s.dma_start(
            out=ksqb[:], in_=kv_d[1536 + 128 * tt:1536 + 128 * (tt + 1), 0:D]), dma=True)
        b.bar()
        b.op("vector", lambda v: v.tensor_copy(out=x1s.ap().rearrange("p c t -> p (c t)"), in_=ksqb[:]))
        b.bar()
        for cc in range(NCH):
            b.op("tensor", lambda t, cc=cc: t.transpose(
                ps_a[:, 0:128], x1s.ap().rearrange("p c t -> p (c t)")[:, cc * 128:(cc + 1) * 128],
                idn[:, :]))
            b.bar()
            b.op("vector", lambda v, cc=cc, tt=tt: v.tensor_copy(
                out=kwTs[:, cc, tt * 128:(tt + 1) * 128], in_=ps_a[:, 0:128]))
            b.bar()

    for kt in range(64):
        b.op("gpsimd", lambda g, kt=kt: g.dma_start(
            out=ksqb[:], in_=kv_d[kt * 128:(kt + 1) * 128, 0:D]), dma=True)
        b.bar()
        b.op("scalar", lambda s: s.activation(out=ksqb[:], in_=ksqb[:], func=AF.Square))
        b.bar()
        b.op("vector", lambda v, kt=kt: v.tensor_reduce(
            out=mkcol[:, 64:80],
            in_=ksqb.ap().rearrange("p (h d) -> p h d", h=16), axis=AX.X, op=ALU.add))
        b.bar()
        b.op("vector", lambda v, kt=kt: v.tensor_reduce(
            out=mkcol[:, kt:kt + 1], in_=mkcol[:, 64:80], axis=AX.X, op=ALU.max))
        b.bar()
    b.op("vector", lambda v: v.tensor_reduce(out=mkcol[:, 64:65], in_=mkcol[:, 0:64], axis=AX.X, op=ALU.max))
    b.bar()
    b.op("tensor", lambda t: t.transpose(ps_a[0:1, 0:128], mkcol[:, 64:65], idn[:, :]))
    b.bar()
    b.op("vector", lambda v: v.tensor_copy(out=zh[:, 0:128], in_=ps_a[0:1, 0:128]))
    b.bar()
    b.op("vector", lambda v: v.tensor_reduce(out=mks[:, 0:1], in_=zh[:, 0:128], axis=AX.X, op=ALU.max))
    b.bar()
    b.op("scalar", lambda s: s.activation(out=mks[:, 0:1], in_=mks[:, 0:1], func=AF.Sqrt))
    b.bar()

    for qt in range(nqt if nqt is not None else NQT):
        q0 = qt * 128
        b.op("sync", lambda s, q0=q0: s.dma_start(
            out=qts[:], in_=qT_d.rearrange("(c p) t -> p c t", p=128)[:, :, q0:q0 + 128]), dma=True)
        b.op("sync", lambda s, q0=q0: s.dma_start(out=qis[:], in_=qiT_d[:, q0:q0 + 128]), dma=True)
        b.op("sync", lambda s, q0=q0: s.dma_start(
            out=x1s[:], in_=x1_d.rearrange("(c p) t -> p c t", p=128)[:, :, q0:q0 + 128]), dma=True)
        b.bar()
        for c in range(NCH):
            b.op("scalar", lambda s, c=c: s.activation(out=ksqb[:, c * 128:(c + 1) * 128],
                                                       in_=qts[:, c, :], func=AF.Square))
        b.bar()
        for h in range(16):
            ps = [ps_a, ps_b][h % 2]
            b.op("tensor", lambda t, h=h, ps=ps: t.matmul(
                ps[0:1, 0:128], onesb[64 * (h % 2):64 * (h % 2) + 64, 0:1],
                ksqb[64 * (h % 2):64 * (h % 2) + 64, (h // 2) * 128:(h // 2 + 1) * 128],
                start=True, stop=True))
            b.bar()
            b.op("vector", lambda v, h=h, ps=ps: v.tensor_copy(out=zh[:, h * 128:(h + 1) * 128], in_=ps[0:1, 0:128]))
            b.bar()
        b.op("vector", lambda v: v.tensor_reduce(out=mks[:, 1:2], in_=zh[:, 0:2048], axis=AX.X, op=ALU.max))
        b.bar()
        b.op("scalar", lambda s: s.activation(out=mks[:, 1:2], in_=mks[:, 1:2], func=AF.Sqrt))
        b.bar()
        b.op("vector", lambda v: v.tensor_tensor(out=mks[:, 2:3], in0=mks[:, 1:2], in1=mks[:, 0:1], op=ALU.mult))
        b.bar()
        b.op("vector", lambda v: v.tensor_scalar(out=mks[:, 2:3], in0=mks[:, 2:3], scalar1=-1.0, scalar2=None, op0=ALU.mult))
        b.bar()
        b.op("tensor", lambda t: t.matmul(ps_a[:, 0:1], onesr[0:1, :], mks[0:1, 2:3], start=True, stop=True))
        b.bar()
        b.op("vector", lambda v: v.tensor_copy(out=ebias[:, :], in_=ps_a[:, 0:1]))
        b.bar()

        for ab in range(NAB):
            ps = [ps_a, ps_b][ab % 2]
            b.op("tensor", lambda t, ab=ab, ps=ps: t.matmul(
                ps[:, :], qis[:, :], kis[:, ab * 512:(ab + 1) * 512], start=True, stop=True))
            b.bar()
            b.op("vector", lambda v, ab=ab, ps=ps: v.tensor_copy(
                out=sidx[:, ab * 512:(ab + 1) * 512], in_=ps[:, :]))
            b.bar()

        for c in range(120):
            vop(lambda v, c=c: v.max(out=cand[:, c * 8:(c + 1) * 8], in_=sidx[:, c * 64:(c + 1) * 64]))
            vop(lambda v, c=c: v.max_index(out=candi[:, c * 8:(c + 1) * 8],
                                           in_max=cand[:, c * 8:(c + 1) * 8],
                                           in_=sidx[:, c * 64:(c + 1) * 64]) if False else
                v.max_index(out=candi[:, c * 8:(c + 1) * 8], in_max=cand[:, c * 8:(c + 1) * 8],
                            in_values=sidx[:, c * 64:(c + 1) * 64]))
        vop(lambda v: v.tensor_copy(out=candg[:, :], in_=cand[:, :]))
        for it in range(8):
            vop(lambda v, it=it: v.max(out=tv[:, it * 8:(it + 1) * 8], in_=candg[:, :]))
            vop(lambda v, it=it: v.match_replace(out=candg[:, :], in_to_replace=tv[:, it * 8:(it + 1) * 8],
                                                 in_values=candg[:, :], imm_value=-1e30))
        vop(lambda v: v.tensor_copy(out=candg[:, :], in_=candi[:, :]))
        vop(lambda v: v.tensor_tensor(out=candg[:, :], in0=candg[:, :], in1=cbs[:, :], op=ALU.add))
        vop(lambda v: v.tensor_scalar(out=cand[:, :], in0=cand[:, :], scalar1=tv[:, 63:64], scalar2=None, op0=ALU.is_ge))
        vop(lambda v: v.tensor_tensor(out=candg[:, :], in0=candg[:, :], in1=cand[:, :], op=ALU.mult))
        for it in range(8):
            vop(lambda v, it=it: v.max(out=picks[:, it * 8:(it + 1) * 8], in_=candg[:, :]))
            vop(lambda v, it=it: v.match_replace(out=candg[:, :], in_to_replace=picks[:, it * 8:(it + 1) * 8],
                                                 in_values=candg[:, :], imm_value=0.0))
        vop(lambda v: v.tensor_scalar(out=xkv[:, 0, :], in0=picks[:, :], scalar1=2.0, scalar2=-2.0,
                                      op0=ALU.mult, op1=ALU.add))
        vop(lambda v: v.tensor_scalar(out=xkv[:, 0, :], in0=xkv[:, 0, :], scalar1=0.0, scalar2=None, op0=ALU.max))
        vop(lambda v: v.tensor_scalar(out=xkv[:, 1, :], in0=picks[:, :], scalar1=1536.5, scalar2=1024.0,
                                      op0=ALU.is_ge, op1=ALU.mult))
        vop(lambda v: v.tensor_tensor(out=xkv[:, 0, :], in0=xkv[:, 0, :], in1=xkv[:, 1, :], op=ALU.add))
        vop(lambda v: v.tensor_scalar(out=xkv[:, 1, :], in0=xkv[:, 0, :], scalar1=1.0, scalar2=None, op0=ALU.add))
        vop(lambda v: v.tensor_copy(out=xkvi[:, :, :], in_=xkv[:, :, :]))
        b.bar()
        for col, bT in [(0, bK), (1, bV)]:
            b.op("sync", lambda s, col=col, bT=bT: s.dma_start(
                out=bT.rearrange("jl (q jh) -> q jl jh", q=128),
                in_=xkvi.ap().rearrange("p a (jl jh) -> p a jl jh", jl=16)[:, col, :, :]), dma=True)
        b.bar()
        for tgt, bT in [(idxk, bK), (idxv, bV)]:
            b.op("sync", lambda s, tgt=tgt, bT=bT: s.dma_start(out=tgt[0:16, :], in_=bT[:, :]), dma=True)
            b.op("sync", lambda s, tgt=tgt, bT=bT: s.dma_start(out=tgt[16:32, :], in_=bT[:, :]), dma=True)
        b.bar()

        for call in range(16):
            b.op("gpsimd", lambda g, call=call: g.dma_gather(
                gkt[:, :, 0:512], kv_d.rearrange("t (a c) -> (t a) c", a=2),
                idxk[:, call * 32:(call + 1) * 32], 512, 512, 1024, transpose=True), dma=True)
            b.op("gpsimd", lambda g, call=call: g.dma_gather(
                gv[:, 0:4, :], kv_d.rearrange("t (a c) -> (t a) c", a=2),
                idxv[:, call * 32:(call + 1) * 32], 512, 512, 1024), dma=True)
            b.bar()
            for mloc in range(4):
                m = call * 4 + mloc
                for h in range(16):
                    b.op("tensor", lambda t, mloc=mloc, m=m, h=h: t.matmul(
                        ps_sc[:, mloc * 32 + h * 2: mloc * 32 + h * 2 + 2],
                        gkt[64 * (h % 2):64 * (h % 2) + 64, h // 2, mloc * 128: mloc * 128 + 128],
                        qts[64 * (h % 2):64 * (h % 2) + 64, h // 2, 2 * m:2 * m + 2],
                        start=True, stop=True))
            b.bar()
            b.op("scalar", lambda s: s.activation(out=pa[:, 0:128], in_=ps_sc[:, 0:128], func=AF.Exp, bias=ebias[:, 0:1]))
            b.bar()
            b.op("vector", lambda v: v.memset(pa.ap().rearrange("p (m h a) -> p m h a", m=8, h=16)[64:128, 0:4, :, 0], 0.0))
            b.op("vector", lambda v: v.memset(pa.ap().rearrange("p (m h a) -> p m h a", m=8, h=16)[0:64, 0:4, :, 1], 0.0))
            b.bar()
            b.op("tensor", lambda t: t.matmul(ps_a[0:1, 0:128], onesb[:, 0:1], pa[:, 0:128], start=True, stop=True))
            b.bar()
            b.op("vector", lambda v, call=call: v.tensor_copy(out=zrow[:, call * 128:(call + 1) * 128], in_=ps_a[0:1, 0:128]))
            b.bar()
            for mloc in range(4):
                for h in range(16):
                    b.op("tensor", lambda t, mloc=mloc, h=h: t.matmul(
                        ps_va[:, mloc * 32 + h * 2: mloc * 32 + h * 2 + 2],
                        gv[:, mloc, h * 64:(h + 1) * 64],
                        pa[:, mloc * 32 + h * 2: mloc * 32 + h * 2 + 2],
                        start=True, stop=True))
            b.bar()
            b.op("vector", lambda v, call=call: v.tensor_copy(out=vas[:, call * 128:(call + 1) * 128], in_=ps_va[:, 0:128]))
            b.bar()

        for hg in range(2):
            for hl in range(8):
                h = hg * 8 + hl
                ps = [ps_a, ps_b][hl % 2]
                for wt in range(4):
                    b.op("tensor", lambda t, h=h, wt=wt, ps=ps: t.matmul(
                        ps[:, wt * 128:(wt + 1) * 128],
                        kwTs[64 * (h % 2):64 * (h % 2) + 64, h // 2, wt * 128:(wt + 1) * 128],
                        qts[64 * (h % 2):64 * (h % 2) + 64, h // 2, :],
                        start=True, stop=True))
                b.bar()
                b.op("scalar", lambda s, hl=hl, ps=ps: s.activation(
                    out=pwt.ap().rearrange("p h w q -> p (h w q)")[:, hl * 512:(hl + 1) * 512],
                    in_=ps[:, :], func=AF.Exp, bias=ebias[:, 0:1]))
                b.bar()
            for hl in range(8):
                h = hg * 8 + hl
                b.op("tensor", lambda t, hl=hl: [t.matmul(
                    ps_b[0:1, 0:128], onesb[:, 0:1], pwt[:, hl, wt, :],
                    start=(wt == 0), stop=(wt == 3)) for wt in range(4)][-1])
                b.bar()
                b.op("vector", lambda v, h=h: v.tensor_copy(
                    out=zh.ap().rearrange("o (m h a) -> o h m a", m=64, h=16)[:, h, :, :],
                    in_=ps_b.ap().rearrange("o (q x) -> o q x", x=1)[0:1, 0:128, :]))
                b.bar()
            if hg == 0:
                for par, pso in [(0, ps_owE), (1, ps_owO)]:
                    for hl in range(8):
                        h = hl
                        for wt in range(4):
                            b.op("tensor", lambda t, par=par, pso=pso, h=h, hl=hl, wt=wt: t.matmul(
                                pso[:, h * 64:(h + 1) * 64],
                                pwt.ap().rearrange("p h w (m a) -> p h w m a", a=2)[:, hl, wt, :, par],
                                vws[:, wt, h * 64:(h + 1) * 64],
                                start=(wt == 0), stop=(wt == 3)))
                b.bar()
        b.op("vector", lambda v: v.tensor_tensor(out=zh[:, :], in0=zh[:, :], in1=zrow[:, :], op=ALU.add))
        b.bar()
        b.op("vector", lambda v: v.tensor_scalar(out=zh[:, :], in0=zh[:, :], scalar1=1e-30, scalar2=None, op0=ALU.add))
        b.bar()
        b.op("vector", lambda v: v.reciprocal(out=zh[:, :], in_=zh[:, :]))
        b.bar()
        for par in range(2):
            for h in range(16):
                b.op("tensor", lambda t, par=par, h=h: t.transpose(
                    ps_b[0:64, par * 16 + h: par * 16 + h + 1],
                    zh.ap().rearrange("o (m h a) -> o m h a", m=64, h=16)[:, :, h, par],
                    idn[0:1, 0:1]))
        b.bar()
        b.op("vector", lambda v: v.tensor_copy(out=rzp.ap().rearrange("p a h -> p (a h)"), in_=ps_b[0:64, 0:32]))
        b.bar()

        for par, pso in [(0, ps_owE), (1, ps_owO)]:
            for hl in range(8):
                h = 8 + hl
                for wt in range(4):
                    b.op("tensor", lambda t, par=par, pso=pso, h=h, hl=hl, wt=wt: t.matmul(
                        pso[:, h * 64:(h + 1) * 64],
                        pwt.ap().rearrange("p h w (m a) -> p h w m a", a=2)[:, hl, wt, :, par],
                        vws[:, wt, h * 64:(h + 1) * 64],
                        start=(wt == 0), stop=(wt == 3)))
        b.bar()

        for g in range(2):
            for cl in range(8):
                b.op("tensor", lambda t, g=g, cl=cl: t.transpose(
                    ps_sc[:, cl * 64:(cl + 1) * 64],
                    vas[:, (g * 8 + cl) * 128:(g * 8 + cl + 1) * 128], idn[0:64, 0:64]))
            b.bar()
            b.op("vector", lambda v: v.tensor_copy(
                out=vtr8.ap().rearrange("p c d -> p (c d)"), in_=ps_sc[:, :]))
            b.bar()
            b.op("sync", lambda s, g=g: s.dma_start(
                out=vaDR.rearrange("(r p) d -> p r d", p=128)[:, g * 8:(g + 1) * 8, :],
                in_=vtr8.ap()), dma=True)
            b.bar()
        b.op("sync", lambda s: s.dma_start(
            out=vaE.ap().rearrange("m (h d) -> m h d", h=16),
            in_=vaDR.rearrange("(m h a) d -> m h a d", h=16, a=2)[:, :, 0, :]), dma=True)
        b.op("sync", lambda s: s.dma_start(
            out=vaO.ap().rearrange("m (h d) -> m h d", h=16),
            in_=vaDR.rearrange("(m h a) d -> m h a d", h=16, a=2)[:, :, 1, :]), dma=True)
        b.bar()

        for par, vaP, pso in [(0, vaE, ps_owE), (1, vaO, ps_owO)]:
            b.op("vector", lambda v, par=par, vaP=vaP, pso=pso: v.tensor_tensor(
                out=atw[:, par, :], in0=pso[:, :], in1=vaP[:, :], op=ALU.add))
        b.bar()
        for par in range(2):
            for h in range(16):
                b.op("vector", lambda v, par=par, h=h: v.tensor_scalar(
                    out=atw[:, par, h * 64:(h + 1) * 64], in0=atw[:, par, h * 64:(h + 1) * 64],
                    scalar1=rzp[:, par, h:h + 1], scalar2=None, op0=ALU.mult))
        b.bar()
        for c in range(NCH):
            for par in range(2):
                b.op("tensor", lambda t, c=c, par=par: t.transpose(
                    [ps_a, ps_b][par][:, c * 64:(c + 1) * 64],
                    atw.ap().rearrange("p a (c e) -> p a c e", c=NCH)[:, par, c, :],
                    idn[0:64, 0:64]))
        b.bar()
        for c in range(NCH):
            b.op("vector", lambda v, c=c: v.tensor_copy(
                out=attn_fm.ap().rearrange("p c (m a) -> p c m a", a=2)[:, c, :, 0],
                in_=ps_a[:, c * 64:(c + 1) * 64]))
            b.op("vector", lambda v, c=c: v.tensor_copy(
                out=attn_fm.ap().rearrange("p c (m a) -> p c m a", a=2)[:, c, :, 1],
                in_=ps_b[:, c * 64:(c + 1) * 64]))
        b.bar()

        b.op("gpsimd", lambda g: g.dma_start(
            out=gv[:], in_=wo_w.rearrange("(c p) o -> p c o", p=128)), dma=True)
        b.bar()
        for og in range(2):
            for ol in range(4):
                b.op("tensor", lambda t, og=og, ol=ol: [t.matmul(
                    ps_sc[:, ol * 128:(ol + 1) * 128],
                    gv[:, ic, (og * 4 + ol) * 128:(og * 4 + ol + 1) * 128], attn_fm[:, ic, :],
                    start=(ic == 0), stop=(ic == NCH - 1)) for ic in range(NCH)][-1])
            b.bar()
            for ol in range(4):
                oc = og * 4 + ol
                b.op("vector", lambda v, oc=oc, ol=ol: v.tensor_scalar(
                    out=x2s[:, oc, :], in0=ps_sc[:, ol * 128:(ol + 1) * 128],
                    scalar1=wobr[:, oc:oc + 1], scalar2=None, op0=ALU.add))
            b.bar()
            for ol in range(4):
                oc = og * 4 + ol
                b.op("vector", lambda v, oc=oc: v.tensor_tensor(
                    out=x2s[:, oc, :], in0=x2s[:, oc, :], in1=x1s[:, oc, :], op=ALU.add))
            b.bar()
        for c in range(NCH):
            b.op("sync", lambda s, c=c, q0=q0: s.dma_start(
                out=o_x2[c * 128:(c + 1) * 128, q0:q0 + 128], in_=x2s[:, c, :]), dma=True)
        b.bar()






def emit_MKN(nc, b, st, t, pfx=""):
    kT_l = t["kT_l"]
    mk_c = t["mk_c"]
    slab = st.enter_context(nc.sbuf_tensor(pfx + "slab", [128, NCH, 512], BF16))
    mkt = st.enter_context(nc.sbuf_tensor(pfx + "mkt", [1, 64], F32))
    m16 = st.enter_context(nc.sbuf_tensor(pfx + "m16", [1, 128], F32))
    onesb = st.enter_context(nc.sbuf_tensor(pfx + "onesb", [128, 1], BF16))
    ps_r = st.enter_context(nc.psum_tensor(pfx + "ps_r", [1, 512], F32))
    b.op("vector", lambda v: v.memset(onesb[:], 1.0))
    b.op("vector", lambda v: v.memset(m16[:], 0.0))
    b.bar()
    for tt in range(4):
        b.op("sync", lambda s, tt=tt: s.dma_start(
            out=slab[:], in_=kT_l.rearrange("(c p) t -> p c t", p=128)[:, :, tt * 512:(tt + 1) * 512]), dma=True)
        b.bar()
        for c in range(NCH):
            b.op("scalar", lambda s, c=c: s.activation(out=slab[:, c, :], in_=slab[:, c, :], func=AF.Square))
        b.bar()
        for h in range(16):
            b.op("tensor", lambda t2, h=h: t2.matmul(
                ps_r[0:1, :], onesb[64 * (h % 2):64 * (h % 2) + 64, 0:1],
                slab[64 * (h % 2):64 * (h % 2) + 64, h // 2, :], start=True, stop=True))
            b.bar()
            b.op("vector", lambda v, h=h, tt=tt: v.tensor_reduce(
                out=mkt[:, h * 4 + tt:h * 4 + tt + 1], in_=ps_r[0:1, :], axis=AX.X, op=ALU.max))
            b.bar()
    for h in range(16):
        b.op("vector", lambda v, h=h: v.tensor_reduce(
            out=m16[:, h:h + 1], in_=mkt[:, h * 4:(h + 1) * 4], axis=AX.X, op=ALU.max))
    b.bar()
    b.op("sync", lambda s: s.dma_start(out=mk_c[:, :], in_=m16[:, :]), dma=True)
    b.bar()



def emit_CATT(nc, b, st, t, pfx=""):
    qT_l = t["qT_l"]
    g_kT = t["g_kT"]
    g_v = t["g_v"]
    g_mk = t["g_mk"]
    cmask2 = t["cmask2"]
    at_fm = t["at_fm"]

    kslab = st.enter_context(nc.sbuf_tensor(pfx + "kslab", [128, 2, T], BF16))
    vaug = st.enter_context(nc.sbuf_tensor(pfx + "vaug", [128, 64, 4, 65], BF16))
    qs = st.enter_context(nc.sbuf_tensor(pfx + "qs", [128, NCH, 512], BF16))
    qsq = st.enter_context(nc.sbuf_tensor(pfx + "qsq", [128, NCH, 512], BF16))
    cmsk = st.enter_context(nc.sbuf_tensor(pfx + "cmsk", [128, 2, 16, 512], BF16))
    pt = st.enter_context(nc.sbuf_tensor(pfx + "pt", [128, 2, 512], BF16))
    m4 = st.enter_context(nc.sbuf_tensor(pfx + "m4", [1, 512], F32))
    mkh = st.enter_context(nc.sbuf_tensor(pfx + "mkh", [1, 16], F32))
    qn = st.enter_context(nc.sbuf_tensor(pfx + "qn", [1, 16], F32))
    brow = st.enter_context(nc.sbuf_tensor(pfx + "brow", [1, 16], F32))
    biasq = st.enter_context(nc.sbuf_tensor(pfx + "biasq", [128, 16], F32))
    rz = st.enter_context(nc.sbuf_tensor(pfx + "rz", [1, 4, 512], F32))
    rzb = st.enter_context(nc.sbuf_tensor(pfx + "rzb", [64, 4, 512], F32))
    atb = st.enter_context(nc.sbuf_tensor(pfx + "atb", [64, 4, 512], BF16))
    onesb = st.enter_context(nc.sbuf_tensor(pfx + "onesb", [128, 1], BF16))
    onesr = st.enter_context(nc.sbuf_tensor(pfx + "onesr", [1, 128], F32))
    ps_s0 = st.enter_context(nc.psum_tensor(pfx + "ps_s0", [128, 512], F32))
    ps_s1 = st.enter_context(nc.psum_tensor(pfx + "ps_s1", [128, 512], F32))
    ps_pv = [st.enter_context(nc.psum_tensor(pfx + f"ps_pv{i}", [128, 512], F32)) for i in range(4)]
    ps_q = st.enter_context(nc.psum_tensor(pfx + "ps_q", [1, 512], F32))
    ps_bq = st.enter_context(nc.psum_tensor(pfx + "ps_bq", [128, 16], F32))
    se_s = st.enter_context(nc.semaphore(pfx + "se_s"))
    se_e = st.enter_context(nc.semaphore(pfx + "se_e"))
    se_m = st.enter_context(nc.semaphore(pfx + "se_m"))
    psl = [ps_s0, ps_s1]
    cnt = {"s": 0, "e": 0, "m": 0}

    b.op("sync", lambda s: s.dma_start(out=cmsk[:], in_=cmask2.rearrange("p (a r q) -> p a r q", a=2, r=16)), dma=True)
    b.op("sync", lambda s: s.dma_start(out=m4[:], in_=g_mk.rearrange("s w -> (s w)")), dma=True)
    b.op("vector", lambda v: v.memset(onesb[:], 1.0))
    b.op("vector", lambda v: v.memset(onesr[:], 1.0))
    b.op("vector", lambda v: v.memset(vaug.ap()[:, :, :, 64:65], 1.0))
    b.bar()
    for h in range(16):
        b.op("vector", lambda v, h=h: v.tensor_reduce(
            out=mkh[:, h:h + 1], in_=m4.ap().rearrange("o (s w) -> o w s", w=128)[:, h, :],
            axis=AX.X, op=ALU.max))
    b.bar()
    b.op("scalar", lambda s: s.activation(out=mkh[:, :], in_=mkh[:, :], func=AF.Sqrt))
    b.bar()
    b.op("vector", lambda v: v.tensor_scalar(out=mkh[:, :], in0=mkh[:, :], scalar1=-1.0, scalar2=None, op0=ALU.mult))
    b.bar()

    for s4 in range(4):
        nkt = 16 * (s4 + 1)
        b.op("sync", lambda s, s4=s4: s.dma_start(
            out=qs[:], in_=qT_l.rearrange("(c p) t -> p c t", p=128)[:, :, s4 * 512:(s4 + 1) * 512]), dma=True)
        b.bar()
        for c in range(NCH):
            b.op("scalar", lambda s, c=c: s.activation(out=qsq[:, c, :], in_=qs[:, c, :], func=AF.Square))
        b.bar()
        for h in range(16):
            b.op("tensor", lambda t2, h=h: t2.matmul(
                ps_q[0:1, :], onesb[64 * (h % 2):64 * (h % 2) + 64, 0:1],
                qsq[64 * (h % 2):64 * (h % 2) + 64, h // 2, :], start=True, stop=True))
            b.bar()
            b.op("vector", lambda v, h=h: v.tensor_reduce(
                out=qn[:, h:h + 1], in_=ps_q[0:1, :], axis=AX.X, op=ALU.max))
            b.bar()
        b.op("scalar", lambda s: s.activation(out=qn[:, :], in_=qn[:, :], func=AF.Sqrt))
        b.bar()
        b.op("vector", lambda v: v.tensor_tensor(out=brow[:, :], in0=qn[:, :], in1=mkh[:, :], op=ALU.mult))
        b.bar()
        b.op("tensor", lambda t2: t2.matmul(ps_bq[:, :], onesr[0:1, :], brow[0:1, :], start=True, stop=True))
        b.bar()
        b.op("vector", lambda v: v.tensor_copy(out=biasq[:, :], in_=ps_bq[:, :]))
        b.bar()

        for g4 in range(4):
            for j2 in range(4):
                for tt in range(s4 + 1):
                    bglob = 4 * tt + (j2 if tt % 2 == 0 else 3 - j2)
                    b.op("sync", lambda s, j2=j2, tt=tt, bglob=bglob, g4=g4: s.dma_start(
                        out=kslab[:, :, bglob * 512:(bglob + 1) * 512],
                        in_=g_kT.rearrange("(s2 c p) t -> p s2 c t", s2=4, p=128)
                        [:, j2, 2 * g4:2 * g4 + 2, tt * 512:(tt + 1) * 512]), dma=True)
                    for n in range(4):
                        b.op("gpsimd", lambda g, j2=j2, tt=tt, bglob=bglob, g4=g4, n=n: g.dma_start(
                            out=vaug[:, bglob * 4 + n, :, 0:64],
                            in_=g_v.rearrange("(r p) (gg hh d) -> p r gg hh d", p=128, gg=4, hh=4)
                            [:, 16 * j2 + 4 * tt + n, g4, :, :]), dma=True)
            b.bar()
            e_base = cnt["e"]
            pv_ops = []
            for kt in range(nkt):
                for hl in range(4):
                    h = 4 * g4 + hl
                    u = kt * 4 + hl
                    wv = e_base + u - 1 if u >= 2 else None

                    def s_fn(t2, kt=kt, hl=hl, h=h, u=u, wv=wv):
                        if wv is not None:
                            t2.wait_ge(se_e, wv)
                        return t2.matmul(psl[u % 2][:, :],
                                         kslab[64 * (hl % 2):64 * (hl % 2) + 64, hl // 2, kt * 128:(kt + 1) * 128],
                                         qs[64 * (h % 2):64 * (h % 2) + 64, h // 2, :],
                                         start=True, stop=True)
                    b.op("tensor", lambda t2, f=s_fn: (f(t2).then_inc(se_s, 1), None)[1])
                    cnt["s"] += 1

                    def e_fn(s, u=u, h=h, sv=cnt["s"]):
                        s.wait_ge(se_s, sv)
                        return s.activation(out=pt[:, u % 2, :], in_=psl[u % 2][:, :], func=AF.Exp,
                                            bias=biasq[:, h:h + 1])
                    b.op("scalar", lambda s, f=e_fn: (f(s).then_inc(se_e, 1), None)[1])
                    cnt["e"] += 1
                    masked = kt >= 16 * s4
                    if masked:
                        r = kt - 16 * s4

                        def m_fn(v, u=u, r=r, s4=s4, ev=cnt["e"]):
                            v.wait_ge(se_e, ev)
                            return v.tensor_tensor(out=pt[:, u % 2, :], in0=pt[:, u % 2, :],
                                                   in1=cmsk[:, s4 % 2, r, :], op=ALU.mult)
                        b.op("vector", lambda v, f=m_fn: (f(v).then_inc(se_m, 1), None)[1])
                        cnt["m"] += 1
                    wv2 = ("m", cnt["m"]) if masked else ("e", cnt["e"])

                    def pv_fn(t2, kt=kt, hl=hl, u=u, nkt=nkt, wv2=wv2):
                        t2.wait_ge(se_m if wv2[0] == "m" else se_e, wv2[1])
                        return t2.matmul(ps_pv[hl][0:65, :], vaug[:, kt, hl, :], pt[:, u % 2, :],
                                         start=(kt == 0), stop=(kt == nkt - 1))
                    pv_ops.append(pv_fn)
                    if len(pv_ops) >= 2:
                        b.op("tensor", lambda t2, f=pv_ops[-2]: f(t2))
            b.op("tensor", lambda t2, f=pv_ops[-1]: f(t2))
            b.bar()
            for hl in range(4):
                b.op("scalar", lambda s, hl=hl: s.activation(out=rz[:, hl, :], in_=ps_pv[hl][64:65, :],
                                                             func=AF.Copy, bias=1e-30))
            b.bar()
            b.op("vector", lambda v: v.reciprocal(out=rz.ap().rearrange("o a q -> o (a q)"),
                                                  in_=rz.ap().rearrange("o a q -> o (a q)")))
            b.bar()
            for hl in range(4):
                dst = psl[hl % 2]
                base = 64 * (hl // 2)
                b.op("tensor", lambda t2, hl=hl, dst=dst, base=base: t2.matmul(
                    dst[base:base + 64, :], onesr[0:1, 0:64], rz[0:1, hl, :], start=True, stop=True))
            b.bar()
            for hl in range(4):
                dst = psl[hl % 2]
                base = 64 * (hl // 2)
                b.op("vector", lambda v, hl=hl, dst=dst, base=base: v.tensor_copy(
                    out=rzb[:, hl, :], in_=dst[base:base + 64, :]))
            b.bar()
            for hl in range(4):
                b.op("vector", lambda v, hl=hl: v.tensor_tensor(
                    out=atb[:, hl, :], in0=ps_pv[hl][0:64, :], in1=rzb[:, hl, :], op=ALU.mult))
            b.bar()
            for hl in range(4):
                h = 4 * g4 + hl
                b.op("sync", lambda s, hl=hl, h=h, s4=s4: s.dma_start(
                    out=at_fm[64 * h:64 * (h + 1), 512 * s4:512 * (s4 + 1)], in_=atb[:, hl, :]), dma=True)
            b.bar()


L4B_DRAMS = [("qT_d", [D, TPC], BF16, "ExternalInput"),
             ("qiT_d", [IDXD, TPC], F32, "ExternalInput"),
             ("g_ki", [4 * IDXD, TPC], F32, "ExternalInput"),
             ("kv_d", [T, 2 * D], BF16, "ExternalInput"),
             ("x1_d", [D, TPC], F32, "ExternalInput"),
             ("wo_w", [D, D], F32, "ExternalInput"),
             ("wo_b", [1, D], F32, "ExternalInput"),
             ("ident", [128, 128], F32, "ExternalInput"),
             ("identb", [128, 128], F32, "ExternalInput"),
             ("cbase", [128, 960], F32, "ExternalInput"),
             ("o_x2", [D, TPC], F32, "ExternalOutput"),
             ("bK", [16, 512], I16, "Internal"),
             ("bV", [16, 512], I16, "Internal"),
             ("vaDR", [2048, 64], F32, "Internal")]

build_L4b = _mk_build(emit_L4b, L4B_DRAMS, "l4b_")


RG4 = [[0, 1, 2, 3], [4, 5, 6, 7]]

MEGA_EXT_IN = ([("x_fm", [D, TPC], F32),
                ("wqT", [D, D], F32), ("wkT", [D, D], F32), ("wvT", [D, D], F32),
                ("n1_w", [D, 1], F32), ("qn_w", [HD, 1], F32), ("kn_w", [HD, 1], F32),
                ("n2_w", [D, 1], F32), ("esel", [4, 512], F32), ("ident", [128, 128], F32),
                ("wq_w", [D, D], F32), ("wk_w", [D, D], F32), ("wv_w", [D, D], F32),
                ("wqi_w", [D, IDXD], F32), ("wki_w", [D, IDXD], F32),
                ("wq_b", [1, D], F32), ("wk_b", [1, D], F32), ("wv_b", [1, D], F32),
                ("wqi_b", [1, IDXD], F32), ("wki_b", [1, IDXD], F32),
                ("wo_w", [D, D], F32), ("wo_b", [1, D], F32),
                ("cbase", [128, 960], F32), ("cmask2", [128, 2 * 16 * 512], BF16)]
               + [(nm, shp, F32) for nm, shp in L3_W_SHAPES])

MEGA_INTERNAL = [("i_lnx", [D, TPC], F32), ("i_qT", [D, TPC], BF16),
                 ("i_kT", [D, TPC], BF16), ("i_v", [TPC, D], BF16),
                 ("i_mkc", [1, 128], F32), ("i_at", [D, TPC], BF16),
                 ("i_x1", [D, TPC], F32), ("i_x2", [D, TPC], F32),
                 ("i_sqT", [D, TPC], BF16), ("i_skv", [TPC, 2 * D], BF16),
                 ("i_sqi", [IDXD, TPC], F32), ("i_ski", [IDXD, TPC], F32),
                 ("g_kT", [4 * D, TPC], BF16), ("g_v", [4 * TPC, D], BF16),
                 ("g_mk", [4, 128], F32), ("g_kv", [T, 2 * D], BF16),
                 ("g_ki", [4 * IDXD, TPC], F32),
                 ("bK", [16, 512], I16), ("bV", [16, 512], I16), ("vaDR", [2048, 64], F32)]


def build_MEGA(dbg=False, upto=3):
    nc = bacc.Bacc("TRN2", target_bir_lowering=False)
    t = {nm: nc.dram_tensor(nm, shp, dt, kind="ExternalInput").ap() for nm, shp, dt in MEGA_EXT_IN}
    t["o_out"] = nc.dram_tensor("o_out", [D, TPC], F32, kind="ExternalOutput").ap()
    dbg_names = {"i_at", "i_x1", "i_x2", "i_lnx"} if dbg else set()
    for nm, shp, dt in MEGA_INTERNAL:
        kind = "ExternalOutput" if nm in dbg_names else "Internal"
        t[nm] = nc.dram_tensor(nm, shp, dt, kind=kind).ap()
    t["identb"] = t["ident"]

    with ExitStack() as st:
        block = st.enter_context(nc.Block())
        sems = {e: st.enter_context(nc.semaphore("s_" + e)) for e in ENGS}
        b = Builder(nc)

        def cc(src, dst):
            b.op("gpsimd", lambda g, src=src, dst=dst: g.collective_compute(
                "AllGather", ALU.bypass, replica_groups=RG4, ins=[src.opt()], outs=[dst.opt()]))

        for half in (0, 1):
            if half == 1 and upto < 3:
                break
            xin = t["x_fm"] if half == 0 else t["i_x2"]
            xout = t["i_x1"] if half == 0 else t["o_out"]
            with ExitStack() as s1:
                emit_L1(nc, b, s1, {**t, "x_fm": xin, "o_lnx": t["i_lnx"], "o_qT": t["i_qT"],
                                    "o_kT": t["i_kT"], "o_v": t["i_v"]}, pfx=f"h{half}l1_")
            b.bar()
            with ExitStack() as s2:
                emit_MKN(nc, b, s2, {"kT_l": t["i_kT"], "mk_c": t["i_mkc"]}, pfx=f"h{half}mk_")
            b.bar()
            cc(t["i_kT"], t["g_kT"])
            b.bar()
            cc(t["i_v"], t["g_v"])
            b.bar()
            cc(t["i_mkc"], t["g_mk"])
            b.bar()
            with ExitStack() as s3:
                emit_CATT(nc, b, s3, {"qT_l": t["i_qT"], "g_kT": t["g_kT"], "g_v": t["g_v"],
                                      "g_mk": t["g_mk"], "cmask2": t["cmask2"],
                                      "at_fm": t["i_at"]}, pfx=f"h{half}ca_")
            b.bar()
            with ExitStack() as s4:
                emit_L3(nc, b, s4, {**t, "at_fm": t["i_at"], "lnx_fm": t["i_lnx"],
                                    "x0_fm": xin, "o_x1": xout}, pfx=f"h{half}l3_")
            b.bar()
            if half == 0 and upto >= 2:
                with ExitStack() as s5:
                    emit_L4a(nc, b, s5, {**t, "x_fm": t["i_x1"], "o_qT": t["i_sqT"],
                                         "o_kv": t["i_skv"], "o_qi": t["i_sqi"],
                                         "o_ki": t["i_ski"]}, pfx="sa_")
                b.bar()
                cc(t["i_skv"], t["g_kv"])
                b.bar()
                cc(t["i_ski"], t["g_ki"])
                b.bar()
                with ExitStack() as s6:
                    emit_L4b(nc, b, s6, {**t, "qT_d": t["i_sqT"], "qiT_d": t["i_sqi"],
                                         "kv_d": t["g_kv"], "x1_d": t["i_x1"],
                                         "o_x2": t["i_x2"]}, pfx="sb_")
                b.bar()
        b.emit(block, sems)
    nc.finalize()
    return nc


def _cmask2_host(j):
    m = np.zeros((128, 2, 16, 512), dtype=BF)
    qv = np.arange(512)[None, :]
    kv = np.arange(128)[:, None]
    for par in range(2):
        jj = j if par == 0 else 3 - j
        for r in range(16):
            if r < 4 * jj:
                m[:, par, r, :] = 1.0
            elif r < 4 * jj + 4:
                rr = r - 4 * jj
                m[:, par, r, :] = (qv >= kv + 128 * rr).astype(BF)
    return np.ascontiguousarray(m.reshape(128, 2 * 16 * 512))


def _mega_in_maps(inputs):
    g = host_prep(inputs)
    perm = _stripe_perm()
    esel = _const_esel()
    cb = _const_cbase()
    x0 = inputs["L0_attn_x"].reshape(NTOK, D)
    shared = {"wqT": g["f_wq"], "wkT": g["f_wk"], "wvT": g["f_wv"],
              "n1_w": g["f_n1_w"], "qn_w": g["f_qn_w"], "kn_w": g["f_kn_w"],
              "n2_w": g["f_n2_w"], "esel": esel, "ident": g["identity"],
              "woT": g["f_wo"], "gateT": g["f_gate_w"],
              "wo_w": g["a_wo_w"], "wo_b": g["a_wo_b"].reshape(1, -1),
              "cbase": cb}
    for nm in ["e0_up", "e0_down", "e1_w1", "e1_w2", "e1_down", "e2_l1",
               "e2_l2", "e2_l3", "e2_l4", "e3_down", "e3_up", "e3_out"]:
        shared[nm + "T"] = g["f_" + nm]
    for nm in ["wq_w", "wk_w", "wv_w", "wqi_w", "wki_w"]:
        shared[nm] = g["a_" + nm]
    for nm in ["wq_b", "wk_b", "wv_b", "wqi_b", "wki_b"]:
        shared[nm] = g["a_" + nm].reshape(1, -1)
    maps = []
    for c in range(NCORES):
        bb, j = c // 4, c % 4
        rows = bb * T + perm[j * TPC:(j + 1) * TPC]
        maps.append({**shared,
                     "x_fm": np.ascontiguousarray(x0[rows].T),
                     "cmask2": _cmask2_host(j)})
    return maps, perm


def _const_cmask():
    cmq = np.zeros((128, 4, 512), dtype=BF)
    for r in range(4):
        for kk in range(128):
            cmq[kk, r, :] = (np.arange(512) >= kk + 128 * r).astype(BF)
    return np.ascontiguousarray(cmq.reshape(128, 4 * 512))


def _const_esel():
    esel = np.zeros((4, 512), dtype=f32)
    for e in range(4):
        esel[e, e * 128:(e + 1) * 128] = 1.0
    return esel


def _const_cbase():
    cb = np.zeros((128, 960), dtype=f32)
    for c in range(120):
        cb[:, c * 8:(c + 1) * 8] = c * 64 + 1
    return cb


def _stripe_perm():
    perm = np.empty(T, dtype=np.int64)
    for j2 in range(4):
        for tt in range(4):
            blk = 4 * tt + (j2 if tt % 2 == 0 else 3 - j2)
            r0 = j2 * TPC + tt * 512
            perm[r0:r0 + 512] = np.arange(blk * 512, (blk + 1) * 512)
    return perm


def kernel(**inputs):
    maps, perm = _mega_in_maps(inputs)
    nc = build_MEGA()
    res = run_bass_kernel_spmd(nc, maps, core_ids=list(range(NCORES))).results
    out = np.empty((NTOK, D), np.float32)
    for c in range(NCORES):
        bb, j = c // 4, c % 4
        rows = bb * T + perm[j * TPC:(j + 1) * TPC]
        out[rows] = res[c]["o_out"].T
    return np.ascontiguousarray(out.reshape(B, T, D))


def kernel_multilaunch(**inputs):
    g = host_prep(inputs)
    cmask = _const_cmask()
    esel = _const_esel()
    cb = _const_cbase()
    nc1 = build_L1()
    nc2 = build_L2()
    nc3 = build_L3()
    nc4a = build_L4a()
    nc4b = build_L4b()

    def run(nc, maps):
        return run_bass_kernel_spmd(nc, maps, core_ids=list(range(NCORES))).results

    l1_w = {"wqT": g["f_wq"], "wkT": g["f_wk"], "wvT": g["f_wv"],
            "n1_w": g["f_n1_w"], "qn_w": g["f_qn_w"], "kn_w": g["f_kn_w"]}
    l3_w = {"n2_w": g["f_n2_w"], "esel": esel, "ident": g["identity"],
            "woT": g["f_wo"], "gateT": g["f_gate_w"]}
    for nm in ["e0_up", "e0_down", "e1_w1", "e1_w2", "e1_down", "e2_l1",
               "e2_l2", "e2_l3", "e2_l4", "e3_down", "e3_up", "e3_out"]:
        l3_w[nm + "T"] = g["f_" + nm]
    l4a_w = {}
    for nm in ["wq_w", "wk_w", "wv_w", "wqi_w", "wki_w"]:
        l4a_w[nm] = g["a_" + nm]
    for nm in ["wq_b", "wk_b", "wv_b", "wqi_b", "wki_b"]:
        l4a_w[nm] = g["a_" + nm].reshape(1, -1)

    def ffn_pass(x_sh):
        r1 = run(nc1, [{"x_fm": x_sh[c], **l1_w} for c in range(NCORES)])
        qT_full = [np.concatenate([r1[bb * 4 + j]["o_qT"] for j in range(4)], axis=1) for bb in range(B)]
        kT_full = [np.concatenate([r1[bb * 4 + j]["o_kT"] for j in range(4)], axis=1) for bb in range(B)]
        v_full = [np.concatenate([r1[bb * 4 + j]["o_v"] for j in range(4)], axis=0) for bb in range(B)]
        m2 = []
        for c in range(NCORES):
            bb = c // 4
            hs = [4 * (c % 4) + i for i in range(4)]
            m2.append({
                "qT_p": np.stack([qT_full[bb][h * HD:(h + 1) * HD] for h in hs]),
                "kT_p": np.stack([kT_full[bb][h * HD:(h + 1) * HD] for h in hs]),
                "v_p": np.stack([np.ascontiguousarray(v_full[bb][:, h * HD:(h + 1) * HD]) for h in hs]),
                "cmask": cmask})
        r2 = run(nc2, m2)
        m3 = []
        for c in range(NCORES):
            bb = c // 4
            t0 = (c * TPC) % T
            at = np.empty((D, TPC), dtype=BF)
            for h in range(NH):
                at[h * HD:(h + 1) * HD] = r2[bb * 4 + h // 4]["o_at"][h % 4][t0:t0 + TPC].T
            m3.append({"at_fm": at, "lnx_fm": r1[c]["o_lnx"], "x0_fm": x_sh[c], **l3_w})
        r3 = run(nc3, m3)
        return [r3[c]["o_x1"] for c in range(NCORES)]

    x0 = inputs["L0_attn_x"].reshape(NTOK, D)
    x0_sh = [np.ascontiguousarray(x0[c * TPC:(c + 1) * TPC].T) for c in range(NCORES)]
    x1_sh = ffn_pass(x0_sh)

    r4a = run(nc4a, [{"x_fm": x1_sh[c], **l4a_w} for c in range(NCORES)])
    perm = _stripe_perm()
    kv_full, ki_full = [], []
    for bb in range(B):
        kv = np.concatenate([r4a[bb * 4 + j]["o_kv"] for j in range(4)], axis=0)
        kv_full.append(np.ascontiguousarray(kv[perm]))
        ki = np.concatenate([r4a[bb * 4 + j]["o_ki"] for j in range(4)], axis=1)
        ki_full.append(np.ascontiguousarray(np.concatenate(
            [ki[:, perm[TPC * j2:TPC * (j2 + 1)]] for j2 in range(4)], axis=0)))
    m4b = []
    for c in range(NCORES):
        bb = c // 4
        m4b.append({"qT_d": r4a[c]["o_qT"], "qiT_d": r4a[c]["o_qi"], "g_ki": ki_full[bb],
                    "kv_d": kv_full[bb], "x1_d": x1_sh[c],
                    "wo_w": g["a_wo_w"], "wo_b": g["a_wo_b"].reshape(1, -1),
                    "ident": g["identity"], "identb": g["identity"], "cbase": cb})
    r4b = run(nc4b, m4b)
    x2_sh = [np.ascontiguousarray(r4b[c]["o_x2"]) for c in range(NCORES)]

    x3_sh = ffn_pass(x2_sh)
    out = np.concatenate([s.T for s in x3_sh], axis=0).reshape(B, T, D)
    return np.ascontiguousarray(out.astype(np.float32))



# version 08
# speedup vs baseline: 1.1856x; 1.0098x over previous
import numpy as np
import ml_dtypes
from contextlib import ExitStack
import concourse.bass as bass
import concourse.bacc as bacc
import concourse.mybir as mybir
from concourse.bass_utils import run_bass_kernel_spmd
from concourse.library_config import mlp as mlp_lib

F32 = mybir.dt.float32
BF16 = mybir.dt.bfloat16
I16 = mybir.dt.int16
U32 = mybir.dt.uint32
AF = mybir.ActivationFunctionType
ALU = mybir.AluOpType
AX = mybir.AxisListType
f32 = np.float32
BF = ml_dtypes.bfloat16

NCORES = 8
D, NH, HD = 1024, 16, 64
B, T = 2, 8192
NTOK = B * T
TPC = NTOK // NCORES
ZM = 256
HID = 2048
EPS_RMS = 1.1920929e-07
EPS_LN = 1e-5
WM, TOPK, IDXD = 512, 64, 64
AL = T - WM
NCH = D // 128

ENGS = ("sync", "gpsimd", "tensor", "vector", "scalar")


class Builder:

    def __init__(self, nc):
        self.nc = nc
        self.phases = [[]]

    def op(self, eng, fn, dma=False):
        self.phases[-1].append((eng, fn, dma))

    def bar(self):
        if self.phases[-1]:
            self.phases.append([])

    def emit(self, block, sems):
        nph = len(self.phases)
        totals = {e: [0] * (nph + 1) for e in ENGS}
        for p, ph in enumerate(self.phases):
            add = {e: 0 for e in ENGS}
            for eng, fn, dma in ph:
                if dma:
                    add[eng] += 16
            has_compute = {e: any((eng == e and not dma) for eng, fn, dma in ph) for e in ENGS}
            for e in ENGS:
                totals[e][p + 1] = totals[e][p] + add[e] + (1 if has_compute[e] else 0)

        def mk(eng_name):
            def emitter(eng):
                waited = {e: 0 for e in ENGS}
                for p, ph in enumerate(self.phases):
                    mine = [(fn, dma) for e, fn, dma in ph if e == eng_name]
                    rest_later = any(
                        e == eng_name for pp in self.phases[p:] for e, _, _ in pp
                    )
                    if not mine and not rest_later:
                        break
                    if p > 0 and mine:
                        for e2 in ENGS:
                            tv = totals[e2][p]
                            if tv > waited[e2]:
                                eng.wait_ge(sems[e2], tv)
                                waited[e2] = tv
                    last_compute = None
                    for i, (fn, dma) in enumerate(mine):
                        if not dma:
                            last_compute = i
                    for i, (fn, dma) in enumerate(mine):
                        ins = fn(eng)
                        if dma:
                            ins.then_inc(sems[eng_name], 16)
                        elif i == last_compute:
                            if ins is None:
                                eng.nop().then_inc(sems[eng_name], 1)
                            else:
                                ins.then_inc(sems[eng_name], 1)
                for e2 in ENGS:
                    tv = totals[e2][nph]
                    if tv > waited[e2]:
                        eng.wait_ge(sems[e2], tv)
            return emitter

        block.sync(mk("sync"))
        block.gpsimd(mk("gpsimd"))
        block.tensor(mk("tensor"))
        block.vector(mk("vector"))
        block.scalar(mk("scalar"))


def host_prep(inputs):
    g = {}
    for name in ["wq", "wk", "wv", "wo", "gate_w", "e0_up", "e0_down", "e1_w1",
                 "e1_w2", "e1_down", "e2_l1", "e2_l2", "e2_l3", "e2_l4",
                 "e3_down", "e3_up", "e3_out"]:
        g["f_" + name] = np.ascontiguousarray(inputs["L0_ffn_" + name].T)
    for name in ["n1_w", "n2_w", "qn_w", "kn_w"]:
        g["f_" + name] = np.ascontiguousarray(inputs["L0_ffn_" + name].reshape(-1, 1))
    for name in ["wq_w", "wk_w", "wv_w", "wo_w", "wqi_w", "wki_w"]:
        g["a_" + name] = np.ascontiguousarray(inputs["L0_attn_" + name])
    for name in ["wq_b", "wk_b", "wv_b", "wo_b", "wqi_b", "wki_b"]:
        g["a_" + name] = np.ascontiguousarray(inputs["L0_attn_" + name].reshape(-1, 1))
    g["identity"] = np.eye(128, dtype=f32)
    return g



def emit_L1(nc, b, st, t, pfx=""):
    x_fm = t["x_fm"]
    wqT = t["wqT"]
    wkT = t["wkT"]
    wvT = t["wvT"]
    n1_w = t["n1_w"]
    qn_w = t["qn_w"]
    kn_w = t["kn_w"]
    o_lnx = t["o_lnx"]
    o_qT = t["o_qT"]
    o_kT = t["o_kT"]
    o_v = t["o_v"]

    NB = TPC // 512

    xs = st.enter_context(nc.sbuf_tensor(pfx + "xs", [128, NCH, 512], F32))
    xcast = st.enter_context(nc.sbuf_tensor(pfx + "xcast", [128, NCH, 512], BF16))
    hb = st.enter_context(nc.sbuf_tensor(pfx + "hb", [128, NCH, 512], BF16))
    lnxt = st.enter_context(nc.sbuf_tensor(pfx + "lnxt", [128, NCH, 512], F32))
    tmpc = st.enter_context(nc.sbuf_tensor(pfx + "tmpc", [128, NCH, 512], F32))
    wq_s = st.enter_context(nc.sbuf_tensor(pfx + "wq_s", [128, NCH, D], BF16))
    wk_s = st.enter_context(nc.sbuf_tensor(pfx + "wk_s", [128, NCH, D], BF16))
    wv_s = st.enter_context(nc.sbuf_tensor(pfx + "wv_s", [128, NCH, D], BF16))
    onesb = st.enter_context(nc.sbuf_tensor(pfx + "onesb", [128, 1], BF16))
    onesf = st.enter_context(nc.sbuf_tensor(pfx + "onesf", [128, 1], F32))
    onesr = st.enter_context(nc.sbuf_tensor(pfx + "onesr", [1, 128], F32))
    n1b = st.enter_context(nc.sbuf_tensor(pfx + "n1b", [128, NCH], F32))
    qn2 = st.enter_context(nc.sbuf_tensor(pfx + "qn2", [128, 1], F32))
    kn2 = st.enter_context(nc.sbuf_tensor(pfx + "kn2", [128, 1], F32))
    rows = st.enter_context(nc.sbuf_tensor(pfx + "rows", [1, 16 * 512], F32))
    rowsb = st.enter_context(nc.sbuf_tensor(pfx + "rowsb", [1, 4 * 512], F32))
    bcs = st.enter_context(nc.sbuf_tensor(pfx + "bcs", [128, 3, 512], F32))
    sq = st.enter_context(nc.sbuf_tensor(pfx + "sq", [128, 512], BF16))
    qtmp = st.enter_context(nc.sbuf_tensor(pfx + "qtmp", [128, 512], F32))
    bchh = st.enter_context(nc.sbuf_tensor(pfx + "bchh", [128, 512], F32))
    stage = st.enter_context(nc.sbuf_tensor(pfx + "stage", [128, 2, 512], BF16))
    qko = st.enter_context(nc.sbuf_tensor(pfx + "qko", [128, 512], BF16))
    ps_mm = st.enter_context(nc.psum_tensor(pfx + "ps_mm", [128, 512], F32))
    ps_mm2 = st.enter_context(nc.psum_tensor(pfx + "ps_mm2", [128, 512], F32))
    ps_r0 = st.enter_context(nc.psum_tensor(pfx + "ps_r0", [1, 512], F32))
    ps_r1 = st.enter_context(nc.psum_tensor(pfx + "ps_r1", [1, 512], F32))
    ps_r2 = st.enter_context(nc.psum_tensor(pfx + "ps_r2", [1, 512], F32))
    ps_r3 = st.enter_context(nc.psum_tensor(pfx + "ps_r3", [1, 512], F32))
    ps_bc = st.enter_context(nc.psum_tensor(pfx + "ps_bc", [128, 512], F32))
    ps_bch = st.enter_context(nc.psum_tensor(pfx + "ps_bch", [128, 512], F32))

    b.op("gpsimd", lambda g: g.dma_start(out=wq_s[:], in_=wqT.rearrange("(c p) o -> p c o", p=128)), dma=True)
    b.op("gpsimd", lambda g: g.dma_start(out=wk_s[:], in_=wkT.rearrange("(c p) o -> p c o", p=128)), dma=True)
    b.op("gpsimd", lambda g: g.dma_start(out=wv_s[:], in_=wvT.rearrange("(c p) o -> p c o", p=128)), dma=True)
    for c in range(NCH):
        b.op("sync", lambda s, c=c: s.dma_start(out=n1b[:, c:c + 1], in_=n1_w[c * 128:(c + 1) * 128, :]), dma=True)
    b.op("sync", lambda s: s.dma_start(out=qn2[0:64, :], in_=qn_w[:, :]), dma=True)
    b.op("sync", lambda s: s.dma_start(out=qn2[64:128, :], in_=qn_w[:, :]), dma=True)
    b.op("sync", lambda s: s.dma_start(out=kn2[0:64, :], in_=kn_w[:, :]), dma=True)
    b.op("sync", lambda s: s.dma_start(out=kn2[64:128, :], in_=kn_w[:, :]), dma=True)
    b.op("vector", lambda v: v.memset(onesb[:], 1.0))
    b.op("vector", lambda v: v.memset(onesf[:], 1.0))
    b.op("vector", lambda v: v.memset(onesr[:], 1.0))
    b.bar()

    for blk in range(NB):
        t0 = blk * 512
        b.op("sync", lambda s, t0=t0: s.dma_start(
            out=xs[:], in_=x_fm.rearrange("(c p) t -> p c t", p=128)[:, :, t0:t0 + 512]), dma=True)
        b.op("gpsimd", lambda g, t0=t0: g.dma_start(
            out=xcast[:], in_=x_fm.rearrange("(c p) t -> p c t", p=128)[:, :, t0:t0 + 512]), dma=True)
        b.bar()
        for c in range(NCH):
            b.op("scalar", lambda s, c=c: s.activation(out=hb[:, c, :], in_=xs[:, c, :], func=AF.Square))
        b.bar()
        for c in range(NCH):
            b.op("tensor", lambda t, c=c: t.matmul(ps_r0[0:1, :], onesb[:, 0:1], xcast[:, c, :],
                                                   start=(c == 0), stop=(c == NCH - 1)))
        for c in range(NCH):
            b.op("tensor", lambda t, c=c: t.matmul(ps_r1[0:1, :], onesb[:, 0:1], hb[:, c, :],
                                                   start=(c == 0), stop=(c == NCH - 1)))
        b.bar()
        b.op("scalar", lambda s: s.activation(out=rows[:, 0:512], in_=ps_r0[0:1, :],
                                              func=AF.Copy, scale=f32(1.0 / D)))
        b.op("scalar", lambda s: s.activation(out=rows[:, 512:1024], in_=ps_r1[0:1, :],
                                              func=AF.Copy, scale=f32(1.0 / D)))
        b.bar()
        b.op("vector", lambda v: v.tensor_tensor(out=rows[:, 1024:1536], in0=rows[:, 0:512],
                                                 in1=rows[:, 0:512], op=ALU.mult))
        b.bar()
        b.op("vector", lambda v: v.tensor_tensor(out=rows[:, 512:1024], in0=rows[:, 512:1024],
                                                 in1=rows[:, 1024:1536], op=ALU.subtract))
        b.bar()
        b.op("scalar", lambda s: s.activation(out=rows[:, 1024:1536], in_=rows[:, 512:1024],
                                              func=AF.Sqrt, bias=f32(EPS_LN)))
        b.bar()
        b.op("vector", lambda v: v.reciprocal(out=rows[:, 1024:1536], in_=rows[:, 1024:1536]))
        b.bar()
        b.op("vector", lambda v: v.tensor_tensor(out=rows[:, 1536:2048], in0=rows[:, 512:1024],
                                                 in1=rows[:, 1024:1536], op=ALU.mult))
        b.bar()
        b.op("vector", lambda v: v.tensor_tensor(out=rows[:, 1536:2048], in0=rows[:, 1536:2048],
                                                 in1=rows[:, 1024:1536], op=ALU.mult))
        b.bar()
        b.op("scalar", lambda s: s.activation(out=rowsb[:, 0:512], in_=rows[:, 1536:2048],
                                              func=AF.Sqrt, bias=f32(EPS_RMS)))
        b.bar()
        b.op("vector", lambda v: v.reciprocal(out=rowsb[:, 0:512], in_=rowsb[:, 0:512]))
        b.bar()
        b.op("vector", lambda v: v.tensor_tensor(out=rows[:, 1536:2048], in0=rowsb[:, 0:512],
                                                 in1=rows[:, 1024:1536], op=ALU.mult))
        b.bar()
        for j, off in enumerate([0, 1024, 1536]):
            b.op("tensor", lambda t, off=off: t.matmul(ps_bc[:, :], onesr[0:1, :], rows[0:1, off:off + 512],
                                                       start=True, stop=True))
            b.bar()
            b.op("vector", lambda v, j=j: v.tensor_copy(out=bcs[:, j, :], in_=ps_bc[:, :]))
            b.bar()
        for c in range(NCH):
            b.op("vector", lambda v, c=c: v.tensor_tensor(out=tmpc[:, c, :], in0=xs[:, c, :],
                                                          in1=bcs[:, 0, :], op=ALU.subtract))
        b.bar()
        for c in range(NCH):
            b.op("vector", lambda v, c=c: v.tensor_tensor(out=lnxt[:, c, :], in0=tmpc[:, c, :],
                                                          in1=bcs[:, 1, :], op=ALU.mult))
        for c in range(NCH):
            b.op("vector", lambda v, c=c: v.tensor_tensor(out=tmpc[:, c, :], in0=tmpc[:, c, :],
                                                          in1=bcs[:, 2, :], op=ALU.mult))
        b.bar()
        for c in range(NCH):
            b.op("sync", lambda s, c=c, t0=t0: s.dma_start(
                out=o_lnx[c * 128:(c + 1) * 128, t0:t0 + 512], in_=lnxt[:, c, :]), dma=True)
            b.op("vector", lambda v, c=c: v.tensor_scalar(
                out=hb[:, c, :], in0=tmpc[:, c, :], scalar1=n1b[:, c:c + 1], scalar2=None, op0=ALU.mult))
        b.bar()
        for which, w_s, nrm, outT, extra in (("q", wq_s, qn2, o_qT, 0.125),
                                            ("k", wk_s, kn2, o_kT, None)):
            for oc in range(NCH):
                b.op("tensor", lambda t, w_s=w_s, oc=oc: [
                    t.matmul(ps_mm[:, :], w_s[:, ic, oc * 128:(oc + 1) * 128], hb[:, ic, :],
                             start=(ic == 0), stop=(ic == NCH - 1))
                    for ic in range(NCH)][-1])
                b.bar()
                b.op("scalar", lambda s: s.activation(out=sq[:], in_=ps_mm[:, :], func=AF.Square))
                b.bar()
                b.op("tensor", lambda t: t.matmul(ps_r2[0:1, :], onesb[0:64, 0:1], sq[0:64, :],
                                                  start=True, stop=True))
                b.op("tensor", lambda t: t.matmul(ps_r3[0:1, :], onesb[64:128, 0:1], sq[64:128, :],
                                                  start=True, stop=True))
                b.bar()
                b.op("scalar", lambda s: s.activation(out=rowsb[:, 512:1024], in_=ps_r2[0:1, :],
                                                      func=AF.Sqrt, scale=f32(1.0 / HD), bias=f32(EPS_RMS)))
                b.op("scalar", lambda s: s.activation(out=rowsb[:, 1024:1536], in_=ps_r3[0:1, :],
                                                      func=AF.Sqrt, scale=f32(1.0 / HD), bias=f32(EPS_RMS)))
                b.bar()
                b.op("vector", lambda v: v.reciprocal(out=rowsb[:, 512:1536], in_=rowsb[:, 512:1536]))
                b.bar()
                b.op("tensor", lambda t: t.matmul(ps_bch[0:64, :], onesr[0:1, 0:64], rowsb[0:1, 512:1024],
                                                  start=True, stop=True))
                b.op("tensor", lambda t: t.matmul(ps_bch[64:128, :], onesr[0:1, 0:64], rowsb[0:1, 1024:1536],
                                                  start=True, stop=True))
                b.bar()
                b.op("vector", lambda v: v.tensor_copy(out=bchh[:, :], in_=ps_bch[:, :]))
                b.bar()
                b.op("vector", lambda v: v.tensor_tensor(out=qtmp[:, :], in0=ps_mm[:, :],
                                                         in1=bchh[:, :], op=ALU.mult))
                b.bar()
                if extra is not None:
                    b.op("vector", lambda v, nrm=nrm, extra=extra: v.tensor_scalar(
                        out=qko[:, :], in0=qtmp[:, :], scalar1=nrm[:, 0:1], scalar2=extra,
                        op0=ALU.mult, op1=ALU.mult))
                else:
                    b.op("vector", lambda v, nrm=nrm: v.tensor_scalar(
                        out=qko[:, :], in0=qtmp[:, :], scalar1=nrm[:, 0:1], scalar2=None, op0=ALU.mult))
                b.bar()
                b.op("sync", lambda s, outT=outT, oc=oc, t0=t0: s.dma_start(
                    out=outT[oc * 128:(oc + 1) * 128, t0:t0 + 512], in_=qko[:, :]), dma=True)
                b.bar()
        for tt in range(4):
            for ob in range(2):
                b.op("tensor", lambda t, tt=tt, ob=ob: [
                    t.matmul(ps_mm2[:, :], hb[:, ic, tt * 128:(tt + 1) * 128],
                             wv_s[:, ic, ob * 512:(ob + 1) * 512],
                             start=(ic == 0), stop=(ic == NCH - 1))
                    for ic in range(NCH)][-1])
                b.bar()
                b.op("scalar", lambda s, ob=ob: s.activation(out=stage[:, ob, :], in_=ps_mm2[:, :], func=AF.Copy))
                b.bar()
            b.op("sync", lambda s, tt=tt, t0=t0: s.dma_start(
                out=o_v[t0 + tt * 128: t0 + (tt + 1) * 128, :], in_=stage.ap().rearrange("p a t -> p (a t)")), dma=True)
            b.bar()







def build_L2():
    nc = bacc.Bacc("TRN2", target_bir_lowering=False)
    qT_p = nc.dram_tensor("qT_p", [4, HD, T], BF16, kind="ExternalInput").ap()
    kT_p = nc.dram_tensor("kT_p", [4, HD, T], BF16, kind="ExternalInput").ap()
    v_p = nc.dram_tensor("v_p", [4, T, HD], BF16, kind="ExternalInput").ap()
    cmask = nc.dram_tensor("cmask", [128, 4 * 512], BF16, kind="ExternalInput").ap()
    o_at = nc.dram_tensor("o_at", [4, T, HD], BF16, kind="ExternalOutput").ap()

    NQB = T // 512

    with ExitStack() as st:
        qTs = st.enter_context(nc.sbuf_tensor("qTs", [64, 4, T], BF16))
        kTs = st.enter_context(nc.sbuf_tensor("kTs", [64, 4, T], BF16))
        vaug = st.enter_context(nc.sbuf_tensor("vaug", [128, 4, T // 128, 65], BF16))
        cm = st.enter_context(nc.sbuf_tensor("cm", [128, 4, 512], BF16))
        onesb = st.enter_context(nc.sbuf_tensor("onesb", [128, 1], BF16))
        onesr = st.enter_context(nc.sbuf_tensor("onesr", [1, 128], F32))
        ksq = st.enter_context(nc.sbuf_tensor("ksq", [64, 512], BF16))
        rows = st.enter_context(nc.sbuf_tensor("rows", [1, T], F32))
        rows2 = st.enter_context(nc.sbuf_tensor("rows2", [1, 64], F32))
        mk1 = st.enter_context(nc.sbuf_tensor("mk1", [1, 1], F32))
        biasq = st.enter_context(nc.sbuf_tensor("biasq", [128, NQB], F32))
        pt = st.enter_context(nc.sbuf_tensor("pt", [128, 2, 512], BF16))
        rz = st.enter_context(nc.sbuf_tensor("rz", [128, 1], F32))
        ot = st.enter_context(nc.sbuf_tensor("ot", [128, 64], BF16))
        ps_s0 = st.enter_context(nc.psum_tensor("ps_s0", [128, 512], F32))
        ps_s1 = st.enter_context(nc.psum_tensor("ps_s1", [128, 512], F32))
        ps_o = [st.enter_context(nc.psum_tensor(f"ps_o{i}", [128, 65], F32)) for i in range(4)]
        ps_r = st.enter_context(nc.psum_tensor("ps_r", [1, 512], F32))
        ps_b = st.enter_context(nc.psum_tensor("ps_b", [128, NQB], F32))
        se_s = st.enter_context(nc.semaphore("se_s"))
        se_e = st.enter_context(nc.semaphore("se_e"))
        se_m = st.enter_context(nc.semaphore("se_m"))
        s_sync = st.enter_context(nc.semaphore("s_sync"))
        s_gpsimd = st.enter_context(nc.semaphore("s_gpsimd"))
        s_tensor = st.enter_context(nc.semaphore("s_tensor"))
        s_vector = st.enter_context(nc.semaphore("s_vector"))
        s_scalar = st.enter_context(nc.semaphore("s_scalar"))
        block = st.enter_context(nc.Block())
        sems = {"sync": s_sync, "gpsimd": s_gpsimd, "tensor": s_tensor,
                "vector": s_vector, "scalar": s_scalar}
        b = Builder(nc)
        ps_s = [ps_s0, ps_s1]

        b.op("sync", lambda s: s.dma_start(out=qTs[:], in_=qT_p.rearrange("a d t -> d a t")), dma=True)
        b.op("sync", lambda s: s.dma_start(out=kTs[:], in_=kT_p.rearrange("a d t -> d a t")), dma=True)
        b.op("sync", lambda s: s.dma_start(
            out=vaug[:, :, :, 0:64], in_=v_p.rearrange("a (n p) d -> p a n d", p=128)), dma=True)
        b.op("sync", lambda s: s.dma_start(out=cm[:], in_=cmask.rearrange("p (r q) -> p r q", r=4)), dma=True)
        b.op("vector", lambda v: v.memset(vaug[:, :, :, 64:65], 1.0))
        b.op("vector", lambda v: v.memset(onesb[:], 1.0))
        b.op("vector", lambda v: v.memset(onesr[:], 1.0))
        b.bar()

        cnt = {"s": 0, "e": 0, "m": 0}

        for pair in range(4):
            for c in range(16):
                b.op("scalar", lambda s, pair=pair, c=c: s.activation(
                    out=ksq[:, :], in_=kTs[:, pair, c * 512:(c + 1) * 512], func=AF.Square))
                b.bar()
                b.op("tensor", lambda t, c=c: t.matmul(ps_r[0:1, :], onesb[0:64, 0:1], ksq[:, :],
                                                       start=True, stop=True))
                b.bar()
                b.op("vector", lambda v, c=c: v.tensor_copy(out=rows[:, c * 512:(c + 1) * 512], in_=ps_r[0:1, :]))
                b.bar()
            b.op("vector", lambda v: v.tensor_reduce(out=mk1[:, :], in_=rows[:, :], axis=AX.X, op=ALU.max))
            b.bar()
            b.op("scalar", lambda s: s.activation(out=mk1[:, :], in_=mk1[:, :], func=AF.Sqrt))
            b.bar()
            for c in range(16):
                b.op("scalar", lambda s, pair=pair, c=c: s.activation(
                    out=ksq[:, :], in_=qTs[:, pair, c * 512:(c + 1) * 512], func=AF.Square))
                b.bar()
                b.op("tensor", lambda t: t.matmul(ps_r[0:1, :], onesb[0:64, 0:1], ksq[:, :],
                                                  start=True, stop=True))
                b.bar()
                b.op("vector", lambda v, c=c: v.tensor_copy(out=rows[:, c * 512:(c + 1) * 512], in_=ps_r[0:1, :]))
                b.bar()
            for qb in range(NQB):
                b.op("vector", lambda v, qb=qb: v.tensor_reduce(
                    out=rows2[:, qb:qb + 1], in_=rows[:, qb * 512:(qb + 1) * 512], axis=AX.X, op=ALU.max))
            b.bar()
            b.op("scalar", lambda s: s.activation(out=rows2[:, 0:NQB], in_=rows2[:, 0:NQB], func=AF.Sqrt))
            b.bar()
            b.op("vector", lambda v: v.tensor_scalar(
                out=rows2[:, 0:NQB], in0=rows2[:, 0:NQB], scalar1=mk1[0:1, 0:1], scalar2=-1.0,
                op0=ALU.mult, op1=ALU.mult))
            b.bar()
            b.op("tensor", lambda t: t.matmul(ps_b[:, :], onesr[0:1, :], rows2[0:1, 0:NQB],
                                              start=True, stop=True))
            b.bar()
            b.op("vector", lambda v: v.tensor_copy(out=biasq[:, :], in_=ps_b[:, :]))
            b.bar()

            for qb in range(NQB):
                nkt = 4 * qb + 4
                e_base = cnt["e"]

                def emit_qblock(pair=pair, qb=qb, nkt=nkt, e_base=e_base, cnt=cnt):
                    pass

                s_waits = []
                for kt in range(nkt):
                    need_e = e_base + kt - 1
                    wait_e = need_e if kt >= 2 else None
                    diag = kt >= 4 * qb
                    r = kt - 4 * qb
                    def s_fn(t, pair=pair, qb=qb, kt=kt, wait_e=wait_e):
                        if wait_e is not None:
                            t.wait_ge(se_e, wait_e)
                        return t.matmul(ps_s[kt % 2][:, :], kTs[:, pair, kt * 128:(kt + 1) * 128],
                                        qTs[:, pair, qb * 512:(qb + 1) * 512], start=True, stop=True)
                    b.op("tensor", lambda t, f=s_fn: (f(t).then_inc(se_s, 1), None)[1])
                    cnt["s"] += 1
                    def e_fn(s, pair=pair, qb=qb, kt=kt, sv=cnt["s"]):
                        s.wait_ge(se_s, sv)
                        return s.activation(out=pt[:, kt % 2, :], in_=ps_s[kt % 2][:, :], func=AF.Exp,
                                            bias=biasq[:, qb:qb + 1])
                    b.op("scalar", lambda s, f=e_fn: (f(s).then_inc(se_e, 1), None)[1])
                    cnt["e"] += 1
                    if diag:
                        def m_fn(v, kt=kt, r=r, ev=cnt["e"]):
                            v.wait_ge(se_e, ev)
                            return v.tensor_tensor(out=pt[:, kt % 2, :], in0=pt[:, kt % 2, :],
                                                   in1=cm[:, r, :], op=ALU.mult)
                        b.op("vector", lambda v, f=m_fn: (f(v).then_inc(se_m, 1), None)[1])
                        cnt["m"] += 1
                    wv = ("m", cnt["m"]) if diag else ("e", cnt["e"])
                    for qs in range(4):
                        def pv_fn(t, pair=pair, kt=kt, qs=qs, nkt=nkt, wv=wv):
                            if qs == 0:
                                t.wait_ge(se_m if wv[0] == "m" else se_e, wv[1])
                            return t.matmul(ps_o[qs][:, :], pt[:, kt % 2, qs * 128:(qs + 1) * 128],
                                            vaug[:, pair, kt, :], start=(kt == 0), stop=(kt == nkt - 1))
                        b.op("tensor", lambda t, f=pv_fn: f(t))
                b.bar()
                for qs in range(4):
                    b.op("scalar", lambda s, qs=qs: s.activation(out=rz[:, :], in_=ps_o[qs][:, 64:65],
                                                                 func=AF.Copy, bias=1e-30))
                    b.bar()
                    b.op("vector", lambda v: v.reciprocal(out=rz[:, :], in_=rz[:, :]))
                    b.bar()
                    b.op("vector", lambda v, qs=qs: v.tensor_scalar(
                        out=ot[:, :], in0=ps_o[qs][:, 0:64], scalar1=rz[:, 0:1], scalar2=None, op0=ALU.mult))
                    b.bar()
                    b.op("sync", lambda s, pair=pair, qb=qb, qs=qs: s.dma_start(
                        out=o_at[pair, qb * 512 + qs * 128: qb * 512 + (qs + 1) * 128, :], in_=ot[:, :]), dma=True)
                    b.bar()

        b.emit(block, sems)
    nc.finalize()
    return nc



def emit_L3(nc, b, st, t, pfx=""):
    at_fm = t["at_fm"]
    lnx_fm = t["lnx_fm"]
    x0_fm = t["x0_fm"]
    n2_w = t["n2_w"]
    esel = t["esel"]
    ident = t["ident"]
    o_x1 = t["o_x1"]
    wdr = {}
    for nm, shp in [("woT", [D, D]), ("gateT", [D, 4]),
                    ("e0_upT", [D, HID]), ("e0_downT", [HID, D]),
                    ("e1_w1T", [D, HID]), ("e1_w2T", [D, HID]), ("e1_downT", [HID, D]),
                    ("e2_l1T", [D, HID]), ("e2_l2T", [HID, HID]), ("e2_l3T", [HID, HID]),
                    ("e2_l4T", [HID, D]),
                    ("e3_downT", [D, D]), ("e3_upT", [D, HID]), ("e3_outT", [HID, D])]:
        wdr[nm] = t[nm]

    SB = 512
    NSB = TPC // SB
    NTT = SB // 128
    NNB = SB // 512

    xp = st.enter_context(nc.sbuf_tensor(pfx + "xp", [128, NCH, SB], F32))
    lnxs = st.enter_context(nc.sbuf_tensor(pfx + "lnxs", [128, NCH, SB], F32))
    ats = st.enter_context(nc.sbuf_tensor(pfx + "ats", [128, NCH, SB], BF16))
    xf = st.enter_context(nc.sbuf_tensor(pfx + "xf", [128, NCH, SB], BF16))
    ga = st.enter_context(nc.sbuf_tensor(pfx + "ga", [128, 16, SB], BF16))
    gb = st.enter_context(nc.sbuf_tensor(pfx + "gb", [128, 16, SB], BF16))
    wbig = st.enter_context(nc.sbuf_tensor(pfx + "wbig", [128, 16, 1024], BF16))
    wgate = st.enter_context(nc.sbuf_tensor(pfx + "wgate", [128, NCH, 4], BF16))
    scr = st.enter_context(nc.sbuf_tensor(pfx + "scr", [128, 512], F32))
    webc = st.enter_context(nc.sbuf_tensor(pfx + "webc", [128, 4, SB], F32))
    wrow = st.enter_context(nc.sbuf_tensor(pfx + "wrow", [4, SB], F32))
    esl = st.enter_context(nc.sbuf_tensor(pfx + "esl", [4, 512], F32))
    idn = st.enter_context(nc.sbuf_tensor(pfx + "idn", [128, 128], F32))
    n2b = st.enter_context(nc.sbuf_tensor(pfx + "n2b", [128, NCH], F32))
    onesb = st.enter_context(nc.sbuf_tensor(pfx + "onesb", [128, 1], BF16))
    onesr = st.enter_context(nc.sbuf_tensor(pfx + "onesr", [1, 128], F32))
    rows = st.enter_context(nc.sbuf_tensor(pfx + "rows", [1, SB], F32))
    bcn = st.enter_context(nc.sbuf_tensor(pfx + "bcn", [128, SB], F32))
    lg = st.enter_context(nc.sbuf_tensor(pfx + "lg", [128, NTT, 4], F32))
    gs = st.enter_context(nc.sbuf_tensor(pfx + "gs", [128, 8], F32))
    ps_a = st.enter_context(nc.psum_tensor(pfx + "ps_a", [128, 512], F32))
    ps_b = st.enter_context(nc.psum_tensor(pfx + "ps_b", [128, 512], F32))
    ps_r = st.enter_context(nc.psum_tensor(pfx + "ps_r", [1, 512], F32))
    ps_bc = st.enter_context(nc.psum_tensor(pfx + "ps_bc", [128, 512], F32))
    ps_g = st.enter_context(nc.psum_tensor(pfx + "ps_g", [128, 4], F32))
    ps_t = st.enter_context(nc.psum_tensor(pfx + "ps_t", [4, 128], F32))
    se_s = st.enter_context(nc.semaphore(pfx + "se_s"))
    se_e = st.enter_context(nc.semaphore(pfx + "se_e"))
    cnt = {"s": 0, "e": 0}
    psl = [ps_a, ps_b]

    b.op("sync", lambda s: s.dma_start(out=esl[:], in_=esel[:, :]), dma=True)
    b.op("sync", lambda s: s.dma_start(out=idn[:], in_=ident[:, :]), dma=True)
    b.op("gpsimd", lambda g: g.dma_start(out=wgate[:], in_=wdr["gateT"].rearrange("(c p) o -> p c o", p=128)), dma=True)
    for c in range(NCH):
        b.op("sync", lambda s, c=c: s.dma_start(out=n2b[:, c:c + 1], in_=n2_w[c * 128:(c + 1) * 128, :]), dma=True)
    b.op("vector", lambda v: v.memset(onesb[:], 1.0))
    b.op("vector", lambda v: v.memset(onesr[:], 1.0))
    b.bar()

    def load_w(name, icr, ocr):
        nic = icr[1] - icr[0]
        oc0, oc1 = ocr
        b.op("gpsimd", lambda g: g.dma_start(
            out=wbig[:, 0:nic, 0:(oc1 - oc0)],
            in_=wdr[name].rearrange("(c p) o -> p c o", p=128)[:, icr[0]:icr[1], oc0:oc1]), dma=True)

    def mm_block(src, oc_list, nic, dst_evac, extra_wait_first=False):
        units = [(oc, nb) for oc in oc_list for nb in range(NNB)]
        e0 = cnt["e"]
        for j, (oc, nb) in enumerate(units):
            ps = psl[j % 2]

            def t_fn(t, oc=oc, nb=nb, ps=ps, wv=(None if j < 2 else e0 + (j - 1))):
                if wv is not None:
                    t.wait_ge(se_e, wv)
                for ic in range(nic):
                    ins = t.matmul(ps[:, :], wbig[:, ic, oc * 128:(oc + 1) * 128],
                                   src[:, ic, nb * 512:(nb + 1) * 512],
                                   start=(ic == 0), stop=(ic == nic - 1))
                return ins
            b.op("tensor", lambda t, f=t_fn: (f(t).then_inc(se_s, 1), None)[1])
            cnt["s"] += 1
            eng, e_fn = dst_evac(j, oc, nb, ps)

            def wrap(e, f=e_fn, sv=cnt["s"]):
                e.wait_ge(se_s, sv)
                return f(e)
            b.op(eng, lambda e, f=wrap: (f(e).then_inc(se_e, 1), None)[1])
            cnt["e"] += 1
        b.bar()

    for sb in range(NSB):
        t0 = sb * SB

        b.op("sync", lambda s, t0=t0: s.dma_start(
            out=ats[:], in_=at_fm.rearrange("(c p) t -> p c t", p=128)[:, :, t0:t0 + SB]), dma=True)
        b.op("sync", lambda s, t0=t0: s.dma_start(
            out=lnxs[:], in_=lnx_fm.rearrange("(c p) t -> p c t", p=128)[:, :, t0:t0 + SB]), dma=True)
        load_w("woT", (0, 8), (0, 1024))
        b.bar()

        for oc in range(2):
            b.op("vector", lambda v, oc=oc: v.tensor_copy(out=xp[:, oc, :], in_=lnxs[:, oc, :]))

        def wo_evac(j, oc, nb, ps):
            def f(v, oc=oc, nb=nb, ps=ps):
                return v.tensor_tensor(out=xp[:, oc, nb * 512:(nb + 1) * 512], in0=ps[:, :],
                                       in1=lnxs[:, oc, nb * 512:(nb + 1) * 512], op=ALU.add)
            return ("vector", f)
        mm_block(ats, list(range(2, NCH)), NCH, wo_evac)

        for c in range(NCH):
            b.op("scalar", lambda s, c=c: s.activation(out=ga[:, c, :], in_=xp[:, c, :], func=AF.Square))
        b.bar()
        for nb in range(NNB):
            for c in range(NCH):
                b.op("tensor", lambda t, c=c, nb=nb: t.matmul(
                    ps_r[0:1, :], onesb[:, 0:1], ga[:, c, nb * 512:(nb + 1) * 512],
                    start=(c == 0), stop=(c == NCH - 1)))
            b.bar()
            b.op("scalar", lambda s, nb=nb: s.activation(out=rows[:, nb * 512:(nb + 1) * 512], in_=ps_r[0:1, :],
                                                         func=AF.Sqrt, scale=f32(1.0 / D), bias=f32(EPS_RMS)))
            b.bar()
        b.op("vector", lambda v: v.reciprocal(out=rows[:, :], in_=rows[:, :]))
        b.bar()
        for nb in range(NNB):
            b.op("tensor", lambda t, nb=nb: t.matmul(ps_bc[:, :], onesr[0:1, :], rows[0:1, nb * 512:(nb + 1) * 512],
                                                     start=True, stop=True))
            b.bar()
            b.op("vector", lambda v, nb=nb: v.tensor_copy(out=bcn[:, nb * 512:(nb + 1) * 512], in_=ps_bc[:, :]))
            b.bar()
        for c in range(NCH):
            for nb in range(NNB):
                b.op("vector", lambda v, c=c, nb=nb: v.tensor_tensor(
                    out=scr[:, :], in0=xp[:, c, nb * 512:(nb + 1) * 512],
                    in1=bcn[:, nb * 512:(nb + 1) * 512], op=ALU.mult))
                b.bar()
                b.op("vector", lambda v, c=c, nb=nb: v.tensor_scalar(
                    out=xf[:, c, nb * 512:(nb + 1) * 512], in0=scr[:, :],
                    scalar1=n2b[:, c:c + 1], scalar2=None, op0=ALU.mult))
                b.bar()

        b.op("sync", lambda s, t0=t0: s.dma_start(
            out=lnxs[:], in_=x0_fm.rearrange("(c p) t -> p c t", p=128)[:, :, t0:t0 + SB]), dma=True)
        b.bar()
        for c in range(NCH):
            b.op("vector", lambda v, c=c: v.scalar_tensor_tensor(
                out=xp[:, c, :], in0=xp[:, c, :], scalar=0.5, in1=lnxs[:, c, :],
                op0=ALU.mult, op1=ALU.add))
        b.bar()

        for tt in range(NTT):
            b.op("tensor", lambda t, tt=tt: [
                t.matmul(ps_g[:, :], xf[:, ic, tt * 128:(tt + 1) * 128], wgate[:, ic, :],
                         start=(ic == 0), stop=(ic == NCH - 1)) for ic in range(NCH)][-1])
            b.bar()
            b.op("vector", lambda v, tt=tt: v.tensor_copy(out=lg[:, tt, :], in_=ps_g[:, :]))
            b.bar()
        for tt in range(NTT):
            b.op("vector", lambda v, tt=tt: v.tensor_reduce(out=gs[:, 0:1], in_=lg[:, tt, :], axis=AX.X, op=ALU.max))
            b.bar()
            b.op("vector", lambda v: v.tensor_scalar(out=gs[:, 1:2], in0=gs[:, 0:1], scalar1=-1.0, scalar2=None, op0=ALU.mult))
            b.bar()
            b.op("scalar", lambda s, tt=tt: s.activation(out=lg[:, tt, :], in_=lg[:, tt, :], func=AF.Exp, bias=gs[:, 1:2]))
            b.bar()
            b.op("vector", lambda v, tt=tt: v.tensor_reduce(out=gs[:, 2:3], in_=lg[:, tt, :], axis=AX.X, op=ALU.add))
            b.bar()
            b.op("vector", lambda v: v.reciprocal(out=gs[:, 3:4], in_=gs[:, 2:3]))
            b.bar()
            b.op("vector", lambda v, tt=tt: v.tensor_scalar(out=lg[:, tt, :], in0=lg[:, tt, :], scalar1=gs[:, 3:4], scalar2=None, op0=ALU.mult))
            b.bar()
            b.op("vector", lambda v, tt=tt: v.tensor_reduce(out=gs[:, 4:5], in_=lg[:, tt, :], axis=AX.X, op=ALU.max))
            b.bar()
            b.op("vector", lambda v, tt=tt: v.tensor_scalar(out=scr[:, 0:4], in0=lg[:, tt, :], scalar1=gs[:, 4:5], scalar2=None, op0=ALU.is_equal))
            b.bar()
            b.op("vector", lambda v: v.tensor_scalar(out=scr[:, 4:8], in0=scr[:, 0:4], scalar1=-1.0, scalar2=1.0, op0=ALU.mult, op1=ALU.add))
            b.bar()
            b.op("vector", lambda v, tt=tt: v.tensor_tensor(out=scr[:, 8:12], in0=lg[:, tt, :], in1=scr[:, 4:8], op=ALU.mult))
            b.bar()
            b.op("vector", lambda v: v.tensor_reduce(out=gs[:, 5:6], in_=scr[:, 8:12], axis=AX.X, op=ALU.max))
            b.bar()
            b.op("vector", lambda v, tt=tt: v.tensor_scalar(out=scr[:, 12:16], in0=lg[:, tt, :], scalar1=gs[:, 5:6], scalar2=None, op0=ALU.is_ge))
            b.bar()
            b.op("vector", lambda v: v.tensor_tensor(out=gs[:, 6:7], in0=gs[:, 4:5], in1=gs[:, 5:6], op=ALU.add))
            b.bar()
            b.op("scalar", lambda s: s.activation(out=gs[:, 6:7], in_=gs[:, 6:7], func=AF.Copy, bias=1e-08))
            b.bar()
            b.op("vector", lambda v: v.reciprocal(out=gs[:, 7:8], in_=gs[:, 6:7]))
            b.bar()
            b.op("vector", lambda v, tt=tt: v.tensor_tensor(out=lg[:, tt, :], in0=lg[:, tt, :], in1=scr[:, 12:16], op=ALU.mult))
            b.bar()
            b.op("vector", lambda v, tt=tt: v.tensor_scalar(out=lg[:, tt, :], in0=lg[:, tt, :], scalar1=gs[:, 7:8], scalar2=None, op0=ALU.mult))
            b.bar()
        for tt in range(NTT):
            b.op("tensor", lambda t, tt=tt: t.transpose(ps_t[:, :], lg[:, tt, :], idn[:, :]))
            b.bar()
            b.op("vector", lambda v, tt=tt: v.tensor_copy(out=wrow[:, tt * 128:(tt + 1) * 128], in_=ps_t[:, :]))
            b.bar()
        for e in range(4):
            for nb in range(NNB):
                b.op("tensor", lambda t, e=e, nb=nb: t.matmul(
                    ps_bc[:, :], esl[:, e * 128:(e + 1) * 128], wrow[0:4, nb * 512:(nb + 1) * 512],
                    start=True, stop=True))
                b.bar()
                b.op("vector", lambda v, e=e, nb=nb: v.tensor_copy(
                    out=webc[:, e, nb * 512:(nb + 1) * 512], in_=ps_bc[:, :]))
                b.bar()

        def act_evac(dst, func):
            def mk(j, oc, nb, ps):
                def f(s, dst=dst, oc=oc, nb=nb, ps=ps, func=func):
                    return s.activation(out=dst[:, oc, nb * 512:(nb + 1) * 512], in_=ps[:, :], func=func)
                return ("scalar", f)
            return mk

        def macc_evac(e, first):
            def mk(j, oc, nb, ps):
                def f(v, e=e, oc=oc, nb=nb, ps=ps):
                    ins = v.tensor_tensor(out=scr[:, :], in0=ps[:, :],
                                          in1=webc[:, e, nb * 512:(nb + 1) * 512], op=ALU.mult)
                    v.scalar_tensor_tensor(out=xp[:, oc, nb * 512:(nb + 1) * 512], in0=scr[:, :],
                                           scalar=0.5, in1=xp[:, oc, nb * 512:(nb + 1) * 512],
                                           op0=ALU.mult, op1=ALU.add)
                    return ins
                return ("vector", f)
            return mk

        load_w("e0_upT", (0, 8), (0, 1024)); b.bar()
        mm_block(xf, list(range(8)), NCH, act_evac(ga, AF.Gelu))
        load_w("e0_upT", (0, 8), (1024, 2048)); b.bar()
        mm_block(xf, list(range(8)), NCH, lambda j, oc, nb, ps: act_evac(ga, AF.Gelu)(j, oc + 8, nb, ps))
        load_w("e0_downT", (0, 16), (0, 1024)); b.bar()
        mm_block(ga, list(range(2, NCH)), 16, macc_evac(0, True))
        load_w("e1_w1T", (0, 8), (0, 1024)); b.bar()
        mm_block(xf, list(range(8)), NCH, act_evac(ga, AF.Silu))
        load_w("e1_w1T", (0, 8), (1024, 2048)); b.bar()
        mm_block(xf, list(range(8)), NCH, lambda j, oc, nb, ps: act_evac(ga, AF.Silu)(j, oc + 8, nb, ps))
        load_w("e1_w2T", (0, 8), (0, 1024)); b.bar()
        mm_block(xf, list(range(8)), NCH, act_evac(gb, AF.Copy))
        load_w("e1_w2T", (0, 8), (1024, 2048)); b.bar()
        mm_block(xf, list(range(8)), NCH, lambda j, oc, nb, ps: act_evac(gb, AF.Copy)(j, oc + 8, nb, ps))
        for hc in range(16):
            b.op("vector", lambda v, hc=hc: v.tensor_tensor(out=ga[:, hc, :], in0=ga[:, hc, :], in1=gb[:, hc, :], op=ALU.mult))
        b.bar()
        load_w("e1_downT", (0, 16), (0, 1024)); b.bar()
        mm_block(ga, list(range(2, NCH)), 16, macc_evac(1, False))
        load_w("e2_l1T", (0, 8), (0, 1024)); b.bar()
        mm_block(xf, list(range(8)), NCH, act_evac(ga, AF.Gelu))
        load_w("e2_l1T", (0, 8), (1024, 2048)); b.bar()
        mm_block(xf, list(range(8)), NCH, lambda j, oc, nb, ps: act_evac(ga, AF.Gelu)(j, oc + 8, nb, ps))
        load_w("e2_l2T", (0, 16), (0, 1024)); b.bar()
        mm_block(ga, list(range(8)), 16, act_evac(gb, AF.Gelu))
        load_w("e2_l2T", (0, 16), (1024, 2048)); b.bar()
        mm_block(ga, list(range(8)), 16, lambda j, oc, nb, ps: act_evac(gb, AF.Gelu)(j, oc + 8, nb, ps))
        load_w("e2_l3T", (0, 16), (0, 1024)); b.bar()
        mm_block(gb, list(range(8)), 16, act_evac(ga, AF.Gelu))
        load_w("e2_l3T", (0, 16), (1024, 2048)); b.bar()
        mm_block(gb, list(range(8)), 16, lambda j, oc, nb, ps: act_evac(ga, AF.Gelu)(j, oc + 8, nb, ps))
        load_w("e2_l4T", (0, 16), (0, 1024)); b.bar()
        mm_block(ga, list(range(2, NCH)), 16, macc_evac(2, False))
        load_w("e3_downT", (0, 8), (0, 1024)); b.bar()
        mm_block(xf, list(range(8)), NCH, act_evac(ga, AF.Gelu))
        load_w("e3_upT", (0, 8), (0, 1024)); b.bar()
        mm_block(ga, list(range(8)), 8, act_evac(gb, AF.Gelu))
        load_w("e3_upT", (0, 8), (1024, 2048)); b.bar()
        mm_block(ga, list(range(8)), 8, lambda j, oc, nb, ps: act_evac(gb, AF.Gelu)(j, oc + 8, nb, ps))
        load_w("e3_outT", (0, 16), (0, 1024)); b.bar()
        mm_block(gb, list(range(2, NCH)), 16, macc_evac(3, False))

        for c in range(NCH):
            b.op("sync", lambda s, c=c, t0=t0: s.dma_start(
                out=o_x1[c * 128:(c + 1) * 128, t0:t0 + SB], in_=xp[:, c, :]), dma=True)
        b.bar()






def emit_L4a(nc, b, st, t, pfx=""):
    x_fm = t["x_fm"]
    wq_w = t["wq_w"]
    wk_w = t["wk_w"]
    wv_w = t["wv_w"]
    wqi_w = t["wqi_w"]
    wki_w = t["wki_w"]
    wq_b = t["wq_b"]
    wk_b = t["wk_b"]
    wv_b = t["wv_b"]
    wqi_b = t["wqi_b"]
    wki_b = t["wki_b"]
    o_qT = t["o_qT"]
    o_kv = t["o_kv"]
    o_qi = t["o_qi"]
    o_ki = t["o_ki"]

    NB = TPC // 512

    xs = st.enter_context(nc.sbuf_tensor(pfx + "xs", [128, NCH, 512], F32))
    xbb = st.enter_context(nc.sbuf_tensor(pfx + "xbb", [128, NCH, 512], BF16))
    lnf = st.enter_context(nc.sbuf_tensor(pfx + "lnf", [128, NCH, 512], F32))
    lnb = st.enter_context(nc.sbuf_tensor(pfx + "lnb", [128, NCH, 512], BF16))
    wqs = st.enter_context(nc.sbuf_tensor(pfx + "wqs", [128, NCH, D], BF16))
    wks = st.enter_context(nc.sbuf_tensor(pfx + "wks", [128, NCH, D], BF16))
    wvs = st.enter_context(nc.sbuf_tensor(pfx + "wvs", [128, NCH, D], BF16))
    wqis = st.enter_context(nc.sbuf_tensor(pfx + "wqis", [128, NCH, IDXD], F32))
    wkis = st.enter_context(nc.sbuf_tensor(pfx + "wkis", [128, NCH, IDXD], F32))
    qbias = st.enter_context(nc.sbuf_tensor(pfx + "qbias", [128, NCH], F32))
    qibias = st.enter_context(nc.sbuf_tensor(pfx + "qibias", [64, 1], F32))
    kibias = st.enter_context(nc.sbuf_tensor(pfx + "kibias", [64, 1], F32))
    kbrow = st.enter_context(nc.sbuf_tensor(pfx + "kbrow", [1, D], F32))
    vbrow = st.enter_context(nc.sbuf_tensor(pfx + "vbrow", [1, D], F32))
    kvbc = st.enter_context(nc.sbuf_tensor(pfx + "kvbc", [128, 2, D], F32))
    onesb = st.enter_context(nc.sbuf_tensor(pfx + "onesb", [128, 1], BF16))
    onesr = st.enter_context(nc.sbuf_tensor(pfx + "onesr", [1, 128], F32))
    rows = st.enter_context(nc.sbuf_tensor(pfx + "rows", [1, 4 * 512], F32))
    bcs = st.enter_context(nc.sbuf_tensor(pfx + "bcs", [128, 2, 512], F32))
    sqv = st.enter_context(nc.sbuf_tensor(pfx + "sqv", [128, 512], BF16))
    qko = st.enter_context(nc.sbuf_tensor(pfx + "qko", [128, 512], BF16))
    kvstage = st.enter_context(nc.sbuf_tensor(pfx + "kvstage", [128, 4, 512], BF16))
    idxo = st.enter_context(nc.sbuf_tensor(pfx + "idxo", [64, 2, 512], F32))
    ps_a = st.enter_context(nc.psum_tensor(pfx + "ps_a", [128, 512], F32))
    ps_b = st.enter_context(nc.psum_tensor(pfx + "ps_b", [128, 512], F32))
    ps_r0 = st.enter_context(nc.psum_tensor(pfx + "ps_r0", [1, 512], F32))
    ps_r1 = st.enter_context(nc.psum_tensor(pfx + "ps_r1", [1, 512], F32))
    ps_bc = st.enter_context(nc.psum_tensor(pfx + "ps_bc", [128, 512], F32))
    ps_i = st.enter_context(nc.psum_tensor(pfx + "ps_i", [64, 512], F32))

    b.op("gpsimd", lambda g: g.dma_start(out=wqs[:], in_=wq_w.rearrange("(c p) o -> p c o", p=128)), dma=True)
    b.op("gpsimd", lambda g: g.dma_start(out=wks[:], in_=wk_w.rearrange("(c p) o -> p c o", p=128)), dma=True)
    b.op("gpsimd", lambda g: g.dma_start(out=wvs[:], in_=wv_w.rearrange("(c p) o -> p c o", p=128)), dma=True)
    b.op("sync", lambda s: s.dma_start(out=wqis[:], in_=wqi_w.rearrange("(c p) o -> p c o", p=128)), dma=True)
    b.op("sync", lambda s: s.dma_start(out=wkis[:], in_=wki_w.rearrange("(c p) o -> p c o", p=128)), dma=True)
    for c in range(NCH):
        b.op("sync", lambda s, c=c: s.dma_start(out=qbias[:, c:c + 1],
                                                in_=wq_b.rearrange("o d -> d o")[c * 128:(c + 1) * 128, :]), dma=True)
    b.op("sync", lambda s: s.dma_start(out=qibias[:], in_=wqi_b.rearrange("o d -> d o")), dma=True)
    b.op("sync", lambda s: s.dma_start(out=kibias[:], in_=wki_b.rearrange("o d -> d o")), dma=True)
    b.op("sync", lambda s: s.dma_start(out=kbrow[:], in_=wk_b[:, :]), dma=True)
    b.op("sync", lambda s: s.dma_start(out=vbrow[:], in_=wv_b[:, :]), dma=True)
    b.op("vector", lambda v: v.memset(onesb[:], 1.0))
    b.op("vector", lambda v: v.memset(onesr[:], 1.0))
    b.bar()
    for j, br in enumerate([kbrow, vbrow]):
        for nb2 in range(2):
            b.op("tensor", lambda t, br=br, nb2=nb2: t.matmul(
                ps_bc[:, :], onesr[0:1, :], br[0:1, nb2 * 512:(nb2 + 1) * 512], start=True, stop=True))
            b.bar()
            b.op("vector", lambda v, j=j, nb2=nb2: v.tensor_copy(
                out=kvbc[:, j, nb2 * 512:(nb2 + 1) * 512], in_=ps_bc[:, :]))
            b.bar()

    for blk in range(NB):
        t0 = blk * 512
        b.op("sync", lambda s, t0=t0: s.dma_start(
            out=xs[:], in_=x_fm.rearrange("(c p) t -> p c t", p=128)[:, :, t0:t0 + 512]), dma=True)
        b.op("gpsimd", lambda g, t0=t0: g.dma_start(
            out=xbb[:], in_=x_fm.rearrange("(c p) t -> p c t", p=128)[:, :, t0:t0 + 512]), dma=True)
        b.bar()
        for c in range(NCH):
            b.op("scalar", lambda s, c=c: s.activation(out=sqv[:, :] if False else lnb[:, c, :],
                                                       in_=xs[:, c, :], func=AF.Square))
        b.bar()
        for c in range(NCH):
            b.op("tensor", lambda t, c=c: t.matmul(ps_r0[0:1, :], onesb[:, 0:1], xbb[:, c, :],
                                                   start=(c == 0), stop=(c == NCH - 1)))
        for c in range(NCH):
            b.op("tensor", lambda t, c=c: t.matmul(ps_r1[0:1, :], onesb[:, 0:1], lnb[:, c, :],
                                                   start=(c == 0), stop=(c == NCH - 1)))
        b.bar()
        b.op("scalar", lambda s: s.activation(out=rows[:, 0:512], in_=ps_r0[0:1, :], func=AF.Copy, scale=f32(1.0 / D)))
        b.op("scalar", lambda s: s.activation(out=rows[:, 512:1024], in_=ps_r1[0:1, :], func=AF.Copy, scale=f32(1.0 / D)))
        b.bar()
        b.op("vector", lambda v: v.tensor_tensor(out=rows[:, 1024:1536], in0=rows[:, 0:512], in1=rows[:, 0:512], op=ALU.mult))
        b.bar()
        b.op("vector", lambda v: v.tensor_tensor(out=rows[:, 512:1024], in0=rows[:, 512:1024], in1=rows[:, 1024:1536], op=ALU.subtract))
        b.bar()
        b.op("scalar", lambda s: s.activation(out=rows[:, 1024:1536], in_=rows[:, 512:1024], func=AF.Sqrt, bias=f32(EPS_LN)))
        b.bar()
        b.op("vector", lambda v: v.reciprocal(out=rows[:, 1024:1536], in_=rows[:, 1024:1536]))
        b.bar()
        for j, off in enumerate([0, 1024]):
            b.op("tensor", lambda t, off=off: t.matmul(ps_bc[:, :], onesr[0:1, :], rows[0:1, off:off + 512],
                                                       start=True, stop=True))
            b.bar()
            b.op("vector", lambda v, j=j: v.tensor_copy(out=bcs[:, j, :], in_=ps_bc[:, :]))
            b.bar()
        for c in range(NCH):
            b.op("vector", lambda v, c=c: v.tensor_tensor(out=lnf[:, c, :], in0=xs[:, c, :], in1=bcs[:, 0, :], op=ALU.subtract))
        b.bar()
        for c in range(NCH):
            b.op("vector", lambda v, c=c: v.tensor_tensor(out=lnf[:, c, :], in0=lnf[:, c, :], in1=bcs[:, 1, :], op=ALU.mult))
        b.bar()
        for c in range(NCH):
            b.op("vector", lambda v, c=c: v.tensor_copy(out=lnb[:, c, :], in_=lnf[:, c, :]))
        b.bar()
        for oc in range(NCH):
            b.op("tensor", lambda t, oc=oc: [
                t.matmul(ps_a[:, :], wqs[:, ic, oc * 128:(oc + 1) * 128], lnb[:, ic, :],
                         start=(ic == 0), stop=(ic == NCH - 1)) for ic in range(NCH)][-1])
            b.bar()
            b.op("vector", lambda v, oc=oc: v.tensor_scalar(out=qko[:, :], in0=ps_a[:, :],
                                                            scalar1=qbias[:, oc:oc + 1], scalar2=0.125,
                                                            op0=ALU.add, op1=ALU.mult))
            b.bar()
            b.op("sync", lambda s, oc=oc, t0=t0: s.dma_start(
                out=o_qT[oc * 128:(oc + 1) * 128, t0:t0 + 512], in_=qko[:, :]), dma=True)
            b.bar()
        for tt in range(4):
            for j, ws in enumerate([wks, wvs]):
                for ob in range(2):
                    b.op("tensor", lambda t, tt=tt, ws=ws, ob=ob: [
                        t.matmul(ps_b[:, :], lnb[:, ic, tt * 128:(tt + 1) * 128],
                                 ws[:, ic, ob * 512:(ob + 1) * 512],
                                 start=(ic == 0), stop=(ic == NCH - 1)) for ic in range(NCH)][-1])
                    b.bar()
                    b.op("vector", lambda v, j=j, ob=ob: v.tensor_tensor(
                        out=kvstage[:, j * 2 + ob, :], in0=ps_b[:, :],
                        in1=kvbc[:, j, ob * 512:(ob + 1) * 512], op=ALU.add))
                    b.bar()
            b.op("sync", lambda s, tt=tt, t0=t0: s.dma_start(
                out=o_kv[t0 + tt * 128: t0 + (tt + 1) * 128, :],
                in_=kvstage.ap().rearrange("p a t -> p (a t)")), dma=True)
            b.bar()
        for j, (ws, bia, oo) in enumerate([(wqis, qibias, o_qi), (wkis, kibias, o_ki)]):
            b.op("tensor", lambda t, ws=ws: [
                t.matmul(ps_i[:, :], ws[:, ic, :], lnf[:, ic, :],
                         start=(ic == 0), stop=(ic == NCH - 1)) for ic in range(NCH)][-1])
            b.bar()
            b.op("vector", lambda v, bia=bia: v.tensor_scalar(out=idxo[:, 0, :], in0=ps_i[:, :],
                                                              scalar1=bia[:, 0:1], scalar2=None, op0=ALU.add))
            b.bar()
            b.op("sync", lambda s, oo=oo, t0=t0: s.dma_start(
                out=oo[:, t0:t0 + 512], in_=idxo[:, 0, :]), dma=True)
            b.bar()






L1_DRAMS = [("x_fm", [D, TPC], F32, "ExternalInput"),
            ("wqT", [D, D], F32, "ExternalInput"),
            ("wkT", [D, D], F32, "ExternalInput"),
            ("wvT", [D, D], F32, "ExternalInput"),
            ("n1_w", [D, 1], F32, "ExternalInput"),
            ("qn_w", [HD, 1], F32, "ExternalInput"),
            ("kn_w", [HD, 1], F32, "ExternalInput"),
            ("o_lnx", [D, TPC], F32, "ExternalOutput"),
            ("o_qT", [D, TPC], BF16, "ExternalOutput"),
            ("o_kT", [D, TPC], BF16, "ExternalOutput"),
            ("o_v", [TPC, D], BF16, "ExternalOutput")]

L3_W_SHAPES = [("woT", [D, D]), ("gateT", [D, 4]),
               ("e0_upT", [D, HID]), ("e0_downT", [HID, D]),
               ("e1_w1T", [D, HID]), ("e1_w2T", [D, HID]), ("e1_downT", [HID, D]),
               ("e2_l1T", [D, HID]), ("e2_l2T", [HID, HID]), ("e2_l3T", [HID, HID]),
               ("e2_l4T", [HID, D]),
               ("e3_downT", [D, D]), ("e3_upT", [D, HID]), ("e3_outT", [HID, D])]

L3_DRAMS = ([("at_fm", [D, TPC], BF16, "ExternalInput"),
             ("lnx_fm", [D, TPC], F32, "ExternalInput"),
             ("x0_fm", [D, TPC], F32, "ExternalInput"),
             ("n2_w", [D, 1], F32, "ExternalInput"),
             ("esel", [4, 512], F32, "ExternalInput"),
             ("ident", [128, 128], F32, "ExternalInput"),
             ("o_x1", [D, TPC], F32, "ExternalOutput")]
            + [(nm, shp, F32, "ExternalInput") for nm, shp in L3_W_SHAPES])

L4A_DRAMS = [("x_fm", [D, TPC], F32, "ExternalInput"),
             ("wq_w", [D, D], F32, "ExternalInput"),
             ("wk_w", [D, D], F32, "ExternalInput"),
             ("wv_w", [D, D], F32, "ExternalInput"),
             ("wqi_w", [D, IDXD], F32, "ExternalInput"),
             ("wki_w", [D, IDXD], F32, "ExternalInput"),
             ("wq_b", [1, D], F32, "ExternalInput"),
             ("wk_b", [1, D], F32, "ExternalInput"),
             ("wv_b", [1, D], F32, "ExternalInput"),
             ("wqi_b", [1, IDXD], F32, "ExternalInput"),
             ("wki_b", [1, IDXD], F32, "ExternalInput"),
             ("o_qT", [D, TPC], BF16, "ExternalOutput"),
             ("o_kv", [TPC, 2 * D], BF16, "ExternalOutput"),
             ("o_qi", [IDXD, TPC], F32, "ExternalOutput"),
             ("o_ki", [IDXD, TPC], F32, "ExternalOutput")]


def _mk_build(emit_fn, drams, pfx):
    def build():
        nc = bacc.Bacc("TRN2", target_bir_lowering=False)
        t = {nm: nc.dram_tensor(nm, shp, dt, kind=kd).ap() for nm, shp, dt, kd in drams}
        with ExitStack() as st:
            block = st.enter_context(nc.Block())
            sems = {e: st.enter_context(nc.semaphore("s_" + e)) for e in ENGS}
            b = Builder(nc)
            with ExitStack() as sst:
                emit_fn(nc, b, sst, t, pfx)
            b.emit(block, sems)
        nc.finalize()
        return nc
    return build


build_L1 = _mk_build(emit_L1, L1_DRAMS, "l1_")
build_L3 = _mk_build(emit_L3, L3_DRAMS, "l3_")
build_L4a = _mk_build(emit_L4a, L4A_DRAMS, "l4a_")


def emit_L4b(nc, b, st, t, pfx="", nqt=None):
    qT_d = t["qT_d"]
    qiT_d = t["qiT_d"]
    g_ki = t["g_ki"]
    kv_d = t["kv_d"]
    x1_d = t["x1_d"]
    wo_w = t["wo_w"]
    wo_b = t["wo_b"]
    ident = t["ident"]
    identb = t["identb"]
    cbase = t["cbase"]
    o_x2 = t["o_x2"]
    bK = t["bK"]
    bV = t["bV"]
    vaDR = t["vaDR"]

    NQT = TPC // 128
    NAB = AL // 512

    qts = st.enter_context(nc.sbuf_tensor(pfx + "qts", [128, NCH, 128], BF16))
    qis = st.enter_context(nc.sbuf_tensor(pfx + "qis", [64, 128], F32))
    kis = st.enter_context(nc.sbuf_tensor(pfx + "kis", [64, AL], F32))
    sidx = st.enter_context(nc.sbuf_tensor(pfx + "sidx", [128, AL], F32))
    cand = st.enter_context(nc.sbuf_tensor(pfx + "cand", [128, 960], F32))
    candg = st.enter_context(nc.sbuf_tensor(pfx + "candg", [128, 960], F32))
    candi = st.enter_context(nc.sbuf_tensor(pfx + "candi", [128, 960], U32))
    cbs = st.enter_context(nc.sbuf_tensor(pfx + "cbs", [128, 960], F32))
    tv = st.enter_context(nc.sbuf_tensor(pfx + "tv", [128, 64], F32))
    picks = st.enter_context(nc.sbuf_tensor(pfx + "picks", [128, 64], F32))
    xkv = st.enter_context(nc.sbuf_tensor(pfx + "xkv", [128, 2, 64], F32))
    xkvi = st.enter_context(nc.sbuf_tensor(pfx + "xkvi", [128, 2, 64], I16))
    idxk = st.enter_context(nc.sbuf_tensor(pfx + "idxk", [128, 512], I16))
    idxv = st.enter_context(nc.sbuf_tensor(pfx + "idxv", [128, 512], I16))
    gkt = st.enter_context(nc.sbuf_tensor(pfx + "gkt", [128, 8, 512], BF16))
    gv = st.enter_context(nc.sbuf_tensor(pfx + "gv", [128, 8, 1024], BF16))
    kwTs = st.enter_context(nc.sbuf_tensor(pfx + "kwTs", [128, NCH, WM], BF16))
    vws = st.enter_context(nc.sbuf_tensor(pfx + "vws", [128, 4, D], BF16))
    wobr = st.enter_context(nc.sbuf_tensor(pfx + "wobr", [128, NCH], F32))
    pa = st.enter_context(nc.sbuf_tensor(pfx + "pa", [128, 256], BF16))
    pwt = st.enter_context(nc.sbuf_tensor(pfx + "pwt", [128, 8, 4, 128], BF16))
    vas = st.enter_context(nc.sbuf_tensor(pfx + "vas", [64, 2048], F32))
    vtr = st.enter_context(nc.sbuf_tensor(pfx + "vtr", [128, 64], F32))
    vtr8 = st.enter_context(nc.sbuf_tensor(pfx + "vtr8", [128, 8, 64], F32))
    vaE = st.enter_context(nc.sbuf_tensor(pfx + "vaE", [64, 1024], F32))
    vaO = st.enter_context(nc.sbuf_tensor(pfx + "vaO", [64, 1024], F32))
    atw = st.enter_context(nc.sbuf_tensor(pfx + "atw", [64, 2, 1024], F32))
    attn_fm = st.enter_context(nc.sbuf_tensor(pfx + "attn_fm", [128, NCH, 128], BF16))
    zrow = st.enter_context(nc.sbuf_tensor(pfx + "zrow", [1, 2048], F32))
    zh = st.enter_context(nc.sbuf_tensor(pfx + "zh", [1, 2048], F32))
    rzp = st.enter_context(nc.sbuf_tensor(pfx + "rzp", [64, 2, 16], F32))
    mkcol = st.enter_context(nc.sbuf_tensor(pfx + "mkcol", [128, 80], F32))
    mks = st.enter_context(nc.sbuf_tensor(pfx + "mks", [1, 8], F32))
    ebias = st.enter_context(nc.sbuf_tensor(pfx + "ebias", [128, 1], F32))
    ksqb = st.enter_context(nc.sbuf_tensor(pfx + "ksqb", [128, 1024], BF16))
    x1s = st.enter_context(nc.sbuf_tensor(pfx + "x1s", [128, NCH, 128], F32))
    x2s = st.enter_context(nc.sbuf_tensor(pfx + "x2s", [128, NCH, 128], F32))
    idn = st.enter_context(nc.sbuf_tensor(pfx + "idn", [128, 128], F32))
    idnb = st.enter_context(nc.sbuf_tensor(pfx + "idnb", [128, 128], BF16))
    onesb = st.enter_context(nc.sbuf_tensor(pfx + "onesb", [128, 1], BF16))
    onesr = st.enter_context(nc.sbuf_tensor(pfx + "onesr", [1, 128], F32))
    ps_a = st.enter_context(nc.psum_tensor(pfx + "ps_a", [128, 512], F32))
    ps_b = st.enter_context(nc.psum_tensor(pfx + "ps_b", [128, 512], F32))
    ps_sc = st.enter_context(nc.psum_tensor(pfx + "ps_sc", [128, 512], F32))
    ps_va = st.enter_context(nc.psum_tensor(pfx + "ps_va", [64, 512], F32))
    ps_owE = st.enter_context(nc.psum_tensor(pfx + "ps_owE", [64, 1024], F32))
    ps_owO = st.enter_context(nc.psum_tensor(pfx + "ps_owO", [64, 1024], F32))
    se_v = st.enter_context(nc.semaphore(pfx + "se_v"))
    cnt = {"v": 0}

    def vop(fn):
        w = cnt["v"]

        def g(v, fn=fn, w=w):
            if w > 0:
                v.wait_ge(se_v, w)
            return fn(v)
        b.op("vector", lambda v, g=g: (g(v).then_inc(se_v, 1), None)[1])
        cnt["v"] += 1

    b.op("gpsimd", lambda g: g.dma_start(out=idnb[:], in_=identb[:, :]), dma=True)
    b.op("sync", lambda s: s.dma_start(out=kis[:, 0:1536], in_=g_ki[0:IDXD, 0:1536]), dma=True)
    for s2 in range(1, 4):
        b.op("sync", lambda s, s2=s2: s.dma_start(
            out=kis[:, 1536 + 2048 * (s2 - 1):1536 + 2048 * s2],
            in_=g_ki[IDXD * s2:IDXD * (s2 + 1), :]), dma=True)
    b.op("sync", lambda s: s.dma_start(
        out=vws[:], in_=kv_d[1536:2048, D:2 * D].rearrange("(n p) d -> p n d", p=128)), dma=True)
    b.op("sync", lambda s: s.dma_start(out=cbs[:], in_=cbase[:, :]), dma=True)
    b.op("sync", lambda s: s.dma_start(out=idn[:], in_=ident[:, :]), dma=True)
    for c in range(NCH):
        b.op("sync", lambda s, c=c: s.dma_start(
            out=wobr[:, c:c + 1], in_=wo_b.rearrange("o d -> d o")[c * 128:(c + 1) * 128, :]), dma=True)
    b.op("vector", lambda v: v.memset(onesb[:], 1.0))
    b.op("vector", lambda v: v.memset(onesr[:], 1.0))
    b.op("vector", lambda v: v.memset(idxk[:], 0))
    b.op("vector", lambda v: v.memset(idxv[:], 0))
    b.bar()

    for tt in range(4):
        b.op("sync", lambda s, tt=tt: s.dma_start(
            out=ksqb[:], in_=kv_d[1536 + 128 * tt:1536 + 128 * (tt + 1), 0:D]), dma=True)
        b.bar()
        b.op("vector", lambda v: v.tensor_copy(out=x1s.ap().rearrange("p c t -> p (c t)"), in_=ksqb[:]))
        b.bar()
        for cc in range(NCH):
            b.op("tensor", lambda t, cc=cc: t.transpose(
                ps_a[:, 0:128], x1s.ap().rearrange("p c t -> p (c t)")[:, cc * 128:(cc + 1) * 128],
                idn[:, :]))
            b.bar()
            b.op("vector", lambda v, cc=cc, tt=tt: v.tensor_copy(
                out=kwTs[:, cc, tt * 128:(tt + 1) * 128], in_=ps_a[:, 0:128]))
            b.bar()

    for kt in range(64):
        b.op("gpsimd", lambda g, kt=kt: g.dma_start(
            out=ksqb[:], in_=kv_d[kt * 128:(kt + 1) * 128, 0:D]), dma=True)
        b.bar()
        b.op("scalar", lambda s: s.activation(out=ksqb[:], in_=ksqb[:], func=AF.Square))
        b.bar()
        b.op("vector", lambda v, kt=kt: v.tensor_reduce(
            out=mkcol[:, 64:80],
            in_=ksqb.ap().rearrange("p (h d) -> p h d", h=16), axis=AX.X, op=ALU.add))
        b.bar()
        b.op("vector", lambda v, kt=kt: v.tensor_reduce(
            out=mkcol[:, kt:kt + 1], in_=mkcol[:, 64:80], axis=AX.X, op=ALU.max))
        b.bar()
    b.op("vector", lambda v: v.tensor_reduce(out=mkcol[:, 64:65], in_=mkcol[:, 0:64], axis=AX.X, op=ALU.max))
    b.bar()
    b.op("tensor", lambda t: t.transpose(ps_a[0:1, 0:128], mkcol[:, 64:65], idn[:, :]))
    b.bar()
    b.op("vector", lambda v: v.tensor_copy(out=zh[:, 0:128], in_=ps_a[0:1, 0:128]))
    b.bar()
    b.op("vector", lambda v: v.tensor_reduce(out=mks[:, 0:1], in_=zh[:, 0:128], axis=AX.X, op=ALU.max))
    b.bar()
    b.op("scalar", lambda s: s.activation(out=mks[:, 0:1], in_=mks[:, 0:1], func=AF.Sqrt))
    b.bar()

    for qt in range(nqt if nqt is not None else NQT):
        q0 = qt * 128
        b.op("sync", lambda s, q0=q0: s.dma_start(
            out=qts[:], in_=qT_d.rearrange("(c p) t -> p c t", p=128)[:, :, q0:q0 + 128]), dma=True)
        b.op("sync", lambda s, q0=q0: s.dma_start(out=qis[:], in_=qiT_d[:, q0:q0 + 128]), dma=True)
        b.op("sync", lambda s, q0=q0: s.dma_start(
            out=x1s[:], in_=x1_d.rearrange("(c p) t -> p c t", p=128)[:, :, q0:q0 + 128]), dma=True)
        b.bar()
        for c in range(NCH):
            b.op("scalar", lambda s, c=c: s.activation(out=ksqb[:, c * 128:(c + 1) * 128],
                                                       in_=qts[:, c, :], func=AF.Square))
        b.bar()
        for h in range(17):
            if h < 16:
                ps = [ps_a, ps_b][h % 2]
                b.op("tensor", lambda t, h=h, ps=ps: t.matmul(
                    ps[0:1, 0:128], onesb[64 * (h % 2):64 * (h % 2) + 64, 0:1],
                    ksqb[64 * (h % 2):64 * (h % 2) + 64, (h // 2) * 128:(h // 2 + 1) * 128],
                    start=True, stop=True))
            if h > 0:
                psp = [ps_a, ps_b][(h - 1) % 2]
                b.op("vector", lambda v, h=h, psp=psp: v.tensor_copy(
                    out=zh[:, (h - 1) * 128:h * 128], in_=psp[0:1, 0:128]))
            b.bar()
        b.op("vector", lambda v: v.tensor_reduce(out=mks[:, 1:2], in_=zh[:, 0:2048], axis=AX.X, op=ALU.max))
        b.bar()
        b.op("scalar", lambda s: s.activation(out=mks[:, 1:2], in_=mks[:, 1:2], func=AF.Sqrt))
        b.bar()
        b.op("vector", lambda v: v.tensor_tensor(out=mks[:, 2:3], in0=mks[:, 1:2], in1=mks[:, 0:1], op=ALU.mult))
        b.bar()
        b.op("vector", lambda v: v.tensor_scalar(out=mks[:, 2:3], in0=mks[:, 2:3], scalar1=-1.0, scalar2=None, op0=ALU.mult))
        b.bar()
        b.op("tensor", lambda t: t.matmul(ps_a[:, 0:1], onesr[0:1, :], mks[0:1, 2:3], start=True, stop=True))
        b.bar()
        b.op("vector", lambda v: v.tensor_copy(out=ebias[:, :], in_=ps_a[:, 0:1]))
        b.bar()

        for ab in range(NAB + 1):
            if ab < NAB:
                ps = [ps_a, ps_b][ab % 2]
                b.op("tensor", lambda t, ab=ab, ps=ps: t.matmul(
                    ps[:, :], qis[:, :], kis[:, ab * 512:(ab + 1) * 512], start=True, stop=True))
            if ab > 0:
                psp = [ps_a, ps_b][(ab - 1) % 2]
                b.op("vector", lambda v, ab=ab, psp=psp: v.tensor_copy(
                    out=sidx[:, (ab - 1) * 512:ab * 512], in_=psp[:, :]))
            b.bar()

        for c in range(120):
            vop(lambda v, c=c: v.max(out=cand[:, c * 8:(c + 1) * 8], in_=sidx[:, c * 64:(c + 1) * 64]))
            vop(lambda v, c=c: v.max_index(out=candi[:, c * 8:(c + 1) * 8],
                                           in_max=cand[:, c * 8:(c + 1) * 8],
                                           in_=sidx[:, c * 64:(c + 1) * 64]) if False else
                v.max_index(out=candi[:, c * 8:(c + 1) * 8], in_max=cand[:, c * 8:(c + 1) * 8],
                            in_values=sidx[:, c * 64:(c + 1) * 64]))
        vop(lambda v: v.tensor_copy(out=candg[:, :], in_=cand[:, :]))
        for it in range(8):
            vop(lambda v, it=it: v.max(out=tv[:, it * 8:(it + 1) * 8], in_=candg[:, :]))
            vop(lambda v, it=it: v.match_replace(out=candg[:, :], in_to_replace=tv[:, it * 8:(it + 1) * 8],
                                                 in_values=candg[:, :], imm_value=-1e30))
        vop(lambda v: v.tensor_copy(out=candg[:, :], in_=candi[:, :]))
        vop(lambda v: v.tensor_tensor(out=candg[:, :], in0=candg[:, :], in1=cbs[:, :], op=ALU.add))
        vop(lambda v: v.tensor_scalar(out=cand[:, :], in0=cand[:, :], scalar1=tv[:, 63:64], scalar2=None, op0=ALU.is_ge))
        vop(lambda v: v.tensor_tensor(out=candg[:, :], in0=candg[:, :], in1=cand[:, :], op=ALU.mult))
        for it in range(8):
            vop(lambda v, it=it: v.max(out=picks[:, it * 8:(it + 1) * 8], in_=candg[:, :]))
            vop(lambda v, it=it: v.match_replace(out=candg[:, :], in_to_replace=picks[:, it * 8:(it + 1) * 8],
                                                 in_values=candg[:, :], imm_value=0.0))
        vop(lambda v: v.tensor_scalar(out=xkv[:, 0, :], in0=picks[:, :], scalar1=2.0, scalar2=-2.0,
                                      op0=ALU.mult, op1=ALU.add))
        vop(lambda v: v.tensor_scalar(out=xkv[:, 0, :], in0=xkv[:, 0, :], scalar1=0.0, scalar2=None, op0=ALU.max))
        vop(lambda v: v.tensor_scalar(out=xkv[:, 1, :], in0=picks[:, :], scalar1=1536.5, scalar2=1024.0,
                                      op0=ALU.is_ge, op1=ALU.mult))
        vop(lambda v: v.tensor_tensor(out=xkv[:, 0, :], in0=xkv[:, 0, :], in1=xkv[:, 1, :], op=ALU.add))
        vop(lambda v: v.tensor_scalar(out=xkv[:, 1, :], in0=xkv[:, 0, :], scalar1=1.0, scalar2=None, op0=ALU.add))
        vop(lambda v: v.tensor_copy(out=xkvi[:, :, :], in_=xkv[:, :, :]))
        b.bar()
        for col, bT in [(0, bK), (1, bV)]:
            b.op("sync", lambda s, col=col, bT=bT: s.dma_start(
                out=bT.rearrange("jl (q jh) -> q jl jh", q=128),
                in_=xkvi.ap().rearrange("p a (jl jh) -> p a jl jh", jl=16)[:, col, :, :]), dma=True)
        b.bar()
        for tgt, bT in [(idxk, bK), (idxv, bV)]:
            b.op("sync", lambda s, tgt=tgt, bT=bT: s.dma_start(out=tgt[0:16, :], in_=bT[:, :]), dma=True)
            b.op("sync", lambda s, tgt=tgt, bT=bT: s.dma_start(out=tgt[16:32, :], in_=bT[:, :]), dma=True)
        b.bar()

        for call in range(16):
            b.op("gpsimd", lambda g, call=call: g.dma_gather(
                gkt[:, :, 0:512], kv_d.rearrange("t (a c) -> (t a) c", a=2),
                idxk[:, call * 32:(call + 1) * 32], 512, 512, 1024, transpose=True), dma=True)
            b.op("gpsimd", lambda g, call=call: g.dma_gather(
                gv[:, 0:4, :], kv_d.rearrange("t (a c) -> (t a) c", a=2),
                idxv[:, call * 32:(call + 1) * 32], 512, 512, 1024), dma=True)
            b.bar()
            for mloc in range(4):
                m = call * 4 + mloc
                for h in range(16):
                    b.op("tensor", lambda t, mloc=mloc, m=m, h=h: t.matmul(
                        ps_sc[:, mloc * 32 + h * 2: mloc * 32 + h * 2 + 2],
                        gkt[64 * (h % 2):64 * (h % 2) + 64, h // 2, mloc * 128: mloc * 128 + 128],
                        qts[64 * (h % 2):64 * (h % 2) + 64, h // 2, 2 * m:2 * m + 2],
                        start=True, stop=True))
            b.bar()
            b.op("scalar", lambda s: s.activation(out=pa[:, 0:128], in_=ps_sc[:, 0:128], func=AF.Exp, bias=ebias[:, 0:1]))
            b.bar()
            b.op("vector", lambda v: v.memset(pa.ap().rearrange("p (m h a) -> p m h a", m=8, h=16)[64:128, 0:4, :, 0], 0.0))
            b.op("vector", lambda v: v.memset(pa.ap().rearrange("p (m h a) -> p m h a", m=8, h=16)[0:64, 0:4, :, 1], 0.0))
            b.bar()
            b.op("tensor", lambda t: t.matmul(ps_a[0:1, 0:128], onesb[:, 0:1], pa[:, 0:128], start=True, stop=True))
            b.bar()
            b.op("vector", lambda v, call=call: v.tensor_copy(out=zrow[:, call * 128:(call + 1) * 128], in_=ps_a[0:1, 0:128]))
            b.bar()
            for mloc in range(4):
                for h in range(16):
                    b.op("tensor", lambda t, mloc=mloc, h=h: t.matmul(
                        ps_va[:, mloc * 32 + h * 2: mloc * 32 + h * 2 + 2],
                        gv[:, mloc, h * 64:(h + 1) * 64],
                        pa[:, mloc * 32 + h * 2: mloc * 32 + h * 2 + 2],
                        start=True, stop=True))
            b.bar()
            b.op("vector", lambda v, call=call: v.tensor_copy(out=vas[:, call * 128:(call + 1) * 128], in_=ps_va[:, 0:128]))
            b.bar()

        for hg in range(2):
            for hl in range(8):
                h = hg * 8 + hl
                ps = [ps_a, ps_b][hl % 2]
                for wt in range(4):
                    b.op("tensor", lambda t, h=h, wt=wt, ps=ps: t.matmul(
                        ps[:, wt * 128:(wt + 1) * 128],
                        kwTs[64 * (h % 2):64 * (h % 2) + 64, h // 2, wt * 128:(wt + 1) * 128],
                        qts[64 * (h % 2):64 * (h % 2) + 64, h // 2, :],
                        start=True, stop=True))
                b.bar()
                b.op("scalar", lambda s, hl=hl, ps=ps: s.activation(
                    out=pwt.ap().rearrange("p h w q -> p (h w q)")[:, hl * 512:(hl + 1) * 512],
                    in_=ps[:, :], func=AF.Exp, bias=ebias[:, 0:1]))
                b.bar()
            for hl in range(8):
                h = hg * 8 + hl
                b.op("tensor", lambda t, hl=hl: [t.matmul(
                    ps_b[0:1, 0:128], onesb[:, 0:1], pwt[:, hl, wt, :],
                    start=(wt == 0), stop=(wt == 3)) for wt in range(4)][-1])
                b.bar()
                b.op("vector", lambda v, h=h: v.tensor_copy(
                    out=zh.ap().rearrange("o (m h a) -> o h m a", m=64, h=16)[:, h, :, :],
                    in_=ps_b.ap().rearrange("o (q x) -> o q x", x=1)[0:1, 0:128, :]))
                b.bar()
            if hg == 0:
                for par, pso in [(0, ps_owE), (1, ps_owO)]:
                    for hl in range(8):
                        h = hl
                        for wt in range(4):
                            b.op("tensor", lambda t, par=par, pso=pso, h=h, hl=hl, wt=wt: t.matmul(
                                pso[:, h * 64:(h + 1) * 64],
                                pwt.ap().rearrange("p h w (m a) -> p h w m a", a=2)[:, hl, wt, :, par],
                                vws[:, wt, h * 64:(h + 1) * 64],
                                start=(wt == 0), stop=(wt == 3)))
                b.bar()
        b.op("vector", lambda v: v.tensor_tensor(out=zh[:, :], in0=zh[:, :], in1=zrow[:, :], op=ALU.add))
        b.bar()
        b.op("vector", lambda v: v.tensor_scalar(out=zh[:, :], in0=zh[:, :], scalar1=1e-30, scalar2=None, op0=ALU.add))
        b.bar()
        b.op("vector", lambda v: v.reciprocal(out=zh[:, :], in_=zh[:, :]))
        b.bar()
        for par in range(2):
            for h in range(16):
                b.op("tensor", lambda t, par=par, h=h: t.transpose(
                    ps_b[0:64, par * 16 + h: par * 16 + h + 1],
                    zh.ap().rearrange("o (m h a) -> o m h a", m=64, h=16)[:, :, h, par],
                    idn[0:1, 0:1]))
        b.bar()
        b.op("vector", lambda v: v.tensor_copy(out=rzp.ap().rearrange("p a h -> p (a h)"), in_=ps_b[0:64, 0:32]))
        b.bar()

        for par, pso in [(0, ps_owE), (1, ps_owO)]:
            for hl in range(8):
                h = 8 + hl
                for wt in range(4):
                    b.op("tensor", lambda t, par=par, pso=pso, h=h, hl=hl, wt=wt: t.matmul(
                        pso[:, h * 64:(h + 1) * 64],
                        pwt.ap().rearrange("p h w (m a) -> p h w m a", a=2)[:, hl, wt, :, par],
                        vws[:, wt, h * 64:(h + 1) * 64],
                        start=(wt == 0), stop=(wt == 3)))
        b.bar()

        for g in range(2):
            for cl in range(8):
                b.op("tensor", lambda t, g=g, cl=cl: t.transpose(
                    ps_sc[:, cl * 64:(cl + 1) * 64],
                    vas[:, (g * 8 + cl) * 128:(g * 8 + cl + 1) * 128], idn[0:64, 0:64]))
            b.bar()
            b.op("vector", lambda v: v.tensor_copy(
                out=vtr8.ap().rearrange("p c d -> p (c d)"), in_=ps_sc[:, :]))
            b.bar()
            b.op("sync", lambda s, g=g: s.dma_start(
                out=vaDR.rearrange("(r p) d -> p r d", p=128)[:, g * 8:(g + 1) * 8, :],
                in_=vtr8.ap()), dma=True)
            b.bar()
        b.op("sync", lambda s: s.dma_start(
            out=vaE.ap().rearrange("m (h d) -> m h d", h=16),
            in_=vaDR.rearrange("(m h a) d -> m h a d", h=16, a=2)[:, :, 0, :]), dma=True)
        b.op("sync", lambda s: s.dma_start(
            out=vaO.ap().rearrange("m (h d) -> m h d", h=16),
            in_=vaDR.rearrange("(m h a) d -> m h a d", h=16, a=2)[:, :, 1, :]), dma=True)
        b.bar()

        for par, vaP, pso in [(0, vaE, ps_owE), (1, vaO, ps_owO)]:
            b.op("vector", lambda v, par=par, vaP=vaP, pso=pso: v.tensor_tensor(
                out=atw[:, par, :], in0=pso[:, :], in1=vaP[:, :], op=ALU.add))
        b.bar()
        for par in range(2):
            for h in range(16):
                b.op("vector", lambda v, par=par, h=h: v.tensor_scalar(
                    out=atw[:, par, h * 64:(h + 1) * 64], in0=atw[:, par, h * 64:(h + 1) * 64],
                    scalar1=rzp[:, par, h:h + 1], scalar2=None, op0=ALU.mult))
        b.bar()
        for c in range(NCH):
            for par in range(2):
                b.op("tensor", lambda t, c=c, par=par: t.transpose(
                    [ps_a, ps_b][par][:, c * 64:(c + 1) * 64],
                    atw.ap().rearrange("p a (c e) -> p a c e", c=NCH)[:, par, c, :],
                    idn[0:64, 0:64]))
        b.bar()
        for c in range(NCH):
            b.op("vector", lambda v, c=c: v.tensor_copy(
                out=attn_fm.ap().rearrange("p c (m a) -> p c m a", a=2)[:, c, :, 0],
                in_=ps_a[:, c * 64:(c + 1) * 64]))
            b.op("vector", lambda v, c=c: v.tensor_copy(
                out=attn_fm.ap().rearrange("p c (m a) -> p c m a", a=2)[:, c, :, 1],
                in_=ps_b[:, c * 64:(c + 1) * 64]))
        b.bar()

        b.op("gpsimd", lambda g: g.dma_start(
            out=gv[:], in_=wo_w.rearrange("(c p) o -> p c o", p=128)), dma=True)
        b.bar()
        for og in range(2):
            for ol in range(4):
                b.op("tensor", lambda t, og=og, ol=ol: [t.matmul(
                    ps_sc[:, ol * 128:(ol + 1) * 128],
                    gv[:, ic, (og * 4 + ol) * 128:(og * 4 + ol + 1) * 128], attn_fm[:, ic, :],
                    start=(ic == 0), stop=(ic == NCH - 1)) for ic in range(NCH)][-1])
            b.bar()
            for ol in range(4):
                oc = og * 4 + ol
                b.op("vector", lambda v, oc=oc, ol=ol: v.tensor_scalar(
                    out=x2s[:, oc, :], in0=ps_sc[:, ol * 128:(ol + 1) * 128],
                    scalar1=wobr[:, oc:oc + 1], scalar2=None, op0=ALU.add))
            b.bar()
            for ol in range(4):
                oc = og * 4 + ol
                b.op("vector", lambda v, oc=oc: v.tensor_tensor(
                    out=x2s[:, oc, :], in0=x2s[:, oc, :], in1=x1s[:, oc, :], op=ALU.add))
            b.bar()
        for c in range(NCH):
            b.op("sync", lambda s, c=c, q0=q0: s.dma_start(
                out=o_x2[c * 128:(c + 1) * 128, q0:q0 + 128], in_=x2s[:, c, :]), dma=True)
        b.bar()






def emit_MKN(nc, b, st, t, pfx=""):
    kT_l = t["kT_l"]
    mk_c = t["mk_c"]
    slab = st.enter_context(nc.sbuf_tensor(pfx + "slab", [128, NCH, 512], BF16))
    mkt = st.enter_context(nc.sbuf_tensor(pfx + "mkt", [1, 64], F32))
    m16 = st.enter_context(nc.sbuf_tensor(pfx + "m16", [1, 128], F32))
    onesb = st.enter_context(nc.sbuf_tensor(pfx + "onesb", [128, 1], BF16))
    ps_r = st.enter_context(nc.psum_tensor(pfx + "ps_r", [1, 512], F32))
    b.op("vector", lambda v: v.memset(onesb[:], 1.0))
    b.op("vector", lambda v: v.memset(m16[:], 0.0))
    b.bar()
    for tt in range(4):
        b.op("sync", lambda s, tt=tt: s.dma_start(
            out=slab[:], in_=kT_l.rearrange("(c p) t -> p c t", p=128)[:, :, tt * 512:(tt + 1) * 512]), dma=True)
        b.bar()
        for c in range(NCH):
            b.op("scalar", lambda s, c=c: s.activation(out=slab[:, c, :], in_=slab[:, c, :], func=AF.Square))
        b.bar()
        for h in range(16):
            b.op("tensor", lambda t2, h=h: t2.matmul(
                ps_r[0:1, :], onesb[64 * (h % 2):64 * (h % 2) + 64, 0:1],
                slab[64 * (h % 2):64 * (h % 2) + 64, h // 2, :], start=True, stop=True))
            b.bar()
            b.op("vector", lambda v, h=h, tt=tt: v.tensor_reduce(
                out=mkt[:, h * 4 + tt:h * 4 + tt + 1], in_=ps_r[0:1, :], axis=AX.X, op=ALU.max))
            b.bar()
    for h in range(16):
        b.op("vector", lambda v, h=h: v.tensor_reduce(
            out=m16[:, h:h + 1], in_=mkt[:, h * 4:(h + 1) * 4], axis=AX.X, op=ALU.max))
    b.bar()
    b.op("sync", lambda s: s.dma_start(out=mk_c[:, :], in_=m16[:, :]), dma=True)
    b.bar()



def emit_CATT(nc, b, st, t, pfx=""):
    qT_l = t["qT_l"]
    g_kT = t["g_kT"]
    g_v = t["g_v"]
    g_mk = t["g_mk"]
    cmask2 = t["cmask2"]
    at_fm = t["at_fm"]

    kslab = st.enter_context(nc.sbuf_tensor(pfx + "kslab", [128, 2, T], BF16))
    vaug = st.enter_context(nc.sbuf_tensor(pfx + "vaug", [128, 64, 4, 65], BF16))
    qs = st.enter_context(nc.sbuf_tensor(pfx + "qs", [128, NCH, 512], BF16))
    qsq = st.enter_context(nc.sbuf_tensor(pfx + "qsq", [128, NCH, 512], BF16))
    cmsk = st.enter_context(nc.sbuf_tensor(pfx + "cmsk", [128, 2, 16, 512], BF16))
    pt = st.enter_context(nc.sbuf_tensor(pfx + "pt", [128, 2, 512], BF16))
    m4 = st.enter_context(nc.sbuf_tensor(pfx + "m4", [1, 512], F32))
    mkh = st.enter_context(nc.sbuf_tensor(pfx + "mkh", [1, 16], F32))
    qn = st.enter_context(nc.sbuf_tensor(pfx + "qn", [1, 16], F32))
    brow = st.enter_context(nc.sbuf_tensor(pfx + "brow", [1, 16], F32))
    biasq = st.enter_context(nc.sbuf_tensor(pfx + "biasq", [128, 16], F32))
    rz = st.enter_context(nc.sbuf_tensor(pfx + "rz", [1, 4, 512], F32))
    rzb = st.enter_context(nc.sbuf_tensor(pfx + "rzb", [64, 4, 512], F32))
    atb = st.enter_context(nc.sbuf_tensor(pfx + "atb", [64, 4, 512], BF16))
    onesb = st.enter_context(nc.sbuf_tensor(pfx + "onesb", [128, 1], BF16))
    onesr = st.enter_context(nc.sbuf_tensor(pfx + "onesr", [1, 128], F32))
    ps_s0 = st.enter_context(nc.psum_tensor(pfx + "ps_s0", [128, 512], F32))
    ps_s1 = st.enter_context(nc.psum_tensor(pfx + "ps_s1", [128, 512], F32))
    ps_pv = [st.enter_context(nc.psum_tensor(pfx + f"ps_pv{i}", [128, 512], F32)) for i in range(4)]
    ps_q = st.enter_context(nc.psum_tensor(pfx + "ps_q", [1, 512], F32))
    ps_bq = st.enter_context(nc.psum_tensor(pfx + "ps_bq", [128, 16], F32))
    se_s = st.enter_context(nc.semaphore(pfx + "se_s"))
    se_e = st.enter_context(nc.semaphore(pfx + "se_e"))
    se_m = st.enter_context(nc.semaphore(pfx + "se_m"))
    psl = [ps_s0, ps_s1]
    cnt = {"s": 0, "e": 0, "m": 0}

    b.op("sync", lambda s: s.dma_start(out=cmsk[:], in_=cmask2.rearrange("p (a r q) -> p a r q", a=2, r=16)), dma=True)
    b.op("sync", lambda s: s.dma_start(out=m4[:], in_=g_mk.rearrange("s w -> (s w)")), dma=True)
    b.op("vector", lambda v: v.memset(onesb[:], 1.0))
    b.op("vector", lambda v: v.memset(onesr[:], 1.0))
    b.op("vector", lambda v: v.memset(vaug.ap()[:, :, :, 64:65], 1.0))
    b.bar()
    for h in range(16):
        b.op("vector", lambda v, h=h: v.tensor_reduce(
            out=mkh[:, h:h + 1], in_=m4.ap().rearrange("o (s w) -> o w s", w=128)[:, h, :],
            axis=AX.X, op=ALU.max))
    b.bar()
    b.op("scalar", lambda s: s.activation(out=mkh[:, :], in_=mkh[:, :], func=AF.Sqrt))
    b.bar()
    b.op("vector", lambda v: v.tensor_scalar(out=mkh[:, :], in0=mkh[:, :], scalar1=-1.0, scalar2=None, op0=ALU.mult))
    b.bar()

    for s4 in range(4):
        nkt = 16 * (s4 + 1)
        b.op("sync", lambda s, s4=s4: s.dma_start(
            out=qs[:], in_=qT_l.rearrange("(c p) t -> p c t", p=128)[:, :, s4 * 512:(s4 + 1) * 512]), dma=True)
        b.bar()
        for c in range(NCH):
            b.op("scalar", lambda s, c=c: s.activation(out=qsq[:, c, :], in_=qs[:, c, :], func=AF.Square))
        b.bar()
        for h in range(16):
            b.op("tensor", lambda t2, h=h: t2.matmul(
                ps_q[0:1, :], onesb[64 * (h % 2):64 * (h % 2) + 64, 0:1],
                qsq[64 * (h % 2):64 * (h % 2) + 64, h // 2, :], start=True, stop=True))
            b.bar()
            b.op("vector", lambda v, h=h: v.tensor_reduce(
                out=qn[:, h:h + 1], in_=ps_q[0:1, :], axis=AX.X, op=ALU.max))
            b.bar()
        b.op("scalar", lambda s: s.activation(out=qn[:, :], in_=qn[:, :], func=AF.Sqrt))
        b.bar()
        b.op("vector", lambda v: v.tensor_tensor(out=brow[:, :], in0=qn[:, :], in1=mkh[:, :], op=ALU.mult))
        b.bar()
        b.op("tensor", lambda t2: t2.matmul(ps_bq[:, :], onesr[0:1, :], brow[0:1, :], start=True, stop=True))
        b.bar()
        b.op("vector", lambda v: v.tensor_copy(out=biasq[:, :], in_=ps_bq[:, :]))
        b.bar()

        for g4 in range(4):
            for j2 in range(4):
                for tt in range(s4 + 1):
                    bglob = 4 * tt + (j2 if tt % 2 == 0 else 3 - j2)
                    b.op("sync", lambda s, j2=j2, tt=tt, bglob=bglob, g4=g4: s.dma_start(
                        out=kslab[:, :, bglob * 512:(bglob + 1) * 512],
                        in_=g_kT.rearrange("(s2 c p) t -> p s2 c t", s2=4, p=128)
                        [:, j2, 2 * g4:2 * g4 + 2, tt * 512:(tt + 1) * 512]), dma=True)
                    for n in range(4):
                        b.op("gpsimd", lambda g, j2=j2, tt=tt, bglob=bglob, g4=g4, n=n: g.dma_start(
                            out=vaug[:, bglob * 4 + n, :, 0:64],
                            in_=g_v.rearrange("(r p) (gg hh d) -> p r gg hh d", p=128, gg=4, hh=4)
                            [:, 16 * j2 + 4 * tt + n, g4, :, :]), dma=True)
            b.bar()
            e_base = cnt["e"]
            pv_ops = []
            for kt in range(nkt):
                for hl in range(4):
                    h = 4 * g4 + hl
                    u = kt * 4 + hl
                    wv = e_base + u - 1 if u >= 2 else None

                    def s_fn(t2, kt=kt, hl=hl, h=h, u=u, wv=wv):
                        if wv is not None:
                            t2.wait_ge(se_e, wv)
                        return t2.matmul(psl[u % 2][:, :],
                                         kslab[64 * (hl % 2):64 * (hl % 2) + 64, hl // 2, kt * 128:(kt + 1) * 128],
                                         qs[64 * (h % 2):64 * (h % 2) + 64, h // 2, :],
                                         start=True, stop=True)
                    b.op("tensor", lambda t2, f=s_fn: (f(t2).then_inc(se_s, 1), None)[1])
                    cnt["s"] += 1

                    def e_fn(s, u=u, h=h, sv=cnt["s"]):
                        s.wait_ge(se_s, sv)
                        return s.activation(out=pt[:, u % 2, :], in_=psl[u % 2][:, :], func=AF.Exp,
                                            bias=biasq[:, h:h + 1])
                    b.op("scalar", lambda s, f=e_fn: (f(s).then_inc(se_e, 1), None)[1])
                    cnt["e"] += 1
                    masked = kt >= 16 * s4
                    if masked:
                        r = kt - 16 * s4

                        def m_fn(v, u=u, r=r, s4=s4, ev=cnt["e"]):
                            v.wait_ge(se_e, ev)
                            return v.tensor_tensor(out=pt[:, u % 2, :], in0=pt[:, u % 2, :],
                                                   in1=cmsk[:, s4 % 2, r, :], op=ALU.mult)
                        b.op("vector", lambda v, f=m_fn: (f(v).then_inc(se_m, 1), None)[1])
                        cnt["m"] += 1
                    wv2 = ("m", cnt["m"]) if masked else ("e", cnt["e"])

                    def pv_fn(t2, kt=kt, hl=hl, u=u, nkt=nkt, wv2=wv2):
                        t2.wait_ge(se_m if wv2[0] == "m" else se_e, wv2[1])
                        return t2.matmul(ps_pv[hl][0:65, :], vaug[:, kt, hl, :], pt[:, u % 2, :],
                                         start=(kt == 0), stop=(kt == nkt - 1))
                    pv_ops.append(pv_fn)
                    if len(pv_ops) >= 2:
                        b.op("tensor", lambda t2, f=pv_ops[-2]: f(t2))
            b.op("tensor", lambda t2, f=pv_ops[-1]: f(t2))
            b.bar()
            for hl in range(4):
                b.op("scalar", lambda s, hl=hl: s.activation(out=rz[:, hl, :], in_=ps_pv[hl][64:65, :],
                                                             func=AF.Copy, bias=1e-30))
            b.bar()
            b.op("vector", lambda v: v.reciprocal(out=rz.ap().rearrange("o a q -> o (a q)"),
                                                  in_=rz.ap().rearrange("o a q -> o (a q)")))
            b.bar()
            for hl in range(4):
                dst = psl[hl % 2]
                base = 64 * (hl // 2)
                b.op("tensor", lambda t2, hl=hl, dst=dst, base=base: t2.matmul(
                    dst[base:base + 64, :], onesr[0:1, 0:64], rz[0:1, hl, :], start=True, stop=True))
            b.bar()
            for hl in range(4):
                dst = psl[hl % 2]
                base = 64 * (hl // 2)
                b.op("vector", lambda v, hl=hl, dst=dst, base=base: v.tensor_copy(
                    out=rzb[:, hl, :], in_=dst[base:base + 64, :]))
            b.bar()
            for hl in range(4):
                b.op("vector", lambda v, hl=hl: v.tensor_tensor(
                    out=atb[:, hl, :], in0=ps_pv[hl][0:64, :], in1=rzb[:, hl, :], op=ALU.mult))
            b.bar()
            for hl in range(4):
                h = 4 * g4 + hl
                b.op("sync", lambda s, hl=hl, h=h, s4=s4: s.dma_start(
                    out=at_fm[64 * h:64 * (h + 1), 512 * s4:512 * (s4 + 1)], in_=atb[:, hl, :]), dma=True)
            b.bar()


L4B_DRAMS = [("qT_d", [D, TPC], BF16, "ExternalInput"),
             ("qiT_d", [IDXD, TPC], F32, "ExternalInput"),
             ("g_ki", [4 * IDXD, TPC], F32, "ExternalInput"),
             ("kv_d", [T, 2 * D], BF16, "ExternalInput"),
             ("x1_d", [D, TPC], F32, "ExternalInput"),
             ("wo_w", [D, D], F32, "ExternalInput"),
             ("wo_b", [1, D], F32, "ExternalInput"),
             ("ident", [128, 128], F32, "ExternalInput"),
             ("identb", [128, 128], F32, "ExternalInput"),
             ("cbase", [128, 960], F32, "ExternalInput"),
             ("o_x2", [D, TPC], F32, "ExternalOutput"),
             ("bK", [16, 512], I16, "Internal"),
             ("bV", [16, 512], I16, "Internal"),
             ("vaDR", [2048, 64], F32, "Internal")]

build_L4b = _mk_build(emit_L4b, L4B_DRAMS, "l4b_")


RG4 = [[0, 1, 2, 3], [4, 5, 6, 7]]

MEGA_EXT_IN = ([("x_fm", [D, TPC], F32),
                ("wqT", [D, D], F32), ("wkT", [D, D], F32), ("wvT", [D, D], F32),
                ("n1_w", [D, 1], F32), ("qn_w", [HD, 1], F32), ("kn_w", [HD, 1], F32),
                ("n2_w", [D, 1], F32), ("esel", [4, 512], F32), ("ident", [128, 128], F32),
                ("wq_w", [D, D], F32), ("wk_w", [D, D], F32), ("wv_w", [D, D], F32),
                ("wqi_w", [D, IDXD], F32), ("wki_w", [D, IDXD], F32),
                ("wq_b", [1, D], F32), ("wk_b", [1, D], F32), ("wv_b", [1, D], F32),
                ("wqi_b", [1, IDXD], F32), ("wki_b", [1, IDXD], F32),
                ("wo_w", [D, D], F32), ("wo_b", [1, D], F32),
                ("cbase", [128, 960], F32), ("cmask2", [128, 2 * 16 * 512], BF16)]
               + [(nm, shp, F32) for nm, shp in L3_W_SHAPES])

MEGA_INTERNAL = [("i_lnx", [D, TPC], F32), ("i_qT", [D, TPC], BF16),
                 ("i_kT", [D, TPC], BF16), ("i_v", [TPC, D], BF16),
                 ("i_mkc", [1, 128], F32), ("i_at", [D, TPC], BF16),
                 ("i_x1", [D, TPC], F32), ("i_x2", [D, TPC], F32),
                 ("i_sqT", [D, TPC], BF16), ("i_skv", [TPC, 2 * D], BF16),
                 ("i_sqi", [IDXD, TPC], F32), ("i_ski", [IDXD, TPC], F32),
                 ("g_kT", [4 * D, TPC], BF16), ("g_v", [4 * TPC, D], BF16),
                 ("g_mk", [4, 128], F32), ("g_kv", [T, 2 * D], BF16),
                 ("g_ki", [4 * IDXD, TPC], F32),
                 ("bK", [16, 512], I16), ("bV", [16, 512], I16), ("vaDR", [2048, 64], F32)]


def build_MEGA(dbg=False, upto=3):
    nc = bacc.Bacc("TRN2", target_bir_lowering=False)
    t = {nm: nc.dram_tensor(nm, shp, dt, kind="ExternalInput").ap() for nm, shp, dt in MEGA_EXT_IN}
    t["o_out"] = nc.dram_tensor("o_out", [D, TPC], F32, kind="ExternalOutput").ap()
    dbg_names = {"i_at", "i_x1", "i_x2", "i_lnx"} if dbg else set()
    for nm, shp, dt in MEGA_INTERNAL:
        kind = "ExternalOutput" if nm in dbg_names else "Internal"
        t[nm] = nc.dram_tensor(nm, shp, dt, kind=kind).ap()
    t["identb"] = t["ident"]

    with ExitStack() as st:
        block = st.enter_context(nc.Block())
        sems = {e: st.enter_context(nc.semaphore("s_" + e)) for e in ENGS}
        b = Builder(nc)

        def cc(src, dst):
            b.op("gpsimd", lambda g, src=src, dst=dst: g.collective_compute(
                "AllGather", ALU.bypass, replica_groups=RG4, ins=[src.opt()], outs=[dst.opt()]))

        for half in (0, 1):
            if half == 1 and upto < 3:
                break
            xin = t["x_fm"] if half == 0 else t["i_x2"]
            xout = t["i_x1"] if half == 0 else t["o_out"]
            with ExitStack() as s1:
                emit_L1(nc, b, s1, {**t, "x_fm": xin, "o_lnx": t["i_lnx"], "o_qT": t["i_qT"],
                                    "o_kT": t["i_kT"], "o_v": t["i_v"]}, pfx=f"h{half}l1_")
            b.bar()
            with ExitStack() as s2:
                emit_MKN(nc, b, s2, {"kT_l": t["i_kT"], "mk_c": t["i_mkc"]}, pfx=f"h{half}mk_")
            b.bar()
            cc(t["i_kT"], t["g_kT"])
            b.bar()
            cc(t["i_v"], t["g_v"])
            b.bar()
            cc(t["i_mkc"], t["g_mk"])
            b.bar()
            with ExitStack() as s3:
                emit_CATT(nc, b, s3, {"qT_l": t["i_qT"], "g_kT": t["g_kT"], "g_v": t["g_v"],
                                      "g_mk": t["g_mk"], "cmask2": t["cmask2"],
                                      "at_fm": t["i_at"]}, pfx=f"h{half}ca_")
            b.bar()
            with ExitStack() as s4:
                emit_L3(nc, b, s4, {**t, "at_fm": t["i_at"], "lnx_fm": t["i_lnx"],
                                    "x0_fm": xin, "o_x1": xout}, pfx=f"h{half}l3_")
            b.bar()
            if half == 0 and upto >= 2:
                with ExitStack() as s5:
                    emit_L4a(nc, b, s5, {**t, "x_fm": t["i_x1"], "o_qT": t["i_sqT"],
                                         "o_kv": t["i_skv"], "o_qi": t["i_sqi"],
                                         "o_ki": t["i_ski"]}, pfx="sa_")
                b.bar()
                cc(t["i_skv"], t["g_kv"])
                b.bar()
                cc(t["i_ski"], t["g_ki"])
                b.bar()
                with ExitStack() as s6:
                    emit_L4b(nc, b, s6, {**t, "qT_d": t["i_sqT"], "qiT_d": t["i_sqi"],
                                         "kv_d": t["g_kv"], "x1_d": t["i_x1"],
                                         "o_x2": t["i_x2"]}, pfx="sb_")
                b.bar()
        b.emit(block, sems)
    nc.finalize()
    return nc


def _cmask2_host(j):
    m = np.zeros((128, 2, 16, 512), dtype=BF)
    qv = np.arange(512)[None, :]
    kv = np.arange(128)[:, None]
    for par in range(2):
        jj = j if par == 0 else 3 - j
        for r in range(16):
            if r < 4 * jj:
                m[:, par, r, :] = 1.0
            elif r < 4 * jj + 4:
                rr = r - 4 * jj
                m[:, par, r, :] = (qv >= kv + 128 * rr).astype(BF)
    return np.ascontiguousarray(m.reshape(128, 2 * 16 * 512))


def _mega_in_maps(inputs):
    g = host_prep(inputs)
    perm = _stripe_perm()
    esel = _const_esel()
    cb = _const_cbase()
    x0 = inputs["L0_attn_x"].reshape(NTOK, D)
    shared = {"wqT": g["f_wq"], "wkT": g["f_wk"], "wvT": g["f_wv"],
              "n1_w": g["f_n1_w"], "qn_w": g["f_qn_w"], "kn_w": g["f_kn_w"],
              "n2_w": g["f_n2_w"], "esel": esel, "ident": g["identity"],
              "woT": g["f_wo"], "gateT": g["f_gate_w"],
              "wo_w": g["a_wo_w"], "wo_b": g["a_wo_b"].reshape(1, -1),
              "cbase": cb}
    for nm in ["e0_up", "e0_down", "e1_w1", "e1_w2", "e1_down", "e2_l1",
               "e2_l2", "e2_l3", "e2_l4", "e3_down", "e3_up", "e3_out"]:
        shared[nm + "T"] = g["f_" + nm]
    for nm in ["wq_w", "wk_w", "wv_w", "wqi_w", "wki_w"]:
        shared[nm] = g["a_" + nm]
    for nm in ["wq_b", "wk_b", "wv_b", "wqi_b", "wki_b"]:
        shared[nm] = g["a_" + nm].reshape(1, -1)
    maps = []
    for c in range(NCORES):
        bb, j = c // 4, c % 4
        rows = bb * T + perm[j * TPC:(j + 1) * TPC]
        maps.append({**shared,
                     "x_fm": np.ascontiguousarray(x0[rows].T),
                     "cmask2": _cmask2_host(j)})
    return maps, perm


def _const_cmask():
    cmq = np.zeros((128, 4, 512), dtype=BF)
    for r in range(4):
        for kk in range(128):
            cmq[kk, r, :] = (np.arange(512) >= kk + 128 * r).astype(BF)
    return np.ascontiguousarray(cmq.reshape(128, 4 * 512))


def _const_esel():
    esel = np.zeros((4, 512), dtype=f32)
    for e in range(4):
        esel[e, e * 128:(e + 1) * 128] = 1.0
    return esel


def _const_cbase():
    cb = np.zeros((128, 960), dtype=f32)
    for c in range(120):
        cb[:, c * 8:(c + 1) * 8] = c * 64 + 1
    return cb


def _stripe_perm():
    perm = np.empty(T, dtype=np.int64)
    for j2 in range(4):
        for tt in range(4):
            blk = 4 * tt + (j2 if tt % 2 == 0 else 3 - j2)
            r0 = j2 * TPC + tt * 512
            perm[r0:r0 + 512] = np.arange(blk * 512, (blk + 1) * 512)
    return perm


def kernel(**inputs):
    maps, perm = _mega_in_maps(inputs)
    nc = build_MEGA()
    res = run_bass_kernel_spmd(nc, maps, core_ids=list(range(NCORES))).results
    out = np.empty((NTOK, D), np.float32)
    for c in range(NCORES):
        bb, j = c // 4, c % 4
        rows = bb * T + perm[j * TPC:(j + 1) * TPC]
        out[rows] = res[c]["o_out"].T
    return np.ascontiguousarray(out.reshape(B, T, D))


def kernel_multilaunch(**inputs):
    g = host_prep(inputs)
    cmask = _const_cmask()
    esel = _const_esel()
    cb = _const_cbase()
    nc1 = build_L1()
    nc2 = build_L2()
    nc3 = build_L3()
    nc4a = build_L4a()
    nc4b = build_L4b()

    def run(nc, maps):
        return run_bass_kernel_spmd(nc, maps, core_ids=list(range(NCORES))).results

    l1_w = {"wqT": g["f_wq"], "wkT": g["f_wk"], "wvT": g["f_wv"],
            "n1_w": g["f_n1_w"], "qn_w": g["f_qn_w"], "kn_w": g["f_kn_w"]}
    l3_w = {"n2_w": g["f_n2_w"], "esel": esel, "ident": g["identity"],
            "woT": g["f_wo"], "gateT": g["f_gate_w"]}
    for nm in ["e0_up", "e0_down", "e1_w1", "e1_w2", "e1_down", "e2_l1",
               "e2_l2", "e2_l3", "e2_l4", "e3_down", "e3_up", "e3_out"]:
        l3_w[nm + "T"] = g["f_" + nm]
    l4a_w = {}
    for nm in ["wq_w", "wk_w", "wv_w", "wqi_w", "wki_w"]:
        l4a_w[nm] = g["a_" + nm]
    for nm in ["wq_b", "wk_b", "wv_b", "wqi_b", "wki_b"]:
        l4a_w[nm] = g["a_" + nm].reshape(1, -1)

    def ffn_pass(x_sh):
        r1 = run(nc1, [{"x_fm": x_sh[c], **l1_w} for c in range(NCORES)])
        qT_full = [np.concatenate([r1[bb * 4 + j]["o_qT"] for j in range(4)], axis=1) for bb in range(B)]
        kT_full = [np.concatenate([r1[bb * 4 + j]["o_kT"] for j in range(4)], axis=1) for bb in range(B)]
        v_full = [np.concatenate([r1[bb * 4 + j]["o_v"] for j in range(4)], axis=0) for bb in range(B)]
        m2 = []
        for c in range(NCORES):
            bb = c // 4
            hs = [4 * (c % 4) + i for i in range(4)]
            m2.append({
                "qT_p": np.stack([qT_full[bb][h * HD:(h + 1) * HD] for h in hs]),
                "kT_p": np.stack([kT_full[bb][h * HD:(h + 1) * HD] for h in hs]),
                "v_p": np.stack([np.ascontiguousarray(v_full[bb][:, h * HD:(h + 1) * HD]) for h in hs]),
                "cmask": cmask})
        r2 = run(nc2, m2)
        m3 = []
        for c in range(NCORES):
            bb = c // 4
            t0 = (c * TPC) % T
            at = np.empty((D, TPC), dtype=BF)
            for h in range(NH):
                at[h * HD:(h + 1) * HD] = r2[bb * 4 + h // 4]["o_at"][h % 4][t0:t0 + TPC].T
            m3.append({"at_fm": at, "lnx_fm": r1[c]["o_lnx"], "x0_fm": x_sh[c], **l3_w})
        r3 = run(nc3, m3)
        return [r3[c]["o_x1"] for c in range(NCORES)]

    x0 = inputs["L0_attn_x"].reshape(NTOK, D)
    x0_sh = [np.ascontiguousarray(x0[c * TPC:(c + 1) * TPC].T) for c in range(NCORES)]
    x1_sh = ffn_pass(x0_sh)

    r4a = run(nc4a, [{"x_fm": x1_sh[c], **l4a_w} for c in range(NCORES)])
    perm = _stripe_perm()
    kv_full, ki_full = [], []
    for bb in range(B):
        kv = np.concatenate([r4a[bb * 4 + j]["o_kv"] for j in range(4)], axis=0)
        kv_full.append(np.ascontiguousarray(kv[perm]))
        ki = np.concatenate([r4a[bb * 4 + j]["o_ki"] for j in range(4)], axis=1)
        ki_full.append(np.ascontiguousarray(np.concatenate(
            [ki[:, perm[TPC * j2:TPC * (j2 + 1)]] for j2 in range(4)], axis=0)))
    m4b = []
    for c in range(NCORES):
        bb = c // 4
        m4b.append({"qT_d": r4a[c]["o_qT"], "qiT_d": r4a[c]["o_qi"], "g_ki": ki_full[bb],
                    "kv_d": kv_full[bb], "x1_d": x1_sh[c],
                    "wo_w": g["a_wo_w"], "wo_b": g["a_wo_b"].reshape(1, -1),
                    "ident": g["identity"], "identb": g["identity"], "cbase": cb})
    r4b = run(nc4b, m4b)
    x2_sh = [np.ascontiguousarray(r4b[c]["o_x2"]) for c in range(NCORES)]

    x3_sh = ffn_pass(x2_sh)
    out = np.concatenate([s.T for s in x3_sh], axis=0).reshape(B, T, D)
    return np.ascontiguousarray(out.astype(np.float32))

